# version 02
# baseline (speedup 1.0000x reference)
import numpy as np
import ml_dtypes
from contextlib import ExitStack

import concourse.bass as bass
import concourse.mybir as mybir
from concourse.bass_utils import run_bass_kernel_spmd

F32 = mybir.dt.float32
BF16 = mybir.dt.bfloat16
FP8 = mybir.dt.float8e4
U8 = mybir.dt.uint8
ALU = mybir.AluOpType
ACTF = mybir.ActivationFunctionType
AX = mybir.AxisListType

P = 128
D = 2048
NT = 16
TOK = 2048
CH = 512
NCH = TOK // CH
INTER = 8192
NJ = 64
DH = 512
NDH = 4
K1 = 1.0 / (np.sqrt(2048.0) * 64.0)
K2 = 1.0 / (np.sqrt(8192.0) * 64.0)
EPS = 1e-5
ARENA = 193536


def _butterfly_levels(n):
    out = []
    h = 1
    while h < n:
        pairs = []
        g = 0
        while g < n:
            for k in range(h):
                pairs.append((g + k, g + h + k))
            g += 2 * h
        out.append(pairs)
        h *= 2
    return out


class _B:
    pass


def _mk():
    b = _B()
    import os as _os
    nc = bass.Bass("TRN2", target_bir_lowering=False,
                   detect_race_conditions=_os.environ.get("KERNEL_RACE_DETECT", "0") == "1")
    b.nc = nc
    b.stack = ExitStack()
    dr = lambda name, shape, dt, kind="Internal": nc.dram_tensor(name, shape, dt, kind=kind)
    b.xT = dr("xT", [NT, P, TOK], F32, "ExternalInput").ap()
    b.flags = dr("flags", [1, 64], F32, "ExternalInput").ap()
    b.wsl = dr("wsl", [32, P, 1024], F32, "ExternalInput").ap()
    b.outT = dr("outT", [NT, P, TOK], F32, "ExternalOutput").ap()
    b.dbg = dr("dbg", [1, 128], F32, "ExternalOutput").ap()
    b.wvT = dr("wvT", [16, P, NT, P], F32, "ExternalInput").ap()
    b.woT = dr("woT", [16, P, NT, P], F32, "ExternalInput").ap()
    b.bv0 = dr("bv0", [1, D], F32, "ExternalInput").ap()
    b.bo0 = dr("bo0", [1, D], F32, "ExternalInput").ap()
    b.wupT = dr("wupT", [NJ, P, NT, P], F32, "ExternalInput").ap()
    b.wgT = dr("wgT", [NJ, P, NT, P], F32, "ExternalInput").ap()
    b.wdT = dr("wdT", [NT, P, NJ, P], F32, "ExternalInput").ap()
    b.su_ug = dr("su_ug", [1, D], F32, "ExternalInput").ap()
    b.su_dp = dr("su_dp", [1, INTER], F32, "ExternalInput").ap()
    b.ws_ug = dr("ws_ug", [1, 2 * INTER], F32, "ExternalInput").ap()
    b.ws_dp = dr("ws_dp", [1, D], F32, "ExternalInput").ap()
    b.wuT = dr("wuT", [NT, P, NT, P], F32, "ExternalInput").ap()
    b.wvT1 = dr("wvT1", [NT, P, NT, P], F32, "ExternalInput").ap()
    b.wg1T = dr("wg1T", [NDH, P, NT, P], F32, "ExternalInput").ap()
    b.wg2T = dr("wg2T", [NT, P, NDH, P], F32, "ExternalInput").ap()
    b.bu1 = dr("bu1", [1, D], F32, "ExternalInput").ap()
    b.bv1 = dr("bv1", [1, D], F32, "ExternalInput").ap()
    b.bg1 = dr("bg1", [1, DH], F32, "ExternalInput").ap()
    b.bg2 = dr("bg2", [1, D], F32, "ExternalInput").ap()
    b.convw = dr("convw", [1, 3 * D], F32, "ExternalInput").ap()
    b.convb = dr("convb", [1, D], F32, "ExternalInput").ap()
    b.wfcT = dr("wfcT", [NJ, P, NT, P], F32, "ExternalInput").ap()
    b.wpjT = dr("wpjT", [NT, P, NJ, P], F32, "ExternalInput").ap()
    b.resid = dr("resid", [NT, P, TOK], F32).ap()
    b.zdr = dr("zdr", [NJ, P, TOK], BF16).ap()
    b.hdr = dr("hdr", [NJ, P, TOK], BF16).ap()
    b.xlndr = dr("xlndr", [NT, P, TOK], BF16).ap()
    b.gatesdr = dr("gatesdr", [NT, P, TOK], BF16).ap()
    b.kcdr = dr("kcdr", [NT, P, TOK], BF16).ap()
    b.xln4dr = dr("xln4dr", [NT, P, TOK], F32).ap()
    b.ydr = dr("ydr", [NT, P, TOK], BF16).ap()
    b.cc1i = dr("cc1i", [P, 48], F32)
    b.cc1o = dr("cc1o", [P, 48], F32)
    b.cc2i = dr("cc2i", [1, 32], F32)
    b.cc2o = dr("cc2o", [1, 32], F32)
    b.cc3i = dr("cc3i", [1, 32], F32)
    b.cc3o = dr("cc3o", [1, 32], F32)
    b.ccmb = dr("ccmb", [1, 32], F32)
    b.ccmc = dr("ccmc", [1, 32], F32)
    Hm = np.array([[1.0]])
    while Hm.shape[0] < P:
        Hm = np.block([[Hm, Hm], [Hm, -Hm]])
    b.h128c = nc.inline_tensor(np.ascontiguousarray(Hm.astype(ml_dtypes.bfloat16)), "h128c").ap()
    b.eyec = nc.inline_tensor(np.eye(P, dtype=np.float32), "eyec").ap()

    st = b.stack
    sb = lambda name, shape, dt: st.enter_context(nc.sbuf_tensor(name, shape, dt))
    b.arena = sb("arena", [P, ARENA], U8)
    b.gbias = sb("gbias", [P, 416], F32)
    b.gb_off = {}
    b.gb_next = 0
    b.grows = sb("grows", [1, 2 * CH], F32)
    b.grows2 = sb("grows2", [1, CH], F32)
    b.gflags = sb("gflags", [1, 64], F32)
    b.gwvp = sb("gwvp", [P, NT * P], BF16)
    b.gmo = sb("gmo", [P, NT * P], BF16)
    b.gh128 = sb("gh128", [P, P], BF16)
    b.gh128su = sb("gh128su", [P, 2 * P], BF16)
    b.geye = sb("geye", [P, P], F32)
    b.gones_bf = sb("gones_bf", [P, 1], BF16)
    b.ginv_bf = sb("ginv_bf", [P, 1], BF16)
    b.gones_row = sb("gones_row", [1, P], F32)
    b.gstkc = sb("gstkc", [P, NT], F32)
    b.ghalo = sb("ghalo", [P, 48], F32)
    b.gedge = sb("gedge", [P, 4 * NT], F32)
    b.gscal = sb("gscal", [P, 8], F32)
    b.gmax = sb("gmax", [P, 8], F32)
    b.gcc2 = sb("gcc2", [1, 32], F32)
    b.grow1 = sb("grow1", [1, 16], F32)
    b.gjunk = sb("gjunk", [1, 8], F32)
    b.pbig0 = st.enter_context(nc.psum_tensor("pbig0", [P, 4 * CH], F32))
    b.pbig1 = st.enter_context(nc.psum_tensor("pbig1", [P, 4 * CH], F32))
    sem = lambda name: st.enter_context(nc.semaphore(name))
    b.sW = sem("sW")
    b.sD = sem("sD")
    b.sPE = sem("sPE")
    b.sACT = sem("sACT")
    b.sDVE = sem("sDVE")
    b.sCC = sem("sCC")
    b.cW = b.cD = b.cPE = b.cACT = b.cDVE = b.cCC = 0
    return b


def gb_alloc(b, name, n):
    off = b.gb_next
    assert off + n <= 416, (name, off, n)
    b.gb_off[name] = (off, n)
    b.gb_next = off + n


def gb(b, name, i=0, n=1):
    off, tot = b.gb_off[name]
    assert i + n <= tot, (name, i, n)
    return b.gbias[:, off + i: off + i + n]


def ar(b, off, dt, cols):
    esz = {F32: 4, BF16: 2, FP8: 1}[dt]
    assert off % 512 == 0 and off + cols * esz <= ARENA, (off, cols)
    return b.arena[:, off: off + cols * esz].bitcast(dt)


def vpad(b, v):
    v.tensor_copy(b.gjunk[:, 4:6], b.gjunk[:, 0:2])
    v.tensor_copy(b.gjunk[:, 6:8], b.gjunk[:, 0:2])


def wait_all(b, e):
    if b.cD:
        e.wait_ge(b.sD, b.cD)
    if b.cW:
        e.wait_ge(b.sW, b.cW)


def _t3(ap, nt=NT):
    return ap.rearrange("l (T t) -> l T t", T=nt)


def emit_setup(b):
    nc = b.nc
    for nm, n in [("bo", NT), ("bu", NT), ("bv1", NT), ("bg2", NT), ("convb", NT),
                  ("bg1", NDH), ("bvp", 1), ("sug", NT), ("sdp", NJ),
                  ("wsu", NJ), ("wsg", NJ), ("wsd", NT), ("cw", 3 * NT),
                  ("bvtmp", NT), ("isodd", 1), ("iseven", 1)]:
        gb_alloc(b, nm, n)
    wtmp = ar(b, 0, F32, NT * P)
    wacc = ar(b, 16384, F32, NT * P)
    wred = ar(b, 32768, F32, P)

    with nc.Block() as blk, nc.allow_non_contiguous_dma(reason="tiny one-time bias loads"):
        @blk.sync
        def _(s):
            def ld(dst, src):
                s.dma_start(out=dst, in_=src).then_inc(b.sW, 16)
                b.cW += 16
            ld(b.gflags[:, :], b.flags)
            ld(b.gh128[:, :], b.h128c)
            ld(b.geye[:, :], b.eyec)
            def ldb(nm, src, ntiles):
                ld(gb(b, nm, 0, ntiles), src.rearrange("o (T l) -> l (o T)", l=P))
            ldb("bo", b.bo0, NT)
            ldb("bu", b.bu1, NT)
            ldb("bv1", b.bv1, NT)
            ldb("bg2", b.bg2, NT)
            ldb("convb", b.convb, NT)
            ldb("bg1", b.bg1, NDH)
            ldb("bvtmp", b.bv0, NT)
            ldb("sug", b.su_ug, NT)
            ldb("sdp", b.su_dp, NJ)
            ldb("wsd", b.ws_dp, NT)
            ld(gb(b, "wsu", 0, NJ), b.ws_ug[:, 0:INTER].rearrange("o (T l) -> l (o T)", l=P))
            ld(gb(b, "wsg", 0, NJ), b.ws_ug[:, INTER:].rearrange("o (T l) -> l (o T)", l=P))
            ld(gb(b, "cw", 0, 3 * NT), b.convw.rearrange("o (k T l) -> l (o k T)", l=P, T=NT))

        @blk.vector
        def _(v):
            v.memset(b.gones_bf[:, :], 1.0)
            v.memset(b.ginv_bf[:, :], 1.0 / 2048.0)
            v.memset(b.gones_row[:, :], 1.0)
            v.memset(b.gstkc[:, :], 0.0)
            v.memset(b.gmax[:, :], 0.0)
            v.memset(b.gedge[:, :], 0.0)
            v.memset(b.gscal[:, :], 0.0)
            v.memset(b.grow1[:, :], 0.0)
            v.memset(b.gjunk[:, :], 0.0)

    with nc.Block() as blk:
        @blk.vector
        def _(v):
            wait_all(b, v)
            for nm in ("wsu", "wsg", "wsd"):
                n = b.gb_off[nm][1]
                v.tensor_single_scalar(gb(b, nm, 0, n), gb(b, nm, 0, n), 64.0, ALU.mult)
            for nm, k in (("sug", K1), ("sdp", K2)):
                n = b.gb_off[nm][1]
                v.tensor_single_scalar(gb(b, nm, 0, n), gb(b, nm, 0, n), float(k), ALU.mult)
            v.reduce_sum(gb(b, "bvp"), gb(b, "bvtmp", 0, NT), AX.X)

    with nc.Block() as blk:
        @blk.tensor
        def _(t):
            t.matmul(b.pbig0[:, 0:2], b.gones_row[:, :], b.gflags[:, 0:2], start=True, stop=True)
    with nc.Block() as blk:
        @blk.vector
        def _(v):
            v.tensor_copy(gb(b, "isodd"), b.pbig0[:, 0:1])
            v.tensor_copy(gb(b, "iseven"), b.pbig0[:, 1:2])

    for h in range(16):
        with nc.Block() as blk:
            @blk.sync
            def _(s):
                s.dma_start(out=wtmp, in_=b.wvT[h].rearrange("l T c -> l (T c)")).then_inc(b.sW, 16)
                b.cW += 16

            @blk.vector
            def _(v):
                v.wait_ge(b.sW, b.cW)
                if h == 0:
                    v.tensor_copy(wacc, wtmp)
                else:
                    v.tensor_add(wacc, wacc, wtmp)
    with nc.Block() as blk:
        @blk.vector
        def _(v):
            v.tensor_copy(b.gwvp[:, :], wacc)
    for ob in range(16):
        with nc.Block() as blk:
            @blk.sync
            def _(s):
                s.dma_start(out=wtmp, in_=b.woT[ob].rearrange("l T c -> l (T c)")).then_inc(b.sW, 16)
                b.cW += 16

            @blk.vector
            def _(v):
                v.wait_ge(b.sW, b.cW)
                v.reduce_sum(wred, wtmp.rearrange("l (T c) -> l c T", T=NT), AX.X)
                vpad(b, v)
                v.tensor_copy(b.gmo[:, ob * P:(ob + 1) * P], wred)
    for p in range(32):
        with nc.Block() as blk:
            @blk.sync
            def _(s):
                s.dma_start(out=ar(b, 0, F32, 1024), in_=b.wsl[p]).then_inc(b.sW, 16)
                b.cW += 16
                s.wait_ge(b.sW, b.cW)

            @blk.vector
            def _(v):
                v.wait_ge(b.sW, b.cW)
                col = 3 if p < 16 else 4
                v.tensor_reduce(b.gmax[:, 2:3], ar(b, 0, F32, 1024), AX.X, ALU.max,
                                apply_absolute_value=True)
                vpad(b, v)
                v.tensor_tensor(b.gmax[:, col:col + 1], b.gmax[:, col:col + 1], b.gmax[:, 2:3], ALU.max)


def emit_stat_rows_and_rep(b):
    nc = b.nc
    with nc.Block() as blk:
        @blk.vector
        def _(v):
            g2 = b.grows2
            tmp = b.grows[:, 0:CH]
            v.tensor_copy(g2[:, 0:CH], b.pbig0[0:1, 0:CH])
            v.tensor_tensor(tmp, g2[:, 0:CH], g2[:, 0:CH], ALU.mult)
            v.tensor_tensor(tmp, b.pbig0[0:1, CH:2 * CH], tmp, ALU.subtract)
            v.tensor_single_scalar(tmp, tmp, EPS, ALU.add)
            v.reciprocal(b.grows[:, CH:2 * CH], tmp)
    with nc.Block() as blk:
        @blk.scalar
        def _(a):
            a.activation(b.grows[:, 0:CH], b.grows[:, CH:2 * CH], ACTF.Sqrt)
    with nc.Block() as blk:
        @blk.vector
        def _(v):
            v.tensor_tensor(b.grows[:, CH:2 * CH], b.grows2[:, 0:CH], b.grows[:, 0:CH], ALU.mult)
            v.tensor_single_scalar(b.grows[:, CH:2 * CH], b.grows[:, CH:2 * CH], -1.0, ALU.mult)
    with nc.Block() as blk:
        @blk.tensor
        def _(t):
            t.matmul(b.pbig0[:, 2 * CH:3 * CH], b.gones_row[:, :], b.grows[:, 0:CH], start=True, stop=True)
            t.matmul(b.pbig0[:, 3 * CH:4 * CH], b.gones_row[:, :], b.grows[:, CH:2 * CH], start=True, stop=True)


def emit_stats_from(b, xb, x2b, nt=NT):
    nc = b.nc
    with nc.Block() as blk:
        @blk.tensor
        def _(t):
            for T in range(nt):
                t.matmul(b.pbig0[0:1, 0:CH], b.ginv_bf[:, 0:1], xb[:, T * CH:(T + 1) * CH],
                         start=(T == 0), stop=(T == nt - 1))
            for T in range(nt):
                t.matmul(b.pbig0[0:1, CH:2 * CH], b.ginv_bf[:, 0:1], x2b[:, T * CH:(T + 1) * CH],
                         start=(T == 0), stop=(T == nt - 1))
    emit_stat_rows_and_rep(b)


def emit_preln(b, c, src_dram):
    nc = b.nc
    c0 = c * CH
    rc = ar(b, 0, F32, NT * CH)
    xb = ar(b, 33280, BF16, NT * CH)
    x2b = ar(b, 50176, BF16, NT * CH)
    with nc.Block() as blk:
        @blk.gpsimd
        def _(g):
            wait_all(b, g)
            g.dma_start(out=_t3(rc), in_=src_dram[:, :, c0:c0 + CH].rearrange("T l t -> l T t")).then_inc(b.sD, 16)
            b.cD += 16
    with nc.Block() as blk:
        @blk.vector
        def _(v):
            v.wait_ge(b.sD, b.cD)
            v.tensor_copy(xb, rc)

        @blk.scalar
        def _(a):
            a.wait_ge(b.sD, b.cD)
            a.activation(x2b, rc, ACTF.Square)
    emit_stats_from(b, xb, x2b)


def emit_affine_to(b, srcf32, dst, nt=NT):
    nc = b.nc
    with nc.Block() as blk:
        @blk.vector
        def _(v):
            for T in range(nt):
                sl = slice(T * CH, (T + 1) * CH)
                v.tensor_tensor(dst[:, sl], srcf32[:, sl], b.pbig0[:, 2 * CH:3 * CH], ALU.mult)
            for T in range(nt):
                sl = slice(T * CH, (T + 1) * CH)
                v.tensor_tensor(dst[:, sl], dst[:, sl], b.pbig0[:, 3 * CH:4 * CH], ALU.add)


def emit_postln_resid(b, c, ybuf, rc, t1, dst_dram, nt=NT):
    nc = b.nc
    c0 = c * CH
    with nc.Block() as blk:
        @blk.vector
        def _(v):
            for T in range(nt):
                sl = slice(T * CH, (T + 1) * CH)
                v.tensor_tensor(t1[:, sl], ybuf[:, sl], b.pbig0[:, 2 * CH:3 * CH], ALU.mult)
            for T in range(nt):
                sl = slice(T * CH, (T + 1) * CH)
                v.tensor_tensor(t1[:, sl], t1[:, sl], b.pbig0[:, 3 * CH:4 * CH], ALU.add)
            for T in range(nt):
                sl = slice(T * CH, (T + 1) * CH)
                v.tensor_tensor(t1[:, sl], t1[:, sl], rc[:, sl], ALU.add)
    with nc.Block() as blk:
        @blk.gpsimd
        def _(g):
            g.dma_start(out=dst_dram[:, :, c0:c0 + CH].rearrange("T l t -> l T t"), in_=_t3(t1)).then_inc(b.sD, 16)
            b.cD += 16


def emit_l0_attn(b):
    nc = b.nc
    XLN, Y, SQ, WB, T1 = 67584, 84480, 101376, 102400, 103424
    for c in range(NCH):
        emit_preln(b, c, b.xT)
        rc = ar(b, 0, F32, NT * CH)
        xln = ar(b, XLN, BF16, NT * CH)
        emit_affine_to(b, rc, xln)
        ybuf = ar(b, Y, BF16, NT * CH)
        sq = ar(b, SQ, BF16, CH)
        wb = ar(b, WB, BF16, CH)
        t1 = ar(b, T1, F32, NT * CH)
        with nc.Block() as blk:
            @blk.tensor
            def _(t):
                for T in range(NT):
                    t.matmul(b.pbig1[:, 0:CH], b.gwvp[:, T * P:(T + 1) * P], xln[:, T * CH:(T + 1) * CH],
                             start=(T == 0), stop=(T == NT - 1))
        with nc.Block() as blk:
            @blk.scalar
            def _(a):
                a.activation(wb, b.pbig1[:, 0:CH], ACTF.Identity, bias=gb(b, "bvp"))
        for ob in range(NT):
            with nc.Block() as blk:
                @blk.tensor
                def _(t):
                    t.matmul(b.pbig1[:, CH:2 * CH], b.gmo[:, ob * P:(ob + 1) * P], wb, start=True, stop=True)
            with nc.Block() as blk:
                @blk.scalar
                def _(a):
                    a.activation(ybuf[:, ob * CH:(ob + 1) * CH], b.pbig1[:, CH:2 * CH], ACTF.Identity,
                                 bias=gb(b, "bo", ob))
            with nc.Block() as blk:
                @blk.scalar
                def _(a):
                    a.activation(sq, ybuf[:, ob * CH:(ob + 1) * CH], ACTF.Square)
            with nc.Block() as blk:
                @blk.tensor
                def _(t):
                    t.matmul(b.pbig0[0:1, 0:CH], b.ginv_bf[:, 0:1], ybuf[:, ob * CH:(ob + 1) * CH],
                             start=(ob == 0), stop=(ob == NT - 1))
                    t.matmul(b.pbig0[0:1, CH:2 * CH], b.ginv_bf[:, 0:1], sq,
                             start=(ob == 0), stop=(ob == NT - 1))
        emit_stat_rows_and_rep(b)
        emit_postln_resid(b, c, ybuf, rc, t1, b.resid)


def emit_l0_ffn(b):
    nc = b.nc
    XQ, BUFA, BUFB = 101376, 67584, 84480
    xq = ar(b, XQ, BF16, NT * TOK)
    for c in range(NCH):
        emit_preln(b, c, b.resid)
        rc = ar(b, 0, F32, NT * CH)
        bufa = ar(b, BUFA, BF16, NT * CH)
        bufb = ar(b, BUFB, BF16, NT * CH)
        emit_affine_to(b, rc, bufb)
        with nc.Block() as blk:
            @blk.vector
            def _(v):
                for T in range(NT):
                    sl = slice(T * CH, (T + 1) * CH)
                    v.tensor_scalar(bufa[:, sl], bufb[:, sl], gb(b, "sug", T), None, ALU.mult)
        bufs, cur = [bufa, bufb], 0
        for pairs in _butterfly_levels(NT):
            src, dst = bufs[cur], bufs[1 - cur]
            with nc.Block() as blk:
                @blk.vector
                def _(v):
                    for (i, j) in pairs:
                        v.tensor_tensor(dst[:, i * CH:(i + 1) * CH], src[:, i * CH:(i + 1) * CH],
                                        src[:, j * CH:(j + 1) * CH], ALU.add)
                        v.tensor_tensor(dst[:, j * CH:(j + 1) * CH], src[:, i * CH:(i + 1) * CH],
                                        src[:, j * CH:(j + 1) * CH], ALU.subtract)
            cur = 1 - cur
        final = bufs[cur]
        for grp in range(NT // 4):
            with nc.Block() as blk:
                @blk.tensor
                def _(t):
                    for k in range(4):
                        T = grp * 4 + k
                        t.matmul(b.pbig1[:, k * CH:(k + 1) * CH], b.gh128[:, :],
                                 final[:, T * CH:(T + 1) * CH], start=True, stop=True)
            with nc.Block() as blk:
                @blk.scalar
                def _(a):
                    for k in range(4):
                        T = grp * 4 + k
                        a.activation(xq[:, T * TOK + c * CH: T * TOK + c * CH + CH],
                                     b.pbig1[:, k * CH:(k + 1) * CH], ACTF.Copy)
    if b.sub_l0ffn >= 2:
        emit_a2(b, xq)
    if b.sub_l0ffn >= 3:
        emit_a3(b)


def emit_a2_serial(b, xq):
    nc = b.nc
    WUP = [0, 8192]
    WG = [16384, 24576]
    WUPB = [32768, 36864]
    WGB = [40960, 45056]
    SILU = 49152
    XMID = 53248
    ZSB = [57344, 61440]
    for j in range(NJ):
        with nc.Block() as blk:
            @blk.sync
            def _(s):
                if j == 0:
                    wait_all(b, s)
                s.dma_start(out=ar(b, WUP[0], F32, NT * P),
                            in_=b.wupT[j].rearrange("l T c -> l (T c)")).then_inc(b.sW, 16)
                s.dma_start(out=ar(b, WG[0], F32, NT * P),
                            in_=b.wgT[j].rearrange("l T c -> l (T c)")).then_inc(b.sW, 16)
                b.cW += 32
                s.wait_ge(b.sW, b.cW)

            @blk.vector
            def _(v):
                v.wait_ge(b.sW, b.cW)
                v.tensor_copy(ar(b, WUPB[0], BF16, NT * P), ar(b, WUP[0], F32, NT * P))
                v.tensor_copy(ar(b, WGB[0], BF16, NT * P), ar(b, WG[0], F32, NT * P))
                v.tensor_scalar(b.gh128su[:, 0:P], b.gh128[:, :], gb(b, "sdp", j), None, ALU.mult)
        with nc.Block() as blk:
            @blk.tensor
            def _(t):
                wupb = ar(b, WUPB[0], BF16, NT * P)
                wgb = ar(b, WGB[0], BF16, NT * P)
                for T in range(NT):
                    for g in range(4):
                        t.matmul(b.pbig0[:, g * CH:(g + 1) * CH], wupb[:, T * P:(T + 1) * P],
                                 xq[:, T * TOK + g * CH: T * TOK + (g + 1) * CH],
                                 start=(T == 0), stop=(T == NT - 1))
                for T in range(NT):
                    for g in range(4):
                        t.matmul(b.pbig1[:, g * CH:(g + 1) * CH], wgb[:, T * P:(T + 1) * P],
                                 xq[:, T * TOK + g * CH: T * TOK + (g + 1) * CH],
                                 start=(T == 0), stop=(T == NT - 1))
        with nc.Block() as blk:
            @blk.scalar
            def _(a):
                a.activation(ar(b, SILU, BF16, TOK), b.pbig1[:, :], ACTF.Silu, scale=gb(b, "wsg", j))
        with nc.Block() as blk:
            @blk.vector
            def _(v):
                v.scalar_tensor_tensor(ar(b, XMID, BF16, TOK), b.pbig0[:, :], gb(b, "wsu", j),
                                       ar(b, SILU, BF16, TOK), ALU.mult, ALU.mult)
        with nc.Block() as blk:
            @blk.tensor
            def _(t):
                xmid = ar(b, XMID, BF16, TOK)
                for g in range(4):
                    t.matmul(b.pbig0[:, g * CH:(g + 1) * CH], b.gh128su[:, 0:P],
                             xmid[:, g * CH:(g + 1) * CH], start=True, stop=True)
        with nc.Block() as blk:
            @blk.scalar
            def _(a):
                a.activation(ar(b, ZSB[0], BF16, TOK), b.pbig0[:, :], ACTF.Copy)
        with nc.Block() as blk:
            @blk.gpsimd
            def _(g):
                g.dma_start(out=b.zdr[j], in_=ar(b, ZSB[0], BF16, TOK)).then_inc(b.sD, 16)
                b.cD += 16
                g.wait_ge(b.sD, b.cD)


def emit_a2_min(b, xq):
    nc = b.nc
    WUP = [0, 8192]
    WG = [16384, 24576]
    WUPB = [32768, 36864]
    WGB = [40960, 45056]
    w0, d0, p0 = b.cW, b.cDVE, b.cPE
    with nc.Block() as blk:
        @blk.sync
        def _(s):
            wait_all(b, s)
            for j in (0, 1):
                s.dma_start(out=ar(b, WUP[j % 2], F32, NT * P),
                            in_=b.wupT[j].rearrange("l T c -> l (T c)")).then_inc(b.sW, 16)
                s.dma_start(out=ar(b, WG[j % 2], F32, NT * P),
                            in_=b.wgT[j].rearrange("l T c -> l (T c)")).then_inc(b.sW, 16)
                b.cW += 32
            for j in range(NJ - 2):
                s.wait_ge(b.sDVE, d0 + j + 1)
                s.dma_start(out=ar(b, WUP[j % 2], F32, NT * P),
                            in_=b.wupT[j + 2].rearrange("l T c -> l (T c)")).then_inc(b.sW, 16)
                s.dma_start(out=ar(b, WG[j % 2], F32, NT * P),
                            in_=b.wgT[j + 2].rearrange("l T c -> l (T c)")).then_inc(b.sW, 16)
                b.cW += 32

        @blk.vector
        def _(v):
            for j in range(NJ):
                v.wait_ge(b.sW, w0 + 32 * (j + 1))
                if j >= 2:
                    v.wait_ge(b.sPE, p0 + j - 1)
                v.tensor_copy(ar(b, WUPB[j % 2], BF16, NT * P), ar(b, WUP[j % 2], F32, NT * P))
                v.tensor_copy(ar(b, WGB[j % 2], BF16, NT * P), ar(b, WG[j % 2], F32, NT * P))
                v.drain().then_inc(b.sDVE, 1)
                b.cDVE += 1

        @blk.tensor
        def _(t):
            for j in range(NJ):
                t.wait_ge(b.sDVE, d0 + j + 1)
                wupb = ar(b, WUPB[j % 2], BF16, NT * P)
                wgb = ar(b, WGB[j % 2], BF16, NT * P)
                for T in range(NT):
                    for g in range(4):
                        t.matmul(b.pbig0[:, g * CH:(g + 1) * CH], wupb[:, T * P:(T + 1) * P],
                                 xq[:, T * TOK + g * CH: T * TOK + (g + 1) * CH],
                                 start=(T == 0), stop=(T == NT - 1))
                for T in range(NT):
                    for g in range(4):
                        t.matmul(b.pbig1[:, g * CH:(g + 1) * CH], wgb[:, T * P:(T + 1) * P],
                                 xq[:, T * TOK + g * CH: T * TOK + (g + 1) * CH],
                                 start=(T == 0), stop=(T == NT - 1))
                t.drain().then_inc(b.sPE, 1)
                b.cPE += 1


def emit_a2_minsilu(b, xq):
    nc = b.nc
    WUP = [0, 8192]
    WG = [16384, 24576]
    WUPB = [32768, 36864]
    WGB = [40960, 45056]
    SILU = 49152
    w0, d0, p0, a0 = b.cW, b.cDVE, b.cPE, b.cACT
    with nc.Block() as blk:
        @blk.sync
        def _(s):
            wait_all(b, s)
            for j in (0, 1):
                s.dma_start(out=ar(b, WUP[j % 2], F32, NT * P),
                            in_=b.wupT[j].rearrange("l T c -> l (T c)")).then_inc(b.sW, 16)
                s.dma_start(out=ar(b, WG[j % 2], F32, NT * P),
                            in_=b.wgT[j].rearrange("l T c -> l (T c)")).then_inc(b.sW, 16)
                b.cW += 32
            for j in range(NJ - 2):
                s.wait_ge(b.sDVE, d0 + j + 1)
                s.dma_start(out=ar(b, WUP[j % 2], F32, NT * P),
                            in_=b.wupT[j + 2].rearrange("l T c -> l (T c)")).then_inc(b.sW, 16)
                s.dma_start(out=ar(b, WG[j % 2], F32, NT * P),
                            in_=b.wgT[j + 2].rearrange("l T c -> l (T c)")).then_inc(b.sW, 16)
                b.cW += 32

        @blk.vector
        def _(v):
            import os as _os4
            dvp = _os4.environ.get("A2_DVE_PSUM", "0") == "1"
            for j in range(NJ):
                v.wait_ge(b.sW, w0 + 32 * (j + 1))
                if j >= 2:
                    v.wait_ge(b.sPE, p0 + j - 1)
                v.tensor_copy(ar(b, WUPB[j % 2], BF16, NT * P), ar(b, WUP[j % 2], F32, NT * P))
                v.tensor_copy(ar(b, WGB[j % 2], BF16, NT * P), ar(b, WG[j % 2], F32, NT * P))
                if dvp and j > 0:
                    v.tensor_copy(ar(b, SILU, BF16, TOK), b.pbig0[:, :])
                v.drain().then_inc(b.sDVE, 1)
                b.cDVE += 1

        @blk.tensor
        def _(t):
            for j in range(NJ):
                t.wait_ge(b.sDVE, d0 + j + 1)
                if j > 0:
                    t.wait_ge(b.sACT, a0 + j)
                wupb = ar(b, WUPB[j % 2], BF16, NT * P)
                wgb = ar(b, WGB[j % 2], BF16, NT * P)
                for T in range(NT):
                    for g in range(4):
                        t.matmul(b.pbig0[:, g * CH:(g + 1) * CH], wupb[:, T * P:(T + 1) * P],
                                 xq[:, T * TOK + g * CH: T * TOK + (g + 1) * CH],
                                 start=(T == 0), stop=(T == NT - 1))
                for T in range(NT):
                    for g in range(4):
                        t.matmul(b.pbig1[:, g * CH:(g + 1) * CH], wgb[:, T * P:(T + 1) * P],
                                 xq[:, T * TOK + g * CH: T * TOK + (g + 1) * CH],
                                 start=(T == 0), stop=(T == NT - 1))
                t.drain().then_inc(b.sPE, 1)
                b.cPE += 1

        @blk.scalar
        def _(a):
            for j in range(NJ):
                a.wait_ge(b.sPE, p0 + j + 1)
                import os as _os3
                if _os3.environ.get("A2_SILU_SBUF", "0") == "1":
                    a.activation(ar(b, SILU, BF16, TOK), xq[:, 0:TOK], ACTF.Silu,
                                 scale=gb(b, "wsg", j)).then_inc(b.sACT, 1)
                else:
                    a.activation(ar(b, SILU, BF16, TOK), b.pbig1[:, :], ACTF.Silu,
                                 scale=gb(b, "wsg", j)).then_inc(b.sACT, 1)
                b.cACT += 1


def emit_a2(b, xq):
    import os as _os2
    if _os2.environ.get("A2_MODE", "pipe") == "serial":
        return emit_a2_serial(b, xq)
    if _os2.environ.get("A2_MODE", "pipe") == "min":
        return emit_a2_min(b, xq)
    if _os2.environ.get("A2_MODE", "pipe") == "minsilu":
        return emit_a2_minsilu(b, xq)
    noz = _os2.environ.get("A2_NOZ", "0") == "1"
    nc = b.nc
    WUP = [0, 8192]
    WG = [16384, 24576]
    WUPB = [32768, 36864]
    WGB = [40960, 45056]
    SILU = 49152
    XMID = 53248
    ZSB = [57344, 61440]
    w0, d0, a0, p0, dd0 = b.cW, b.cDVE, b.cACT, b.cPE, b.cD
    with nc.Block() as blk:
        @blk.sync
        def _(s):
            wait_all(b, s)
            for j in (0, 1):
                s.dma_start(out=ar(b, WUP[j % 2], F32, NT * P),
                            in_=b.wupT[j].rearrange("l T c -> l (T c)")).then_inc(b.sW, 16)
                s.dma_start(out=ar(b, WG[j % 2], F32, NT * P),
                            in_=b.wgT[j].rearrange("l T c -> l (T c)")).then_inc(b.sW, 16)
                b.cW += 32
            for j in range(NJ - 2):
                s.wait_ge(b.sDVE, d0 + 2 * j + 1)
                s.dma_start(out=ar(b, WUP[j % 2], F32, NT * P),
                            in_=b.wupT[j + 2].rearrange("l T c -> l (T c)")).then_inc(b.sW, 16)
                s.dma_start(out=ar(b, WG[j % 2], F32, NT * P),
                            in_=b.wgT[j + 2].rearrange("l T c -> l (T c)")).then_inc(b.sW, 16)
                b.cW += 32

        @blk.vector
        def _(v):
            for j in range(NJ):
                v.wait_ge(b.sW, w0 + 32 * (j + 1))
                if j >= 2:
                    v.wait_ge(b.sPE, p0 + 2 * (j - 2) + 2)
                v.tensor_copy(ar(b, WUPB[j % 2], BF16, NT * P), ar(b, WUP[j % 2], F32, NT * P))
                v.tensor_copy(ar(b, WGB[j % 2], BF16, NT * P), ar(b, WG[j % 2], F32, NT * P))
                v.tensor_scalar(b.gh128su[:, (j % 2) * P:(j % 2 + 1) * P], b.gh128[:, :],
                                gb(b, "sdp", j), None, ALU.mult)
                v.drain().then_inc(b.sDVE, 1)
                b.cDVE += 1
                v.wait_ge(b.sACT, a0 + 2 * j + 1)
                v.scalar_tensor_tensor(ar(b, XMID, BF16, TOK), b.pbig0[:, :], gb(b, "wsu", j),
                                       ar(b, SILU, BF16, TOK), ALU.mult, ALU.mult)
                v.drain().then_inc(b.sDVE, 1)
                b.cDVE += 1

        @blk.tensor
        def _(t):
            for j in range(NJ):
                t.wait_ge(b.sDVE, d0 + 2 * j + 1)
                if j > 0:
                    t.wait_ge(b.sACT, a0 + 2 * j)
                wupb = ar(b, WUPB[j % 2], BF16, NT * P)
                wgb = ar(b, WGB[j % 2], BF16, NT * P)
                for T in range(NT):
                    for g in range(4):
                        t.matmul(b.pbig0[:, g * CH:(g + 1) * CH], wupb[:, T * P:(T + 1) * P],
                                 xq[:, T * TOK + g * CH: T * TOK + (g + 1) * CH],
                                 start=(T == 0), stop=(T == NT - 1))
                for T in range(NT):
                    for g in range(4):
                        t.matmul(b.pbig1[:, g * CH:(g + 1) * CH], wgb[:, T * P:(T + 1) * P],
                                 xq[:, T * TOK + g * CH: T * TOK + (g + 1) * CH],
                                 start=(T == 0), stop=(T == NT - 1))
                t.drain().then_inc(b.sPE, 1)
                b.cPE += 1
                t.wait_ge(b.sDVE, d0 + 2 * j + 2)
                if not noz:
                    xmid = ar(b, XMID, BF16, TOK)
                    for g in range(4):
                        t.matmul(b.pbig0[:, g * CH:(g + 1) * CH], b.gh128su[:, (j % 2) * P:(j % 2 + 1) * P],
                                 xmid[:, g * CH:(g + 1) * CH], start=True, stop=True)
                t.drain().then_inc(b.sPE, 1)
                b.cPE += 1

        @blk.scalar
        def _(a):
            for j in range(NJ):
                a.wait_ge(b.sPE, p0 + 2 * j + 1)
                a.activation(ar(b, SILU, BF16, TOK), b.pbig1[:, :], ACTF.Silu, scale=gb(b, "wsg", j))
                a.drain().then_inc(b.sACT, 1)
                b.cACT += 1
                a.wait_ge(b.sPE, p0 + 2 * j + 2)
                if (not noz) and j >= 2:
                    a.wait_ge(b.sD, dd0 + 16 * (j - 1))
                if not noz:
                    a.activation(ar(b, ZSB[j % 2], BF16, TOK), b.pbig0[:, :], ACTF.Copy)
                a.drain().then_inc(b.sACT, 1)
                b.cACT += 1

        @blk.gpsimd
        def _(g):
            if not noz:
                for j in range(NJ):
                    g.wait_ge(b.sACT, a0 + 2 * j + 2)
                    g.dma_start(out=b.zdr[j], in_=ar(b, ZSB[j % 2], BF16, TOK)).then_inc(b.sD, 16)
                    b.cD += 16


def emit_a3(b):
    nc = b.nc
    ZA, ZB = 0, 65536
    WD = [131072, 139264]
    WDB = [147456, 151552]
    YB, SQ = 155648, 172032
    T1 = 32768
    for c in range(NCH):
        c0 = c * CH
        za = ar(b, ZA, BF16, NJ * CH)
        zb = ar(b, ZB, BF16, NJ * CH)
        with nc.Block() as blk:
            @blk.gpsimd
            def _(g):
                wait_all(b, g)
                for j in range(NJ):
                    g.dma_start(out=za[:, j * CH:(j + 1) * CH],
                                in_=b.zdr[j, :, c0:c0 + CH]).then_inc(b.sD, 16)
                    b.cD += 16
        with nc.Block() as blk:
            @blk.vector
            def _(v):
                v.wait_ge(b.sD, b.cD)
        bufs, cur = [za, zb], 0
        for pairs in _butterfly_levels(NJ):
            src, dst = bufs[cur], bufs[1 - cur]
            with nc.Block() as blk:
                @blk.vector
                def _(v):
                    for (i, j) in pairs:
                        v.tensor_tensor(dst[:, i * CH:(i + 1) * CH], src[:, i * CH:(i + 1) * CH],
                                        src[:, j * CH:(j + 1) * CH], ALU.add)
                        v.tensor_tensor(dst[:, j * CH:(j + 1) * CH], src[:, i * CH:(i + 1) * CH],
                                        src[:, j * CH:(j + 1) * CH], ALU.subtract)
            cur = 1 - cur
        zfin = bufs[cur]
        ybuf = ar(b, YB, BF16, NT * CH)
        sq = ar(b, SQ, BF16, CH)
        for ob in range(NT):
            with nc.Block() as blk:
                @blk.sync
                def _(s):
                    if ob == 0:
                        wait_all(b, s)
                    for q in range(4):
                        s.dma_start(out=ar(b, WD[q % 2], F32, NT * P),
                                    in_=b.wdT[ob, :, q * NT:(q + 1) * NT, :].rearrange("l T c -> l (T c)")).then_inc(b.sW, 16)
                        b.cW += 16
                        s.wait_ge(b.sW, b.cW)

                @blk.vector
                def _(v):
                    for q in range(4):
                        v.wait_ge(b.sW, b.cW - 16 * (3 - q))
                        if q >= 2:
                            v.wait_ge(b.sPE, b.cPE + q - 1)
                        v.tensor_copy(ar(b, WDB[q % 2], BF16, NT * P), ar(b, WD[q % 2], F32, NT * P))
                        v.drain().then_inc(b.sDVE, 1)
                        b.cDVE += 1

                @blk.tensor
                def _(t):
                    for q in range(4):
                        t.wait_ge(b.sDVE, b.cDVE - (3 - q))
                        wdb = ar(b, WDB[q % 2], BF16, NT * P)
                        for Ti in range(NT):
                            i = q * NT + Ti
                            t.matmul(b.pbig1[:, (ob % 2) * CH:(ob % 2 + 1) * CH],
                                     wdb[:, Ti * P:(Ti + 1) * P], zfin[:, i * CH:(i + 1) * CH],
                                     start=(i == 0), stop=(i == NJ - 1))
                        t.drain().then_inc(b.sPE, 1)
                        b.cPE += 1
            with nc.Block() as blk:
                @blk.scalar
                def _(a):
                    a.activation(ybuf[:, ob * CH:(ob + 1) * CH], b.pbig1[:, (ob % 2) * CH:(ob % 2 + 1) * CH],
                                 ACTF.Identity, scale=gb(b, "wsd", ob))
            with nc.Block() as blk:
                @blk.scalar
                def _(a):
                    a.activation(sq, ybuf[:, ob * CH:(ob + 1) * CH], ACTF.Square)
            with nc.Block() as blk:
                @blk.tensor
                def _(t):
                    t.matmul(b.pbig0[0:1, 0:CH], b.ginv_bf[:, 0:1], ybuf[:, ob * CH:(ob + 1) * CH],
                             start=(ob == 0), stop=(ob == NT - 1))
                    t.matmul(b.pbig0[0:1, CH:2 * CH], b.ginv_bf[:, 0:1], sq,
                             start=(ob == 0), stop=(ob == NT - 1))
        emit_stat_rows_and_rep(b)
        rc = ar(b, ZA, F32, NT * CH)
        t1 = ar(b, T1, F32, NT * CH)
        with nc.Block() as blk:
            @blk.gpsimd
            def _(g):
                wait_all(b, g)
                g.dma_start(out=_t3(rc), in_=b.resid[:, :, c0:c0 + CH].rearrange("T l t -> l T t")).then_inc(b.sD, 16)
                b.cD += 16
        with nc.Block() as blk:
            @blk.vector
            def _(v):
                v.wait_ge(b.sD, b.cD)
        emit_postln_resid(b, c, ybuf, rc, t1, b.resid)


def emit_l1_attn(b):
    nc = b.nc
    XLN = 67584
    for c in range(NCH):
        emit_preln(b, c, b.resid)
        rc = ar(b, 0, F32, NT * CH)
        xln = ar(b, XLN, BF16, NT * CH)
        emit_affine_to(b, rc, xln)
        c0 = c * CH
        with nc.Block() as blk:
            @blk.vector
            def _(v):
                if c == 0:
                    for T in range(NT):
                        v.tensor_copy(b.gedge[:, T:T + 1], xln[:, T * CH:T * CH + 1])
                if c == NCH - 1:
                    for T in range(NT):
                        v.tensor_copy(b.gedge[:, NT + T:NT + T + 1], xln[:, T * CH + CH - 1:T * CH + CH])

            @blk.gpsimd
            def _(g):
                g.dma_start(out=b.xlndr[:, :, c0:c0 + CH].rearrange("T l t -> l T t"), in_=_t3(xln)).then_inc(b.sD, 16)
                b.cD += 16

    WG1B, WG2B, WTMP = 33280, 49664, 0
    XLNC, HT, GATE = 98816, 115200, 119296
    with nc.Block() as blk:
        @blk.sync
        def _(s):
            wait_all(b, s)
            s.dma_start(out=ar(b, WTMP, F32, NDH * NT * P).rearrange("l (r T c) -> l r (T c)", r=NDH, T=NT),
                        in_=b.wg1T.rearrange("r l T c -> l r (T c)")).then_inc(b.sW, 16)
            b.cW += 16
            s.wait_ge(b.sW, b.cW)

        @blk.vector
        def _(v):
            v.wait_ge(b.sW, b.cW)
            v.tensor_copy(ar(b, WG1B, BF16, NDH * NT * P), ar(b, WTMP, F32, NDH * NT * P))
    with nc.Block() as blk:
        @blk.sync
        def _(s):
            s.dma_start(out=ar(b, WTMP, F32, NT * NDH * P).rearrange("l (r T c) -> l r (T c)", r=NT, T=NDH),
                        in_=b.wg2T.rearrange("r l T c -> l r (T c)")).then_inc(b.sW, 16)
            b.cW += 16
            s.wait_ge(b.sW, b.cW)

        @blk.vector
        def _(v):
            v.wait_ge(b.sW, b.cW)
            v.tensor_copy(ar(b, WG2B, BF16, NT * NDH * P), ar(b, WTMP, F32, NT * NDH * P))
    for c in range(NCH):
        c0 = c * CH
        xlnc = ar(b, XLNC, BF16, NT * CH)
        ht = ar(b, HT, BF16, NDH * CH)
        gate = ar(b, GATE, BF16, NT * CH)
        wg1b = ar(b, WG1B, BF16, NDH * NT * P)
        wg2b = ar(b, WG2B, BF16, NT * NDH * P)
        with nc.Block() as blk:
            @blk.gpsimd
            def _(g):
                wait_all(b, g)
                g.dma_start(out=_t3(xlnc), in_=b.xlndr[:, :, c0:c0 + CH].rearrange("T l t -> l T t")).then_inc(b.sD, 16)
                b.cD += 16
        with nc.Block() as blk:
            @blk.tensor
            def _(t):
                t.wait_ge(b.sD, b.cD)
                for db in range(NDH):
                    for T in range(NT):
                        t.matmul(b.pbig1[:, db * CH:(db + 1) * CH],
                                 wg1b[:, (db * NT + T) * P:(db * NT + T + 1) * P],
                                 xlnc[:, T * CH:(T + 1) * CH], start=(T == 0), stop=(T == NT - 1))
        with nc.Block() as blk:
            @blk.scalar
            def _(a):
                for db in range(NDH):
                    a.activation(ht[:, db * CH:(db + 1) * CH], b.pbig1[:, db * CH:(db + 1) * CH],
                                 ACTF.Relu, bias=gb(b, "bg1", db))
        a0, p0 = b.cACT, b.cPE
        with nc.Block() as blk:
            @blk.tensor
            def _(t):
                for grpi in range(NT // 4):
                    if grpi > 0:
                        t.wait_ge(b.sACT, a0 + grpi)
                    for k in range(4):
                        rb = grpi * 4 + k
                        for db in range(NDH):
                            t.matmul(b.pbig0[:, k * CH:(k + 1) * CH],
                                     wg2b[:, (rb * NDH + db) * P:(rb * NDH + db + 1) * P],
                                     ht[:, db * CH:(db + 1) * CH], start=(db == 0), stop=(db == NDH - 1))
                    t.drain().then_inc(b.sPE, 1)
                    b.cPE += 1

            @blk.scalar
            def _(a):
                for grpi in range(NT // 4):
                    a.wait_ge(b.sPE, p0 + grpi + 1)
                    for k in range(4):
                        rb = grpi * 4 + k
                        a.activation(gate[:, rb * CH:(rb + 1) * CH], b.pbig0[:, k * CH:(k + 1) * CH],
                                     ACTF.Sigmoid, bias=gb(b, "bg2", rb))
                    a.drain().then_inc(b.sACT, 1)
                    b.cACT += 1
        with nc.Block() as blk:
            @blk.gpsimd
            def _(g):
                g.dma_start(out=b.gatesdr[:, :, c0:c0 + CH].rearrange("T l t -> l T t"), in_=_t3(gate)).then_inc(b.sD, 16)
                b.cD += 16

    WUBF, XLNC3, GATE3, UG, KC, WTMP3 = 0, 65536, 81920, 98304, 114688, 131072
    wubf = ar(b, WUBF, BF16, NT * NT * P)
    for rb in range(NT):
        with nc.Block() as blk:
            @blk.sync
            def _(s):
                if rb == 0:
                    wait_all(b, s)
                s.dma_start(out=ar(b, WTMP3, F32, NT * P),
                            in_=b.wuT[rb].rearrange("l T c -> l (T c)")).then_inc(b.sW, 16)
                b.cW += 16
                s.wait_ge(b.sW, b.cW)

            @blk.vector
            def _(v):
                v.wait_ge(b.sW, b.cW)
                v.tensor_copy(wubf[:, rb * NT * P:(rb + 1) * NT * P], ar(b, WTMP3, F32, NT * P))
    for c in range(NCH):
        c0 = c * CH
        xlnc = ar(b, XLNC3, BF16, NT * CH)
        gate = ar(b, GATE3, BF16, NT * CH)
        ug = ar(b, UG, BF16, NT * CH)
        kc = ar(b, KC, BF16, NT * CH)
        with nc.Block() as blk:
            @blk.gpsimd
            def _(g):
                wait_all(b, g)
                g.dma_start(out=_t3(xlnc), in_=b.xlndr[:, :, c0:c0 + CH].rearrange("T l t -> l T t")).then_inc(b.sD, 16)
                g.dma_start(out=_t3(gate), in_=b.gatesdr[:, :, c0:c0 + CH].rearrange("T l t -> l T t")).then_inc(b.sD, 16)
                b.cD += 32
        a0, p0 = b.cACT, b.cPE
        with nc.Block() as blk:
            @blk.tensor
            def _(t):
                t.wait_ge(b.sD, b.cD)
                for grpi in range(NT // 4):
                    if grpi > 0:
                        t.wait_ge(b.sACT, a0 + grpi)
                    for k in range(4):
                        rb = grpi * 4 + k
                        for T in range(NT):
                            t.matmul(b.pbig0[:, k * CH:(k + 1) * CH],
                                     wubf[:, (rb * NT + T) * P:(rb * NT + T + 1) * P],
                                     xlnc[:, T * CH:(T + 1) * CH], start=(T == 0), stop=(T == NT - 1))
                    t.drain().then_inc(b.sPE, 1)
                    b.cPE += 1

            @blk.scalar
            def _(a):
                for grpi in range(NT // 4):
                    a.wait_ge(b.sPE, p0 + grpi + 1)
                    for k in range(4):
                        rb = grpi * 4 + k
                        a.activation(ug[:, rb * CH:(rb + 1) * CH], b.pbig0[:, k * CH:(k + 1) * CH],
                                     ACTF.Identity, bias=gb(b, "bu", rb))
                    a.drain().then_inc(b.sACT, 1)
                    b.cACT += 1
        with nc.Block() as blk:
            @blk.vector
            def _(v):
                for rb in range(NT):
                    sl = slice(rb * CH, (rb + 1) * CH)
                    v.tensor_tensor(ug[:, sl], ug[:, sl], gate[:, sl], ALU.mult)
        with nc.Block() as blk:
            @blk.vector
            def _(v):
                for rb in range(NT):
                    sl = slice(rb * CH, (rb + 1) * CH)
                    v.tensor_tensor_scan(kc[:, sl], ug[:, sl], ug[:, sl], b.gstkc[:, rb:rb + 1],
                                         ALU.add, ALU.bypass)
                for rb in range(NT):
                    v.tensor_copy(b.gstkc[:, rb:rb + 1], kc[:, rb * CH + CH - 1:rb * CH + CH])
        with nc.Block() as blk:
            @blk.gpsimd
            def _(g):
                g.dma_start(out=b.kcdr[:, :, c0:c0 + CH].rearrange("T l t -> l T t"), in_=_t3(kc)).then_inc(b.sD, 16)
                b.cD += 16

    CONTRIB = 131072
    contrib = ar(b, CONTRIB, F32, 48)
    with nc.Block() as blk:
        @blk.vector
        def _(v):
            v.tensor_scalar(contrib[:, 0:NT], b.gedge[:, NT:2 * NT], gb(b, "iseven"), None, ALU.mult)
            v.tensor_scalar(contrib[:, NT:2 * NT], b.gedge[:, 0:NT], gb(b, "isodd"), None, ALU.mult)
            v.tensor_scalar(contrib[:, 2 * NT:3 * NT], b.gstkc[:, :], gb(b, "iseven"), None, ALU.mult)
    with nc.Block() as blk:
        @blk.gpsimd
        def _(g):
            wait_all(b, g)
            g.dma_start(out=b.cc1i.ap(), in_=contrib).then_inc(b.sD, 16)
            b.cD += 16
            g.wait_ge(b.sD, b.cD)
            g.collective_compute("AllReduce", ALU.add, replica_groups=[[0, 1], [2, 3], [4, 5], [6, 7]],
                                 ins=[b.cc1i.ap()], outs=[b.cc1o.ap()]).then_inc(b.sCC, 1)
            b.cCC += 1
            g.wait_ge(b.sCC, b.cCC)
            g.dma_start(out=b.ghalo[:, :], in_=b.cc1o.ap()).then_inc(b.sD, 16)
            b.cD += 16
    with nc.Block() as blk:
        @blk.vector
        def _(v):
            v.wait_ge(b.sD, b.cD)
            v.tensor_scalar(b.gedge[:, 2 * NT:3 * NT], b.ghalo[:, 0:NT], gb(b, "isodd"), None, ALU.mult)
            v.tensor_scalar(b.gedge[:, 3 * NT:4 * NT], b.ghalo[:, NT:2 * NT], gb(b, "iseven"), None, ALU.mult)
            v.tensor_scalar(b.gstkc[:, :], b.ghalo[:, 2 * NT:3 * NT], gb(b, "isodd"), None, ALU.mult)

    WV1 = [65536, 73728]
    WV1B = [81920, 86016]
    XLNC4, GATE4, VG, KCC, GO = 90112, 106496, 122880, 139264, 155648
    XPAD, TC1, EDGL, EDGR, YB4, SQ4 = 172032, 175104, 177664, 178176, 106496, 65536
    RC4, T14 = 122880, 65536
    for c in range(NCH):
        c0 = c * CH
        xlnc = ar(b, XLNC4, BF16, NT * CH)
        gate = ar(b, GATE4, BF16, NT * CH)
        vg = ar(b, VG, BF16, NT * CH)
        kcc = ar(b, KCC, BF16, NT * CH)
        go = ar(b, GO, BF16, NT * CH)
        ybuf = ar(b, YB4, BF16, NT * CH)
        sq = ar(b, SQ4, BF16, CH)
        edgl = ar(b, EDGL, BF16, NT)
        edgr = ar(b, EDGR, BF16, NT)
        with nc.Block() as blk:
            @blk.gpsimd
            def _(g):
                wait_all(b, g)
                g.dma_start(out=_t3(xlnc), in_=b.xlndr[:, :, c0:c0 + CH].rearrange("T l t -> l T t")).then_inc(b.sD, 16)
                g.dma_start(out=_t3(gate), in_=b.gatesdr[:, :, c0:c0 + CH].rearrange("T l t -> l T t")).then_inc(b.sD, 16)
                g.dma_start(out=_t3(kcc), in_=b.kcdr[:, :, c0:c0 + CH].rearrange("T l t -> l T t")).then_inc(b.sD, 16)
                b.cD += 48
                if c > 0:
                    g.dma_start(out=_t3(edgl), in_=b.xlndr[:, :, c0 - 1:c0].rearrange("T l t -> l T t"), allow_slow_non_contiguous=True).then_inc(b.sD, 16)
                    b.cD += 16
                if c < NCH - 1:
                    g.dma_start(out=_t3(edgr), in_=b.xlndr[:, :, c0 + CH:c0 + CH + 1].rearrange("T l t -> l T t"), allow_slow_non_contiguous=True).then_inc(b.sD, 16)
                    b.cD += 16
        for rb in range(NT):
            with nc.Block() as blk:
                @blk.sync
                def _(s):
                    if rb == 0 and c == 0:
                        wait_all(b, s)
                    s.dma_start(out=ar(b, WV1[rb % 2], F32, NT * P),
                                in_=b.wvT1[rb].rearrange("l T c -> l (T c)")).then_inc(b.sW, 16)
                    b.cW += 16
                    s.wait_ge(b.sW, b.cW)

                @blk.vector
                def _(v):
                    v.wait_ge(b.sW, b.cW)
                    v.tensor_copy(ar(b, WV1B[rb % 2], BF16, NT * P), ar(b, WV1[rb % 2], F32, NT * P))
            with nc.Block() as blk:
                @blk.tensor
                def _(t):
                    t.wait_ge(b.sD, b.cD)
                    wv1b = ar(b, WV1B[rb % 2], BF16, NT * P)
                    for T in range(NT):
                        t.matmul(b.pbig1[:, (rb % 4) * CH:(rb % 4 + 1) * CH],
                                 wv1b[:, T * P:(T + 1) * P], xlnc[:, T * CH:(T + 1) * CH],
                                 start=(T == 0), stop=(T == NT - 1))
            with nc.Block() as blk:
                @blk.scalar
                def _(a):
                    a.activation(vg[:, rb * CH:(rb + 1) * CH], b.pbig1[:, (rb % 4) * CH:(rb % 4 + 1) * CH],
                                 ACTF.Identity, bias=gb(b, "bv1", rb))
        with nc.Block() as blk:
            @blk.vector
            def _(v):
                for rb in range(NT):
                    sl = slice(rb * CH, (rb + 1) * CH)
                    v.tensor_tensor(vg[:, sl], vg[:, sl], gate[:, sl], ALU.mult)
                for rb in range(NT):
                    sl = slice(rb * CH, (rb + 1) * CH)
                    v.scalar_tensor_tensor(go[:, sl], kcc[:, sl], b.gstkc[:, rb:rb + 1], vg[:, sl],
                                           ALU.add, ALU.mult)
        for ob in range(NT):
            with nc.Block() as blk:
                @blk.tensor
                def _(t):
                    for rT in range(NT):
                        t.matmul(b.pbig1[:, (ob % 4) * CH:(ob % 4 + 1) * CH],
                                 wubf[:, (ob * NT + rT) * P:(ob * NT + rT + 1) * P],
                                 go[:, rT * CH:(rT + 1) * CH], start=(rT == 0), stop=(rT == NT - 1))
            xpad = ar(b, XPAD, F32, CH + 2)
            tc1 = ar(b, TC1, F32, CH)
            with nc.Block() as blk:
                @blk.vector
                def _(v):
                    v.tensor_copy(xpad[:, 1:CH + 1], xlnc[:, ob * CH:(ob + 1) * CH])
                    if c == 0:
                        v.tensor_copy(xpad[:, 0:1], b.gedge[:, 2 * NT + ob:2 * NT + ob + 1])
                    else:
                        v.tensor_copy(xpad[:, 0:1], edgl[:, ob:ob + 1])
                    if c == NCH - 1:
                        v.tensor_copy(xpad[:, CH + 1:CH + 2], b.gedge[:, 3 * NT + ob:3 * NT + ob + 1])
                    else:
                        v.tensor_copy(xpad[:, CH + 1:CH + 2], edgr[:, ob:ob + 1])
                    v.tensor_scalar(tc1, xpad[:, 1:CH + 1], gb(b, "cw", NT + ob), gb(b, "convb", ob),
                                    ALU.mult, ALU.add)
                    v.scalar_tensor_tensor(tc1, xpad[:, 0:CH], gb(b, "cw", ob), tc1, ALU.mult, ALU.add)
                    v.scalar_tensor_tensor(tc1, xpad[:, 2:CH + 2], gb(b, "cw", 2 * NT + ob), tc1, ALU.mult, ALU.add)
                    v.tensor_tensor(ybuf[:, ob * CH:(ob + 1) * CH], tc1,
                                    b.pbig1[:, (ob % 4) * CH:(ob % 4 + 1) * CH], ALU.add)
            with nc.Block() as blk:
                @blk.scalar
                def _(a):
                    a.activation(sq, ybuf[:, ob * CH:(ob + 1) * CH], ACTF.Square)
            with nc.Block() as blk:
                @blk.tensor
                def _(t):
                    t.matmul(b.pbig0[0:1, 0:CH], b.ginv_bf[:, 0:1], ybuf[:, ob * CH:(ob + 1) * CH],
                             start=(ob == 0), stop=(ob == NT - 1))
                    t.matmul(b.pbig0[0:1, CH:2 * CH], b.ginv_bf[:, 0:1], sq,
                             start=(ob == 0), stop=(ob == NT - 1))
        emit_stat_rows_and_rep(b)
        rc = ar(b, RC4, F32, NT * CH)
        t1 = ar(b, T14, F32, NT * CH)
        with nc.Block() as blk:
            @blk.gpsimd
            def _(g):
                wait_all(b, g)
                g.dma_start(out=_t3(rc), in_=b.resid[:, :, c0:c0 + CH].rearrange("T l t -> l T t")).then_inc(b.sD, 16)
                b.cD += 16
        with nc.Block() as blk:
            @blk.vector
            def _(v):
                v.wait_ge(b.sD, b.cD)
        emit_postln_resid(b, c, ybuf, rc, t1, b.resid)


def emit_l1_ffn(b, final_out):
    nc = b.nc
    XLN4 = 67584
    for c in range(NCH):
        emit_preln(b, c, b.resid)
        rc = ar(b, 0, F32, NT * CH)
        xln4 = ar(b, XLN4, F32, NT * CH)
        c0 = c * CH
        with nc.Block() as blk:
            @blk.vector
            def _(v):
                for T in range(NT):
                    sl = slice(T * CH, (T + 1) * CH)
                    v.tensor_tensor(xln4[:, sl], rc[:, sl], b.pbig0[:, 2 * CH:3 * CH], ALU.mult)
                for T in range(NT):
                    sl = slice(T * CH, (T + 1) * CH)
                    v.tensor_tensor(xln4[:, sl], xln4[:, sl], b.pbig0[:, 3 * CH:4 * CH], ALU.add)
        with nc.Block() as blk:
            @blk.vector
            def _(v):
                v.tensor_reduce(b.gmax[:, 2:3], xln4, AX.X, ALU.max, apply_absolute_value=True)
                vpad(b, v)
                v.tensor_tensor(b.gmax[:, 0:1], b.gmax[:, 0:1], b.gmax[:, 2:3], ALU.max)

            @blk.gpsimd
            def _(g):
                g.dma_start(out=b.xln4dr[:, :, c0:c0 + CH].rearrange("T l t -> l T t"), in_=_t3(xln4)).then_inc(b.sD, 16)
                b.cD += 16
    emit_scalar_allreduce(b, [(0, 0), (3, 1), (4, 2)], b.cc2i, b.cc2o)
    with nc.Block() as blk:
        @blk.gpsimd
        def _(g):
            g.dma_start(out=b.dbg[:, 0:16], in_=b.grow1[:, 0:16]).then_inc(b.sD, 16)
            b.cD += 16
    with nc.Block() as blk:
        @blk.vector
        def _(v):
            v.tensor_single_scalar(b.grow1[:, 3:6], b.grow1[:, 0:3], 1.0 / 448.0, ALU.mult)
            vpad(b, v)
            v.tensor_single_scalar(b.grow1[:, 3:6], b.grow1[:, 3:6], 1e-6, ALU.max)
            vpad(b, v)
            v.reciprocal(b.grow1[:, 6:9], b.grow1[:, 3:6])
            vpad(b, v)
            v.tensor_single_scalar(b.grow1[:, 6:9], b.grow1[:, 6:9], 0.5, ALU.mult)
            vpad(b, v)
    with nc.Block() as blk:
        @blk.gpsimd
        def _(g):
            g.dma_start(out=b.dbg[:, 16:32], in_=b.grow1[:, 0:16]).then_inc(b.sD, 16)
            b.cD += 16
    with nc.Block() as blk:
        @blk.tensor
        def _(t):
            t.matmul(b.pbig0[:, 0:3], b.gones_row[:, :], b.grow1[:, 6:9], start=True, stop=True)
    with nc.Block() as blk:
        @blk.vector
        def _(v):
            v.tensor_copy(b.gscal[:, 0:3], b.pbig0[:, 0:3])

    X8, XL4C, QT = 0, 33280, 66560
    x8 = ar(b, X8, FP8, NT * TOK)
    for c in range(NCH):
        c0 = c * CH
        xl4c = ar(b, XL4C, F32, NT * CH)
        qt = ar(b, QT, F32, NT * CH)
        with nc.Block() as blk:
            @blk.gpsimd
            def _(g):
                wait_all(b, g)
                g.dma_start(out=_t3(xl4c), in_=b.xln4dr[:, :, c0:c0 + CH].rearrange("T l t -> l T t")).then_inc(b.sD, 16)
                b.cD += 16
        with nc.Block() as blk:
            @blk.vector
            def _(v):
                v.wait_ge(b.sD, b.cD)
                v.tensor_scalar(qt, xl4c, b.gscal[:, 0:1], None, ALU.mult)
                v.tensor_copy(x8.rearrange("l (T t) -> l T t", T=NT)[:, :, c0:c0 + CH],
                              qt.rearrange("l (T t) -> l T t", T=NT))
    emit_c2_fc(b, x8)
    emit_scalar_allreduce(b, [(1, 0)], b.cc3i, b.cc3o)
    with nc.Block() as blk:
        @blk.vector
        def _(v):
            v.tensor_tensor(b.grow1[:, 9:10], b.grow1[:, 3:4], b.grow1[:, 4:5], ALU.mult)
            vpad(b, v)
            v.tensor_single_scalar(b.grow1[:, 9:10], b.grow1[:, 9:10], 4.0, ALU.mult)
            vpad(b, v)
            v.tensor_tensor(b.grow1[:, 10:11], b.grow1[:, 9:10], b.grow1[:, 9:10], ALU.mult)
            vpad(b, v)
            v.tensor_tensor(b.grow1[:, 11:12], b.grow1[:, 0:1], b.grow1[:, 10:11], ALU.mult)
            vpad(b, v)
            v.tensor_single_scalar(b.grow1[:, 11:12], b.grow1[:, 11:12], 1.0 / 448.0, ALU.mult)
            vpad(b, v)
            v.tensor_single_scalar(b.grow1[:, 11:12], b.grow1[:, 11:12], 1e-6, ALU.max)
            vpad(b, v)
            v.tensor_single_scalar(b.grow1[:, 12:13], b.grow1[:, 11:12], 2.0, ALU.mult)
            vpad(b, v)
            v.reciprocal(b.grow1[:, 13:14], b.grow1[:, 12:13])
            vpad(b, v)
            v.tensor_tensor(b.grow1[:, 13:14], b.grow1[:, 10:11], b.grow1[:, 13:14], ALU.mult)
            vpad(b, v)
            v.tensor_tensor(b.grow1[:, 14:15], b.grow1[:, 11:12], b.grow1[:, 5:6], ALU.mult)
            vpad(b, v)
            v.tensor_single_scalar(b.grow1[:, 14:15], b.grow1[:, 14:15], 4.0, ALU.mult)
            vpad(b, v)
    with nc.Block() as blk:
        @blk.tensor
        def _(t):
            t.matmul(b.pbig0[:, 0:2], b.gones_row[:, :], b.grow1[:, 13:15], start=True, stop=True)
    with nc.Block() as blk:
        @blk.vector
        def _(v):
            v.tensor_copy(b.gscal[:, 3:5], b.pbig0[:, 0:2])
    with nc.Block() as blk:
        @blk.vector
        def _(v):
            v.tensor_copy(b.gcc2[:, 0:16], b.grow1[:, 0:16])

        @blk.gpsimd
        def _(g):
            g.dma_start(out=b.dbg[:, 100:116], in_=b.grow1[:, 0:16]).then_inc(b.sD, 16)
            g.dma_start(out=b.dbg[:, 116:124], in_=b.gscal[0:1, 0:8]).then_inc(b.sD, 16)
            g.dma_start(out=b.dbg[:, 48:56], in_=b.gjunk[:, 0:8]).then_inc(b.sD, 16)
            b.cD += 48
    emit_c4_proj(b)
    emit_c5_final(b, final_out)


def emit_scalar_allreduce(b, colmap, cci, cco):
    nc = b.nc
    with nc.Block() as blk:
        @blk.tensor
        def _(t):
            for i, (src, _) in enumerate(colmap):
                t.matmul(b.pbig0[0:1, i * P:(i + 1) * P], b.gmax[:, src:src + 1], b.geye[:, :],
                         start=True, stop=True)
    with nc.Block() as blk:
        @blk.vector
        def _(v):
            for i, (_, dst) in enumerate(colmap):
                v.reduce_max(b.grows2[:, i:i + 1], b.pbig0[0:1, i * P:(i + 1) * P], AX.X)
            vpad(b, v)
            v.memset(b.gcc2[:, :], 0.0)
            for i in range(len(colmap)):
                for s in range(8):
                    v.tensor_copy(b.gcc2[:, 4 * s + i:4 * s + i + 1], b.grows2[:, i:i + 1])
            vpad(b, v)
            v.tensor_tensor(b.gcc2[:, :], b.gcc2[:, :], b.gflags[:, 32:64], ALU.mult)
    with nc.Block() as blk:
        @blk.gpsimd
        def _(g):
            wait_all(b, g)
            if cci is b.cc2i:
                g.dma_start(out=b.dbg[:, 64:96], in_=b.gcc2[:, :]).then_inc(b.sD, 16)
                g.dma_start(out=b.dbg[:, 96:100], in_=b.grows2[:, 0:4]).then_inc(b.sD, 16)
                b.cD += 32
            g.dma_start(out=cci.ap(), in_=b.gcc2[:, :]).then_inc(b.sD, 16)
            b.cD += 16
            g.wait_ge(b.sD, b.cD)
            rounds = [
                ([[0, 1], [2, 3], [4, 5], [6, 7]], cci.ap(), cco.ap()),
                ([[0, 2], [1, 3], [4, 6], [5, 7]], cco.ap(), b.ccmb.ap()),
                ([[0, 4], [1, 5], [2, 6], [3, 7]], b.ccmb.ap(), b.ccmc.ap()),
            ]
            for rg, rin, rout in rounds:
                g.collective_compute("AllReduce", ALU.add, replica_groups=rg,
                                     ins=[rin], outs=[rout]).then_inc(b.sCC, 1)
                b.cCC += 1
                g.wait_ge(b.sCC, b.cCC)
            g.dma_start(out=b.gcc2[:, :], in_=b.ccmc.ap()).then_inc(b.sD, 16)
            b.cD += 16
    with nc.Block() as blk:
        @blk.vector
        def _(v):
            v.wait_ge(b.sD, b.cD)
            v.reduce_max(b.grows2[:, 0:4], b.gcc2.rearrange("o (s k) -> o k s", s=8), AX.X)
            vpad(b, v)
            for i, (_, dst) in enumerate(colmap):
                v.tensor_copy(b.grow1[:, dst:dst + 1], b.grows2[:, i:i + 1])
            if cci is b.cc2i:
                v.tensor_copy(b.grows[:, 900:908], b.gcc2[:, 0:8])
                v.tensor_copy(b.grows[:, 908:912], b.grows2[:, 0:4])
                v.tensor_copy(b.grows[:, 912:916], b.grow1[:, 0:4])
    if cci is b.cc2i:
        with nc.Block() as blk:
            @blk.gpsimd
            def _(g):
                g.dma_start(out=b.dbg[:, 32:48], in_=b.grows[:, 900:916]).then_inc(b.sD, 16)
                b.cD += 16


def emit_c2_fc(b, x8):
    nc = b.nc
    WFC = [99840, 108032]
    W8FC = [116224, 118272]
    QW = 120320
    RELU = 129024
    HS2 = [137728, 141824]
    w0, d0, a0, p0, dd0 = b.cW, b.cDVE, b.cACT, b.cPE, b.cD
    with nc.Block() as blk:
        @blk.sync
        def _(s):
            wait_all(b, s)
            for j in (0, 1):
                s.dma_start(out=ar(b, WFC[j % 2], F32, NT * P),
                            in_=b.wfcT[j].rearrange("l T c -> l (T c)")).then_inc(b.sW, 16)
                b.cW += 16
            for j in range(NJ - 2):
                s.wait_ge(b.sDVE, d0 + 2 * j + 1)
                s.dma_start(out=ar(b, WFC[j % 2], F32, NT * P),
                            in_=b.wfcT[j + 2].rearrange("l T c -> l (T c)")).then_inc(b.sW, 16)
                b.cW += 16

        @blk.vector
        def _(v):
            for j in range(NJ):
                v.wait_ge(b.sW, w0 + 16 * (j + 1))
                if j >= 2:
                    v.wait_ge(b.sPE, p0 + j - 1)
                v.tensor_scalar(ar(b, QW, F32, NT * P), ar(b, WFC[j % 2], F32, NT * P),
                                b.gscal[:, 1:2], None, ALU.mult)
                v.tensor_copy(ar(b, W8FC[j % 2], FP8, NT * P), ar(b, QW, F32, NT * P))
                v.drain().then_inc(b.sDVE, 1)
                b.cDVE += 1
                v.wait_ge(b.sACT, a0 + 2 * j + 2)
                v.tensor_reduce(b.gmax[:, 2:3], ar(b, HS2[j % 2], BF16, TOK), AX.X, ALU.max)
                vpad(b, v)
                v.tensor_tensor(b.gmax[:, 1:2], b.gmax[:, 1:2], b.gmax[:, 2:3], ALU.max)
                v.drain().then_inc(b.sDVE, 1)
                b.cDVE += 1

        @blk.tensor
        def _(t):
            for j in range(NJ):
                t.wait_ge(b.sDVE, d0 + 2 * j + 1)
                if j >= 2:
                    t.wait_ge(b.sACT, a0 + 2 * (j - 1))
                w8 = ar(b, W8FC[j % 2], FP8, NT * P)
                pb = b.pbig0 if j % 2 == 0 else b.pbig1
                for T in range(NT):
                    for g in range(4):
                        t.matmul(pb[:, g * CH:(g + 1) * CH], w8[:, T * P:(T + 1) * P],
                                 x8.rearrange("l (T t) -> l T t", T=NT)[:, T, g * CH:(g + 1) * CH],
                                 start=(T == 0), stop=(T == NT - 1))
                t.drain().then_inc(b.sPE, 1)
                b.cPE += 1

        @blk.scalar
        def _(a):
            for j in range(NJ):
                a.wait_ge(b.sPE, p0 + j + 1)
                if j >= 2:
                    a.wait_ge(b.sD, dd0 + 16 * (j - 1))
                    a.wait_ge(b.sDVE, d0 + 2 * (j - 2) + 2)
                pb = b.pbig0 if j % 2 == 0 else b.pbig1
                a.activation(ar(b, RELU, F32, TOK), pb[:, :], ACTF.Relu)
                a.drain().then_inc(b.sACT, 1)
                b.cACT += 1
                a.activation(ar(b, HS2[j % 2], BF16, TOK), ar(b, RELU, F32, TOK), ACTF.Square)
                a.drain().then_inc(b.sACT, 1)
                b.cACT += 1

        @blk.gpsimd
        def _(g):
            for j in range(NJ):
                g.wait_ge(b.sACT, a0 + 2 * j + 2)
                g.dma_start(out=b.hdr[j], in_=ar(b, HS2[j % 2], BF16, TOK)).then_inc(b.sD, 16)
                b.cD += 16


def emit_c4_proj(b):
    nc = b.nc
    H8 = 0
    WPJ = [131072, 139264]
    W8PJ = [147456, 155648]
    QW4 = 164352
    HS2L = [173056, 177152]
    YT = [181760, 185856]
    h8 = ar(b, H8, FP8, NJ * TOK)
    d0, dd0 = b.cDVE, b.cD
    with nc.Block() as blk:
        @blk.gpsimd
        def _(g):
            wait_all(b, g)
            for j in range(NJ):
                if j >= 2:
                    g.wait_ge(b.sDVE, d0 + j - 1)
                g.dma_start(out=ar(b, HS2L[j % 2], BF16, TOK), in_=b.hdr[j]).then_inc(b.sD, 16)
                b.cD += 16

        @blk.vector
        def _(v):
            for j in range(NJ):
                v.wait_ge(b.sD, dd0 + 16 * (j + 1))
                v.tensor_scalar(ar(b, QW4, F32, TOK), ar(b, HS2L[j % 2], BF16, TOK),
                                b.gscal[:, 3:4], None, ALU.mult)
                v.tensor_copy(h8[:, j * TOK:(j + 1) * TOK], ar(b, QW4, F32, TOK))
                v.drain().then_inc(b.sDVE, 1)
                b.cDVE += 1
    w0, d0, a0, p0, dd4 = b.cW, b.cDVE, b.cACT, b.cPE, b.cD
    with nc.Block() as blk:
        @blk.sync
        def _(s):
            wait_all(b, s)
            for k in range(2):
                ob, q = divmod(k, 4)
                s.dma_start(out=ar(b, WPJ[k % 2], F32, NT * P),
                            in_=b.wpjT[ob, :, q * NT:(q + 1) * NT, :].rearrange("l T c -> l (T c)")).then_inc(b.sW, 16)
                b.cW += 16
            for k in range(4 * NT - 2):
                ob, q = divmod(k + 2, 4)
                s.wait_ge(b.sDVE, d0 + k + 1)
                s.dma_start(out=ar(b, WPJ[k % 2], F32, NT * P),
                            in_=b.wpjT[ob, :, q * NT:(q + 1) * NT, :].rearrange("l T c -> l (T c)")).then_inc(b.sW, 16)
                b.cW += 16

        @blk.vector
        def _(v):
            for k in range(4 * NT):
                ob, q = divmod(k, 4)
                v.wait_ge(b.sW, w0 + 16 * (k + 1))
                if q == 0 and ob >= 2:
                    v.wait_ge(b.sPE, p0 + ob - 1)
                v.tensor_scalar(ar(b, QW4, F32, NT * P), ar(b, WPJ[k % 2], F32, NT * P),
                                b.gscal[:, 2:3], None, ALU.mult)
                v.tensor_copy(ar(b, W8PJ[ob % 2] + q * 2048, FP8, NT * P), ar(b, QW4, F32, NT * P))
                v.drain().then_inc(b.sDVE, 1)
                b.cDVE += 1

        @blk.tensor
        def _(t):
            for ob in range(NT):
                t.wait_ge(b.sDVE, d0 + 4 * (ob + 1))
                if ob >= 2:
                    t.wait_ge(b.sACT, a0 + ob - 1)
                pb = b.pbig0 if ob % 2 == 0 else b.pbig1
                for q in range(4):
                    w8 = ar(b, W8PJ[ob % 2] + q * 2048, FP8, NT * P)
                    for Ti in range(NT):
                        i = q * NT + Ti
                        for g in range(4):
                            t.matmul(pb[:, g * CH:(g + 1) * CH], w8[:, Ti * P:(Ti + 1) * P],
                                     h8[:, i * TOK + g * CH: i * TOK + (g + 1) * CH],
                                     start=(i == 0), stop=(i == NJ - 1))
                t.drain().then_inc(b.sPE, 1)
                b.cPE += 1

        @blk.scalar
        def _(a):
            for ob in range(NT):
                a.wait_ge(b.sPE, p0 + ob + 1)
                if ob >= 2:
                    a.wait_ge(b.sD, dd4 + 16 * (ob - 1))
                pb = b.pbig0 if ob % 2 == 0 else b.pbig1
                a.activation(ar(b, YT[ob % 2], BF16, TOK), pb[:, :], ACTF.Identity,
                             scale=b.gscal[:, 4:5])
                a.drain().then_inc(b.sACT, 1)
                b.cACT += 1

        @blk.gpsimd
        def _(g):
            for ob in range(NT):
                g.wait_ge(b.sACT, a0 + ob + 1)
                g.dma_start(out=b.ydr[ob], in_=ar(b, YT[ob % 2], BF16, TOK)).then_inc(b.sD, 16)
                b.cD += 16


def emit_c5_final(b, final_out):
    nc = b.nc
    YC, Y2C, RC, T1 = 0, 16384, 33280, 66560
    for c in range(NCH):
        c0 = c * CH
        yc = ar(b, YC, BF16, NT * CH)
        y2c = ar(b, Y2C, BF16, NT * CH)
        rc = ar(b, RC, F32, NT * CH)
        t1 = ar(b, T1, F32, NT * CH)
        with nc.Block() as blk:
            @blk.gpsimd
            def _(g):
                wait_all(b, g)
                g.dma_start(out=_t3(yc), in_=b.ydr[:, :, c0:c0 + CH].rearrange("T l t -> l T t")).then_inc(b.sD, 16)
                g.dma_start(out=_t3(rc), in_=b.resid[:, :, c0:c0 + CH].rearrange("T l t -> l T t")).then_inc(b.sD, 16)
                b.cD += 32
        with nc.Block() as blk:
            @blk.scalar
            def _(a):
                a.wait_ge(b.sD, b.cD)
                a.activation(y2c, yc, ACTF.Square)
        emit_stats_from(b, yc, y2c)
        emit_postln_resid(b, c, yc, rc, t1, final_out)


def emit_out_debug(b):
    nc = b.nc
    for c in range(NCH):
        c0 = c * CH
        rc = ar(b, 0, F32, NT * CH)
        with nc.Block() as blk:
            @blk.gpsimd
            def _(g):
                wait_all(b, g)
                g.dma_start(out=_t3(rc), in_=b.resid[:, :, c0:c0 + CH].rearrange("T l t -> l T t")).then_inc(b.sD, 16)
                b.cD += 16
                g.wait_ge(b.sD, b.cD)
                g.dma_start(out=b.outT[:, :, c0:c0 + CH].rearrange("T l t -> l T t"), in_=_t3(rc)).then_inc(b.sD, 16)
                b.cD += 16


def emit_final_wait(b):
    nc = b.nc
    with nc.Block() as blk:
        @blk.gpsimd
        def _(g):
            g.wait_ge(b.sD, b.cD)
            if b.cW:
                g.wait_ge(b.sW, b.cW)


def build(upto=4):
    b = _mk()
    b.sub_l0ffn = {21: 1, 22: 2}.get(upto, 3)
    if upto in (21, 22):
        upto = 2
    emit_setup(b)
    emit_l0_attn(b)
    if upto >= 2:
        emit_l0_ffn(b)
    if upto >= 3:
        emit_l1_attn(b)
    if upto >= 4:
        emit_l1_ffn(b, b.outT)
    else:
        emit_out_debug(b)
    emit_final_wait(b)
    b.stack.close()
    return b.nc


def _lhsT_blocks(WT):
    K, M = WT.shape
    return np.ascontiguousarray(
        WT.reshape(K // 128, 128, M // 128, 128).transpose(2, 1, 0, 3))


_NC_CACHE = {}


def _get_nc(upto=4):
    if upto not in _NC_CACHE:
        _NC_CACHE[upto] = build(upto)
    return _NC_CACHE[upto]


def make_inmaps(inputs):
    f32 = lambda a: np.ascontiguousarray(np.asarray(a), dtype=np.float32)
    x = f32(inputs["L0_attn_x"])
    shared = {
        "wvT": _lhsT_blocks(f32(inputs["L0_attn_Wv"]).T),
        "woT": _lhsT_blocks(f32(inputs["L0_attn_Wo"]).T),
        "bv0": f32(inputs["L0_attn_bv"]).reshape(1, D),
        "bo0": f32(inputs["L0_attn_bo"]).reshape(1, D),
        "wupT": _lhsT_blocks(f32(inputs["L0_ffn_W_up"]).T),
        "wgT": _lhsT_blocks(f32(inputs["L0_ffn_W_gate"]).T),
        "wdT": _lhsT_blocks(f32(inputs["L0_ffn_W_down"]).T),
        "su_ug": f32(inputs["L0_ffn_SU_ug"]).reshape(1, D),
        "su_dp": f32(inputs["L0_ffn_SU_dp"]).reshape(1, INTER),
        "ws_ug": f32(inputs["L0_ffn_Wscale_ug"]).reshape(1, 2 * INTER),
        "ws_dp": f32(inputs["L0_ffn_Wscale_dp"]).reshape(1, D),
        "wuT": _lhsT_blocks(f32(inputs["L1_attn_Wu"]).T),
        "wvT1": _lhsT_blocks(f32(inputs["L1_attn_Wv"]).T),
        "wg1T": _lhsT_blocks(f32(inputs["L1_attn_Wg1"]).T),
        "wg2T": _lhsT_blocks(f32(inputs["L1_attn_Wg2"]).T),
        "bu1": f32(inputs["L1_attn_bu"]).reshape(1, D),
        "bv1": f32(inputs["L1_attn_bv"]).reshape(1, D),
        "bg1": f32(inputs["L1_attn_bg1"]).reshape(1, DH),
        "bg2": f32(inputs["L1_attn_bg2"]).reshape(1, D),
        "convw": np.ascontiguousarray(f32(inputs["L1_attn_conv_w"])[:, 0, :].T).reshape(1, 3 * D),
        "convb": f32(inputs["L1_attn_conv_b"]).reshape(1, D),
        "wfcT": _lhsT_blocks(f32(inputs["L1_ffn_w_fc"]).T),
        "wpjT": _lhsT_blocks(f32(inputs["L1_ffn_w_proj"]).T),
    }
    wfc = f32(inputs["L1_ffn_w_fc"])
    wpj = f32(inputs["L1_ffn_w_proj"])
    in_maps = []
    for core in range(8):
        bidx, half = core // 2, core % 2
        sl = slice(2048 * half, 2048 * half + 2048)
        xTc = np.ascontiguousarray(x[bidx, sl, :].T).reshape(NT, P, TOK)
        fl = np.zeros((1, 64), dtype=np.float32)
        fl[0, 0] = float(half)
        fl[0, 1] = 1.0 - float(half)
        for k in range(8):
            for j in range(4):
                fl[0, 32 + 4 * k + j] = 1.0 if k == core else 0.0
        wf = wfc.reshape(8, 1024, 2048)[core].reshape(16, 128, 1024)
        wp = wpj.reshape(8, 256, 8192)[core].reshape(16, 128, 1024)
        wslc = np.ascontiguousarray(np.concatenate([wf, wp], axis=0))
        m = dict(shared)
        m["xT"] = xTc
        m["flags"] = fl
        m["wsl"] = wslc
        in_maps.append(m)
    return in_maps


_LAST_RES = None


def kernel(**inputs):
    global _LAST_RES
    upto = int(inputs.pop("_upto", 4))
    in_maps = make_inmaps(inputs)
    nc = _get_nc(upto)
    res = run_bass_kernel_spmd(nc, in_maps, core_ids=list(range(8)))
    _LAST_RES = res
    out = np.empty((4, 4096, D), dtype=np.float32)
    for core in range(8):
        bidx, half = core // 2, core % 2
        o = np.asarray(res.results[core]["outT"]).reshape(D, TOK)
        out[bidx, 2048 * half:2048 * half + 2048, :] = o.T
    return out


# version 04
# speedup vs baseline: 1.1617x; 1.1617x over previous
import numpy as np
import ml_dtypes
from contextlib import ExitStack

import concourse.bass as bass
import concourse.mybir as mybir
from concourse.bass_utils import run_bass_kernel_spmd

F32 = mybir.dt.float32
BF16 = mybir.dt.bfloat16
FP8 = mybir.dt.float8e4
U8 = mybir.dt.uint8
ALU = mybir.AluOpType
ACTF = mybir.ActivationFunctionType
AX = mybir.AxisListType

P = 128
D = 2048
NT = 16
TOK = 2048
CH = 512
NCH = TOK // CH
INTER = 8192
NJ = 64
DH = 512
NDH = 4
K1 = 1.0 / (np.sqrt(2048.0) * 64.0)
K2 = 1.0 / (np.sqrt(8192.0) * 64.0)
EPS = 1e-5
ARENA = 193536


def _butterfly_levels(n):
    out = []
    h = 1
    while h < n:
        pairs = []
        g = 0
        while g < n:
            for k in range(h):
                pairs.append((g + k, g + h + k))
            g += 2 * h
        out.append(pairs)
        h *= 2
    return out


class _B:
    pass


def _mk():
    b = _B()
    import os as _os
    nc = bass.Bass("TRN2", target_bir_lowering=False,
                   detect_race_conditions=_os.environ.get("KERNEL_RACE_DETECT", "0") == "1")
    b.nc = nc
    b.stack = ExitStack()
    dr = lambda name, shape, dt, kind="Internal": nc.dram_tensor(name, shape, dt, kind=kind)
    b.xT = dr("xT", [NT, P, TOK], F32, "ExternalInput").ap()
    b.flags = dr("flags", [1, 64], F32, "ExternalInput").ap()
    b.wsl = dr("wsl", [32, P, 1024], F32, "ExternalInput").ap()
    b.outT = dr("outT", [NT, P, TOK], F32, "ExternalOutput").ap()
    b.dbg = dr("dbg", [1, 128], F32, "ExternalOutput").ap()
    b.wvT = dr("wvT", [16, P, NT, P], F32, "ExternalInput").ap()
    b.woT = dr("woT", [16, P, NT, P], F32, "ExternalInput").ap()
    b.bv0 = dr("bv0", [1, D], F32, "ExternalInput").ap()
    b.bo0 = dr("bo0", [1, D], F32, "ExternalInput").ap()
    b.wupT = dr("wupT", [NJ, P, NT, P], F32, "ExternalInput").ap()
    b.wgT = dr("wgT", [NJ, P, NT, P], F32, "ExternalInput").ap()
    b.wdT = dr("wdT", [NT, P, NJ, P], F32, "ExternalInput").ap()
    b.su_ug = dr("su_ug", [1, D], F32, "ExternalInput").ap()
    b.su_dp = dr("su_dp", [1, INTER], F32, "ExternalInput").ap()
    b.ws_ug = dr("ws_ug", [1, 2 * INTER], F32, "ExternalInput").ap()
    b.ws_dp = dr("ws_dp", [1, D], F32, "ExternalInput").ap()
    b.wuT = dr("wuT", [NT, P, NT, P], F32, "ExternalInput").ap()
    b.wvT1 = dr("wvT1", [NT, P, NT, P], F32, "ExternalInput").ap()
    b.wg1T = dr("wg1T", [NDH, P, NT, P], F32, "ExternalInput").ap()
    b.wg2T = dr("wg2T", [NT, P, NDH, P], F32, "ExternalInput").ap()
    b.bu1 = dr("bu1", [1, D], F32, "ExternalInput").ap()
    b.bv1 = dr("bv1", [1, D], F32, "ExternalInput").ap()
    b.bg1 = dr("bg1", [1, DH], F32, "ExternalInput").ap()
    b.bg2 = dr("bg2", [1, D], F32, "ExternalInput").ap()
    b.convw = dr("convw", [1, 3 * D], F32, "ExternalInput").ap()
    b.convb = dr("convb", [1, D], F32, "ExternalInput").ap()
    b.wfcT = dr("wfcT", [NJ, P, NT, P], F32, "ExternalInput").ap()
    b.wpjT = dr("wpjT", [NT, P, NJ, P], F32, "ExternalInput").ap()
    b.resid = dr("resid", [NT, P, TOK], F32).ap()
    b.zdr = dr("zdr", [NJ, P, TOK], BF16).ap()
    b.hdr = dr("hdr", [NJ, P, TOK], BF16).ap()
    b.xlndr = dr("xlndr", [NT, P, TOK], BF16).ap()
    b.gatesdr = dr("gatesdr", [NT, P, TOK], BF16).ap()
    b.kcdr = dr("kcdr", [NT, P, TOK], BF16).ap()
    b.xln4dr = dr("xln4dr", [NT, P, TOK], F32).ap()
    b.ydr = dr("ydr", [NT, P, TOK], BF16).ap()
    b.cc1i = dr("cc1i", [P, 48], F32)
    b.cc1o = dr("cc1o", [P, 48], F32)
    b.cc2i = dr("cc2i", [1, 32], F32)
    b.cc2o = dr("cc2o", [1, 32], F32)
    b.cc3i = dr("cc3i", [1, 32], F32)
    b.cc3o = dr("cc3o", [1, 32], F32)
    b.ccmb = dr("ccmb", [1, 32], F32)
    b.ccmc = dr("ccmc", [1, 32], F32)
    Hm = np.array([[1.0]])
    while Hm.shape[0] < P:
        Hm = np.block([[Hm, Hm], [Hm, -Hm]])
    b.h128c = nc.inline_tensor(np.ascontiguousarray(Hm.astype(ml_dtypes.bfloat16)), "h128c").ap()
    b.eyec = nc.inline_tensor(np.eye(P, dtype=np.float32), "eyec").ap()

    st = b.stack
    sb = lambda name, shape, dt: st.enter_context(nc.sbuf_tensor(name, shape, dt))
    b.arena = sb("arena", [P, ARENA], U8)
    b.gbias = sb("gbias", [P, 416], F32)
    b.gb_off = {}
    b.gb_next = 0
    b.grows = sb("grows", [1, 2 * CH], F32)
    b.grows2 = sb("grows2", [1, CH], F32)
    b.gflags = sb("gflags", [1, 64], F32)
    b.gwvp = sb("gwvp", [P, NT * P], BF16)
    b.gmo = sb("gmo", [P, NT * P], BF16)
    b.gh128 = sb("gh128", [P, P], BF16)
    b.gh128su = sb("gh128su", [P, 2 * P], BF16)
    b.geye = sb("geye", [P, P], F32)
    b.gones_bf = sb("gones_bf", [P, 1], BF16)
    b.ginv_bf = sb("ginv_bf", [P, 1], BF16)
    b.gones_row = sb("gones_row", [1, P], F32)
    b.gstkc = sb("gstkc", [P, NT], F32)
    b.ghalo = sb("ghalo", [P, 48], F32)
    b.gedge = sb("gedge", [P, 4 * NT], F32)
    b.gscal = sb("gscal", [P, 8], F32)
    b.gmax = sb("gmax", [P, 8], F32)
    b.gcc2 = sb("gcc2", [1, 32], F32)
    b.grow1 = sb("grow1", [1, 16], F32)
    b.gjunk = sb("gjunk", [1, 8], F32)
    b.pbig0 = st.enter_context(nc.psum_tensor("pbig0", [P, 4 * CH], F32))
    b.pbig1 = st.enter_context(nc.psum_tensor("pbig1", [P, 4 * CH], F32))
    sem = lambda name: st.enter_context(nc.semaphore(name))
    b.sW = sem("sW")
    b.sD = sem("sD")
    b.sPE = sem("sPE")
    b.sACT = sem("sACT")
    b.sDVE = sem("sDVE")
    b.sCC = sem("sCC")
    b.sQ = [sem(f"sQ{i}") for i in range(8)]
    b.cQ = [0] * 8
    b.cW = b.cD = b.cPE = b.cACT = b.cDVE = b.cCC = 0
    return b


def gb_alloc(b, name, n):
    off = b.gb_next
    assert off + n <= 416, (name, off, n)
    b.gb_off[name] = (off, n)
    b.gb_next = off + n


def gb(b, name, i=0, n=1):
    off, tot = b.gb_off[name]
    assert i + n <= tot, (name, i, n)
    return b.gbias[:, off + i: off + i + n]


def ar(b, off, dt, cols):
    esz = {F32: 4, BF16: 2, FP8: 1}[dt]
    assert off % 512 == 0 and off + cols * esz <= ARENA, (off, cols)
    return b.arena[:, off: off + cols * esz].bitcast(dt)


def vpad(b, v):
    v.tensor_copy(b.gjunk[:, 4:6], b.gjunk[:, 0:2])
    v.tensor_copy(b.gjunk[:, 6:8], b.gjunk[:, 0:2])


def wait_all(b, e):
    if b.cD:
        e.wait_ge(b.sD, b.cD)
    if b.cW:
        e.wait_ge(b.sW, b.cW)


def _t3(ap, nt=NT):
    return ap.rearrange("l (T t) -> l T t", T=nt)


def emit_setup(b):
    nc = b.nc
    for nm, n in [("bo", NT), ("bu", NT), ("bv1", NT), ("bg2", NT), ("convb", NT),
                  ("bg1", NDH), ("bvp", 1), ("sug", NT), ("sdp", NJ),
                  ("wsu", NJ), ("wsg", NJ), ("wsd", NT), ("cw", 3 * NT),
                  ("bvtmp", NT), ("isodd", 1), ("iseven", 1)]:
        gb_alloc(b, nm, n)
    wtmp = ar(b, 0, F32, NT * P)
    wacc = ar(b, 16384, F32, NT * P)
    wred = ar(b, 32768, F32, P)

    with nc.Block() as blk, nc.allow_non_contiguous_dma(reason="tiny one-time bias loads"):
        @blk.sync
        def _(s):
            def ld(dst, src):
                s.dma_start(out=dst, in_=src).then_inc(b.sW, 16)
                b.cW += 16
            ld(b.gflags[:, :], b.flags)
            ld(b.gh128[:, :], b.h128c)
            ld(b.geye[:, :], b.eyec)
            def ldb(nm, src, ntiles):
                ld(gb(b, nm, 0, ntiles), src.rearrange("o (T l) -> l (o T)", l=P))
            ldb("bo", b.bo0, NT)
            ldb("bu", b.bu1, NT)
            ldb("bv1", b.bv1, NT)
            ldb("bg2", b.bg2, NT)
            ldb("convb", b.convb, NT)
            ldb("bg1", b.bg1, NDH)
            ldb("bvtmp", b.bv0, NT)
            ldb("sug", b.su_ug, NT)
            ldb("sdp", b.su_dp, NJ)
            ldb("wsd", b.ws_dp, NT)
            ld(gb(b, "wsu", 0, NJ), b.ws_ug[:, 0:INTER].rearrange("o (T l) -> l (o T)", l=P))
            ld(gb(b, "wsg", 0, NJ), b.ws_ug[:, INTER:].rearrange("o (T l) -> l (o T)", l=P))
            ld(gb(b, "cw", 0, 3 * NT), b.convw.rearrange("o (k T l) -> l (o k T)", l=P, T=NT))

        @blk.vector
        def _(v):
            v.memset(b.gones_bf[:, :], 1.0)
            v.memset(b.ginv_bf[:, :], 1.0 / 2048.0)
            v.memset(b.gones_row[:, :], 1.0)
            v.memset(b.gstkc[:, :], 0.0)
            v.memset(b.gmax[:, :], 0.0)
            v.memset(b.gedge[:, :], 0.0)
            v.memset(b.gscal[:, :], 0.0)
            v.memset(b.grow1[:, :], 0.0)
            v.memset(b.gjunk[:, :], 0.0)

    with nc.Block() as blk:
        @blk.vector
        def _(v):
            wait_all(b, v)
            for nm in ("wsu", "wsg", "wsd"):
                n = b.gb_off[nm][1]
                v.tensor_single_scalar(gb(b, nm, 0, n), gb(b, nm, 0, n), 64.0, ALU.mult)
            for nm, k in (("sug", K1), ("sdp", K2)):
                n = b.gb_off[nm][1]
                v.tensor_single_scalar(gb(b, nm, 0, n), gb(b, nm, 0, n), float(k), ALU.mult)
            v.reduce_sum(gb(b, "bvp"), gb(b, "bvtmp", 0, NT), AX.X)

    with nc.Block() as blk:
        @blk.tensor
        def _(t):
            t.matmul(b.pbig0[:, 0:2], b.gones_row[:, :], b.gflags[:, 0:2], start=True, stop=True)
    with nc.Block() as blk:
        @blk.vector
        def _(v):
            v.tensor_copy(gb(b, "isodd"), b.pbig0[:, 0:1])
            v.tensor_copy(gb(b, "iseven"), b.pbig0[:, 1:2])

    w0, d0 = b.cW, b.cDVE
    srcs = ([("wv", h) for h in range(16)] + [("wo", ob) for ob in range(16)]
            + [("wsl", p) for p in range(32)])
    WT2 = [0, 8192]

    def _src_ap(kind, i):
        if kind == "wv":
            return b.wvT[i].rearrange("l T c -> l (T c)"), NT * P
        if kind == "wo":
            return b.woT[i].rearrange("l T c -> l (T c)"), NT * P
        return b.wsl[i], 1024

    q0 = [b.cQ[0], b.cQ[1]]
    with nc.Block() as blk:
        @blk.sync
        def _(s):
            for k, (kind, i) in enumerate(srcs):
                if k >= 2:
                    s.wait_ge(b.sDVE, d0 + k - 1)
                apx, cols = _src_ap(kind, i)
                s.dma_start(out=ar(b, WT2[k % 2], F32, cols), in_=apx).then_inc(b.sQ[k % 2], 16)
                b.cQ[k % 2] += 16

        @blk.vector
        def _(v):
            for k, (kind, i) in enumerate(srcs):
                v.wait_ge(b.sQ[k % 2], q0[k % 2] + 16 * (k // 2 + 1))
                apx, cols = _src_ap(kind, i)
                buf = ar(b, WT2[k % 2], F32, cols)
                if kind == "wv":
                    if i == 0:
                        v.tensor_copy(wacc, buf)
                    else:
                        v.tensor_add(wacc, wacc, buf)
                    if i == 15:
                        vpad(b, v)
                        v.tensor_copy(b.gwvp[:, :], wacc)
                elif kind == "wo":
                    v.reduce_sum(wred, buf.rearrange("l (T c) -> l c T", T=NT), AX.X)
                    vpad(b, v)
                    v.tensor_copy(b.gmo[:, i * P:(i + 1) * P], wred)
                else:
                    col = 3 if i < 16 else 4
                    v.tensor_reduce(b.gmax[:, 2:3], buf, AX.X, ALU.max, apply_absolute_value=True)
                    vpad(b, v)
                    v.tensor_tensor(b.gmax[:, col:col + 1], b.gmax[:, col:col + 1], b.gmax[:, 2:3], ALU.max)
                v.drain().then_inc(b.sDVE, 1)
                b.cDVE += 1


def emit_stat_rows_and_rep(b):
    nc = b.nc
    d0, a0, p0 = b.cDVE, b.cACT, b.cPE
    with nc.Block() as blk:
        @blk.vector
        def _(v):
            g2 = b.grows2
            tmp = b.grows[:, 0:CH]
            v.tensor_copy(g2[:, 0:CH], b.pbig0[0:1, 0:CH])
            v.tensor_tensor(tmp, g2[:, 0:CH], g2[:, 0:CH], ALU.mult)
            v.tensor_tensor(tmp, b.pbig0[0:1, CH:2 * CH], tmp, ALU.subtract)
            v.tensor_single_scalar(tmp, tmp, EPS, ALU.add)
            v.reciprocal(b.grows[:, CH:2 * CH], tmp).then_inc(b.sDVE, 1)
            b.cDVE += 1
            v.wait_ge(b.sACT, a0 + 1)
            v.tensor_tensor(b.grows[:, CH:2 * CH], b.grows2[:, 0:CH], b.grows[:, 0:CH], ALU.mult)
            v.tensor_single_scalar(b.grows[:, CH:2 * CH], b.grows[:, CH:2 * CH],
                                   -1.0, ALU.mult).then_inc(b.sDVE, 1)
            b.cDVE += 1

        @blk.scalar
        def _(a):
            a.wait_ge(b.sDVE, d0 + 1)
            a.activation(b.grows[:, 0:CH], b.grows[:, CH:2 * CH], ACTF.Sqrt).then_inc(b.sACT, 1)
            b.cACT += 1

        @blk.tensor
        def _(t):
            t.wait_ge(b.sDVE, d0 + 2)
            t.matmul(b.pbig0[:, 2 * CH:3 * CH], b.gones_row[:, :], b.grows[:, 0:CH], start=True, stop=True)
            t.matmul(b.pbig0[:, 3 * CH:4 * CH], b.gones_row[:, :], b.grows[:, CH:2 * CH], start=True, stop=True)


def emit_stats_from(b, xb, x2b, nt=NT):
    nc = b.nc
    with nc.Block() as blk:
        @blk.tensor
        def _(t):
            for T in range(nt):
                t.matmul(b.pbig0[0:1, 0:CH], b.ginv_bf[:, 0:1], xb[:, T * CH:(T + 1) * CH],
                         start=(T == 0), stop=(T == nt - 1))
            for T in range(nt):
                t.matmul(b.pbig0[0:1, CH:2 * CH], b.ginv_bf[:, 0:1], x2b[:, T * CH:(T + 1) * CH],
                         start=(T == 0), stop=(T == nt - 1))
    emit_stat_rows_and_rep(b)


def emit_preln(b, c, src_dram, affine_dst=None, affine_nt=NT):
    nc = b.nc
    c0 = c * CH
    rc = ar(b, 0, F32, NT * CH)
    xb = ar(b, 33280, BF16, NT * CH)
    x2b = ar(b, 50176, BF16, NT * CH)
    dd0, d0, a0, p0 = b.cD, b.cDVE, b.cACT, b.cPE
    with nc.Block() as blk:
        @blk.gpsimd
        def _(g):
            wait_all(b, g)
            g.dma_start(out=_t3(rc), in_=src_dram[:, :, c0:c0 + CH].rearrange("T l t -> l T t")).then_inc(b.sD, 16)
            b.cD += 16

        @blk.vector
        def _(v):
            v.wait_ge(b.sD, b.cD)
            v.tensor_copy(xb, rc).then_inc(b.sDVE, 1)
            b.cDVE += 1
            v.wait_ge(b.sPE, p0 + 1)
            g2 = b.grows2
            tmp = b.grows[:, 0:CH]
            v.tensor_copy(g2[:, 0:CH], b.pbig0[0:1, 0:CH])
            v.tensor_tensor(tmp, g2[:, 0:CH], g2[:, 0:CH], ALU.mult)
            v.tensor_tensor(tmp, b.pbig0[0:1, CH:2 * CH], tmp, ALU.subtract)
            v.tensor_single_scalar(tmp, tmp, EPS, ALU.add)
            v.reciprocal(b.grows[:, CH:2 * CH], tmp).then_inc(b.sDVE, 1)
            b.cDVE += 1
            v.wait_ge(b.sACT, a0 + 2)
            v.tensor_tensor(b.grows[:, CH:2 * CH], b.grows2[:, 0:CH], b.grows[:, 0:CH], ALU.mult)
            v.tensor_single_scalar(b.grows[:, CH:2 * CH], b.grows[:, CH:2 * CH],
                                   -1.0, ALU.mult).then_inc(b.sDVE, 1)
            b.cDVE += 1
            if affine_dst is not None:
                v.wait_ge(b.sPE, p0 + 2)
                for T in range(affine_nt):
                    sl = slice(T * CH, (T + 1) * CH)
                    v.tensor_tensor(affine_dst[:, sl], rc[:, sl], b.pbig0[:, 2 * CH:3 * CH], ALU.mult)
                for T in range(affine_nt):
                    sl = slice(T * CH, (T + 1) * CH)
                    v.tensor_tensor(affine_dst[:, sl], affine_dst[:, sl], b.pbig0[:, 3 * CH:4 * CH], ALU.add)

        @blk.scalar
        def _(a):
            a.wait_ge(b.sD, b.cD)
            a.activation(x2b, rc, ACTF.Square).then_inc(b.sACT, 1)
            b.cACT += 1
            a.wait_ge(b.sDVE, d0 + 2)
            a.activation(b.grows[:, 0:CH], b.grows[:, CH:2 * CH], ACTF.Sqrt).then_inc(b.sACT, 1)
            b.cACT += 1

        @blk.tensor
        def _(t):
            t.wait_ge(b.sDVE, d0 + 1)
            t.wait_ge(b.sACT, a0 + 1)
            for T in range(NT):
                t.matmul(b.pbig0[0:1, 0:CH], b.ginv_bf[:, 0:1], xb[:, T * CH:(T + 1) * CH],
                         start=(T == 0), stop=(T == NT - 1))
            for T in range(NT):
                t.matmul(b.pbig0[0:1, CH:2 * CH], b.ginv_bf[:, 0:1], x2b[:, T * CH:(T + 1) * CH],
                         start=(T == 0), stop=(T == NT - 1))
            t.drain().then_inc(b.sPE, 1)
            b.cPE += 1
            t.wait_ge(b.sDVE, d0 + 3)
            t.matmul(b.pbig0[:, 2 * CH:3 * CH], b.gones_row[:, :], b.grows[:, 0:CH], start=True, stop=True)
            t.matmul(b.pbig0[:, 3 * CH:4 * CH], b.gones_row[:, :], b.grows[:, CH:2 * CH],
                     start=True, stop=True).then_inc(b.sPE, 1)
            b.cPE += 1


def emit_affine_to(b, srcf32, dst, nt=NT):
    nc = b.nc
    with nc.Block() as blk:
        @blk.vector
        def _(v):
            for T in range(nt):
                sl = slice(T * CH, (T + 1) * CH)
                v.tensor_tensor(dst[:, sl], srcf32[:, sl], b.pbig0[:, 2 * CH:3 * CH], ALU.mult)
            for T in range(nt):
                sl = slice(T * CH, (T + 1) * CH)
                v.tensor_tensor(dst[:, sl], dst[:, sl], b.pbig0[:, 3 * CH:4 * CH], ALU.add)


def emit_postln_resid(b, c, ybuf, rc, t1, dst_dram, nt=NT):
    nc = b.nc
    c0 = c * CH
    d0 = b.cDVE
    with nc.Block() as blk:
        @blk.vector
        def _(v):
            for T in range(nt):
                sl = slice(T * CH, (T + 1) * CH)
                v.tensor_tensor(t1[:, sl], ybuf[:, sl], b.pbig0[:, 2 * CH:3 * CH], ALU.mult)
            for T in range(nt):
                sl = slice(T * CH, (T + 1) * CH)
                v.tensor_tensor(t1[:, sl], t1[:, sl], b.pbig0[:, 3 * CH:4 * CH], ALU.add)
            for T in range(nt):
                sl = slice(T * CH, (T + 1) * CH)
                v.tensor_tensor(t1[:, sl], t1[:, sl], rc[:, sl], ALU.add)
            v.drain().then_inc(b.sDVE, 1)
            b.cDVE += 1

        @blk.gpsimd
        def _(g):
            g.wait_ge(b.sDVE, d0 + 1)
            g.dma_start(out=dst_dram[:, :, c0:c0 + CH].rearrange("T l t -> l T t"), in_=_t3(t1)).then_inc(b.sD, 16)
            b.cD += 16


def emit_l0_attn(b):
    nc = b.nc
    XLN, Y, SQ, WB, T1 = 67584, 84480, 101376, 103424, 103936
    for c in range(NCH):
        rc = ar(b, 0, F32, NT * CH)
        xln = ar(b, XLN, BF16, NT * CH)
        emit_preln(b, c, b.xT, affine_dst=xln)
        ybuf = ar(b, Y, BF16, NT * CH)
        sq = ar(b, SQ, BF16, 2 * CH)
        wb = ar(b, WB, BF16, CH)
        t1 = ar(b, T1, F32, NT * CH)
        with nc.Block() as blk:
            @blk.tensor
            def _(t):
                for T in range(NT):
                    t.matmul(b.pbig1[:, 0:CH], b.gwvp[:, T * P:(T + 1) * P], xln[:, T * CH:(T + 1) * CH],
                             start=(T == 0), stop=(T == NT - 1))
        with nc.Block() as blk:
            @blk.scalar
            def _(a):
                a.activation(wb, b.pbig1[:, 0:CH], ACTF.Identity, bias=gb(b, "bvp"))
        a0, p0 = b.cACT, b.cPE
        with nc.Block() as blk:
            @blk.tensor
            def _(t):
                for ob in range(NT):
                    if ob >= 2:
                        t.wait_ge(b.sACT, a0 + 2 * (ob - 1))
                    t.matmul(b.pbig1[:, (1 + ob % 2) * CH:(2 + ob % 2) * CH],
                             b.gmo[:, ob * P:(ob + 1) * P], wb, start=True, stop=True)
                    if ob >= 1:
                        t.wait_ge(b.sACT, a0 + 2 * ob)
                        pob = ob - 1
                        t.matmul(b.pbig0[0:1, 0:CH], b.ginv_bf[:, 0:1], ybuf[:, pob * CH:(pob + 1) * CH],
                                 start=(pob == 0), stop=False)
                        t.matmul(b.pbig0[0:1, CH:2 * CH], b.ginv_bf[:, 0:1],
                                 sq[:, (pob % 2) * CH:(pob % 2 + 1) * CH],
                                 start=(pob == 0), stop=False)
                    t.drain().then_inc(b.sPE, 1)
                    b.cPE += 1
                t.wait_ge(b.sACT, a0 + 2 * NT)
                pob = NT - 1
                t.matmul(b.pbig0[0:1, 0:CH], b.ginv_bf[:, 0:1], ybuf[:, pob * CH:(pob + 1) * CH],
                         start=False, stop=True)
                t.matmul(b.pbig0[0:1, CH:2 * CH], b.ginv_bf[:, 0:1],
                         sq[:, (pob % 2) * CH:(pob % 2 + 1) * CH],
                         start=False, stop=True)

            @blk.scalar
            def _(a):
                for ob in range(NT):
                    a.wait_ge(b.sPE, p0 + ob + 1)
                    a.activation(ybuf[:, ob * CH:(ob + 1) * CH], b.pbig1[:, (1 + ob % 2) * CH:(2 + ob % 2) * CH],
                                 ACTF.Identity, bias=gb(b, "bo", ob))
                    a.activation(sq[:, (ob % 2) * CH:(ob % 2 + 1) * CH],
                                 ybuf[:, ob * CH:(ob + 1) * CH], ACTF.Square).then_inc(b.sACT, 2)
                    b.cACT += 2
        emit_stat_rows_and_rep(b)
        emit_postln_resid(b, c, ybuf, rc, t1, b.resid)


def emit_l0_ffn(b):
    nc = b.nc
    XQ, BUFA, BUFB = 101376, 67584, 84480
    xq = ar(b, XQ, BF16, NT * TOK)
    for c in range(NCH):
        rc = ar(b, 0, F32, NT * CH)
        bufa = ar(b, BUFA, BF16, NT * CH)
        bufb = ar(b, BUFB, BF16, NT * CH)
        emit_preln(b, c, b.resid, affine_dst=bufb)
        with nc.Block() as blk:
            @blk.vector
            def _(v):
                for T in range(NT):
                    sl = slice(T * CH, (T + 1) * CH)
                    v.tensor_scalar(bufa[:, sl], bufb[:, sl], gb(b, "sug", T), None, ALU.mult)
        bufs, cur = [bufa, bufb], 0
        for pairs in _butterfly_levels(NT):
            src, dst = bufs[cur], bufs[1 - cur]
            with nc.Block() as blk:
                @blk.vector
                def _(v):
                    for (i, j) in pairs:
                        v.tensor_tensor(dst[:, i * CH:(i + 1) * CH], src[:, i * CH:(i + 1) * CH],
                                        src[:, j * CH:(j + 1) * CH], ALU.add)
                        v.tensor_tensor(dst[:, j * CH:(j + 1) * CH], src[:, i * CH:(i + 1) * CH],
                                        src[:, j * CH:(j + 1) * CH], ALU.subtract)
            cur = 1 - cur
        final = bufs[cur]
        for grp in range(NT // 4):
            with nc.Block() as blk:
                @blk.tensor
                def _(t):
                    for k in range(4):
                        T = grp * 4 + k
                        t.matmul(b.pbig1[:, k * CH:(k + 1) * CH], b.gh128[:, :],
                                 final[:, T * CH:(T + 1) * CH], start=True, stop=True)
            with nc.Block() as blk:
                @blk.scalar
                def _(a):
                    for k in range(4):
                        T = grp * 4 + k
                        a.activation(xq[:, T * TOK + c * CH: T * TOK + c * CH + CH],
                                     b.pbig1[:, k * CH:(k + 1) * CH], ACTF.Copy)
    if b.sub_l0ffn >= 2:
        emit_a2(b, xq)
    if b.sub_l0ffn >= 3:
        emit_a3(b)


def emit_a2_serial(b, xq):
    nc = b.nc
    WUP = [0, 8192]
    WG = [16384, 24576]
    WUPB = [32768, 36864]
    WGB = [40960, 45056]
    SILU = 49152
    XMID = 53248
    ZSB = [57344, 61440]
    for j in range(NJ):
        with nc.Block() as blk:
            @blk.sync
            def _(s):
                if j == 0:
                    wait_all(b, s)
                s.dma_start(out=ar(b, WUP[0], F32, NT * P),
                            in_=b.wupT[j].rearrange("l T c -> l (T c)")).then_inc(b.sW, 16)
                s.dma_start(out=ar(b, WG[0], F32, NT * P),
                            in_=b.wgT[j].rearrange("l T c -> l (T c)")).then_inc(b.sW, 16)
                b.cW += 32
                s.wait_ge(b.sW, b.cW)

            @blk.vector
            def _(v):
                v.wait_ge(b.sW, b.cW)
                v.tensor_copy(ar(b, WUPB[0], BF16, NT * P), ar(b, WUP[0], F32, NT * P))
                v.tensor_copy(ar(b, WGB[0], BF16, NT * P), ar(b, WG[0], F32, NT * P))
                v.tensor_scalar(b.gh128su[:, 0:P], b.gh128[:, :], gb(b, "sdp", j), None, ALU.mult)
        with nc.Block() as blk:
            @blk.tensor
            def _(t):
                wupb = ar(b, WUPB[0], BF16, NT * P)
                wgb = ar(b, WGB[0], BF16, NT * P)
                for T in range(NT):
                    for g in range(4):
                        t.matmul(b.pbig0[:, g * CH:(g + 1) * CH], wupb[:, T * P:(T + 1) * P],
                                 xq[:, T * TOK + g * CH: T * TOK + (g + 1) * CH],
                                 start=(T == 0), stop=(T == NT - 1))
                for T in range(NT):
                    for g in range(4):
                        t.matmul(b.pbig1[:, g * CH:(g + 1) * CH], wgb[:, T * P:(T + 1) * P],
                                 xq[:, T * TOK + g * CH: T * TOK + (g + 1) * CH],
                                 start=(T == 0), stop=(T == NT - 1))
        with nc.Block() as blk:
            @blk.scalar
            def _(a):
                a.activation(ar(b, SILU, BF16, TOK), b.pbig1[:, :], ACTF.Silu, scale=gb(b, "wsg", j))
        with nc.Block() as blk:
            @blk.vector
            def _(v):
                v.scalar_tensor_tensor(ar(b, XMID, BF16, TOK), b.pbig0[:, :], gb(b, "wsu", j),
                                       ar(b, SILU, BF16, TOK), ALU.mult, ALU.mult)
        with nc.Block() as blk:
            @blk.tensor
            def _(t):
                xmid = ar(b, XMID, BF16, TOK)
                for g in range(4):
                    t.matmul(b.pbig0[:, g * CH:(g + 1) * CH], b.gh128su[:, 0:P],
                             xmid[:, g * CH:(g + 1) * CH], start=True, stop=True)
        with nc.Block() as blk:
            @blk.scalar
            def _(a):
                a.activation(ar(b, ZSB[0], BF16, TOK), b.pbig0[:, :], ACTF.Copy)
        with nc.Block() as blk:
            @blk.gpsimd
            def _(g):
                g.dma_start(out=b.zdr[j], in_=ar(b, ZSB[0], BF16, TOK)).then_inc(b.sD, 16)
                b.cD += 16
                g.wait_ge(b.sD, b.cD)


def emit_a2_min(b, xq):
    nc = b.nc
    WUP = [0, 8192]
    WG = [16384, 24576]
    WUPB = [32768, 36864]
    WGB = [40960, 45056]
    w0, d0, p0 = b.cW, b.cDVE, b.cPE
    with nc.Block() as blk:
        @blk.sync
        def _(s):
            wait_all(b, s)
            for j in (0, 1):
                s.dma_start(out=ar(b, WUP[j % 2], F32, NT * P),
                            in_=b.wupT[j].rearrange("l T c -> l (T c)")).then_inc(b.sW, 16)
                s.dma_start(out=ar(b, WG[j % 2], F32, NT * P),
                            in_=b.wgT[j].rearrange("l T c -> l (T c)")).then_inc(b.sW, 16)
                b.cW += 32
            for j in range(NJ - 2):
                s.wait_ge(b.sDVE, d0 + j + 1)
                s.dma_start(out=ar(b, WUP[j % 2], F32, NT * P),
                            in_=b.wupT[j + 2].rearrange("l T c -> l (T c)")).then_inc(b.sW, 16)
                s.dma_start(out=ar(b, WG[j % 2], F32, NT * P),
                            in_=b.wgT[j + 2].rearrange("l T c -> l (T c)")).then_inc(b.sW, 16)
                b.cW += 32

        @blk.vector
        def _(v):
            for j in range(NJ):
                v.wait_ge(b.sW, w0 + 32 * (j + 1))
                if j >= 2:
                    v.wait_ge(b.sPE, p0 + j - 1)
                v.tensor_copy(ar(b, WUPB[j % 2], BF16, NT * P), ar(b, WUP[j % 2], F32, NT * P))
                v.tensor_copy(ar(b, WGB[j % 2], BF16, NT * P), ar(b, WG[j % 2], F32, NT * P))
                v.drain().then_inc(b.sDVE, 1)
                b.cDVE += 1

        @blk.tensor
        def _(t):
            for j in range(NJ):
                t.wait_ge(b.sDVE, d0 + j + 1)
                wupb = ar(b, WUPB[j % 2], BF16, NT * P)
                wgb = ar(b, WGB[j % 2], BF16, NT * P)
                for T in range(NT):
                    for g in range(4):
                        t.matmul(b.pbig0[:, g * CH:(g + 1) * CH], wupb[:, T * P:(T + 1) * P],
                                 xq[:, T * TOK + g * CH: T * TOK + (g + 1) * CH],
                                 start=(T == 0), stop=(T == NT - 1))
                for T in range(NT):
                    for g in range(4):
                        t.matmul(b.pbig1[:, g * CH:(g + 1) * CH], wgb[:, T * P:(T + 1) * P],
                                 xq[:, T * TOK + g * CH: T * TOK + (g + 1) * CH],
                                 start=(T == 0), stop=(T == NT - 1))
                t.drain().then_inc(b.sPE, 1)
                b.cPE += 1


def emit_a2_minsilu(b, xq):
    nc = b.nc
    WUP = [0, 8192]
    WG = [16384, 24576]
    WUPB = [32768, 36864]
    WGB = [40960, 45056]
    SILU = 49152
    w0, d0, p0, a0 = b.cW, b.cDVE, b.cPE, b.cACT
    with nc.Block() as blk:
        @blk.sync
        def _(s):
            wait_all(b, s)
            for j in (0, 1):
                s.dma_start(out=ar(b, WUP[j % 2], F32, NT * P),
                            in_=b.wupT[j].rearrange("l T c -> l (T c)")).then_inc(b.sW, 16)
                s.dma_start(out=ar(b, WG[j % 2], F32, NT * P),
                            in_=b.wgT[j].rearrange("l T c -> l (T c)")).then_inc(b.sW, 16)
                b.cW += 32
            for j in range(NJ - 2):
                s.wait_ge(b.sDVE, d0 + j + 1)
                s.dma_start(out=ar(b, WUP[j % 2], F32, NT * P),
                            in_=b.wupT[j + 2].rearrange("l T c -> l (T c)")).then_inc(b.sW, 16)
                s.dma_start(out=ar(b, WG[j % 2], F32, NT * P),
                            in_=b.wgT[j + 2].rearrange("l T c -> l (T c)")).then_inc(b.sW, 16)
                b.cW += 32

        @blk.vector
        def _(v):
            import os as _os4
            dvp = _os4.environ.get("A2_DVE_PSUM", "0") == "1"
            for j in range(NJ):
                v.wait_ge(b.sW, w0 + 32 * (j + 1))
                if j >= 2:
                    v.wait_ge(b.sPE, p0 + j - 1)
                v.tensor_copy(ar(b, WUPB[j % 2], BF16, NT * P), ar(b, WUP[j % 2], F32, NT * P))
                v.tensor_copy(ar(b, WGB[j % 2], BF16, NT * P), ar(b, WG[j % 2], F32, NT * P))
                if dvp and j > 0:
                    v.tensor_copy(ar(b, SILU, BF16, TOK), b.pbig0[:, :])
                v.drain().then_inc(b.sDVE, 1)
                b.cDVE += 1

        @blk.tensor
        def _(t):
            for j in range(NJ):
                t.wait_ge(b.sDVE, d0 + j + 1)
                if j > 0:
                    t.wait_ge(b.sACT, a0 + j)
                wupb = ar(b, WUPB[j % 2], BF16, NT * P)
                wgb = ar(b, WGB[j % 2], BF16, NT * P)
                for T in range(NT):
                    for g in range(4):
                        t.matmul(b.pbig0[:, g * CH:(g + 1) * CH], wupb[:, T * P:(T + 1) * P],
                                 xq[:, T * TOK + g * CH: T * TOK + (g + 1) * CH],
                                 start=(T == 0), stop=(T == NT - 1))
                for T in range(NT):
                    for g in range(4):
                        t.matmul(b.pbig1[:, g * CH:(g + 1) * CH], wgb[:, T * P:(T + 1) * P],
                                 xq[:, T * TOK + g * CH: T * TOK + (g + 1) * CH],
                                 start=(T == 0), stop=(T == NT - 1))
                t.drain().then_inc(b.sPE, 1)
                b.cPE += 1

        @blk.scalar
        def _(a):
            for j in range(NJ):
                a.wait_ge(b.sPE, p0 + j + 1)
                import os as _os3
                if _os3.environ.get("A2_SILU_SBUF", "0") == "1":
                    a.activation(ar(b, SILU, BF16, TOK), xq[:, 0:TOK], ACTF.Silu,
                                 scale=gb(b, "wsg", j)).then_inc(b.sACT, 1)
                else:
                    a.activation(ar(b, SILU, BF16, TOK), b.pbig1[:, :], ACTF.Silu,
                                 scale=gb(b, "wsg", j)).then_inc(b.sACT, 1)
                b.cACT += 1


def emit_a2(b, xq):
    import os as _os2
    if _os2.environ.get("A2_MODE", "pipe") == "serial":
        return emit_a2_serial(b, xq)
    if _os2.environ.get("A2_MODE", "pipe") == "min":
        return emit_a2_min(b, xq)
    if _os2.environ.get("A2_MODE", "pipe") == "minsilu":
        return emit_a2_minsilu(b, xq)
    noz = _os2.environ.get("A2_NOZ", "0") == "1"
    nc = b.nc
    WUP = [0, 8192]
    WG = [16384, 24576]
    WUPB = [32768, 36864]
    WGB = [40960, 45056]
    SILU = 49152
    XMID = 53248
    ZSB = [57344, 61440]
    w0, d0, a0, p0, dd0 = b.cW, b.cDVE, b.cACT, b.cPE, b.cD
    q0 = list(b.cQ)
    with nc.Block() as blk:
        @blk.sync
        def _(s):
            wait_all(b, s)
            for j in (0, 1):
                s.dma_start(out=ar(b, WUP[j % 2], F32, NT * P),
                            in_=b.wupT[j].rearrange("l T c -> l (T c)")).then_inc(b.sQ[j % 2], 16)
                s.dma_start(out=ar(b, WG[j % 2], F32, NT * P),
                            in_=b.wgT[j].rearrange("l T c -> l (T c)")).then_inc(b.sQ[2 + j % 2], 16)
                b.cQ[j % 2] += 16
                b.cQ[2 + j % 2] += 16
            for j in range(NJ - 2):
                s.wait_ge(b.sDVE, d0 + 2 * j + 1)
                s.dma_start(out=ar(b, WUP[j % 2], F32, NT * P),
                            in_=b.wupT[j + 2].rearrange("l T c -> l (T c)")).then_inc(b.sQ[j % 2], 16)
                s.dma_start(out=ar(b, WG[j % 2], F32, NT * P),
                            in_=b.wgT[j + 2].rearrange("l T c -> l (T c)")).then_inc(b.sQ[2 + j % 2], 16)
                b.cQ[j % 2] += 16
                b.cQ[2 + j % 2] += 16

        @blk.vector
        def _(v):
            for j in range(NJ):
                v.wait_ge(b.sQ[j % 2], q0[j % 2] + 16 * (j // 2 + 1))
                v.wait_ge(b.sQ[2 + j % 2], q0[2 + j % 2] + 16 * (j // 2 + 1))
                if j >= 2:
                    v.wait_ge(b.sPE, p0 + 2 * (j - 2) + 2)
                v.tensor_copy(ar(b, WUPB[j % 2], BF16, NT * P), ar(b, WUP[j % 2], F32, NT * P))
                v.tensor_copy(ar(b, WGB[j % 2], BF16, NT * P), ar(b, WG[j % 2], F32, NT * P))
                v.tensor_scalar(b.gh128su[:, (j % 2) * P:(j % 2 + 1) * P], b.gh128[:, :],
                                gb(b, "sdp", j), None, ALU.mult)
                v.drain().then_inc(b.sDVE, 1)
                b.cDVE += 1
                v.wait_ge(b.sACT, a0 + 2 * j + 1)
                v.scalar_tensor_tensor(ar(b, XMID, BF16, TOK), b.pbig0[:, :], gb(b, "wsu", j),
                                       ar(b, SILU, BF16, TOK), ALU.mult, ALU.mult)
                v.drain().then_inc(b.sDVE, 1)
                b.cDVE += 1

        @blk.tensor
        def _(t):
            for j in range(NJ):
                t.wait_ge(b.sDVE, d0 + 2 * j + 1)
                if j > 0:
                    t.wait_ge(b.sACT, a0 + 2 * j)
                wupb = ar(b, WUPB[j % 2], BF16, NT * P)
                wgb = ar(b, WGB[j % 2], BF16, NT * P)
                for T in range(NT):
                    for g in range(4):
                        t.matmul(b.pbig0[:, g * CH:(g + 1) * CH], wupb[:, T * P:(T + 1) * P],
                                 xq[:, T * TOK + g * CH: T * TOK + (g + 1) * CH],
                                 start=(T == 0), stop=(T == NT - 1))
                for T in range(NT):
                    for g in range(4):
                        t.matmul(b.pbig1[:, g * CH:(g + 1) * CH], wgb[:, T * P:(T + 1) * P],
                                 xq[:, T * TOK + g * CH: T * TOK + (g + 1) * CH],
                                 start=(T == 0), stop=(T == NT - 1))
                t.drain().then_inc(b.sPE, 1)
                b.cPE += 1
                t.wait_ge(b.sDVE, d0 + 2 * j + 2)
                if not noz:
                    xmid = ar(b, XMID, BF16, TOK)
                    for g in range(4):
                        t.matmul(b.pbig0[:, g * CH:(g + 1) * CH], b.gh128su[:, (j % 2) * P:(j % 2 + 1) * P],
                                 xmid[:, g * CH:(g + 1) * CH], start=True, stop=True)
                t.drain().then_inc(b.sPE, 1)
                b.cPE += 1

        @blk.scalar
        def _(a):
            for j in range(NJ):
                a.wait_ge(b.sPE, p0 + 2 * j + 1)
                a.activation(ar(b, SILU, BF16, TOK), b.pbig1[:, :], ACTF.Silu, scale=gb(b, "wsg", j))
                a.drain().then_inc(b.sACT, 1)
                b.cACT += 1
                a.wait_ge(b.sPE, p0 + 2 * j + 2)
                if (not noz) and j >= 2:
                    a.wait_ge(b.sQ[4 + j % 2], q0[4 + j % 2] + 16 * ((j - 2) // 2 + 1))
                if not noz:
                    a.activation(ar(b, ZSB[j % 2], BF16, TOK), b.pbig0[:, :], ACTF.Copy)
                a.drain().then_inc(b.sACT, 1)
                b.cACT += 1

        @blk.gpsimd
        def _(g):
            if not noz:
                for j in range(NJ):
                    g.wait_ge(b.sACT, a0 + 2 * j + 2)
                    g.dma_start(out=b.zdr[j], in_=ar(b, ZSB[j % 2], BF16, TOK)).then_inc(b.sQ[4 + j % 2], 16)
                    b.cQ[4 + j % 2] += 16


def emit_a3(b):
    nc = b.nc
    ZA, ZB = 0, 65536
    WD = [131072, 139264]
    WDB = [147456, 151552]
    YB, SQ = 155648, 172032
    T1 = 32768
    for c in range(NCH):
        c0 = c * CH
        za = ar(b, ZA, BF16, NJ * CH)
        zb = ar(b, ZB, BF16, NJ * CH)
        with nc.Block() as blk:
            @blk.gpsimd
            def _(g):
                wait_all(b, g)
                for j in range(NJ):
                    g.dma_start(out=za[:, j * CH:(j + 1) * CH],
                                in_=b.zdr[j, :, c0:c0 + CH]).then_inc(b.sD, 16)
                    b.cD += 16
        with nc.Block() as blk:
            @blk.vector
            def _(v):
                v.wait_ge(b.sD, b.cD)
        bufs, cur = [za, zb], 0
        for pairs in _butterfly_levels(NJ):
            src, dst = bufs[cur], bufs[1 - cur]
            with nc.Block() as blk:
                @blk.vector
                def _(v):
                    for (i, j) in pairs:
                        v.tensor_tensor(dst[:, i * CH:(i + 1) * CH], src[:, i * CH:(i + 1) * CH],
                                        src[:, j * CH:(j + 1) * CH], ALU.add)
                        v.tensor_tensor(dst[:, j * CH:(j + 1) * CH], src[:, i * CH:(i + 1) * CH],
                                        src[:, j * CH:(j + 1) * CH], ALU.subtract)
            cur = 1 - cur
        zfin = bufs[cur]
        ybuf = ar(b, YB, BF16, NT * CH)
        sq = ar(b, SQ, BF16, 2 * CH)
        w0, d0, a0, p0 = b.cW, b.cDVE, b.cACT, b.cPE
        q0 = list(b.cQ)
        with nc.Block() as blk:
            @blk.sync
            def _(s):
                wait_all(b, s)
                for k in range(2):
                    ob, q = divmod(k, 4)
                    s.dma_start(out=ar(b, WD[k % 2], F32, NT * P),
                                in_=b.wdT[ob, :, q * NT:(q + 1) * NT, :].rearrange("l T c -> l (T c)")).then_inc(b.sQ[k % 2], 16)
                    b.cQ[k % 2] += 16
                for k in range(4 * NT - 2):
                    ob, q = divmod(k + 2, 4)
                    s.wait_ge(b.sDVE, d0 + k + 1)
                    s.dma_start(out=ar(b, WD[k % 2], F32, NT * P),
                                in_=b.wdT[ob, :, q * NT:(q + 1) * NT, :].rearrange("l T c -> l (T c)")).then_inc(b.sQ[k % 2], 16)
                    b.cQ[k % 2] += 16

            @blk.vector
            def _(v):
                for k in range(4 * NT):
                    v.wait_ge(b.sQ[k % 2], q0[k % 2] + 16 * (k // 2 + 1))
                    if k >= 2:
                        v.wait_ge(b.sPE, p0 + k - 1)
                    v.tensor_copy(ar(b, WDB[k % 2], BF16, NT * P), ar(b, WD[k % 2], F32, NT * P))
                    v.drain().then_inc(b.sDVE, 1)
                    b.cDVE += 1

            @blk.tensor
            def _(t):
                for k in range(4 * NT):
                    ob, q = divmod(k, 4)
                    t.wait_ge(b.sDVE, d0 + k + 1)
                    if q == 0 and ob >= 2:
                        t.wait_ge(b.sACT, a0 + 2 * (ob - 1))
                    wdb = ar(b, WDB[k % 2], BF16, NT * P)
                    for Ti in range(NT):
                        i = q * NT + Ti
                        t.matmul(b.pbig1[:, (ob % 2) * CH:(ob % 2 + 1) * CH],
                                 wdb[:, Ti * P:(Ti + 1) * P], zfin[:, i * CH:(i + 1) * CH],
                                 start=(i == 0), stop=(i == NJ - 1))
                    if q == 3 and ob >= 1:
                        t.wait_ge(b.sACT, a0 + 2 * ob)
                        pob = ob - 1
                        t.matmul(b.pbig0[0:1, 0:CH], b.ginv_bf[:, 0:1], ybuf[:, pob * CH:(pob + 1) * CH],
                                 start=(pob == 0), stop=False)
                        t.matmul(b.pbig0[0:1, CH:2 * CH], b.ginv_bf[:, 0:1],
                                 sq[:, (pob % 2) * CH:(pob % 2 + 1) * CH],
                                 start=(pob == 0), stop=False)
                    t.drain().then_inc(b.sPE, 1)
                    b.cPE += 1
                t.wait_ge(b.sACT, a0 + 2 * NT)
                pob = NT - 1
                t.matmul(b.pbig0[0:1, 0:CH], b.ginv_bf[:, 0:1], ybuf[:, pob * CH:(pob + 1) * CH],
                         start=False, stop=True)
                t.matmul(b.pbig0[0:1, CH:2 * CH], b.ginv_bf[:, 0:1],
                         sq[:, (pob % 2) * CH:(pob % 2 + 1) * CH],
                         start=False, stop=True)

            @blk.scalar
            def _(a):
                for ob in range(NT):
                    a.wait_ge(b.sPE, p0 + 4 * (ob + 1))
                    a.activation(ybuf[:, ob * CH:(ob + 1) * CH], b.pbig1[:, (ob % 2) * CH:(ob % 2 + 1) * CH],
                                 ACTF.Identity, scale=gb(b, "wsd", ob))
                    a.activation(sq[:, (ob % 2) * CH:(ob % 2 + 1) * CH],
                                 ybuf[:, ob * CH:(ob + 1) * CH], ACTF.Square).then_inc(b.sACT, 2)
                    b.cACT += 2
        emit_stat_rows_and_rep(b)
        rc = ar(b, ZA, F32, NT * CH)
        t1 = ar(b, T1, F32, NT * CH)
        with nc.Block() as blk:
            @blk.gpsimd
            def _(g):
                wait_all(b, g)
                g.dma_start(out=_t3(rc), in_=b.resid[:, :, c0:c0 + CH].rearrange("T l t -> l T t")).then_inc(b.sD, 16)
                b.cD += 16
        with nc.Block() as blk:
            @blk.vector
            def _(v):
                v.wait_ge(b.sD, b.cD)
        emit_postln_resid(b, c, ybuf, rc, t1, b.resid)


def emit_l1_attn(b):
    nc = b.nc
    XLN = 67584
    for c in range(NCH):
        rc = ar(b, 0, F32, NT * CH)
        xln = ar(b, XLN, BF16, NT * CH)
        emit_preln(b, c, b.resid, affine_dst=xln)
        c0 = c * CH
        with nc.Block() as blk:
            @blk.vector
            def _(v):
                if c == 0:
                    for T in range(NT):
                        v.tensor_copy(b.gedge[:, T:T + 1], xln[:, T * CH:T * CH + 1])
                if c == NCH - 1:
                    for T in range(NT):
                        v.tensor_copy(b.gedge[:, NT + T:NT + T + 1], xln[:, T * CH + CH - 1:T * CH + CH])

            @blk.gpsimd
            def _(g):
                g.dma_start(out=b.xlndr[:, :, c0:c0 + CH].rearrange("T l t -> l T t"), in_=_t3(xln)).then_inc(b.sD, 16)
                b.cD += 16

    WG1B, WG2B, WTMP = 33280, 49664, 0
    XLNC, HT, GATE = 98816, 115200, 119296
    with nc.Block() as blk:
        @blk.sync
        def _(s):
            wait_all(b, s)
            s.dma_start(out=ar(b, WTMP, F32, NDH * NT * P).rearrange("l (r T c) -> l r (T c)", r=NDH, T=NT),
                        in_=b.wg1T.rearrange("r l T c -> l r (T c)")).then_inc(b.sW, 16)
            b.cW += 16
            s.wait_ge(b.sW, b.cW)

        @blk.vector
        def _(v):
            v.wait_ge(b.sW, b.cW)
            v.tensor_copy(ar(b, WG1B, BF16, NDH * NT * P), ar(b, WTMP, F32, NDH * NT * P))
    with nc.Block() as blk:
        @blk.sync
        def _(s):
            s.dma_start(out=ar(b, WTMP, F32, NT * NDH * P).rearrange("l (r T c) -> l r (T c)", r=NT, T=NDH),
                        in_=b.wg2T.rearrange("r l T c -> l r (T c)")).then_inc(b.sW, 16)
            b.cW += 16
            s.wait_ge(b.sW, b.cW)

        @blk.vector
        def _(v):
            v.wait_ge(b.sW, b.cW)
            v.tensor_copy(ar(b, WG2B, BF16, NT * NDH * P), ar(b, WTMP, F32, NT * NDH * P))
    for c in range(NCH):
        c0 = c * CH
        xlnc = ar(b, XLNC, BF16, NT * CH)
        ht = ar(b, HT, BF16, NDH * CH)
        gate = ar(b, GATE, BF16, NT * CH)
        wg1b = ar(b, WG1B, BF16, NDH * NT * P)
        wg2b = ar(b, WG2B, BF16, NT * NDH * P)
        with nc.Block() as blk:
            @blk.gpsimd
            def _(g):
                wait_all(b, g)
                g.dma_start(out=_t3(xlnc), in_=b.xlndr[:, :, c0:c0 + CH].rearrange("T l t -> l T t")).then_inc(b.sD, 16)
                b.cD += 16
        with nc.Block() as blk:
            @blk.tensor
            def _(t):
                t.wait_ge(b.sD, b.cD)
                for db in range(NDH):
                    for T in range(NT):
                        t.matmul(b.pbig1[:, db * CH:(db + 1) * CH],
                                 wg1b[:, (db * NT + T) * P:(db * NT + T + 1) * P],
                                 xlnc[:, T * CH:(T + 1) * CH], start=(T == 0), stop=(T == NT - 1))
        with nc.Block() as blk:
            @blk.scalar
            def _(a):
                for db in range(NDH):
                    a.activation(ht[:, db * CH:(db + 1) * CH], b.pbig1[:, db * CH:(db + 1) * CH],
                                 ACTF.Relu, bias=gb(b, "bg1", db))
        a0, p0 = b.cACT, b.cPE
        with nc.Block() as blk:
            @blk.tensor
            def _(t):
                for grpi in range(NT // 4):
                    if grpi > 0:
                        t.wait_ge(b.sACT, a0 + grpi)
                    for k in range(4):
                        rb = grpi * 4 + k
                        for db in range(NDH):
                            t.matmul(b.pbig0[:, k * CH:(k + 1) * CH],
                                     wg2b[:, (rb * NDH + db) * P:(rb * NDH + db + 1) * P],
                                     ht[:, db * CH:(db + 1) * CH], start=(db == 0), stop=(db == NDH - 1))
                    t.drain().then_inc(b.sPE, 1)
                    b.cPE += 1

            @blk.scalar
            def _(a):
                for grpi in range(NT // 4):
                    a.wait_ge(b.sPE, p0 + grpi + 1)
                    for k in range(4):
                        rb = grpi * 4 + k
                        a.activation(gate[:, rb * CH:(rb + 1) * CH], b.pbig0[:, k * CH:(k + 1) * CH],
                                     ACTF.Sigmoid, bias=gb(b, "bg2", rb))
                    a.drain().then_inc(b.sACT, 1)
                    b.cACT += 1
        with nc.Block() as blk:
            @blk.gpsimd
            def _(g):
                g.dma_start(out=b.gatesdr[:, :, c0:c0 + CH].rearrange("T l t -> l T t"), in_=_t3(gate)).then_inc(b.sD, 16)
                b.cD += 16

    WUBF, XLNC3, GATE3, UG, KC, WTMP3 = 0, 65536, 81920, 98304, 114688, 131072
    wubf = ar(b, WUBF, BF16, NT * NT * P)
    for rb in range(NT):
        with nc.Block() as blk:
            @blk.sync
            def _(s):
                if rb == 0:
                    wait_all(b, s)
                s.dma_start(out=ar(b, WTMP3, F32, NT * P),
                            in_=b.wuT[rb].rearrange("l T c -> l (T c)")).then_inc(b.sW, 16)
                b.cW += 16
                s.wait_ge(b.sW, b.cW)

            @blk.vector
            def _(v):
                v.wait_ge(b.sW, b.cW)
                v.tensor_copy(wubf[:, rb * NT * P:(rb + 1) * NT * P], ar(b, WTMP3, F32, NT * P))
    for c in range(NCH):
        c0 = c * CH
        xlnc = ar(b, XLNC3, BF16, NT * CH)
        gate = ar(b, GATE3, BF16, NT * CH)
        ug = ar(b, UG, BF16, NT * CH)
        kc = ar(b, KC, BF16, NT * CH)
        with nc.Block() as blk:
            @blk.gpsimd
            def _(g):
                wait_all(b, g)
                g.dma_start(out=_t3(xlnc), in_=b.xlndr[:, :, c0:c0 + CH].rearrange("T l t -> l T t")).then_inc(b.sD, 16)
                g.dma_start(out=_t3(gate), in_=b.gatesdr[:, :, c0:c0 + CH].rearrange("T l t -> l T t")).then_inc(b.sD, 16)
                b.cD += 32
        a0, p0 = b.cACT, b.cPE
        with nc.Block() as blk:
            @blk.tensor
            def _(t):
                t.wait_ge(b.sD, b.cD)
                for grpi in range(NT // 4):
                    if grpi > 0:
                        t.wait_ge(b.sACT, a0 + grpi)
                    for k in range(4):
                        rb = grpi * 4 + k
                        for T in range(NT):
                            t.matmul(b.pbig0[:, k * CH:(k + 1) * CH],
                                     wubf[:, (rb * NT + T) * P:(rb * NT + T + 1) * P],
                                     xlnc[:, T * CH:(T + 1) * CH], start=(T == 0), stop=(T == NT - 1))
                    t.drain().then_inc(b.sPE, 1)
                    b.cPE += 1

            @blk.scalar
            def _(a):
                for grpi in range(NT // 4):
                    a.wait_ge(b.sPE, p0 + grpi + 1)
                    for k in range(4):
                        rb = grpi * 4 + k
                        a.activation(ug[:, rb * CH:(rb + 1) * CH], b.pbig0[:, k * CH:(k + 1) * CH],
                                     ACTF.Identity, bias=gb(b, "bu", rb))
                    a.drain().then_inc(b.sACT, 1)
                    b.cACT += 1
        with nc.Block() as blk:
            @blk.vector
            def _(v):
                for rb in range(NT):
                    sl = slice(rb * CH, (rb + 1) * CH)
                    v.tensor_tensor(ug[:, sl], ug[:, sl], gate[:, sl], ALU.mult)
        with nc.Block() as blk:
            @blk.vector
            def _(v):
                for rb in range(NT):
                    sl = slice(rb * CH, (rb + 1) * CH)
                    v.tensor_tensor_scan(kc[:, sl], ug[:, sl], ug[:, sl], b.gstkc[:, rb:rb + 1],
                                         ALU.add, ALU.bypass)
                for rb in range(NT):
                    v.tensor_copy(b.gstkc[:, rb:rb + 1], kc[:, rb * CH + CH - 1:rb * CH + CH])
        with nc.Block() as blk:
            @blk.gpsimd
            def _(g):
                g.dma_start(out=b.kcdr[:, :, c0:c0 + CH].rearrange("T l t -> l T t"), in_=_t3(kc)).then_inc(b.sD, 16)
                b.cD += 16

    CONTRIB = 131072
    contrib = ar(b, CONTRIB, F32, 48)
    with nc.Block() as blk:
        @blk.vector
        def _(v):
            v.tensor_scalar(contrib[:, 0:NT], b.gedge[:, NT:2 * NT], gb(b, "iseven"), None, ALU.mult)
            v.tensor_scalar(contrib[:, NT:2 * NT], b.gedge[:, 0:NT], gb(b, "isodd"), None, ALU.mult)
            v.tensor_scalar(contrib[:, 2 * NT:3 * NT], b.gstkc[:, :], gb(b, "iseven"), None, ALU.mult)
    with nc.Block() as blk:
        @blk.gpsimd
        def _(g):
            wait_all(b, g)
            g.dma_start(out=b.cc1i.ap(), in_=contrib).then_inc(b.sD, 16)
            b.cD += 16
            g.wait_ge(b.sD, b.cD)
            g.collective_compute("AllReduce", ALU.add, replica_groups=[[0, 1], [2, 3], [4, 5], [6, 7]],
                                 ins=[b.cc1i.ap()], outs=[b.cc1o.ap()]).then_inc(b.sCC, 1)
            b.cCC += 1
            g.wait_ge(b.sCC, b.cCC)
            g.dma_start(out=b.ghalo[:, :], in_=b.cc1o.ap()).then_inc(b.sD, 16)
            b.cD += 16
    with nc.Block() as blk:
        @blk.vector
        def _(v):
            v.wait_ge(b.sD, b.cD)
            v.tensor_scalar(b.gedge[:, 2 * NT:3 * NT], b.ghalo[:, 0:NT], gb(b, "isodd"), None, ALU.mult)
            v.tensor_scalar(b.gedge[:, 3 * NT:4 * NT], b.ghalo[:, NT:2 * NT], gb(b, "iseven"), None, ALU.mult)
            v.tensor_scalar(b.gstkc[:, :], b.ghalo[:, 2 * NT:3 * NT], gb(b, "isodd"), None, ALU.mult)

    WV1 = [65536, 73728]
    WV1B = [81920, 86016]
    XLNC4, GATE4, VG, KCC, GO = 90112, 106496, 122880, 139264, 155648
    XPAD, TC1, EDGL, EDGR, YB4, SQ4 = 172032, 175104, 177664, 178176, 106496, 65536
    RC4, T14 = 122880, 65536
    for c in range(NCH):
        c0 = c * CH
        xlnc = ar(b, XLNC4, BF16, NT * CH)
        gate = ar(b, GATE4, BF16, NT * CH)
        vg = ar(b, VG, BF16, NT * CH)
        kcc = ar(b, KCC, BF16, NT * CH)
        go = ar(b, GO, BF16, NT * CH)
        ybuf = ar(b, YB4, BF16, NT * CH)
        sq = ar(b, SQ4, BF16, 2 * CH)
        edgl = ar(b, EDGL, BF16, NT)
        edgr = ar(b, EDGR, BF16, NT)
        with nc.Block() as blk:
            @blk.gpsimd
            def _(g):
                wait_all(b, g)
                g.dma_start(out=_t3(xlnc), in_=b.xlndr[:, :, c0:c0 + CH].rearrange("T l t -> l T t")).then_inc(b.sD, 16)
                g.dma_start(out=_t3(gate), in_=b.gatesdr[:, :, c0:c0 + CH].rearrange("T l t -> l T t")).then_inc(b.sD, 16)
                g.dma_start(out=_t3(kcc), in_=b.kcdr[:, :, c0:c0 + CH].rearrange("T l t -> l T t")).then_inc(b.sD, 16)
                b.cD += 48
                if c > 0:
                    g.dma_start(out=_t3(edgl), in_=b.xlndr[:, :, c0 - 1:c0].rearrange("T l t -> l T t"), allow_slow_non_contiguous=True).then_inc(b.sD, 16)
                    b.cD += 16
                if c < NCH - 1:
                    g.dma_start(out=_t3(edgr), in_=b.xlndr[:, :, c0 + CH:c0 + CH + 1].rearrange("T l t -> l T t"), allow_slow_non_contiguous=True).then_inc(b.sD, 16)
                    b.cD += 16
        w0, d0, a0, p0 = b.cW, b.cDVE, b.cACT, b.cPE
        q0 = list(b.cQ)
        with nc.Block() as blk:
            @blk.sync
            def _(s):
                for rb in (0, 1):
                    s.dma_start(out=ar(b, WV1[rb % 2], F32, NT * P),
                                in_=b.wvT1[rb].rearrange("l T c -> l (T c)")).then_inc(b.sQ[rb % 2], 16)
                    b.cQ[rb % 2] += 16
                for rb in range(NT - 2):
                    s.wait_ge(b.sDVE, d0 + rb + 1)
                    s.dma_start(out=ar(b, WV1[rb % 2], F32, NT * P),
                                in_=b.wvT1[rb + 2].rearrange("l T c -> l (T c)")).then_inc(b.sQ[rb % 2], 16)
                    b.cQ[rb % 2] += 16

            @blk.vector
            def _(v):
                for rb in range(NT):
                    v.wait_ge(b.sQ[rb % 2], q0[rb % 2] + 16 * (rb // 2 + 1))
                    if rb >= 2:
                        v.wait_ge(b.sPE, p0 + rb - 1)
                    v.tensor_copy(ar(b, WV1B[rb % 2], BF16, NT * P), ar(b, WV1[rb % 2], F32, NT * P))
                    v.drain().then_inc(b.sDVE, 1)
                    b.cDVE += 1

            @blk.tensor
            def _(t):
                t.wait_ge(b.sD, b.cD)
                for rb in range(NT):
                    t.wait_ge(b.sDVE, d0 + rb + 1)
                    if rb >= 4:
                        t.wait_ge(b.sACT, a0 + rb - 3)
                    wv1b = ar(b, WV1B[rb % 2], BF16, NT * P)
                    for T in range(NT):
                        t.matmul(b.pbig1[:, (rb % 4) * CH:(rb % 4 + 1) * CH],
                                 wv1b[:, T * P:(T + 1) * P], xlnc[:, T * CH:(T + 1) * CH],
                                 start=(T == 0), stop=(T == NT - 1))
                    t.drain().then_inc(b.sPE, 1)
                    b.cPE += 1

            @blk.scalar
            def _(a):
                for rb in range(NT):
                    a.wait_ge(b.sPE, p0 + rb + 1)
                    a.activation(vg[:, rb * CH:(rb + 1) * CH], b.pbig1[:, (rb % 4) * CH:(rb % 4 + 1) * CH],
                                 ACTF.Identity, bias=gb(b, "bv1", rb)).then_inc(b.sACT, 1)
                    b.cACT += 1
        with nc.Block() as blk:
            @blk.vector
            def _(v):
                for rb in range(NT):
                    sl = slice(rb * CH, (rb + 1) * CH)
                    v.tensor_tensor(vg[:, sl], vg[:, sl], gate[:, sl], ALU.mult)
                for rb in range(NT):
                    sl = slice(rb * CH, (rb + 1) * CH)
                    v.scalar_tensor_tensor(go[:, sl], kcc[:, sl], b.gstkc[:, rb:rb + 1], vg[:, sl],
                                           ALU.add, ALU.mult)
        xpad = ar(b, XPAD, F32, CH + 2)
        tc1 = ar(b, TC1, F32, CH)
        w0, d0, a0, p0 = b.cW, b.cDVE, b.cACT, b.cPE
        with nc.Block() as blk:
            @blk.tensor
            def _(t):
                for ob in range(NT):
                    if ob >= 4:
                        t.wait_ge(b.sDVE, d0 + ob - 3)
                    for rT in range(NT):
                        t.matmul(b.pbig1[:, (ob % 4) * CH:(ob % 4 + 1) * CH],
                                 wubf[:, (ob * NT + rT) * P:(ob * NT + rT + 1) * P],
                                 go[:, rT * CH:(rT + 1) * CH], start=(rT == 0), stop=(rT == NT - 1))
                    if ob >= 1:
                        t.wait_ge(b.sACT, a0 + ob)
                        pob = ob - 1
                        t.matmul(b.pbig0[0:1, 0:CH], b.ginv_bf[:, 0:1], ybuf[:, pob * CH:(pob + 1) * CH],
                                 start=(pob == 0), stop=False)
                        t.matmul(b.pbig0[0:1, CH:2 * CH], b.ginv_bf[:, 0:1],
                                 sq[:, (pob % 2) * CH:(pob % 2 + 1) * CH],
                                 start=(pob == 0), stop=False)
                    t.drain().then_inc(b.sPE, 1)
                    b.cPE += 1
                t.wait_ge(b.sACT, a0 + NT)
                pob = NT - 1
                t.matmul(b.pbig0[0:1, 0:CH], b.ginv_bf[:, 0:1], ybuf[:, pob * CH:(pob + 1) * CH],
                         start=False, stop=True)
                t.matmul(b.pbig0[0:1, CH:2 * CH], b.ginv_bf[:, 0:1],
                         sq[:, (pob % 2) * CH:(pob % 2 + 1) * CH],
                         start=False, stop=True)

            @blk.vector
            def _(v):
                for ob in range(NT):
                    v.wait_ge(b.sPE, p0 + ob + 1)
                    v.tensor_copy(xpad[:, 1:CH + 1], xlnc[:, ob * CH:(ob + 1) * CH])
                    if c == 0:
                        v.tensor_copy(xpad[:, 0:1], b.gedge[:, 2 * NT + ob:2 * NT + ob + 1])
                    else:
                        v.tensor_copy(xpad[:, 0:1], edgl[:, ob:ob + 1])
                    if c == NCH - 1:
                        v.tensor_copy(xpad[:, CH + 1:CH + 2], b.gedge[:, 3 * NT + ob:3 * NT + ob + 1])
                    else:
                        v.tensor_copy(xpad[:, CH + 1:CH + 2], edgr[:, ob:ob + 1])
                    v.tensor_scalar(tc1, xpad[:, 1:CH + 1], gb(b, "cw", NT + ob), gb(b, "convb", ob),
                                    ALU.mult, ALU.add)
                    v.scalar_tensor_tensor(tc1, xpad[:, 0:CH], gb(b, "cw", ob), tc1, ALU.mult, ALU.add)
                    v.scalar_tensor_tensor(tc1, xpad[:, 2:CH + 2], gb(b, "cw", 2 * NT + ob), tc1, ALU.mult, ALU.add)
                    v.tensor_tensor(ybuf[:, ob * CH:(ob + 1) * CH], tc1,
                                    b.pbig1[:, (ob % 4) * CH:(ob % 4 + 1) * CH], ALU.add)
                    v.drain().then_inc(b.sDVE, 1)
                    b.cDVE += 1

            @blk.scalar
            def _(a):
                for ob in range(NT):
                    a.wait_ge(b.sDVE, d0 + ob + 1)
                    a.activation(sq[:, (ob % 2) * CH:(ob % 2 + 1) * CH],
                                 ybuf[:, ob * CH:(ob + 1) * CH], ACTF.Square).then_inc(b.sACT, 1)
                    b.cACT += 1
        emit_stat_rows_and_rep(b)
        rc = ar(b, RC4, F32, NT * CH)
        t1 = ar(b, T14, F32, NT * CH)
        with nc.Block() as blk:
            @blk.gpsimd
            def _(g):
                wait_all(b, g)
                g.dma_start(out=_t3(rc), in_=b.resid[:, :, c0:c0 + CH].rearrange("T l t -> l T t")).then_inc(b.sD, 16)
                b.cD += 16
        with nc.Block() as blk:
            @blk.vector
            def _(v):
                v.wait_ge(b.sD, b.cD)
        emit_postln_resid(b, c, ybuf, rc, t1, b.resid)


def emit_l1_ffn(b, final_out):
    nc = b.nc
    XLN4 = 67584
    for c in range(NCH):
        rc = ar(b, 0, F32, NT * CH)
        xln4 = ar(b, XLN4, F32, NT * CH)
        emit_preln(b, c, b.resid, affine_dst=xln4)
        c0 = c * CH
        with nc.Block() as blk:
            @blk.vector
            def _(v):
                v.tensor_reduce(b.gmax[:, 2:3], xln4, AX.X, ALU.max, apply_absolute_value=True)
                vpad(b, v)
                v.tensor_tensor(b.gmax[:, 0:1], b.gmax[:, 0:1], b.gmax[:, 2:3], ALU.max)

            @blk.gpsimd
            def _(g):
                g.dma_start(out=b.xln4dr[:, :, c0:c0 + CH].rearrange("T l t -> l T t"), in_=_t3(xln4)).then_inc(b.sD, 16)
                b.cD += 16
    emit_scalar_allreduce(b, [(0, 0), (3, 1), (4, 2)], b.cc2i, b.cc2o)
    with nc.Block() as blk:
        @blk.gpsimd
        def _(g):
            g.dma_start(out=b.dbg[:, 0:16], in_=b.grow1[:, 0:16]).then_inc(b.sD, 16)
            b.cD += 16
    with nc.Block() as blk:
        @blk.vector
        def _(v):
            v.tensor_single_scalar(b.grow1[:, 3:6], b.grow1[:, 0:3], 1.0 / 448.0, ALU.mult)
            vpad(b, v)
            v.tensor_single_scalar(b.grow1[:, 3:6], b.grow1[:, 3:6], 1e-6, ALU.max)
            vpad(b, v)
            v.reciprocal(b.grow1[:, 6:9], b.grow1[:, 3:6])
            vpad(b, v)
            v.tensor_single_scalar(b.grow1[:, 6:9], b.grow1[:, 6:9], 0.5, ALU.mult)
            vpad(b, v)
    with nc.Block() as blk:
        @blk.gpsimd
        def _(g):
            g.dma_start(out=b.dbg[:, 16:32], in_=b.grow1[:, 0:16]).then_inc(b.sD, 16)
            b.cD += 16
    with nc.Block() as blk:
        @blk.tensor
        def _(t):
            t.matmul(b.pbig0[:, 0:3], b.gones_row[:, :], b.grow1[:, 6:9], start=True, stop=True)
    with nc.Block() as blk:
        @blk.vector
        def _(v):
            v.tensor_copy(b.gscal[:, 0:3], b.pbig0[:, 0:3])

    X8, XL4C, QT = 0, 33280, 66560
    x8 = ar(b, X8, FP8, NT * TOK)
    for c in range(NCH):
        c0 = c * CH
        xl4c = ar(b, XL4C, F32, NT * CH)
        qt = ar(b, QT, F32, NT * CH)
        with nc.Block() as blk:
            @blk.gpsimd
            def _(g):
                wait_all(b, g)
                g.dma_start(out=_t3(xl4c), in_=b.xln4dr[:, :, c0:c0 + CH].rearrange("T l t -> l T t")).then_inc(b.sD, 16)
                b.cD += 16
        with nc.Block() as blk:
            @blk.vector
            def _(v):
                v.wait_ge(b.sD, b.cD)
                v.tensor_scalar(qt, xl4c, b.gscal[:, 0:1], None, ALU.mult)
                v.tensor_copy(x8.rearrange("l (T t) -> l T t", T=NT)[:, :, c0:c0 + CH],
                              qt.rearrange("l (T t) -> l T t", T=NT))
    emit_c2_fc(b, x8)
    emit_scalar_allreduce(b, [(1, 0)], b.cc3i, b.cc3o)
    with nc.Block() as blk:
        @blk.vector
        def _(v):
            v.tensor_tensor(b.grow1[:, 9:10], b.grow1[:, 3:4], b.grow1[:, 4:5], ALU.mult)
            vpad(b, v)
            v.tensor_single_scalar(b.grow1[:, 9:10], b.grow1[:, 9:10], 4.0, ALU.mult)
            vpad(b, v)
            v.tensor_tensor(b.grow1[:, 10:11], b.grow1[:, 9:10], b.grow1[:, 9:10], ALU.mult)
            vpad(b, v)
            v.tensor_tensor(b.grow1[:, 11:12], b.grow1[:, 0:1], b.grow1[:, 10:11], ALU.mult)
            vpad(b, v)
            v.tensor_single_scalar(b.grow1[:, 11:12], b.grow1[:, 11:12], 1.0 / 448.0, ALU.mult)
            vpad(b, v)
            v.tensor_single_scalar(b.grow1[:, 11:12], b.grow1[:, 11:12], 1e-6, ALU.max)
            vpad(b, v)
            v.tensor_single_scalar(b.grow1[:, 12:13], b.grow1[:, 11:12], 2.0, ALU.mult)
            vpad(b, v)
            v.reciprocal(b.grow1[:, 13:14], b.grow1[:, 12:13])
            vpad(b, v)
            v.tensor_tensor(b.grow1[:, 13:14], b.grow1[:, 10:11], b.grow1[:, 13:14], ALU.mult)
            vpad(b, v)
            v.tensor_tensor(b.grow1[:, 14:15], b.grow1[:, 11:12], b.grow1[:, 5:6], ALU.mult)
            vpad(b, v)
            v.tensor_single_scalar(b.grow1[:, 14:15], b.grow1[:, 14:15], 4.0, ALU.mult)
            vpad(b, v)
    with nc.Block() as blk:
        @blk.tensor
        def _(t):
            t.matmul(b.pbig0[:, 0:2], b.gones_row[:, :], b.grow1[:, 13:15], start=True, stop=True)
    with nc.Block() as blk:
        @blk.vector
        def _(v):
            v.tensor_copy(b.gscal[:, 3:5], b.pbig0[:, 0:2])
    with nc.Block() as blk:
        @blk.vector
        def _(v):
            v.tensor_copy(b.gcc2[:, 0:16], b.grow1[:, 0:16])

        @blk.gpsimd
        def _(g):
            g.dma_start(out=b.dbg[:, 100:116], in_=b.grow1[:, 0:16]).then_inc(b.sD, 16)
            g.dma_start(out=b.dbg[:, 116:124], in_=b.gscal[0:1, 0:8]).then_inc(b.sD, 16)
            g.dma_start(out=b.dbg[:, 48:56], in_=b.gjunk[:, 0:8]).then_inc(b.sD, 16)
            b.cD += 48
    emit_c4_proj(b)
    emit_c5_final(b, final_out)


def emit_scalar_allreduce(b, colmap, cci, cco):
    nc = b.nc
    with nc.Block() as blk:
        @blk.tensor
        def _(t):
            for i, (src, _) in enumerate(colmap):
                t.matmul(b.pbig0[0:1, i * P:(i + 1) * P], b.gmax[:, src:src + 1], b.geye[:, :],
                         start=True, stop=True)
    with nc.Block() as blk:
        @blk.vector
        def _(v):
            for i, (_, dst) in enumerate(colmap):
                v.reduce_max(b.grows2[:, i:i + 1], b.pbig0[0:1, i * P:(i + 1) * P], AX.X)
            vpad(b, v)
            v.memset(b.gcc2[:, :], 0.0)
            for i in range(len(colmap)):
                for s in range(8):
                    v.tensor_copy(b.gcc2[:, 4 * s + i:4 * s + i + 1], b.grows2[:, i:i + 1])
            vpad(b, v)
            v.tensor_tensor(b.gcc2[:, :], b.gcc2[:, :], b.gflags[:, 32:64], ALU.mult)
    with nc.Block() as blk:
        @blk.gpsimd
        def _(g):
            wait_all(b, g)
            if cci is b.cc2i:
                g.dma_start(out=b.dbg[:, 64:96], in_=b.gcc2[:, :]).then_inc(b.sD, 16)
                g.dma_start(out=b.dbg[:, 96:100], in_=b.grows2[:, 0:4]).then_inc(b.sD, 16)
                b.cD += 32
            g.dma_start(out=cci.ap(), in_=b.gcc2[:, :]).then_inc(b.sD, 16)
            b.cD += 16
            g.wait_ge(b.sD, b.cD)
            rounds = [
                ([[0, 1], [2, 3], [4, 5], [6, 7]], cci.ap(), cco.ap()),
                ([[0, 2], [1, 3], [4, 6], [5, 7]], cco.ap(), b.ccmb.ap()),
                ([[0, 4], [1, 5], [2, 6], [3, 7]], b.ccmb.ap(), b.ccmc.ap()),
            ]
            for rg, rin, rout in rounds:
                g.collective_compute("AllReduce", ALU.add, replica_groups=rg,
                                     ins=[rin], outs=[rout]).then_inc(b.sCC, 1)
                b.cCC += 1
                g.wait_ge(b.sCC, b.cCC)
            g.dma_start(out=b.gcc2[:, :], in_=b.ccmc.ap()).then_inc(b.sD, 16)
            b.cD += 16
    with nc.Block() as blk:
        @blk.vector
        def _(v):
            v.wait_ge(b.sD, b.cD)
            v.reduce_max(b.grows2[:, 0:4], b.gcc2.rearrange("o (s k) -> o k s", s=8), AX.X)
            vpad(b, v)
            for i, (_, dst) in enumerate(colmap):
                v.tensor_copy(b.grow1[:, dst:dst + 1], b.grows2[:, i:i + 1])
            if cci is b.cc2i:
                v.tensor_copy(b.grows[:, 900:908], b.gcc2[:, 0:8])
                v.tensor_copy(b.grows[:, 908:912], b.grows2[:, 0:4])
                v.tensor_copy(b.grows[:, 912:916], b.grow1[:, 0:4])
    if cci is b.cc2i:
        with nc.Block() as blk:
            @blk.gpsimd
            def _(g):
                g.dma_start(out=b.dbg[:, 32:48], in_=b.grows[:, 900:916]).then_inc(b.sD, 16)
                b.cD += 16


def emit_c2_fc(b, x8):
    nc = b.nc
    WFC = [99840, 108032]
    W8FC = [116224, 118272]
    QW = 120320
    RELU = 129024
    HS2 = [137728, 141824]
    w0, d0, a0, p0, dd0 = b.cW, b.cDVE, b.cACT, b.cPE, b.cD
    q0 = list(b.cQ)
    with nc.Block() as blk:
        @blk.sync
        def _(s):
            wait_all(b, s)
            for j in (0, 1):
                s.dma_start(out=ar(b, WFC[j % 2], F32, NT * P),
                            in_=b.wfcT[j].rearrange("l T c -> l (T c)")).then_inc(b.sQ[j % 2], 16)
                b.cQ[j % 2] += 16
            for j in range(NJ - 2):
                s.wait_ge(b.sDVE, d0 + 2 * j + 1)
                s.dma_start(out=ar(b, WFC[j % 2], F32, NT * P),
                            in_=b.wfcT[j + 2].rearrange("l T c -> l (T c)")).then_inc(b.sQ[j % 2], 16)
                b.cQ[j % 2] += 16

        @blk.vector
        def _(v):
            for j in range(NJ):
                v.wait_ge(b.sQ[j % 2], q0[j % 2] + 16 * (j // 2 + 1))
                if j >= 2:
                    v.wait_ge(b.sPE, p0 + j - 1)
                v.tensor_scalar(ar(b, QW, F32, NT * P), ar(b, WFC[j % 2], F32, NT * P),
                                b.gscal[:, 1:2], None, ALU.mult)
                v.tensor_copy(ar(b, W8FC[j % 2], FP8, NT * P), ar(b, QW, F32, NT * P))
                v.drain().then_inc(b.sDVE, 1)
                b.cDVE += 1
                v.wait_ge(b.sACT, a0 + 2 * j + 2)
                v.tensor_reduce(b.gmax[:, 2:3], ar(b, HS2[j % 2], BF16, TOK), AX.X, ALU.max)
                vpad(b, v)
                v.tensor_tensor(b.gmax[:, 1:2], b.gmax[:, 1:2], b.gmax[:, 2:3], ALU.max)
                v.drain().then_inc(b.sDVE, 1)
                b.cDVE += 1

        @blk.tensor
        def _(t):
            for j in range(NJ):
                t.wait_ge(b.sDVE, d0 + 2 * j + 1)
                if j >= 2:
                    t.wait_ge(b.sACT, a0 + 2 * (j - 1))
                w8 = ar(b, W8FC[j % 2], FP8, NT * P)
                pb = b.pbig0 if j % 2 == 0 else b.pbig1
                for T in range(NT):
                    for g in range(4):
                        t.matmul(pb[:, g * CH:(g + 1) * CH], w8[:, T * P:(T + 1) * P],
                                 x8.rearrange("l (T t) -> l T t", T=NT)[:, T, g * CH:(g + 1) * CH],
                                 start=(T == 0), stop=(T == NT - 1))
                t.drain().then_inc(b.sPE, 1)
                b.cPE += 1

        @blk.scalar
        def _(a):
            for j in range(NJ):
                a.wait_ge(b.sPE, p0 + j + 1)
                if j >= 2:
                    a.wait_ge(b.sQ[4 + j % 2], q0[4 + j % 2] + 16 * ((j - 2) // 2 + 1))
                    a.wait_ge(b.sDVE, d0 + 2 * (j - 2) + 2)
                pb = b.pbig0 if j % 2 == 0 else b.pbig1
                a.activation(ar(b, RELU, F32, TOK), pb[:, :], ACTF.Relu)
                a.drain().then_inc(b.sACT, 1)
                b.cACT += 1
                a.activation(ar(b, HS2[j % 2], BF16, TOK), ar(b, RELU, F32, TOK), ACTF.Square)
                a.drain().then_inc(b.sACT, 1)
                b.cACT += 1

        @blk.gpsimd
        def _(g):
            for j in range(NJ):
                g.wait_ge(b.sACT, a0 + 2 * j + 2)
                g.dma_start(out=b.hdr[j], in_=ar(b, HS2[j % 2], BF16, TOK)).then_inc(b.sQ[4 + j % 2], 16)
                b.cQ[4 + j % 2] += 16


def emit_c4_proj(b):
    nc = b.nc
    H8 = 0
    WPJ = [131072, 139264]
    W8PJ = [147456, 155648]
    QW4 = 164352
    HS2L = [173056, 177152]
    YT = [181760, 185856]
    h8 = ar(b, H8, FP8, NJ * TOK)
    d0, dd0 = b.cDVE, b.cD
    q0 = list(b.cQ)
    with nc.Block() as blk:
        @blk.gpsimd
        def _(g):
            wait_all(b, g)
            for j in range(NJ):
                if j >= 2:
                    g.wait_ge(b.sDVE, d0 + j - 1)
                g.dma_start(out=ar(b, HS2L[j % 2], BF16, TOK), in_=b.hdr[j]).then_inc(b.sQ[6 + j % 2], 16)
                b.cQ[6 + j % 2] += 16

        @blk.vector
        def _(v):
            for j in range(NJ):
                v.wait_ge(b.sQ[6 + j % 2], q0[6 + j % 2] + 16 * (j // 2 + 1))
                v.tensor_scalar(ar(b, QW4, F32, TOK), ar(b, HS2L[j % 2], BF16, TOK),
                                b.gscal[:, 3:4], None, ALU.mult)
                v.tensor_copy(h8[:, j * TOK:(j + 1) * TOK], ar(b, QW4, F32, TOK))
                v.drain().then_inc(b.sDVE, 1)
                b.cDVE += 1
    w0, d0, a0, p0, dd4 = b.cW, b.cDVE, b.cACT, b.cPE, b.cD
    q0 = list(b.cQ)
    with nc.Block() as blk:
        @blk.sync
        def _(s):
            wait_all(b, s)
            for k in range(2):
                ob, q = divmod(k, 4)
                s.dma_start(out=ar(b, WPJ[k % 2], F32, NT * P),
                            in_=b.wpjT[ob, :, q * NT:(q + 1) * NT, :].rearrange("l T c -> l (T c)")).then_inc(b.sQ[k % 2], 16)
                b.cQ[k % 2] += 16
            for k in range(4 * NT - 2):
                ob, q = divmod(k + 2, 4)
                s.wait_ge(b.sDVE, d0 + k + 1)
                s.dma_start(out=ar(b, WPJ[k % 2], F32, NT * P),
                            in_=b.wpjT[ob, :, q * NT:(q + 1) * NT, :].rearrange("l T c -> l (T c)")).then_inc(b.sQ[k % 2], 16)
                b.cQ[k % 2] += 16

        @blk.vector
        def _(v):
            for k in range(4 * NT):
                ob, q = divmod(k, 4)
                v.wait_ge(b.sQ[k % 2], q0[k % 2] + 16 * (k // 2 + 1))
                if q == 0 and ob >= 2:
                    v.wait_ge(b.sPE, p0 + ob - 1)
                v.tensor_scalar(ar(b, QW4, F32, NT * P), ar(b, WPJ[k % 2], F32, NT * P),
                                b.gscal[:, 2:3], None, ALU.mult)
                v.tensor_copy(ar(b, W8PJ[ob % 2] + q * 2048, FP8, NT * P), ar(b, QW4, F32, NT * P))
                v.drain().then_inc(b.sDVE, 1)
                b.cDVE += 1

        @blk.tensor
        def _(t):
            for ob in range(NT):
                t.wait_ge(b.sDVE, d0 + 4 * (ob + 1))
                if ob >= 2:
                    t.wait_ge(b.sACT, a0 + ob - 1)
                pb = b.pbig0 if ob % 2 == 0 else b.pbig1
                for q in range(4):
                    w8 = ar(b, W8PJ[ob % 2] + q * 2048, FP8, NT * P)
                    for Ti in range(NT):
                        i = q * NT + Ti
                        for g in range(4):
                            t.matmul(pb[:, g * CH:(g + 1) * CH], w8[:, Ti * P:(Ti + 1) * P],
                                     h8[:, i * TOK + g * CH: i * TOK + (g + 1) * CH],
                                     start=(i == 0), stop=(i == NJ - 1))
                t.drain().then_inc(b.sPE, 1)
                b.cPE += 1

        @blk.scalar
        def _(a):
            for ob in range(NT):
                a.wait_ge(b.sPE, p0 + ob + 1)
                if ob >= 2:
                    a.wait_ge(b.sQ[2 + ob % 2], q0[2 + ob % 2] + 16 * ((ob - 2) // 2 + 1))
                pb = b.pbig0 if ob % 2 == 0 else b.pbig1
                a.activation(ar(b, YT[ob % 2], BF16, TOK), pb[:, :], ACTF.Identity,
                             scale=b.gscal[:, 4:5])
                a.drain().then_inc(b.sACT, 1)
                b.cACT += 1

        @blk.gpsimd
        def _(g):
            for ob in range(NT):
                g.wait_ge(b.sACT, a0 + ob + 1)
                g.dma_start(out=b.ydr[ob], in_=ar(b, YT[ob % 2], BF16, TOK)).then_inc(b.sQ[2 + ob % 2], 16)
                b.cQ[2 + ob % 2] += 16


def emit_c5_final(b, final_out):
    nc = b.nc
    YC, Y2C, RC, T1 = 0, 16384, 33280, 66560
    for c in range(NCH):
        c0 = c * CH
        yc = ar(b, YC, BF16, NT * CH)
        y2c = ar(b, Y2C, BF16, NT * CH)
        rc = ar(b, RC, F32, NT * CH)
        t1 = ar(b, T1, F32, NT * CH)
        with nc.Block() as blk:
            @blk.gpsimd
            def _(g):
                wait_all(b, g)
                g.dma_start(out=_t3(yc), in_=b.ydr[:, :, c0:c0 + CH].rearrange("T l t -> l T t")).then_inc(b.sD, 16)
                g.dma_start(out=_t3(rc), in_=b.resid[:, :, c0:c0 + CH].rearrange("T l t -> l T t")).then_inc(b.sD, 16)
                b.cD += 32
        with nc.Block() as blk:
            @blk.scalar
            def _(a):
                a.wait_ge(b.sD, b.cD)
                a.activation(y2c, yc, ACTF.Square)
        emit_stats_from(b, yc, y2c)
        emit_postln_resid(b, c, yc, rc, t1, final_out)


def emit_out_debug(b):
    nc = b.nc
    for c in range(NCH):
        c0 = c * CH
        rc = ar(b, 0, F32, NT * CH)
        with nc.Block() as blk:
            @blk.gpsimd
            def _(g):
                wait_all(b, g)
                g.dma_start(out=_t3(rc), in_=b.resid[:, :, c0:c0 + CH].rearrange("T l t -> l T t")).then_inc(b.sD, 16)
                b.cD += 16
                g.wait_ge(b.sD, b.cD)
                g.dma_start(out=b.outT[:, :, c0:c0 + CH].rearrange("T l t -> l T t"), in_=_t3(rc)).then_inc(b.sD, 16)
                b.cD += 16


def emit_final_wait(b):
    nc = b.nc
    with nc.Block() as blk:
        @blk.gpsimd
        def _(g):
            g.wait_ge(b.sD, b.cD)
            if b.cW:
                g.wait_ge(b.sW, b.cW)


def build(upto=4):
    b = _mk()
    b.sub_l0ffn = {21: 1, 22: 2}.get(upto, 3)
    if upto in (21, 22):
        upto = 2
    emit_setup(b)
    emit_l0_attn(b)
    if upto >= 2:
        emit_l0_ffn(b)
    if upto >= 3:
        emit_l1_attn(b)
    if upto >= 4:
        emit_l1_ffn(b, b.outT)
    else:
        emit_out_debug(b)
    emit_final_wait(b)
    b.stack.close()
    return b.nc


def _lhsT_blocks(WT):
    K, M = WT.shape
    return np.ascontiguousarray(
        WT.reshape(K // 128, 128, M // 128, 128).transpose(2, 1, 0, 3))


_NC_CACHE = {}


def _get_nc(upto=4):
    if upto not in _NC_CACHE:
        _NC_CACHE[upto] = build(upto)
    return _NC_CACHE[upto]


def make_inmaps(inputs):
    f32 = lambda a: np.ascontiguousarray(np.asarray(a), dtype=np.float32)
    x = f32(inputs["L0_attn_x"])
    shared = {
        "wvT": _lhsT_blocks(f32(inputs["L0_attn_Wv"]).T),
        "woT": _lhsT_blocks(f32(inputs["L0_attn_Wo"]).T),
        "bv0": f32(inputs["L0_attn_bv"]).reshape(1, D),
        "bo0": f32(inputs["L0_attn_bo"]).reshape(1, D),
        "wupT": _lhsT_blocks(f32(inputs["L0_ffn_W_up"]).T),
        "wgT": _lhsT_blocks(f32(inputs["L0_ffn_W_gate"]).T),
        "wdT": _lhsT_blocks(f32(inputs["L0_ffn_W_down"]).T),
        "su_ug": f32(inputs["L0_ffn_SU_ug"]).reshape(1, D),
        "su_dp": f32(inputs["L0_ffn_SU_dp"]).reshape(1, INTER),
        "ws_ug": f32(inputs["L0_ffn_Wscale_ug"]).reshape(1, 2 * INTER),
        "ws_dp": f32(inputs["L0_ffn_Wscale_dp"]).reshape(1, D),
        "wuT": _lhsT_blocks(f32(inputs["L1_attn_Wu"]).T),
        "wvT1": _lhsT_blocks(f32(inputs["L1_attn_Wv"]).T),
        "wg1T": _lhsT_blocks(f32(inputs["L1_attn_Wg1"]).T),
        "wg2T": _lhsT_blocks(f32(inputs["L1_attn_Wg2"]).T),
        "bu1": f32(inputs["L1_attn_bu"]).reshape(1, D),
        "bv1": f32(inputs["L1_attn_bv"]).reshape(1, D),
        "bg1": f32(inputs["L1_attn_bg1"]).reshape(1, DH),
        "bg2": f32(inputs["L1_attn_bg2"]).reshape(1, D),
        "convw": np.ascontiguousarray(f32(inputs["L1_attn_conv_w"])[:, 0, :].T).reshape(1, 3 * D),
        "convb": f32(inputs["L1_attn_conv_b"]).reshape(1, D),
        "wfcT": _lhsT_blocks(f32(inputs["L1_ffn_w_fc"]).T),
        "wpjT": _lhsT_blocks(f32(inputs["L1_ffn_w_proj"]).T),
    }
    wfc = f32(inputs["L1_ffn_w_fc"])
    wpj = f32(inputs["L1_ffn_w_proj"])
    in_maps = []
    for core in range(8):
        bidx, half = core // 2, core % 2
        sl = slice(2048 * half, 2048 * half + 2048)
        xTc = np.ascontiguousarray(x[bidx, sl, :].T).reshape(NT, P, TOK)
        fl = np.zeros((1, 64), dtype=np.float32)
        fl[0, 0] = float(half)
        fl[0, 1] = 1.0 - float(half)
        for k in range(8):
            for j in range(4):
                fl[0, 32 + 4 * k + j] = 1.0 if k == core else 0.0
        wf = wfc.reshape(8, 1024, 2048)[core].reshape(16, 128, 1024)
        wp = wpj.reshape(8, 256, 8192)[core].reshape(16, 128, 1024)
        wslc = np.ascontiguousarray(np.concatenate([wf, wp], axis=0))
        m = dict(shared)
        m["xT"] = xTc
        m["flags"] = fl
        m["wsl"] = wslc
        in_maps.append(m)
    return in_maps


_LAST_RES = None


def kernel(**inputs):
    global _LAST_RES
    upto = int(inputs.pop("_upto", 4))
    in_maps = make_inmaps(inputs)
    nc = _get_nc(upto)
    res = run_bass_kernel_spmd(nc, in_maps, core_ids=list(range(8)))
    _LAST_RES = res
    out = np.empty((4, 4096, D), dtype=np.float32)
    for core in range(8):
        bidx, half = core // 2, core % 2
        o = np.asarray(res.results[core]["outT"]).reshape(D, TOK)
        out[bidx, 2048 * half:2048 * half + 2048, :] = o.T
    return out


# version 05
# speedup vs baseline: 1.1695x; 1.0067x over previous
import numpy as np
import ml_dtypes
from contextlib import ExitStack

import concourse.bass as bass
import concourse.mybir as mybir
from concourse.bass_utils import run_bass_kernel_spmd

F32 = mybir.dt.float32
BF16 = mybir.dt.bfloat16
FP8 = mybir.dt.float8e4
U8 = mybir.dt.uint8
ALU = mybir.AluOpType
ACTF = mybir.ActivationFunctionType
AX = mybir.AxisListType

P = 128
D = 2048
NT = 16
TOK = 2048
CH = 512
NCH = TOK // CH
INTER = 8192
NJ = 64
DH = 512
NDH = 4
K1 = 1.0 / (np.sqrt(2048.0) * 64.0)
K2 = 1.0 / (np.sqrt(8192.0) * 64.0)
EPS = 1e-5
ARENA = 193536


def _butterfly_levels(n):
    out = []
    h = 1
    while h < n:
        pairs = []
        g = 0
        while g < n:
            for k in range(h):
                pairs.append((g + k, g + h + k))
            g += 2 * h
        out.append(pairs)
        h *= 2
    return out


class _B:
    pass


def _mk():
    b = _B()
    import os as _os
    nc = bass.Bass("TRN2", target_bir_lowering=False,
                   detect_race_conditions=_os.environ.get("KERNEL_RACE_DETECT", "0") == "1")
    b.nc = nc
    b.stack = ExitStack()
    dr = lambda name, shape, dt, kind="Internal": nc.dram_tensor(name, shape, dt, kind=kind)
    b.xT = dr("xT", [NT, P, TOK], F32, "ExternalInput").ap()
    b.flags = dr("flags", [1, 64], F32, "ExternalInput").ap()
    b.wsl = dr("wsl", [32, P, 1024], F32, "ExternalInput").ap()
    b.outT = dr("outT", [NT, P, TOK], F32, "ExternalOutput").ap()
    b.dbg = dr("dbg", [1, 128], F32, "ExternalOutput").ap()
    b.wvT = dr("wvT", [16, P, NT, P], F32, "ExternalInput").ap()
    b.woT = dr("woT", [16, P, NT, P], F32, "ExternalInput").ap()
    b.bv0 = dr("bv0", [1, D], F32, "ExternalInput").ap()
    b.bo0 = dr("bo0", [1, D], F32, "ExternalInput").ap()
    b.wupT = dr("wupT", [NJ, P, NT, P], F32, "ExternalInput").ap()
    b.wgT = dr("wgT", [NJ, P, NT, P], F32, "ExternalInput").ap()
    b.wdT = dr("wdT", [NT, P, NJ, P], F32, "ExternalInput").ap()
    b.su_ug = dr("su_ug", [1, D], F32, "ExternalInput").ap()
    b.su_dp = dr("su_dp", [1, INTER], F32, "ExternalInput").ap()
    b.ws_ug = dr("ws_ug", [1, 2 * INTER], F32, "ExternalInput").ap()
    b.ws_dp = dr("ws_dp", [1, D], F32, "ExternalInput").ap()
    b.wuT = dr("wuT", [NT, P, NT, P], F32, "ExternalInput").ap()
    b.wvT1 = dr("wvT1", [NT, P, NT, P], F32, "ExternalInput").ap()
    b.wg1T = dr("wg1T", [NDH, P, NT, P], F32, "ExternalInput").ap()
    b.wg2T = dr("wg2T", [NT, P, NDH, P], F32, "ExternalInput").ap()
    b.bu1 = dr("bu1", [1, D], F32, "ExternalInput").ap()
    b.bv1 = dr("bv1", [1, D], F32, "ExternalInput").ap()
    b.bg1 = dr("bg1", [1, DH], F32, "ExternalInput").ap()
    b.bg2 = dr("bg2", [1, D], F32, "ExternalInput").ap()
    b.convw = dr("convw", [1, 3 * D], F32, "ExternalInput").ap()
    b.convb = dr("convb", [1, D], F32, "ExternalInput").ap()
    b.wfcT = dr("wfcT", [NJ, P, NT, P], F32, "ExternalInput").ap()
    b.wpjT = dr("wpjT", [NT, P, NJ, P], F32, "ExternalInput").ap()
    b.resid = dr("resid", [NT, P, TOK], F32).ap()
    b.zdr = dr("zdr", [NJ, P, TOK], BF16).ap()
    b.hdr = dr("hdr", [NJ, P, TOK], BF16).ap()
    b.xlndr = dr("xlndr", [NT, P, TOK], BF16).ap()
    b.gatesdr = dr("gatesdr", [NT, P, TOK], BF16).ap()
    b.kcdr = dr("kcdr", [NT, P, TOK], BF16).ap()
    b.xln4dr = dr("xln4dr", [NT, P, TOK], F32).ap()
    b.ydr = dr("ydr", [NT, P, TOK], BF16).ap()
    b.cc1i = dr("cc1i", [P, 48], F32)
    b.cc1o = dr("cc1o", [P, 48], F32)
    b.cc2i = dr("cc2i", [1, 32], F32)
    b.cc2o = dr("cc2o", [1, 32], F32)
    b.cc3i = dr("cc3i", [1, 32], F32)
    b.cc3o = dr("cc3o", [1, 32], F32)
    b.ccmb = dr("ccmb", [1, 32], F32)
    b.ccmc = dr("ccmc", [1, 32], F32)
    Hm = np.array([[1.0]])
    while Hm.shape[0] < P:
        Hm = np.block([[Hm, Hm], [Hm, -Hm]])
    b.h128c = nc.inline_tensor(np.ascontiguousarray(Hm.astype(ml_dtypes.bfloat16)), "h128c").ap()
    b.eyec = nc.inline_tensor(np.eye(P, dtype=np.float32), "eyec").ap()

    st = b.stack
    sb = lambda name, shape, dt: st.enter_context(nc.sbuf_tensor(name, shape, dt))
    b.arena = sb("arena", [P, ARENA], U8)
    b.gbias = sb("gbias", [P, 416], F32)
    b.gb_off = {}
    b.gb_next = 0
    b.grows = sb("grows", [1, 2 * CH], F32)
    b.grows2 = sb("grows2", [1, CH], F32)
    b.gflags = sb("gflags", [1, 64], F32)
    b.gwvp = sb("gwvp", [P, NT * P], BF16)
    b.gmo = sb("gmo", [P, NT * P], BF16)
    b.gh128 = sb("gh128", [P, P], BF16)
    b.gh128su = sb("gh128su", [P, 2 * P], BF16)
    b.geye = sb("geye", [P, P], F32)
    b.gones_bf = sb("gones_bf", [P, 1], BF16)
    b.ginv_bf = sb("ginv_bf", [P, 1], BF16)
    b.gones_row = sb("gones_row", [1, P], F32)
    b.gstkc = sb("gstkc", [P, NT], F32)
    b.ghalo = sb("ghalo", [P, 48], F32)
    b.gedge = sb("gedge", [P, 4 * NT], F32)
    b.gscal = sb("gscal", [P, 8], F32)
    b.gmax = sb("gmax", [P, 8], F32)
    b.gcc2 = sb("gcc2", [1, 32], F32)
    b.grow1 = sb("grow1", [1, 16], F32)
    b.gjunk = sb("gjunk", [1, 8], F32)
    b.pbig0 = st.enter_context(nc.psum_tensor("pbig0", [P, 4 * CH], F32))
    b.pbig1 = st.enter_context(nc.psum_tensor("pbig1", [P, 4 * CH], F32))
    sem = lambda name: st.enter_context(nc.semaphore(name))
    b.sW = sem("sW")
    b.sD = sem("sD")
    b.sPE = sem("sPE")
    b.sACT = sem("sACT")
    b.sDVE = sem("sDVE")
    b.sCC = sem("sCC")
    b.sQ = [sem(f"sQ{i}") for i in range(8)]
    b.cQ = [0] * 8
    b.cW = b.cD = b.cPE = b.cACT = b.cDVE = b.cCC = 0
    return b


def gb_alloc(b, name, n):
    off = b.gb_next
    assert off + n <= 416, (name, off, n)
    b.gb_off[name] = (off, n)
    b.gb_next = off + n


def gb(b, name, i=0, n=1):
    off, tot = b.gb_off[name]
    assert i + n <= tot, (name, i, n)
    return b.gbias[:, off + i: off + i + n]


def ar(b, off, dt, cols):
    esz = {F32: 4, BF16: 2, FP8: 1}[dt]
    assert off % 512 == 0 and off + cols * esz <= ARENA, (off, cols)
    return b.arena[:, off: off + cols * esz].bitcast(dt)


def vpad(b, v):
    v.tensor_copy(b.gjunk[:, 4:6], b.gjunk[:, 0:2])
    v.tensor_copy(b.gjunk[:, 6:8], b.gjunk[:, 0:2])


def wait_all(b, e):
    if b.cD:
        e.wait_ge(b.sD, b.cD)
    if b.cW:
        e.wait_ge(b.sW, b.cW)


def _t3(ap, nt=NT):
    return ap.rearrange("l (T t) -> l T t", T=nt)


def emit_setup(b):
    nc = b.nc
    for nm, n in [("bo", NT), ("bu", NT), ("bv1", NT), ("bg2", NT), ("convb", NT),
                  ("bg1", NDH), ("bvp", 1), ("sug", NT), ("sdp", NJ),
                  ("wsu", NJ), ("wsg", NJ), ("wsd", NT), ("cw", 3 * NT),
                  ("bvtmp", NT), ("isodd", 1), ("iseven", 1)]:
        gb_alloc(b, nm, n)
    wtmp = ar(b, 0, F32, NT * P)
    wacc = ar(b, 16384, F32, NT * P)
    wred = ar(b, 32768, F32, P)

    with nc.Block() as blk, nc.allow_non_contiguous_dma(reason="tiny one-time bias loads"):
        @blk.sync
        def _(s):
            def ld(dst, src):
                s.dma_start(out=dst, in_=src).then_inc(b.sW, 16)
                b.cW += 16
            ld(b.gflags[:, :], b.flags)
            ld(b.gh128[:, :], b.h128c)
            ld(b.geye[:, :], b.eyec)
            def ldb(nm, src, ntiles):
                ld(gb(b, nm, 0, ntiles), src.rearrange("o (T l) -> l (o T)", l=P))
            ldb("bo", b.bo0, NT)
            ldb("bu", b.bu1, NT)
            ldb("bv1", b.bv1, NT)
            ldb("bg2", b.bg2, NT)
            ldb("convb", b.convb, NT)
            ldb("bg1", b.bg1, NDH)
            ldb("bvtmp", b.bv0, NT)
            ldb("sug", b.su_ug, NT)
            ldb("sdp", b.su_dp, NJ)
            ldb("wsd", b.ws_dp, NT)
            ld(gb(b, "wsu", 0, NJ), b.ws_ug[:, 0:INTER].rearrange("o (T l) -> l (o T)", l=P))
            ld(gb(b, "wsg", 0, NJ), b.ws_ug[:, INTER:].rearrange("o (T l) -> l (o T)", l=P))
            ld(gb(b, "cw", 0, 3 * NT), b.convw.rearrange("o (k T l) -> l (o k T)", l=P, T=NT))

        @blk.vector
        def _(v):
            v.memset(b.gones_bf[:, :], 1.0)
            v.memset(b.ginv_bf[:, :], 1.0 / 2048.0)
            v.memset(b.gones_row[:, :], 1.0)
            v.memset(b.gstkc[:, :], 0.0)
            v.memset(b.gmax[:, :], 0.0)
            v.memset(b.gedge[:, :], 0.0)
            v.memset(b.gscal[:, :], 0.0)
            v.memset(b.grow1[:, :], 0.0)
            v.memset(b.gjunk[:, :], 0.0)

    with nc.Block() as blk:
        @blk.vector
        def _(v):
            wait_all(b, v)
            for nm in ("wsu", "wsg", "wsd"):
                n = b.gb_off[nm][1]
                v.tensor_single_scalar(gb(b, nm, 0, n), gb(b, nm, 0, n), 64.0, ALU.mult)
            for nm, k in (("sug", K1), ("sdp", K2)):
                n = b.gb_off[nm][1]
                v.tensor_single_scalar(gb(b, nm, 0, n), gb(b, nm, 0, n), float(k), ALU.mult)
            v.reduce_sum(gb(b, "bvp"), gb(b, "bvtmp", 0, NT), AX.X)

    with nc.Block() as blk:
        @blk.tensor
        def _(t):
            t.matmul(b.pbig0[:, 0:2], b.gones_row[:, :], b.gflags[:, 0:2], start=True, stop=True)
    with nc.Block() as blk:
        @blk.vector
        def _(v):
            v.tensor_copy(gb(b, "isodd"), b.pbig0[:, 0:1])
            v.tensor_copy(gb(b, "iseven"), b.pbig0[:, 1:2])

    w0, d0 = b.cW, b.cDVE
    srcs = ([("wv", h) for h in range(16)] + [("wo", ob) for ob in range(16)]
            + [("wsl", p) for p in range(32)])
    WT2 = [0, 8192]

    def _src_ap(kind, i):
        if kind == "wv":
            return b.wvT[i].rearrange("l T c -> l (T c)"), NT * P
        if kind == "wo":
            return b.woT[i].rearrange("l T c -> l (T c)"), NT * P
        return b.wsl[i], 1024

    q0 = [b.cQ[0], b.cQ[1]]
    with nc.Block() as blk:
        @blk.sync
        def _(s):
            for k, (kind, i) in enumerate(srcs):
                if k >= 2:
                    s.wait_ge(b.sDVE, d0 + k - 1)
                apx, cols = _src_ap(kind, i)
                s.dma_start(out=ar(b, WT2[k % 2], F32, cols), in_=apx).then_inc(b.sQ[k % 2], 16)
                b.cQ[k % 2] += 16

        @blk.vector
        def _(v):
            for k, (kind, i) in enumerate(srcs):
                v.wait_ge(b.sQ[k % 2], q0[k % 2] + 16 * (k // 2 + 1))
                apx, cols = _src_ap(kind, i)
                buf = ar(b, WT2[k % 2], F32, cols)
                if kind == "wv":
                    if i == 0:
                        v.tensor_copy(wacc, buf)
                    else:
                        v.tensor_add(wacc, wacc, buf)
                    if i == 15:
                        vpad(b, v)
                        v.tensor_copy(b.gwvp[:, :], wacc)
                elif kind == "wo":
                    v.reduce_sum(wred, buf.rearrange("l (T c) -> l c T", T=NT), AX.X)
                    vpad(b, v)
                    v.tensor_copy(b.gmo[:, i * P:(i + 1) * P], wred)
                else:
                    col = 3 if i < 16 else 4
                    v.tensor_reduce(b.gmax[:, 2:3], buf, AX.X, ALU.max, apply_absolute_value=True)
                    vpad(b, v)
                    v.tensor_tensor(b.gmax[:, col:col + 1], b.gmax[:, col:col + 1], b.gmax[:, 2:3], ALU.max)
                v.drain().then_inc(b.sDVE, 1)
                b.cDVE += 1


def emit_stat_rows_and_rep(b):
    nc = b.nc
    d0, a0, p0 = b.cDVE, b.cACT, b.cPE
    with nc.Block() as blk:
        @blk.vector
        def _(v):
            g2 = b.grows2
            tmp = b.grows[:, 0:CH]
            v.tensor_copy(g2[:, 0:CH], b.pbig0[0:1, 0:CH])
            v.tensor_tensor(tmp, g2[:, 0:CH], g2[:, 0:CH], ALU.mult)
            v.tensor_tensor(tmp, b.pbig0[0:1, CH:2 * CH], tmp, ALU.subtract)
            v.tensor_single_scalar(tmp, tmp, EPS, ALU.add)
            v.reciprocal(b.grows[:, CH:2 * CH], tmp).then_inc(b.sDVE, 1)
            b.cDVE += 1
            v.wait_ge(b.sACT, a0 + 1)
            v.tensor_tensor(b.grows[:, CH:2 * CH], b.grows2[:, 0:CH], b.grows[:, 0:CH], ALU.mult)
            v.tensor_single_scalar(b.grows[:, CH:2 * CH], b.grows[:, CH:2 * CH],
                                   -1.0, ALU.mult).then_inc(b.sDVE, 1)
            b.cDVE += 1

        @blk.scalar
        def _(a):
            a.wait_ge(b.sDVE, d0 + 1)
            a.activation(b.grows[:, 0:CH], b.grows[:, CH:2 * CH], ACTF.Sqrt).then_inc(b.sACT, 1)
            b.cACT += 1

        @blk.tensor
        def _(t):
            t.wait_ge(b.sDVE, d0 + 2)
            t.matmul(b.pbig0[:, 2 * CH:3 * CH], b.gones_row[:, :], b.grows[:, 0:CH], start=True, stop=True)
            t.matmul(b.pbig0[:, 3 * CH:4 * CH], b.gones_row[:, :], b.grows[:, CH:2 * CH], start=True, stop=True)


def emit_stats_from(b, xb, x2b, nt=NT):
    nc = b.nc
    with nc.Block() as blk:
        @blk.tensor
        def _(t):
            for T in range(nt):
                t.matmul(b.pbig0[0:1, 0:CH], b.ginv_bf[:, 0:1], xb[:, T * CH:(T + 1) * CH],
                         start=(T == 0), stop=(T == nt - 1))
            for T in range(nt):
                t.matmul(b.pbig0[0:1, CH:2 * CH], b.ginv_bf[:, 0:1], x2b[:, T * CH:(T + 1) * CH],
                         start=(T == 0), stop=(T == nt - 1))
    emit_stat_rows_and_rep(b)


def emit_preln(b, c, src_dram, affine_dst=None, affine_nt=NT):
    nc = b.nc
    c0 = c * CH
    rc = ar(b, 0, F32, NT * CH)
    xb = ar(b, 33280, BF16, NT * CH)
    x2b = ar(b, 50176, BF16, NT * CH)
    dd0, d0, a0, p0 = b.cD, b.cDVE, b.cACT, b.cPE
    with nc.Block() as blk:
        @blk.gpsimd
        def _(g):
            wait_all(b, g)
            g.dma_start(out=_t3(rc), in_=src_dram[:, :, c0:c0 + CH].rearrange("T l t -> l T t")).then_inc(b.sD, 16)
            b.cD += 16

        @blk.vector
        def _(v):
            v.wait_ge(b.sD, b.cD)
            v.tensor_copy(xb, rc).then_inc(b.sDVE, 1)
            b.cDVE += 1
            v.wait_ge(b.sPE, p0 + 1)
            g2 = b.grows2
            tmp = b.grows[:, 0:CH]
            v.tensor_copy(g2[:, 0:CH], b.pbig0[0:1, 0:CH])
            v.tensor_tensor(tmp, g2[:, 0:CH], g2[:, 0:CH], ALU.mult)
            v.tensor_tensor(tmp, b.pbig0[0:1, CH:2 * CH], tmp, ALU.subtract)
            v.tensor_single_scalar(tmp, tmp, EPS, ALU.add)
            v.reciprocal(b.grows[:, CH:2 * CH], tmp).then_inc(b.sDVE, 1)
            b.cDVE += 1
            v.wait_ge(b.sACT, a0 + 2)
            v.tensor_tensor(b.grows[:, CH:2 * CH], b.grows2[:, 0:CH], b.grows[:, 0:CH], ALU.mult)
            v.tensor_single_scalar(b.grows[:, CH:2 * CH], b.grows[:, CH:2 * CH],
                                   -1.0, ALU.mult).then_inc(b.sDVE, 1)
            b.cDVE += 1
            if affine_dst is not None:
                v.wait_ge(b.sPE, p0 + 2)
                for T in range(affine_nt):
                    sl = slice(T * CH, (T + 1) * CH)
                    v.tensor_tensor(affine_dst[:, sl], rc[:, sl], b.pbig0[:, 2 * CH:3 * CH], ALU.mult)
                for T in range(affine_nt):
                    sl = slice(T * CH, (T + 1) * CH)
                    v.tensor_tensor(affine_dst[:, sl], affine_dst[:, sl], b.pbig0[:, 3 * CH:4 * CH], ALU.add)

        @blk.scalar
        def _(a):
            a.wait_ge(b.sD, b.cD)
            a.activation(x2b, rc, ACTF.Square).then_inc(b.sACT, 1)
            b.cACT += 1
            a.wait_ge(b.sDVE, d0 + 2)
            a.activation(b.grows[:, 0:CH], b.grows[:, CH:2 * CH], ACTF.Sqrt).then_inc(b.sACT, 1)
            b.cACT += 1

        @blk.tensor
        def _(t):
            t.wait_ge(b.sDVE, d0 + 1)
            t.wait_ge(b.sACT, a0 + 1)
            for T in range(NT):
                t.matmul(b.pbig0[0:1, 0:CH], b.ginv_bf[:, 0:1], xb[:, T * CH:(T + 1) * CH],
                         start=(T == 0), stop=(T == NT - 1))
            for T in range(NT):
                t.matmul(b.pbig0[0:1, CH:2 * CH], b.ginv_bf[:, 0:1], x2b[:, T * CH:(T + 1) * CH],
                         start=(T == 0), stop=(T == NT - 1))
            t.drain().then_inc(b.sPE, 1)
            b.cPE += 1
            t.wait_ge(b.sDVE, d0 + 3)
            t.matmul(b.pbig0[:, 2 * CH:3 * CH], b.gones_row[:, :], b.grows[:, 0:CH], start=True, stop=True)
            t.matmul(b.pbig0[:, 3 * CH:4 * CH], b.gones_row[:, :], b.grows[:, CH:2 * CH],
                     start=True, stop=True).then_inc(b.sPE, 1)
            b.cPE += 1


def emit_affine_to(b, srcf32, dst, nt=NT):
    nc = b.nc
    with nc.Block() as blk:
        @blk.vector
        def _(v):
            for T in range(nt):
                sl = slice(T * CH, (T + 1) * CH)
                v.tensor_tensor(dst[:, sl], srcf32[:, sl], b.pbig0[:, 2 * CH:3 * CH], ALU.mult)
            for T in range(nt):
                sl = slice(T * CH, (T + 1) * CH)
                v.tensor_tensor(dst[:, sl], dst[:, sl], b.pbig0[:, 3 * CH:4 * CH], ALU.add)


def emit_postln_resid(b, c, ybuf, rc, t1, dst_dram, nt=NT):
    nc = b.nc
    c0 = c * CH
    d0 = b.cDVE
    with nc.Block() as blk:
        @blk.vector
        def _(v):
            for T in range(nt):
                sl = slice(T * CH, (T + 1) * CH)
                v.tensor_tensor(t1[:, sl], ybuf[:, sl], b.pbig0[:, 2 * CH:3 * CH], ALU.mult)
            for T in range(nt):
                sl = slice(T * CH, (T + 1) * CH)
                v.tensor_tensor(t1[:, sl], t1[:, sl], b.pbig0[:, 3 * CH:4 * CH], ALU.add)
            for T in range(nt):
                sl = slice(T * CH, (T + 1) * CH)
                v.tensor_tensor(t1[:, sl], t1[:, sl], rc[:, sl], ALU.add)
            v.drain().then_inc(b.sDVE, 1)
            b.cDVE += 1

        @blk.gpsimd
        def _(g):
            g.wait_ge(b.sDVE, d0 + 1)
            g.dma_start(out=dst_dram[:, :, c0:c0 + CH].rearrange("T l t -> l T t"), in_=_t3(t1)).then_inc(b.sD, 16)
            b.cD += 16


def emit_l0_attn(b):
    nc = b.nc
    XLN, Y, SQ, WB, T1 = 67584, 84480, 101376, 103424, 103936
    for c in range(NCH):
        rc = ar(b, 0, F32, NT * CH)
        xln = ar(b, XLN, BF16, NT * CH)
        emit_preln(b, c, b.xT, affine_dst=xln)
        ybuf = ar(b, Y, BF16, NT * CH)
        sq = ar(b, SQ, BF16, 2 * CH)
        wb = ar(b, WB, BF16, CH)
        t1 = ar(b, T1, F32, NT * CH)
        with nc.Block() as blk:
            @blk.tensor
            def _(t):
                for T in range(NT):
                    t.matmul(b.pbig1[:, 0:CH], b.gwvp[:, T * P:(T + 1) * P], xln[:, T * CH:(T + 1) * CH],
                             start=(T == 0), stop=(T == NT - 1))
        with nc.Block() as blk:
            @blk.scalar
            def _(a):
                a.activation(wb, b.pbig1[:, 0:CH], ACTF.Identity, bias=gb(b, "bvp"))
        a0, p0 = b.cACT, b.cPE
        with nc.Block() as blk:
            @blk.tensor
            def _(t):
                for ob in range(NT):
                    if ob >= 2:
                        t.wait_ge(b.sACT, a0 + 2 * (ob - 1))
                    t.matmul(b.pbig1[:, (1 + ob % 2) * CH:(2 + ob % 2) * CH],
                             b.gmo[:, ob * P:(ob + 1) * P], wb, start=True, stop=True)
                    if ob >= 1:
                        t.wait_ge(b.sACT, a0 + 2 * ob)
                        pob = ob - 1
                        t.matmul(b.pbig0[0:1, 0:CH], b.ginv_bf[:, 0:1], ybuf[:, pob * CH:(pob + 1) * CH],
                                 start=(pob == 0), stop=False)
                        t.matmul(b.pbig0[0:1, CH:2 * CH], b.ginv_bf[:, 0:1],
                                 sq[:, (pob % 2) * CH:(pob % 2 + 1) * CH],
                                 start=(pob == 0), stop=False)
                    t.drain().then_inc(b.sPE, 1)
                    b.cPE += 1
                t.wait_ge(b.sACT, a0 + 2 * NT)
                pob = NT - 1
                t.matmul(b.pbig0[0:1, 0:CH], b.ginv_bf[:, 0:1], ybuf[:, pob * CH:(pob + 1) * CH],
                         start=False, stop=True)
                t.matmul(b.pbig0[0:1, CH:2 * CH], b.ginv_bf[:, 0:1],
                         sq[:, (pob % 2) * CH:(pob % 2 + 1) * CH],
                         start=False, stop=True)

            @blk.scalar
            def _(a):
                for ob in range(NT):
                    a.wait_ge(b.sPE, p0 + ob + 1)
                    a.activation(ybuf[:, ob * CH:(ob + 1) * CH], b.pbig1[:, (1 + ob % 2) * CH:(2 + ob % 2) * CH],
                                 ACTF.Identity, bias=gb(b, "bo", ob))
                    a.activation(sq[:, (ob % 2) * CH:(ob % 2 + 1) * CH],
                                 ybuf[:, ob * CH:(ob + 1) * CH], ACTF.Square).then_inc(b.sACT, 2)
                    b.cACT += 2
        emit_stat_rows_and_rep(b)
        emit_postln_resid(b, c, ybuf, rc, t1, b.resid)


def emit_l0_ffn(b):
    nc = b.nc
    XQ, BUFA, BUFB = 101376, 67584, 84480
    xq = ar(b, XQ, BF16, NT * TOK)
    for c in range(NCH):
        rc = ar(b, 0, F32, NT * CH)
        bufa = ar(b, BUFA, BF16, NT * CH)
        bufb = ar(b, BUFB, BF16, NT * CH)
        emit_preln(b, c, b.resid, affine_dst=bufb)
        with nc.Block() as blk:
            @blk.vector
            def _(v):
                for T in range(NT):
                    sl = slice(T * CH, (T + 1) * CH)
                    v.tensor_scalar(bufa[:, sl], bufb[:, sl], gb(b, "sug", T), None, ALU.mult)
        bufs, cur = [bufa, bufb], 0
        for pairs in _butterfly_levels(NT):
            src, dst = bufs[cur], bufs[1 - cur]
            with nc.Block() as blk:
                @blk.vector
                def _(v):
                    for (i, j) in pairs:
                        v.tensor_tensor(dst[:, i * CH:(i + 1) * CH], src[:, i * CH:(i + 1) * CH],
                                        src[:, j * CH:(j + 1) * CH], ALU.add)
                        v.tensor_tensor(dst[:, j * CH:(j + 1) * CH], src[:, i * CH:(i + 1) * CH],
                                        src[:, j * CH:(j + 1) * CH], ALU.subtract)
            cur = 1 - cur
        final = bufs[cur]
        for grp in range(NT // 4):
            with nc.Block() as blk:
                @blk.tensor
                def _(t):
                    for k in range(4):
                        T = grp * 4 + k
                        t.matmul(b.pbig1[:, k * CH:(k + 1) * CH], b.gh128[:, :],
                                 final[:, T * CH:(T + 1) * CH], start=True, stop=True)
            with nc.Block() as blk:
                @blk.scalar
                def _(a):
                    for k in range(4):
                        T = grp * 4 + k
                        a.activation(xq[:, T * TOK + c * CH: T * TOK + c * CH + CH],
                                     b.pbig1[:, k * CH:(k + 1) * CH], ACTF.Copy)
    if b.sub_l0ffn >= 2:
        emit_a2(b, xq)
    if b.sub_l0ffn >= 3:
        emit_a3(b)


def emit_a2_serial(b, xq):
    nc = b.nc
    WUP = [0, 8192]
    WG = [16384, 24576]
    WUPB = [32768, 36864]
    WGB = [40960, 45056]
    SILU = 49152
    XMID = 53248
    ZSB = [57344, 61440]
    for j in range(NJ):
        with nc.Block() as blk:
            @blk.sync
            def _(s):
                if j == 0:
                    wait_all(b, s)
                s.dma_start(out=ar(b, WUP[0], F32, NT * P),
                            in_=b.wupT[j].rearrange("l T c -> l (T c)")).then_inc(b.sW, 16)
                s.dma_start(out=ar(b, WG[0], F32, NT * P),
                            in_=b.wgT[j].rearrange("l T c -> l (T c)")).then_inc(b.sW, 16)
                b.cW += 32
                s.wait_ge(b.sW, b.cW)

            @blk.vector
            def _(v):
                v.wait_ge(b.sW, b.cW)
                v.tensor_copy(ar(b, WUPB[0], BF16, NT * P), ar(b, WUP[0], F32, NT * P))
                v.tensor_copy(ar(b, WGB[0], BF16, NT * P), ar(b, WG[0], F32, NT * P))
                v.tensor_scalar(b.gh128su[:, 0:P], b.gh128[:, :], gb(b, "sdp", j), None, ALU.mult)
        with nc.Block() as blk:
            @blk.tensor
            def _(t):
                wupb = ar(b, WUPB[0], BF16, NT * P)
                wgb = ar(b, WGB[0], BF16, NT * P)
                for T in range(NT):
                    for g in range(4):
                        t.matmul(b.pbig0[:, g * CH:(g + 1) * CH], wupb[:, T * P:(T + 1) * P],
                                 xq[:, T * TOK + g * CH: T * TOK + (g + 1) * CH],
                                 start=(T == 0), stop=(T == NT - 1))
                for T in range(NT):
                    for g in range(4):
                        t.matmul(b.pbig1[:, g * CH:(g + 1) * CH], wgb[:, T * P:(T + 1) * P],
                                 xq[:, T * TOK + g * CH: T * TOK + (g + 1) * CH],
                                 start=(T == 0), stop=(T == NT - 1))
        with nc.Block() as blk:
            @blk.scalar
            def _(a):
                a.activation(ar(b, SILU, BF16, TOK), b.pbig1[:, :], ACTF.Silu, scale=gb(b, "wsg", j))
        with nc.Block() as blk:
            @blk.vector
            def _(v):
                v.scalar_tensor_tensor(ar(b, XMID, BF16, TOK), b.pbig0[:, :], gb(b, "wsu", j),
                                       ar(b, SILU, BF16, TOK), ALU.mult, ALU.mult)
        with nc.Block() as blk:
            @blk.tensor
            def _(t):
                xmid = ar(b, XMID, BF16, TOK)
                for g in range(4):
                    t.matmul(b.pbig0[:, g * CH:(g + 1) * CH], b.gh128su[:, 0:P],
                             xmid[:, g * CH:(g + 1) * CH], start=True, stop=True)
        with nc.Block() as blk:
            @blk.scalar
            def _(a):
                a.activation(ar(b, ZSB[0], BF16, TOK), b.pbig0[:, :], ACTF.Copy)
        with nc.Block() as blk:
            @blk.gpsimd
            def _(g):
                g.dma_start(out=b.zdr[j], in_=ar(b, ZSB[0], BF16, TOK)).then_inc(b.sD, 16)
                b.cD += 16
                g.wait_ge(b.sD, b.cD)


def emit_a2_min(b, xq):
    nc = b.nc
    WUP = [0, 8192]
    WG = [16384, 24576]
    WUPB = [32768, 36864]
    WGB = [40960, 45056]
    w0, d0, p0 = b.cW, b.cDVE, b.cPE
    with nc.Block() as blk:
        @blk.sync
        def _(s):
            wait_all(b, s)
            for j in (0, 1):
                s.dma_start(out=ar(b, WUP[j % 2], F32, NT * P),
                            in_=b.wupT[j].rearrange("l T c -> l (T c)")).then_inc(b.sW, 16)
                s.dma_start(out=ar(b, WG[j % 2], F32, NT * P),
                            in_=b.wgT[j].rearrange("l T c -> l (T c)")).then_inc(b.sW, 16)
                b.cW += 32
            for j in range(NJ - 2):
                s.wait_ge(b.sDVE, d0 + j + 1)
                s.dma_start(out=ar(b, WUP[j % 2], F32, NT * P),
                            in_=b.wupT[j + 2].rearrange("l T c -> l (T c)")).then_inc(b.sW, 16)
                s.dma_start(out=ar(b, WG[j % 2], F32, NT * P),
                            in_=b.wgT[j + 2].rearrange("l T c -> l (T c)")).then_inc(b.sW, 16)
                b.cW += 32

        @blk.vector
        def _(v):
            for j in range(NJ):
                v.wait_ge(b.sW, w0 + 32 * (j + 1))
                if j >= 2:
                    v.wait_ge(b.sPE, p0 + j - 1)
                v.tensor_copy(ar(b, WUPB[j % 2], BF16, NT * P), ar(b, WUP[j % 2], F32, NT * P))
                v.tensor_copy(ar(b, WGB[j % 2], BF16, NT * P), ar(b, WG[j % 2], F32, NT * P))
                v.drain().then_inc(b.sDVE, 1)
                b.cDVE += 1

        @blk.tensor
        def _(t):
            for j in range(NJ):
                t.wait_ge(b.sDVE, d0 + j + 1)
                wupb = ar(b, WUPB[j % 2], BF16, NT * P)
                wgb = ar(b, WGB[j % 2], BF16, NT * P)
                for T in range(NT):
                    for g in range(4):
                        t.matmul(b.pbig0[:, g * CH:(g + 1) * CH], wupb[:, T * P:(T + 1) * P],
                                 xq[:, T * TOK + g * CH: T * TOK + (g + 1) * CH],
                                 start=(T == 0), stop=(T == NT - 1))
                for T in range(NT):
                    for g in range(4):
                        t.matmul(b.pbig1[:, g * CH:(g + 1) * CH], wgb[:, T * P:(T + 1) * P],
                                 xq[:, T * TOK + g * CH: T * TOK + (g + 1) * CH],
                                 start=(T == 0), stop=(T == NT - 1))
                t.drain().then_inc(b.sPE, 1)
                b.cPE += 1


def emit_a2_minsilu(b, xq):
    nc = b.nc
    WUP = [0, 8192]
    WG = [16384, 24576]
    WUPB = [32768, 36864]
    WGB = [40960, 45056]
    SILU = 49152
    w0, d0, p0, a0 = b.cW, b.cDVE, b.cPE, b.cACT
    with nc.Block() as blk:
        @blk.sync
        def _(s):
            wait_all(b, s)
            for j in (0, 1):
                s.dma_start(out=ar(b, WUP[j % 2], F32, NT * P),
                            in_=b.wupT[j].rearrange("l T c -> l (T c)")).then_inc(b.sW, 16)
                s.dma_start(out=ar(b, WG[j % 2], F32, NT * P),
                            in_=b.wgT[j].rearrange("l T c -> l (T c)")).then_inc(b.sW, 16)
                b.cW += 32
            for j in range(NJ - 2):
                s.wait_ge(b.sDVE, d0 + j + 1)
                s.dma_start(out=ar(b, WUP[j % 2], F32, NT * P),
                            in_=b.wupT[j + 2].rearrange("l T c -> l (T c)")).then_inc(b.sW, 16)
                s.dma_start(out=ar(b, WG[j % 2], F32, NT * P),
                            in_=b.wgT[j + 2].rearrange("l T c -> l (T c)")).then_inc(b.sW, 16)
                b.cW += 32

        @blk.vector
        def _(v):
            import os as _os4
            dvp = _os4.environ.get("A2_DVE_PSUM", "0") == "1"
            for j in range(NJ):
                v.wait_ge(b.sW, w0 + 32 * (j + 1))
                if j >= 2:
                    v.wait_ge(b.sPE, p0 + j - 1)
                v.tensor_copy(ar(b, WUPB[j % 2], BF16, NT * P), ar(b, WUP[j % 2], F32, NT * P))
                v.tensor_copy(ar(b, WGB[j % 2], BF16, NT * P), ar(b, WG[j % 2], F32, NT * P))
                if dvp and j > 0:
                    v.tensor_copy(ar(b, SILU, BF16, TOK), b.pbig0[:, :])
                v.drain().then_inc(b.sDVE, 1)
                b.cDVE += 1

        @blk.tensor
        def _(t):
            for j in range(NJ):
                t.wait_ge(b.sDVE, d0 + j + 1)
                if j > 0:
                    t.wait_ge(b.sACT, a0 + j)
                wupb = ar(b, WUPB[j % 2], BF16, NT * P)
                wgb = ar(b, WGB[j % 2], BF16, NT * P)
                for T in range(NT):
                    for g in range(4):
                        t.matmul(b.pbig0[:, g * CH:(g + 1) * CH], wupb[:, T * P:(T + 1) * P],
                                 xq[:, T * TOK + g * CH: T * TOK + (g + 1) * CH],
                                 start=(T == 0), stop=(T == NT - 1))
                for T in range(NT):
                    for g in range(4):
                        t.matmul(b.pbig1[:, g * CH:(g + 1) * CH], wgb[:, T * P:(T + 1) * P],
                                 xq[:, T * TOK + g * CH: T * TOK + (g + 1) * CH],
                                 start=(T == 0), stop=(T == NT - 1))
                t.drain().then_inc(b.sPE, 1)
                b.cPE += 1

        @blk.scalar
        def _(a):
            for j in range(NJ):
                a.wait_ge(b.sPE, p0 + j + 1)
                import os as _os3
                if _os3.environ.get("A2_SILU_SBUF", "0") == "1":
                    a.activation(ar(b, SILU, BF16, TOK), xq[:, 0:TOK], ACTF.Silu,
                                 scale=gb(b, "wsg", j)).then_inc(b.sACT, 1)
                else:
                    a.activation(ar(b, SILU, BF16, TOK), b.pbig1[:, :], ACTF.Silu,
                                 scale=gb(b, "wsg", j)).then_inc(b.sACT, 1)
                b.cACT += 1


def emit_a2(b, xq):
    import os as _os2
    if _os2.environ.get("A2_MODE", "pipe") == "serial":
        return emit_a2_serial(b, xq)
    if _os2.environ.get("A2_MODE", "pipe") == "min":
        return emit_a2_min(b, xq)
    if _os2.environ.get("A2_MODE", "pipe") == "minsilu":
        return emit_a2_minsilu(b, xq)
    noz = _os2.environ.get("A2_NOZ", "0") == "1"
    nc = b.nc
    WUP = [0, 8192]
    WG = [16384, 24576]
    WUPB = [32768, 36864]
    WGB = [40960, 45056]
    SILU = 49152
    XMID = 53248
    ZSB = [57344, 61440]
    w0, d0, a0, p0, dd0 = b.cW, b.cDVE, b.cACT, b.cPE, b.cD
    q0 = list(b.cQ)
    with nc.Block() as blk:
        @blk.sync
        def _(s):
            wait_all(b, s)
            for j in (0, 1):
                s.dma_start(out=ar(b, WUP[j % 2], F32, NT * P),
                            in_=b.wupT[j].rearrange("l T c -> l (T c)")).then_inc(b.sQ[j % 2], 16)
                s.dma_start(out=ar(b, WG[j % 2], F32, NT * P),
                            in_=b.wgT[j].rearrange("l T c -> l (T c)")).then_inc(b.sQ[2 + j % 2], 16)
                b.cQ[j % 2] += 16
                b.cQ[2 + j % 2] += 16
            for j in range(NJ - 2):
                s.wait_ge(b.sDVE, d0 + 2 * j + 1)
                s.dma_start(out=ar(b, WUP[j % 2], F32, NT * P),
                            in_=b.wupT[j + 2].rearrange("l T c -> l (T c)")).then_inc(b.sQ[j % 2], 16)
                s.dma_start(out=ar(b, WG[j % 2], F32, NT * P),
                            in_=b.wgT[j + 2].rearrange("l T c -> l (T c)")).then_inc(b.sQ[2 + j % 2], 16)
                b.cQ[j % 2] += 16
                b.cQ[2 + j % 2] += 16

        @blk.vector
        def _(v):
            for j in range(NJ):
                v.wait_ge(b.sQ[j % 2], q0[j % 2] + 16 * (j // 2 + 1))
                v.wait_ge(b.sQ[2 + j % 2], q0[2 + j % 2] + 16 * (j // 2 + 1))
                if j >= 2:
                    v.wait_ge(b.sPE, p0 + 2 * (j - 2) + 2)
                v.tensor_copy(ar(b, WUPB[j % 2], BF16, NT * P), ar(b, WUP[j % 2], F32, NT * P))
                v.tensor_copy(ar(b, WGB[j % 2], BF16, NT * P), ar(b, WG[j % 2], F32, NT * P))
                v.tensor_scalar(b.gh128su[:, (j % 2) * P:(j % 2 + 1) * P], b.gh128[:, :],
                                gb(b, "sdp", j), None, ALU.mult)
                v.drain().then_inc(b.sDVE, 1)
                b.cDVE += 1
                v.wait_ge(b.sACT, a0 + 2 * j + 1)
                v.scalar_tensor_tensor(ar(b, XMID, BF16, TOK), b.pbig0[:, :], gb(b, "wsu", j),
                                       ar(b, SILU, BF16, TOK), ALU.mult, ALU.mult)
                v.drain().then_inc(b.sDVE, 1)
                b.cDVE += 1

        @blk.tensor
        def _(t):
            for j in range(NJ):
                t.wait_ge(b.sDVE, d0 + 2 * j + 1)
                if j > 0:
                    t.wait_ge(b.sACT, a0 + 2 * j)
                wupb = ar(b, WUPB[j % 2], BF16, NT * P)
                wgb = ar(b, WGB[j % 2], BF16, NT * P)
                for T in range(NT):
                    for g in range(4):
                        t.matmul(b.pbig0[:, g * CH:(g + 1) * CH], wupb[:, T * P:(T + 1) * P],
                                 xq[:, T * TOK + g * CH: T * TOK + (g + 1) * CH],
                                 start=(T == 0), stop=(T == NT - 1))
                for T in range(NT):
                    for g in range(4):
                        t.matmul(b.pbig1[:, g * CH:(g + 1) * CH], wgb[:, T * P:(T + 1) * P],
                                 xq[:, T * TOK + g * CH: T * TOK + (g + 1) * CH],
                                 start=(T == 0), stop=(T == NT - 1))
                t.drain().then_inc(b.sPE, 1)
                b.cPE += 1
                t.wait_ge(b.sDVE, d0 + 2 * j + 2)
                if not noz:
                    xmid = ar(b, XMID, BF16, TOK)
                    for g in range(4):
                        t.matmul(b.pbig0[:, g * CH:(g + 1) * CH], b.gh128su[:, (j % 2) * P:(j % 2 + 1) * P],
                                 xmid[:, g * CH:(g + 1) * CH], start=True, stop=True)
                t.drain().then_inc(b.sPE, 1)
                b.cPE += 1

        @blk.scalar
        def _(a):
            for j in range(NJ):
                a.wait_ge(b.sPE, p0 + 2 * j + 1)
                a.activation(ar(b, SILU, BF16, TOK), b.pbig1[:, :], ACTF.Silu, scale=gb(b, "wsg", j))
                a.drain().then_inc(b.sACT, 1)
                b.cACT += 1
                a.wait_ge(b.sPE, p0 + 2 * j + 2)
                if (not noz) and j >= 2:
                    a.wait_ge(b.sQ[4 + j % 2], q0[4 + j % 2] + 16 * ((j - 2) // 2 + 1))
                if not noz:
                    a.activation(ar(b, ZSB[j % 2], BF16, TOK), b.pbig0[:, :], ACTF.Copy)
                a.drain().then_inc(b.sACT, 1)
                b.cACT += 1

        @blk.gpsimd
        def _(g):
            if not noz:
                for j in range(NJ):
                    g.wait_ge(b.sACT, a0 + 2 * j + 2)
                    g.dma_start(out=b.zdr[j], in_=ar(b, ZSB[j % 2], BF16, TOK)).then_inc(b.sQ[4 + j % 2], 16)
                    b.cQ[4 + j % 2] += 16


def emit_a3(b):
    nc = b.nc
    ZA, ZB = 0, 65536
    WD = [131072, 139264]
    WDB = [147456, 151552]
    YB, SQ = 155648, 172032
    T1 = 32768
    for c in range(NCH):
        c0 = c * CH
        za = ar(b, ZA, BF16, NJ * CH)
        zb = ar(b, ZB, BF16, NJ * CH)
        with nc.Block() as blk:
            @blk.gpsimd
            def _(g):
                wait_all(b, g)
                for j in range(NJ):
                    g.dma_start(out=za[:, j * CH:(j + 1) * CH],
                                in_=b.zdr[j, :, c0:c0 + CH]).then_inc(b.sD, 16)
                    b.cD += 16
        with nc.Block() as blk:
            @blk.vector
            def _(v):
                v.wait_ge(b.sD, b.cD)
        bufs, cur = [za, zb], 0
        for pairs in _butterfly_levels(NJ):
            src, dst = bufs[cur], bufs[1 - cur]
            with nc.Block() as blk:
                @blk.vector
                def _(v):
                    for (i, j) in pairs:
                        v.tensor_tensor(dst[:, i * CH:(i + 1) * CH], src[:, i * CH:(i + 1) * CH],
                                        src[:, j * CH:(j + 1) * CH], ALU.add)
                        v.tensor_tensor(dst[:, j * CH:(j + 1) * CH], src[:, i * CH:(i + 1) * CH],
                                        src[:, j * CH:(j + 1) * CH], ALU.subtract)
            cur = 1 - cur
        zfin = bufs[cur]
        ybuf = ar(b, YB, BF16, NT * CH)
        sq = ar(b, SQ, BF16, 2 * CH)
        w0, d0, a0, p0 = b.cW, b.cDVE, b.cACT, b.cPE
        q0 = list(b.cQ)
        with nc.Block() as blk:
            @blk.sync
            def _(s):
                wait_all(b, s)
                for k in range(2):
                    ob, q = divmod(k, 4)
                    s.dma_start(out=ar(b, WD[k % 2], F32, NT * P),
                                in_=b.wdT[ob, :, q * NT:(q + 1) * NT, :].rearrange("l T c -> l (T c)")).then_inc(b.sQ[k % 2], 16)
                    b.cQ[k % 2] += 16
                for k in range(4 * NT - 2):
                    ob, q = divmod(k + 2, 4)
                    s.wait_ge(b.sDVE, d0 + k + 1)
                    s.dma_start(out=ar(b, WD[k % 2], F32, NT * P),
                                in_=b.wdT[ob, :, q * NT:(q + 1) * NT, :].rearrange("l T c -> l (T c)")).then_inc(b.sQ[k % 2], 16)
                    b.cQ[k % 2] += 16

            @blk.vector
            def _(v):
                for k in range(4 * NT):
                    v.wait_ge(b.sQ[k % 2], q0[k % 2] + 16 * (k // 2 + 1))
                    if k >= 2:
                        v.wait_ge(b.sPE, p0 + k - 1)
                    v.tensor_copy(ar(b, WDB[k % 2], BF16, NT * P), ar(b, WD[k % 2], F32, NT * P))
                    v.drain().then_inc(b.sDVE, 1)
                    b.cDVE += 1

            @blk.tensor
            def _(t):
                for k in range(4 * NT):
                    ob, q = divmod(k, 4)
                    t.wait_ge(b.sDVE, d0 + k + 1)
                    if q == 0 and ob >= 2:
                        t.wait_ge(b.sACT, a0 + 2 * (ob - 1))
                    wdb = ar(b, WDB[k % 2], BF16, NT * P)
                    for Ti in range(NT):
                        i = q * NT + Ti
                        t.matmul(b.pbig1[:, (ob % 2) * CH:(ob % 2 + 1) * CH],
                                 wdb[:, Ti * P:(Ti + 1) * P], zfin[:, i * CH:(i + 1) * CH],
                                 start=(i == 0), stop=(i == NJ - 1))
                    if q == 3 and ob >= 1:
                        t.wait_ge(b.sACT, a0 + 2 * ob)
                        pob = ob - 1
                        t.matmul(b.pbig0[0:1, 0:CH], b.ginv_bf[:, 0:1], ybuf[:, pob * CH:(pob + 1) * CH],
                                 start=(pob == 0), stop=False)
                        t.matmul(b.pbig0[0:1, CH:2 * CH], b.ginv_bf[:, 0:1],
                                 sq[:, (pob % 2) * CH:(pob % 2 + 1) * CH],
                                 start=(pob == 0), stop=False)
                    t.drain().then_inc(b.sPE, 1)
                    b.cPE += 1
                t.wait_ge(b.sACT, a0 + 2 * NT)
                pob = NT - 1
                t.matmul(b.pbig0[0:1, 0:CH], b.ginv_bf[:, 0:1], ybuf[:, pob * CH:(pob + 1) * CH],
                         start=False, stop=True)
                t.matmul(b.pbig0[0:1, CH:2 * CH], b.ginv_bf[:, 0:1],
                         sq[:, (pob % 2) * CH:(pob % 2 + 1) * CH],
                         start=False, stop=True)

            @blk.scalar
            def _(a):
                for ob in range(NT):
                    a.wait_ge(b.sPE, p0 + 4 * (ob + 1))
                    a.activation(ybuf[:, ob * CH:(ob + 1) * CH], b.pbig1[:, (ob % 2) * CH:(ob % 2 + 1) * CH],
                                 ACTF.Identity, scale=gb(b, "wsd", ob))
                    a.activation(sq[:, (ob % 2) * CH:(ob % 2 + 1) * CH],
                                 ybuf[:, ob * CH:(ob + 1) * CH], ACTF.Square).then_inc(b.sACT, 2)
                    b.cACT += 2
        emit_stat_rows_and_rep(b)
        rc = ar(b, ZA, F32, NT * CH)
        t1 = ar(b, T1, F32, NT * CH)
        with nc.Block() as blk:
            @blk.gpsimd
            def _(g):
                wait_all(b, g)
                g.dma_start(out=_t3(rc), in_=b.resid[:, :, c0:c0 + CH].rearrange("T l t -> l T t")).then_inc(b.sD, 16)
                b.cD += 16
        with nc.Block() as blk:
            @blk.vector
            def _(v):
                v.wait_ge(b.sD, b.cD)
        emit_postln_resid(b, c, ybuf, rc, t1, b.resid)


def emit_l1_attn(b):
    nc = b.nc
    XLN = 67584
    for c in range(NCH):
        rc = ar(b, 0, F32, NT * CH)
        xln = ar(b, XLN, BF16, NT * CH)
        emit_preln(b, c, b.resid, affine_dst=xln)
        c0 = c * CH
        with nc.Block() as blk:
            @blk.vector
            def _(v):
                if c == 0:
                    for T in range(NT):
                        v.tensor_copy(b.gedge[:, T:T + 1], xln[:, T * CH:T * CH + 1])
                if c == NCH - 1:
                    for T in range(NT):
                        v.tensor_copy(b.gedge[:, NT + T:NT + T + 1], xln[:, T * CH + CH - 1:T * CH + CH])

            @blk.gpsimd
            def _(g):
                g.dma_start(out=b.xlndr[:, :, c0:c0 + CH].rearrange("T l t -> l T t"), in_=_t3(xln)).then_inc(b.sD, 16)
                b.cD += 16

    WG1B, WG2B, WTMP = 33280, 49664, 0
    XLNC, HT, GATE = 98816, 115200, 119296
    with nc.Block() as blk:
        @blk.sync
        def _(s):
            wait_all(b, s)
            s.dma_start(out=ar(b, WTMP, F32, NDH * NT * P).rearrange("l (r T c) -> l r (T c)", r=NDH, T=NT),
                        in_=b.wg1T.rearrange("r l T c -> l r (T c)")).then_inc(b.sW, 16)
            b.cW += 16
            s.wait_ge(b.sW, b.cW)

        @blk.vector
        def _(v):
            v.wait_ge(b.sW, b.cW)
            v.tensor_copy(ar(b, WG1B, BF16, NDH * NT * P), ar(b, WTMP, F32, NDH * NT * P))
    with nc.Block() as blk:
        @blk.sync
        def _(s):
            s.dma_start(out=ar(b, WTMP, F32, NT * NDH * P).rearrange("l (r T c) -> l r (T c)", r=NT, T=NDH),
                        in_=b.wg2T.rearrange("r l T c -> l r (T c)")).then_inc(b.sW, 16)
            b.cW += 16
            s.wait_ge(b.sW, b.cW)

        @blk.vector
        def _(v):
            v.wait_ge(b.sW, b.cW)
            v.tensor_copy(ar(b, WG2B, BF16, NT * NDH * P), ar(b, WTMP, F32, NT * NDH * P))
    XLNC2 = [XLNC, 135680]
    qb2 = [b.cQ[6], b.cQ[7]]
    with nc.Block() as blk:
        @blk.gpsimd
        def _(g):
            wait_all(b, g)
            g.dma_start(out=_t3(ar(b, XLNC2[0], BF16, NT * CH)),
                        in_=b.xlndr[:, :, 0:CH].rearrange("T l t -> l T t")).then_inc(b.sQ[6], 16)
            b.cQ[6] += 16
    for c in range(NCH):
        c0 = c * CH
        xlnc = ar(b, XLNC2[c % 2], BF16, NT * CH)
        ht = ar(b, HT, BF16, NDH * CH)
        gate = ar(b, GATE, BF16, NT * CH)
        wg1b = ar(b, WG1B, BF16, NDH * NT * P)
        wg2b = ar(b, WG2B, BF16, NT * NDH * P)
        with nc.Block() as blk:
            @blk.gpsimd
            def _(g):
                if c + 1 < NCH:
                    g.dma_start(out=_t3(ar(b, XLNC2[(c + 1) % 2], BF16, NT * CH)),
                                in_=b.xlndr[:, :, (c + 1) * CH:(c + 2) * CH].rearrange("T l t -> l T t")).then_inc(b.sQ[6 + (c + 1) % 2], 16)
                    b.cQ[6 + (c + 1) % 2] += 16

            @blk.tensor
            def _(t):
                t.wait_ge(b.sQ[6 + c % 2], qb2[c % 2] + 16 * (c // 2 + 1))
                for db in range(NDH):
                    for T in range(NT):
                        t.matmul(b.pbig1[:, db * CH:(db + 1) * CH],
                                 wg1b[:, (db * NT + T) * P:(db * NT + T + 1) * P],
                                 xlnc[:, T * CH:(T + 1) * CH], start=(T == 0), stop=(T == NT - 1))
        with nc.Block() as blk:
            @blk.scalar
            def _(a):
                for db in range(NDH):
                    a.activation(ht[:, db * CH:(db + 1) * CH], b.pbig1[:, db * CH:(db + 1) * CH],
                                 ACTF.Relu, bias=gb(b, "bg1", db))
        a0, p0 = b.cACT, b.cPE
        with nc.Block() as blk:
            @blk.tensor
            def _(t):
                for grpi in range(NT // 4):
                    if grpi > 0:
                        t.wait_ge(b.sACT, a0 + grpi)
                    for k in range(4):
                        rb = grpi * 4 + k
                        for db in range(NDH):
                            t.matmul(b.pbig0[:, k * CH:(k + 1) * CH],
                                     wg2b[:, (rb * NDH + db) * P:(rb * NDH + db + 1) * P],
                                     ht[:, db * CH:(db + 1) * CH], start=(db == 0), stop=(db == NDH - 1))
                    t.drain().then_inc(b.sPE, 1)
                    b.cPE += 1

            @blk.scalar
            def _(a):
                a.wait_ge(b.sD, b.cD)
                for grpi in range(NT // 4):
                    a.wait_ge(b.sPE, p0 + grpi + 1)
                    for k in range(4):
                        rb = grpi * 4 + k
                        a.activation(gate[:, rb * CH:(rb + 1) * CH], b.pbig0[:, k * CH:(k + 1) * CH],
                                     ACTF.Sigmoid, bias=gb(b, "bg2", rb))
                    a.drain().then_inc(b.sACT, 1)
                    b.cACT += 1
        with nc.Block() as blk:
            @blk.gpsimd
            def _(g):
                g.dma_start(out=b.gatesdr[:, :, c0:c0 + CH].rearrange("T l t -> l T t"), in_=_t3(gate)).then_inc(b.sD, 16)
                b.cD += 16

    WUBF, XLNC3, GATE3, UG, KC, WTMP3 = 0, 65536, 81920, 98304, 114688, 131072
    wubf = ar(b, WUBF, BF16, NT * NT * P)
    for rb in range(NT):
        with nc.Block() as blk:
            @blk.sync
            def _(s):
                if rb == 0:
                    wait_all(b, s)
                s.dma_start(out=ar(b, WTMP3, F32, NT * P),
                            in_=b.wuT[rb].rearrange("l T c -> l (T c)")).then_inc(b.sW, 16)
                b.cW += 16
                s.wait_ge(b.sW, b.cW)

            @blk.vector
            def _(v):
                v.wait_ge(b.sW, b.cW)
                v.tensor_copy(wubf[:, rb * NT * P:(rb + 1) * NT * P], ar(b, WTMP3, F32, NT * P))
    XLNC3P = [XLNC3, 139264]
    GATE3P = [GATE3, 155648]
    qb3 = [b.cQ[6], b.cQ[7]]
    with nc.Block() as blk:
        @blk.gpsimd
        def _(g):
            wait_all(b, g)
            g.dma_start(out=_t3(ar(b, XLNC3P[0], BF16, NT * CH)),
                        in_=b.xlndr[:, :, 0:CH].rearrange("T l t -> l T t")).then_inc(b.sQ[6], 16)
            g.dma_start(out=_t3(ar(b, GATE3P[0], BF16, NT * CH)),
                        in_=b.gatesdr[:, :, 0:CH].rearrange("T l t -> l T t")).then_inc(b.sQ[6], 16)
            b.cQ[6] += 32
    for c in range(NCH):
        c0 = c * CH
        xlnc = ar(b, XLNC3P[c % 2], BF16, NT * CH)
        gate = ar(b, GATE3P[c % 2], BF16, NT * CH)
        ug = ar(b, UG, BF16, NT * CH)
        kc = ar(b, KC, BF16, NT * CH)
        with nc.Block() as blk:
            @blk.gpsimd
            def _(g):
                if c + 1 < NCH:
                    g.dma_start(out=_t3(ar(b, XLNC3P[(c + 1) % 2], BF16, NT * CH)),
                                in_=b.xlndr[:, :, (c + 1) * CH:(c + 2) * CH].rearrange("T l t -> l T t")).then_inc(b.sQ[6 + (c + 1) % 2], 16)
                    g.dma_start(out=_t3(ar(b, GATE3P[(c + 1) % 2], BF16, NT * CH)),
                                in_=b.gatesdr[:, :, (c + 1) * CH:(c + 2) * CH].rearrange("T l t -> l T t")).then_inc(b.sQ[6 + (c + 1) % 2], 16)
                    b.cQ[6 + (c + 1) % 2] += 32
        a0, p0 = b.cACT, b.cPE
        with nc.Block() as blk:
            @blk.tensor
            def _(t):
                t.wait_ge(b.sQ[6 + c % 2], qb3[c % 2] + 32 * (c // 2 + 1))
                for grpi in range(NT // 4):
                    if grpi > 0:
                        t.wait_ge(b.sACT, a0 + grpi)
                    for k in range(4):
                        rb = grpi * 4 + k
                        for T in range(NT):
                            t.matmul(b.pbig0[:, k * CH:(k + 1) * CH],
                                     wubf[:, (rb * NT + T) * P:(rb * NT + T + 1) * P],
                                     xlnc[:, T * CH:(T + 1) * CH], start=(T == 0), stop=(T == NT - 1))
                    t.drain().then_inc(b.sPE, 1)
                    b.cPE += 1

            @blk.scalar
            def _(a):
                for grpi in range(NT // 4):
                    a.wait_ge(b.sPE, p0 + grpi + 1)
                    for k in range(4):
                        rb = grpi * 4 + k
                        a.activation(ug[:, rb * CH:(rb + 1) * CH], b.pbig0[:, k * CH:(k + 1) * CH],
                                     ACTF.Identity, bias=gb(b, "bu", rb))
                    a.drain().then_inc(b.sACT, 1)
                    b.cACT += 1
        with nc.Block() as blk:
            @blk.vector
            def _(v):
                for rb in range(NT):
                    sl = slice(rb * CH, (rb + 1) * CH)
                    v.tensor_tensor(ug[:, sl], ug[:, sl], gate[:, sl], ALU.mult)
        with nc.Block() as blk:
            @blk.vector
            def _(v):
                v.wait_ge(b.sD, b.cD)
                for rb in range(NT):
                    sl = slice(rb * CH, (rb + 1) * CH)
                    v.tensor_tensor_scan(kc[:, sl], ug[:, sl], ug[:, sl], b.gstkc[:, rb:rb + 1],
                                         ALU.add, ALU.bypass)
                for rb in range(NT):
                    v.tensor_copy(b.gstkc[:, rb:rb + 1], kc[:, rb * CH + CH - 1:rb * CH + CH])
        with nc.Block() as blk:
            @blk.gpsimd
            def _(g):
                g.dma_start(out=b.kcdr[:, :, c0:c0 + CH].rearrange("T l t -> l T t"), in_=_t3(kc)).then_inc(b.sD, 16)
                b.cD += 16

    CONTRIB = 131072
    contrib = ar(b, CONTRIB, F32, 48)
    with nc.Block() as blk:
        @blk.vector
        def _(v):
            v.tensor_scalar(contrib[:, 0:NT], b.gedge[:, NT:2 * NT], gb(b, "iseven"), None, ALU.mult)
            v.tensor_scalar(contrib[:, NT:2 * NT], b.gedge[:, 0:NT], gb(b, "isodd"), None, ALU.mult)
            v.tensor_scalar(contrib[:, 2 * NT:3 * NT], b.gstkc[:, :], gb(b, "iseven"), None, ALU.mult)
    with nc.Block() as blk:
        @blk.gpsimd
        def _(g):
            wait_all(b, g)
            g.dma_start(out=b.cc1i.ap(), in_=contrib).then_inc(b.sD, 16)
            b.cD += 16
            g.wait_ge(b.sD, b.cD)
            g.collective_compute("AllReduce", ALU.add, replica_groups=[[0, 1], [2, 3], [4, 5], [6, 7]],
                                 ins=[b.cc1i.ap()], outs=[b.cc1o.ap()]).then_inc(b.sCC, 1)
            b.cCC += 1
            g.wait_ge(b.sCC, b.cCC)
            g.dma_start(out=b.ghalo[:, :], in_=b.cc1o.ap()).then_inc(b.sD, 16)
            b.cD += 16
    with nc.Block() as blk:
        @blk.vector
        def _(v):
            v.wait_ge(b.sD, b.cD)
            v.tensor_scalar(b.gedge[:, 2 * NT:3 * NT], b.ghalo[:, 0:NT], gb(b, "isodd"), None, ALU.mult)
            v.tensor_scalar(b.gedge[:, 3 * NT:4 * NT], b.ghalo[:, NT:2 * NT], gb(b, "iseven"), None, ALU.mult)
            v.tensor_scalar(b.gstkc[:, :], b.ghalo[:, 2 * NT:3 * NT], gb(b, "isodd"), None, ALU.mult)

    WV1 = [65536, 73728]
    WV1B = [81920, 86016]
    XLNC4, GATE4, VG, KCC, GO = 90112, 106496, 122880, 139264, 155648
    XPAD, TC1, EDGL, EDGR, YB4, SQ4 = 172032, 175104, 177664, 178176, 106496, 65536
    RC4, T14 = 122880, 65536
    for c in range(NCH):
        c0 = c * CH
        xlnc = ar(b, XLNC4, BF16, NT * CH)
        gate = ar(b, GATE4, BF16, NT * CH)
        vg = ar(b, VG, BF16, NT * CH)
        kcc = ar(b, KCC, BF16, NT * CH)
        go = ar(b, GO, BF16, NT * CH)
        ybuf = ar(b, YB4, BF16, NT * CH)
        sq = ar(b, SQ4, BF16, 2 * CH)
        edgl = ar(b, EDGL, BF16, NT)
        edgr = ar(b, EDGR, BF16, NT)
        with nc.Block() as blk:
            @blk.gpsimd
            def _(g):
                wait_all(b, g)
                g.dma_start(out=_t3(xlnc), in_=b.xlndr[:, :, c0:c0 + CH].rearrange("T l t -> l T t")).then_inc(b.sD, 16)
                g.dma_start(out=_t3(gate), in_=b.gatesdr[:, :, c0:c0 + CH].rearrange("T l t -> l T t")).then_inc(b.sD, 16)
                g.dma_start(out=_t3(kcc), in_=b.kcdr[:, :, c0:c0 + CH].rearrange("T l t -> l T t")).then_inc(b.sD, 16)
                b.cD += 48
                if c > 0:
                    g.dma_start(out=_t3(edgl), in_=b.xlndr[:, :, c0 - 1:c0].rearrange("T l t -> l T t"), allow_slow_non_contiguous=True).then_inc(b.sD, 16)
                    b.cD += 16
                if c < NCH - 1:
                    g.dma_start(out=_t3(edgr), in_=b.xlndr[:, :, c0 + CH:c0 + CH + 1].rearrange("T l t -> l T t"), allow_slow_non_contiguous=True).then_inc(b.sD, 16)
                    b.cD += 16
        w0, d0, a0, p0 = b.cW, b.cDVE, b.cACT, b.cPE
        q0 = list(b.cQ)
        with nc.Block() as blk:
            @blk.sync
            def _(s):
                for rb in (0, 1):
                    s.dma_start(out=ar(b, WV1[rb % 2], F32, NT * P),
                                in_=b.wvT1[rb].rearrange("l T c -> l (T c)")).then_inc(b.sQ[rb % 2], 16)
                    b.cQ[rb % 2] += 16
                for rb in range(NT - 2):
                    s.wait_ge(b.sDVE, d0 + rb + 1)
                    s.dma_start(out=ar(b, WV1[rb % 2], F32, NT * P),
                                in_=b.wvT1[rb + 2].rearrange("l T c -> l (T c)")).then_inc(b.sQ[rb % 2], 16)
                    b.cQ[rb % 2] += 16

            @blk.vector
            def _(v):
                for rb in range(NT):
                    v.wait_ge(b.sQ[rb % 2], q0[rb % 2] + 16 * (rb // 2 + 1))
                    if rb >= 2:
                        v.wait_ge(b.sPE, p0 + rb - 1)
                    v.tensor_copy(ar(b, WV1B[rb % 2], BF16, NT * P), ar(b, WV1[rb % 2], F32, NT * P))
                    v.drain().then_inc(b.sDVE, 1)
                    b.cDVE += 1

            @blk.tensor
            def _(t):
                t.wait_ge(b.sD, b.cD)
                for rb in range(NT):
                    t.wait_ge(b.sDVE, d0 + rb + 1)
                    if rb >= 4:
                        t.wait_ge(b.sACT, a0 + rb - 3)
                    wv1b = ar(b, WV1B[rb % 2], BF16, NT * P)
                    for T in range(NT):
                        t.matmul(b.pbig1[:, (rb % 4) * CH:(rb % 4 + 1) * CH],
                                 wv1b[:, T * P:(T + 1) * P], xlnc[:, T * CH:(T + 1) * CH],
                                 start=(T == 0), stop=(T == NT - 1))
                    t.drain().then_inc(b.sPE, 1)
                    b.cPE += 1

            @blk.scalar
            def _(a):
                for rb in range(NT):
                    a.wait_ge(b.sPE, p0 + rb + 1)
                    a.activation(vg[:, rb * CH:(rb + 1) * CH], b.pbig1[:, (rb % 4) * CH:(rb % 4 + 1) * CH],
                                 ACTF.Identity, bias=gb(b, "bv1", rb)).then_inc(b.sACT, 1)
                    b.cACT += 1
        with nc.Block() as blk:
            @blk.vector
            def _(v):
                for rb in range(NT):
                    sl = slice(rb * CH, (rb + 1) * CH)
                    v.tensor_tensor(vg[:, sl], vg[:, sl], gate[:, sl], ALU.mult)
                for rb in range(NT):
                    sl = slice(rb * CH, (rb + 1) * CH)
                    v.scalar_tensor_tensor(go[:, sl], kcc[:, sl], b.gstkc[:, rb:rb + 1], vg[:, sl],
                                           ALU.add, ALU.mult)
        xpad = ar(b, XPAD, F32, CH + 2)
        tc1 = ar(b, TC1, F32, CH)
        w0, d0, a0, p0 = b.cW, b.cDVE, b.cACT, b.cPE
        with nc.Block() as blk:
            @blk.tensor
            def _(t):
                for ob in range(NT):
                    if ob >= 4:
                        t.wait_ge(b.sDVE, d0 + ob - 3)
                    for rT in range(NT):
                        t.matmul(b.pbig1[:, (ob % 4) * CH:(ob % 4 + 1) * CH],
                                 wubf[:, (ob * NT + rT) * P:(ob * NT + rT + 1) * P],
                                 go[:, rT * CH:(rT + 1) * CH], start=(rT == 0), stop=(rT == NT - 1))
                    if ob >= 1:
                        t.wait_ge(b.sACT, a0 + ob)
                        pob = ob - 1
                        t.matmul(b.pbig0[0:1, 0:CH], b.ginv_bf[:, 0:1], ybuf[:, pob * CH:(pob + 1) * CH],
                                 start=(pob == 0), stop=False)
                        t.matmul(b.pbig0[0:1, CH:2 * CH], b.ginv_bf[:, 0:1],
                                 sq[:, (pob % 2) * CH:(pob % 2 + 1) * CH],
                                 start=(pob == 0), stop=False)
                    t.drain().then_inc(b.sPE, 1)
                    b.cPE += 1
                t.wait_ge(b.sACT, a0 + NT)
                pob = NT - 1
                t.matmul(b.pbig0[0:1, 0:CH], b.ginv_bf[:, 0:1], ybuf[:, pob * CH:(pob + 1) * CH],
                         start=False, stop=True)
                t.matmul(b.pbig0[0:1, CH:2 * CH], b.ginv_bf[:, 0:1],
                         sq[:, (pob % 2) * CH:(pob % 2 + 1) * CH],
                         start=False, stop=True)

            @blk.vector
            def _(v):
                for ob in range(NT):
                    v.wait_ge(b.sPE, p0 + ob + 1)
                    v.tensor_copy(xpad[:, 1:CH + 1], xlnc[:, ob * CH:(ob + 1) * CH])
                    if c == 0:
                        v.tensor_copy(xpad[:, 0:1], b.gedge[:, 2 * NT + ob:2 * NT + ob + 1])
                    else:
                        v.tensor_copy(xpad[:, 0:1], edgl[:, ob:ob + 1])
                    if c == NCH - 1:
                        v.tensor_copy(xpad[:, CH + 1:CH + 2], b.gedge[:, 3 * NT + ob:3 * NT + ob + 1])
                    else:
                        v.tensor_copy(xpad[:, CH + 1:CH + 2], edgr[:, ob:ob + 1])
                    v.tensor_scalar(tc1, xpad[:, 1:CH + 1], gb(b, "cw", NT + ob), gb(b, "convb", ob),
                                    ALU.mult, ALU.add)
                    v.scalar_tensor_tensor(tc1, xpad[:, 0:CH], gb(b, "cw", ob), tc1, ALU.mult, ALU.add)
                    v.scalar_tensor_tensor(tc1, xpad[:, 2:CH + 2], gb(b, "cw", 2 * NT + ob), tc1, ALU.mult, ALU.add)
                    v.tensor_tensor(ybuf[:, ob * CH:(ob + 1) * CH], tc1,
                                    b.pbig1[:, (ob % 4) * CH:(ob % 4 + 1) * CH], ALU.add)
                    v.drain().then_inc(b.sDVE, 1)
                    b.cDVE += 1

            @blk.scalar
            def _(a):
                for ob in range(NT):
                    a.wait_ge(b.sDVE, d0 + ob + 1)
                    a.activation(sq[:, (ob % 2) * CH:(ob % 2 + 1) * CH],
                                 ybuf[:, ob * CH:(ob + 1) * CH], ACTF.Square).then_inc(b.sACT, 1)
                    b.cACT += 1
        emit_stat_rows_and_rep(b)
        rc = ar(b, RC4, F32, NT * CH)
        t1 = ar(b, T14, F32, NT * CH)
        with nc.Block() as blk:
            @blk.gpsimd
            def _(g):
                wait_all(b, g)
                g.dma_start(out=_t3(rc), in_=b.resid[:, :, c0:c0 + CH].rearrange("T l t -> l T t")).then_inc(b.sD, 16)
                b.cD += 16
        with nc.Block() as blk:
            @blk.vector
            def _(v):
                v.wait_ge(b.sD, b.cD)
        emit_postln_resid(b, c, ybuf, rc, t1, b.resid)


def emit_l1_ffn(b, final_out):
    nc = b.nc
    XLN4 = 67584
    for c in range(NCH):
        rc = ar(b, 0, F32, NT * CH)
        xln4 = ar(b, XLN4, F32, NT * CH)
        emit_preln(b, c, b.resid, affine_dst=xln4)
        c0 = c * CH
        with nc.Block() as blk:
            @blk.vector
            def _(v):
                v.tensor_reduce(b.gmax[:, 2:3], xln4, AX.X, ALU.max, apply_absolute_value=True)
                vpad(b, v)
                v.tensor_tensor(b.gmax[:, 0:1], b.gmax[:, 0:1], b.gmax[:, 2:3], ALU.max)

            @blk.gpsimd
            def _(g):
                g.dma_start(out=b.xln4dr[:, :, c0:c0 + CH].rearrange("T l t -> l T t"), in_=_t3(xln4)).then_inc(b.sD, 16)
                b.cD += 16
    emit_scalar_allreduce(b, [(0, 0), (3, 1), (4, 2)], b.cc2i, b.cc2o)
    with nc.Block() as blk:
        @blk.gpsimd
        def _(g):
            g.dma_start(out=b.dbg[:, 0:16], in_=b.grow1[:, 0:16]).then_inc(b.sD, 16)
            b.cD += 16
    with nc.Block() as blk:
        @blk.vector
        def _(v):
            v.tensor_single_scalar(b.grow1[:, 3:6], b.grow1[:, 0:3], 1.0 / 448.0, ALU.mult)
            vpad(b, v)
            v.tensor_single_scalar(b.grow1[:, 3:6], b.grow1[:, 3:6], 1e-6, ALU.max)
            vpad(b, v)
            v.reciprocal(b.grow1[:, 6:9], b.grow1[:, 3:6])
            vpad(b, v)
            v.tensor_single_scalar(b.grow1[:, 6:9], b.grow1[:, 6:9], 0.5, ALU.mult)
            vpad(b, v)
    with nc.Block() as blk:
        @blk.gpsimd
        def _(g):
            g.dma_start(out=b.dbg[:, 16:32], in_=b.grow1[:, 0:16]).then_inc(b.sD, 16)
            b.cD += 16
    with nc.Block() as blk:
        @blk.tensor
        def _(t):
            t.matmul(b.pbig0[:, 0:3], b.gones_row[:, :], b.grow1[:, 6:9], start=True, stop=True)
    with nc.Block() as blk:
        @blk.vector
        def _(v):
            v.tensor_copy(b.gscal[:, 0:3], b.pbig0[:, 0:3])

    X8, XL4C, QT = 0, 33280, 66560
    x8 = ar(b, X8, FP8, NT * TOK)
    for c in range(NCH):
        c0 = c * CH
        xl4c = ar(b, XL4C, F32, NT * CH)
        qt = ar(b, QT, F32, NT * CH)
        with nc.Block() as blk:
            @blk.gpsimd
            def _(g):
                wait_all(b, g)
                g.dma_start(out=_t3(xl4c), in_=b.xln4dr[:, :, c0:c0 + CH].rearrange("T l t -> l T t")).then_inc(b.sD, 16)
                b.cD += 16
        with nc.Block() as blk:
            @blk.vector
            def _(v):
                v.wait_ge(b.sD, b.cD)
                v.tensor_scalar(qt, xl4c, b.gscal[:, 0:1], None, ALU.mult)
                v.tensor_copy(x8.rearrange("l (T t) -> l T t", T=NT)[:, :, c0:c0 + CH],
                              qt.rearrange("l (T t) -> l T t", T=NT))
    emit_c2_fc(b, x8)
    emit_scalar_allreduce(b, [(1, 0)], b.cc3i, b.cc3o)
    with nc.Block() as blk:
        @blk.vector
        def _(v):
            v.tensor_tensor(b.grow1[:, 9:10], b.grow1[:, 3:4], b.grow1[:, 4:5], ALU.mult)
            vpad(b, v)
            v.tensor_single_scalar(b.grow1[:, 9:10], b.grow1[:, 9:10], 4.0, ALU.mult)
            vpad(b, v)
            v.tensor_tensor(b.grow1[:, 10:11], b.grow1[:, 9:10], b.grow1[:, 9:10], ALU.mult)
            vpad(b, v)
            v.tensor_tensor(b.grow1[:, 11:12], b.grow1[:, 0:1], b.grow1[:, 10:11], ALU.mult)
            vpad(b, v)
            v.tensor_single_scalar(b.grow1[:, 11:12], b.grow1[:, 11:12], 1.0 / 448.0, ALU.mult)
            vpad(b, v)
            v.tensor_single_scalar(b.grow1[:, 11:12], b.grow1[:, 11:12], 1e-6, ALU.max)
            vpad(b, v)
            v.tensor_single_scalar(b.grow1[:, 12:13], b.grow1[:, 11:12], 2.0, ALU.mult)
            vpad(b, v)
            v.reciprocal(b.grow1[:, 13:14], b.grow1[:, 12:13])
            vpad(b, v)
            v.tensor_tensor(b.grow1[:, 13:14], b.grow1[:, 10:11], b.grow1[:, 13:14], ALU.mult)
            vpad(b, v)
            v.tensor_tensor(b.grow1[:, 14:15], b.grow1[:, 11:12], b.grow1[:, 5:6], ALU.mult)
            vpad(b, v)
            v.tensor_single_scalar(b.grow1[:, 14:15], b.grow1[:, 14:15], 4.0, ALU.mult)
            vpad(b, v)
    with nc.Block() as blk:
        @blk.tensor
        def _(t):
            t.matmul(b.pbig0[:, 0:2], b.gones_row[:, :], b.grow1[:, 13:15], start=True, stop=True)
    with nc.Block() as blk:
        @blk.vector
        def _(v):
            v.tensor_copy(b.gscal[:, 3:5], b.pbig0[:, 0:2])
    with nc.Block() as blk:
        @blk.vector
        def _(v):
            v.tensor_copy(b.gcc2[:, 0:16], b.grow1[:, 0:16])

        @blk.gpsimd
        def _(g):
            g.dma_start(out=b.dbg[:, 100:116], in_=b.grow1[:, 0:16]).then_inc(b.sD, 16)
            g.dma_start(out=b.dbg[:, 116:124], in_=b.gscal[0:1, 0:8]).then_inc(b.sD, 16)
            g.dma_start(out=b.dbg[:, 48:56], in_=b.gjunk[:, 0:8]).then_inc(b.sD, 16)
            b.cD += 48
    emit_c4_proj(b)
    emit_c5_final(b, final_out)


def emit_scalar_allreduce(b, colmap, cci, cco):
    nc = b.nc
    with nc.Block() as blk:
        @blk.tensor
        def _(t):
            for i, (src, _) in enumerate(colmap):
                t.matmul(b.pbig0[0:1, i * P:(i + 1) * P], b.gmax[:, src:src + 1], b.geye[:, :],
                         start=True, stop=True)
    with nc.Block() as blk:
        @blk.vector
        def _(v):
            for i, (_, dst) in enumerate(colmap):
                v.reduce_max(b.grows2[:, i:i + 1], b.pbig0[0:1, i * P:(i + 1) * P], AX.X)
            vpad(b, v)
            v.memset(b.gcc2[:, :], 0.0)
            for i in range(len(colmap)):
                for s in range(8):
                    v.tensor_copy(b.gcc2[:, 4 * s + i:4 * s + i + 1], b.grows2[:, i:i + 1])
            vpad(b, v)
            v.tensor_tensor(b.gcc2[:, :], b.gcc2[:, :], b.gflags[:, 32:64], ALU.mult)
    with nc.Block() as blk:
        @blk.gpsimd
        def _(g):
            wait_all(b, g)
            if cci is b.cc2i:
                g.dma_start(out=b.dbg[:, 64:96], in_=b.gcc2[:, :]).then_inc(b.sD, 16)
                g.dma_start(out=b.dbg[:, 96:100], in_=b.grows2[:, 0:4]).then_inc(b.sD, 16)
                b.cD += 32
            g.dma_start(out=cci.ap(), in_=b.gcc2[:, :]).then_inc(b.sD, 16)
            b.cD += 16
            g.wait_ge(b.sD, b.cD)
            rounds = [
                ([[0, 1], [2, 3], [4, 5], [6, 7]], cci.ap(), cco.ap()),
                ([[0, 2], [1, 3], [4, 6], [5, 7]], cco.ap(), b.ccmb.ap()),
                ([[0, 4], [1, 5], [2, 6], [3, 7]], b.ccmb.ap(), b.ccmc.ap()),
            ]
            for rg, rin, rout in rounds:
                g.collective_compute("AllReduce", ALU.add, replica_groups=rg,
                                     ins=[rin], outs=[rout]).then_inc(b.sCC, 1)
                b.cCC += 1
                g.wait_ge(b.sCC, b.cCC)
            g.dma_start(out=b.gcc2[:, :], in_=b.ccmc.ap()).then_inc(b.sD, 16)
            b.cD += 16
    with nc.Block() as blk:
        @blk.vector
        def _(v):
            v.wait_ge(b.sD, b.cD)
            v.reduce_max(b.grows2[:, 0:4], b.gcc2.rearrange("o (s k) -> o k s", s=8), AX.X)
            vpad(b, v)
            for i, (_, dst) in enumerate(colmap):
                v.tensor_copy(b.grow1[:, dst:dst + 1], b.grows2[:, i:i + 1])
            if cci is b.cc2i:
                v.tensor_copy(b.grows[:, 900:908], b.gcc2[:, 0:8])
                v.tensor_copy(b.grows[:, 908:912], b.grows2[:, 0:4])
                v.tensor_copy(b.grows[:, 912:916], b.grow1[:, 0:4])
    if cci is b.cc2i:
        with nc.Block() as blk:
            @blk.gpsimd
            def _(g):
                g.dma_start(out=b.dbg[:, 32:48], in_=b.grows[:, 900:916]).then_inc(b.sD, 16)
                b.cD += 16


def emit_c2_fc(b, x8):
    nc = b.nc
    WFC = [99840, 108032]
    W8FC = [116224, 118272]
    QW = 120320
    RELU = 129024
    HS2 = [137728, 141824]
    w0, d0, a0, p0, dd0 = b.cW, b.cDVE, b.cACT, b.cPE, b.cD
    q0 = list(b.cQ)
    with nc.Block() as blk:
        @blk.sync
        def _(s):
            wait_all(b, s)
            for j in (0, 1):
                s.dma_start(out=ar(b, WFC[j % 2], F32, NT * P),
                            in_=b.wfcT[j].rearrange("l T c -> l (T c)")).then_inc(b.sQ[j % 2], 16)
                b.cQ[j % 2] += 16
            for j in range(NJ - 2):
                s.wait_ge(b.sDVE, d0 + 2 * j + 1)
                s.dma_start(out=ar(b, WFC[j % 2], F32, NT * P),
                            in_=b.wfcT[j + 2].rearrange("l T c -> l (T c)")).then_inc(b.sQ[j % 2], 16)
                b.cQ[j % 2] += 16

        @blk.vector
        def _(v):
            for j in range(NJ):
                v.wait_ge(b.sQ[j % 2], q0[j % 2] + 16 * (j // 2 + 1))
                if j >= 2:
                    v.wait_ge(b.sPE, p0 + j - 1)
                v.tensor_scalar(ar(b, QW, F32, NT * P), ar(b, WFC[j % 2], F32, NT * P),
                                b.gscal[:, 1:2], None, ALU.mult)
                v.tensor_copy(ar(b, W8FC[j % 2], FP8, NT * P), ar(b, QW, F32, NT * P))
                v.drain().then_inc(b.sDVE, 1)
                b.cDVE += 1
                v.wait_ge(b.sACT, a0 + 2 * j + 2)
                v.tensor_reduce(b.gmax[:, 2:3], ar(b, HS2[j % 2], BF16, TOK), AX.X, ALU.max)
                vpad(b, v)
                v.tensor_tensor(b.gmax[:, 1:2], b.gmax[:, 1:2], b.gmax[:, 2:3], ALU.max)
                v.drain().then_inc(b.sDVE, 1)
                b.cDVE += 1

        @blk.tensor
        def _(t):
            for j in range(NJ):
                t.wait_ge(b.sDVE, d0 + 2 * j + 1)
                if j >= 2:
                    t.wait_ge(b.sACT, a0 + 2 * (j - 1))
                w8 = ar(b, W8FC[j % 2], FP8, NT * P)
                pb = b.pbig0 if j % 2 == 0 else b.pbig1
                for T in range(NT):
                    for g in range(4):
                        t.matmul(pb[:, g * CH:(g + 1) * CH], w8[:, T * P:(T + 1) * P],
                                 x8.rearrange("l (T t) -> l T t", T=NT)[:, T, g * CH:(g + 1) * CH],
                                 start=(T == 0), stop=(T == NT - 1))
                t.drain().then_inc(b.sPE, 1)
                b.cPE += 1

        @blk.scalar
        def _(a):
            for j in range(NJ):
                a.wait_ge(b.sPE, p0 + j + 1)
                if j >= 2:
                    a.wait_ge(b.sQ[4 + j % 2], q0[4 + j % 2] + 16 * ((j - 2) // 2 + 1))
                    a.wait_ge(b.sDVE, d0 + 2 * (j - 2) + 2)
                pb = b.pbig0 if j % 2 == 0 else b.pbig1
                a.activation(ar(b, RELU, F32, TOK), pb[:, :], ACTF.Relu)
                a.drain().then_inc(b.sACT, 1)
                b.cACT += 1
                a.activation(ar(b, HS2[j % 2], BF16, TOK), ar(b, RELU, F32, TOK), ACTF.Square)
                a.drain().then_inc(b.sACT, 1)
                b.cACT += 1

        @blk.gpsimd
        def _(g):
            for j in range(NJ):
                g.wait_ge(b.sACT, a0 + 2 * j + 2)
                g.dma_start(out=b.hdr[j], in_=ar(b, HS2[j % 2], BF16, TOK)).then_inc(b.sQ[4 + j % 2], 16)
                b.cQ[4 + j % 2] += 16


def emit_c4_proj(b):
    nc = b.nc
    H8 = 0
    WPJ = [131072, 139264]
    W8PJ = [147456, 155648]
    QW4 = 164352
    HS2L = [173056, 177152]
    YT = [181760, 185856]
    h8 = ar(b, H8, FP8, NJ * TOK)
    d0, dd0 = b.cDVE, b.cD
    q0 = list(b.cQ)
    with nc.Block() as blk:
        @blk.gpsimd
        def _(g):
            wait_all(b, g)
            for j in range(NJ):
                if j >= 2:
                    g.wait_ge(b.sDVE, d0 + j - 1)
                g.dma_start(out=ar(b, HS2L[j % 2], BF16, TOK), in_=b.hdr[j]).then_inc(b.sQ[6 + j % 2], 16)
                b.cQ[6 + j % 2] += 16

        @blk.vector
        def _(v):
            for j in range(NJ):
                v.wait_ge(b.sQ[6 + j % 2], q0[6 + j % 2] + 16 * (j // 2 + 1))
                v.tensor_scalar(ar(b, QW4, F32, TOK), ar(b, HS2L[j % 2], BF16, TOK),
                                b.gscal[:, 3:4], None, ALU.mult)
                v.tensor_copy(h8[:, j * TOK:(j + 1) * TOK], ar(b, QW4, F32, TOK))
                v.drain().then_inc(b.sDVE, 1)
                b.cDVE += 1
    w0, d0, a0, p0, dd4 = b.cW, b.cDVE, b.cACT, b.cPE, b.cD
    q0 = list(b.cQ)
    with nc.Block() as blk:
        @blk.sync
        def _(s):
            wait_all(b, s)
            for k in range(2):
                ob, q = divmod(k, 4)
                s.dma_start(out=ar(b, WPJ[k % 2], F32, NT * P),
                            in_=b.wpjT[ob, :, q * NT:(q + 1) * NT, :].rearrange("l T c -> l (T c)")).then_inc(b.sQ[k % 2], 16)
                b.cQ[k % 2] += 16
            for k in range(4 * NT - 2):
                ob, q = divmod(k + 2, 4)
                s.wait_ge(b.sDVE, d0 + k + 1)
                s.dma_start(out=ar(b, WPJ[k % 2], F32, NT * P),
                            in_=b.wpjT[ob, :, q * NT:(q + 1) * NT, :].rearrange("l T c -> l (T c)")).then_inc(b.sQ[k % 2], 16)
                b.cQ[k % 2] += 16

        @blk.vector
        def _(v):
            for k in range(4 * NT):
                ob, q = divmod(k, 4)
                v.wait_ge(b.sQ[k % 2], q0[k % 2] + 16 * (k // 2 + 1))
                if q == 0 and ob >= 2:
                    v.wait_ge(b.sPE, p0 + ob - 1)
                v.tensor_scalar(ar(b, QW4, F32, NT * P), ar(b, WPJ[k % 2], F32, NT * P),
                                b.gscal[:, 2:3], None, ALU.mult)
                v.tensor_copy(ar(b, W8PJ[ob % 2] + q * 2048, FP8, NT * P), ar(b, QW4, F32, NT * P))
                v.drain().then_inc(b.sDVE, 1)
                b.cDVE += 1

        @blk.tensor
        def _(t):
            for ob in range(NT):
                t.wait_ge(b.sDVE, d0 + 4 * (ob + 1))
                if ob >= 2:
                    t.wait_ge(b.sACT, a0 + ob - 1)
                pb = b.pbig0 if ob % 2 == 0 else b.pbig1
                for q in range(4):
                    w8 = ar(b, W8PJ[ob % 2] + q * 2048, FP8, NT * P)
                    for Ti in range(NT):
                        i = q * NT + Ti
                        for g in range(4):
                            t.matmul(pb[:, g * CH:(g + 1) * CH], w8[:, Ti * P:(Ti + 1) * P],
                                     h8[:, i * TOK + g * CH: i * TOK + (g + 1) * CH],
                                     start=(i == 0), stop=(i == NJ - 1))
                t.drain().then_inc(b.sPE, 1)
                b.cPE += 1

        @blk.scalar
        def _(a):
            for ob in range(NT):
                a.wait_ge(b.sPE, p0 + ob + 1)
                if ob >= 2:
                    a.wait_ge(b.sQ[2 + ob % 2], q0[2 + ob % 2] + 16 * ((ob - 2) // 2 + 1))
                pb = b.pbig0 if ob % 2 == 0 else b.pbig1
                a.activation(ar(b, YT[ob % 2], BF16, TOK), pb[:, :], ACTF.Identity,
                             scale=b.gscal[:, 4:5])
                a.drain().then_inc(b.sACT, 1)
                b.cACT += 1

        @blk.gpsimd
        def _(g):
            for ob in range(NT):
                g.wait_ge(b.sACT, a0 + ob + 1)
                g.dma_start(out=b.ydr[ob], in_=ar(b, YT[ob % 2], BF16, TOK)).then_inc(b.sQ[2 + ob % 2], 16)
                b.cQ[2 + ob % 2] += 16


def emit_c5_final(b, final_out):
    nc = b.nc
    YC, Y2C, RC, T1 = 0, 16384, 33280, 66560
    for c in range(NCH):
        c0 = c * CH
        yc = ar(b, YC, BF16, NT * CH)
        y2c = ar(b, Y2C, BF16, NT * CH)
        rc = ar(b, RC, F32, NT * CH)
        t1 = ar(b, T1, F32, NT * CH)
        with nc.Block() as blk:
            @blk.gpsimd
            def _(g):
                wait_all(b, g)
                g.dma_start(out=_t3(yc), in_=b.ydr[:, :, c0:c0 + CH].rearrange("T l t -> l T t")).then_inc(b.sD, 16)
                g.dma_start(out=_t3(rc), in_=b.resid[:, :, c0:c0 + CH].rearrange("T l t -> l T t")).then_inc(b.sD, 16)
                b.cD += 32
        with nc.Block() as blk:
            @blk.scalar
            def _(a):
                a.wait_ge(b.sD, b.cD)
                a.activation(y2c, yc, ACTF.Square)
        emit_stats_from(b, yc, y2c)
        emit_postln_resid(b, c, yc, rc, t1, final_out)


def emit_out_debug(b):
    nc = b.nc
    for c in range(NCH):
        c0 = c * CH
        rc = ar(b, 0, F32, NT * CH)
        with nc.Block() as blk:
            @blk.gpsimd
            def _(g):
                wait_all(b, g)
                g.dma_start(out=_t3(rc), in_=b.resid[:, :, c0:c0 + CH].rearrange("T l t -> l T t")).then_inc(b.sD, 16)
                b.cD += 16
                g.wait_ge(b.sD, b.cD)
                g.dma_start(out=b.outT[:, :, c0:c0 + CH].rearrange("T l t -> l T t"), in_=_t3(rc)).then_inc(b.sD, 16)
                b.cD += 16


def emit_final_wait(b):
    nc = b.nc
    with nc.Block() as blk:
        @blk.gpsimd
        def _(g):
            g.wait_ge(b.sD, b.cD)
            if b.cW:
                g.wait_ge(b.sW, b.cW)


def build(upto=4):
    b = _mk()
    b.sub_l0ffn = {21: 1, 22: 2}.get(upto, 3)
    if upto in (21, 22):
        upto = 2
    emit_setup(b)
    emit_l0_attn(b)
    if upto >= 2:
        emit_l0_ffn(b)
    if upto >= 3:
        emit_l1_attn(b)
    if upto >= 4:
        emit_l1_ffn(b, b.outT)
    else:
        emit_out_debug(b)
    emit_final_wait(b)
    b.stack.close()
    return b.nc


def _lhsT_blocks(WT):
    K, M = WT.shape
    return np.ascontiguousarray(
        WT.reshape(K // 128, 128, M // 128, 128).transpose(2, 1, 0, 3))


_NC_CACHE = {}


def _get_nc(upto=4):
    if upto not in _NC_CACHE:
        _NC_CACHE[upto] = build(upto)
    return _NC_CACHE[upto]


def make_inmaps(inputs):
    f32 = lambda a: np.ascontiguousarray(np.asarray(a), dtype=np.float32)
    x = f32(inputs["L0_attn_x"])
    shared = {
        "wvT": _lhsT_blocks(f32(inputs["L0_attn_Wv"]).T),
        "woT": _lhsT_blocks(f32(inputs["L0_attn_Wo"]).T),
        "bv0": f32(inputs["L0_attn_bv"]).reshape(1, D),
        "bo0": f32(inputs["L0_attn_bo"]).reshape(1, D),
        "wupT": _lhsT_blocks(f32(inputs["L0_ffn_W_up"]).T),
        "wgT": _lhsT_blocks(f32(inputs["L0_ffn_W_gate"]).T),
        "wdT": _lhsT_blocks(f32(inputs["L0_ffn_W_down"]).T),
        "su_ug": f32(inputs["L0_ffn_SU_ug"]).reshape(1, D),
        "su_dp": f32(inputs["L0_ffn_SU_dp"]).reshape(1, INTER),
        "ws_ug": f32(inputs["L0_ffn_Wscale_ug"]).reshape(1, 2 * INTER),
        "ws_dp": f32(inputs["L0_ffn_Wscale_dp"]).reshape(1, D),
        "wuT": _lhsT_blocks(f32(inputs["L1_attn_Wu"]).T),
        "wvT1": _lhsT_blocks(f32(inputs["L1_attn_Wv"]).T),
        "wg1T": _lhsT_blocks(f32(inputs["L1_attn_Wg1"]).T),
        "wg2T": _lhsT_blocks(f32(inputs["L1_attn_Wg2"]).T),
        "bu1": f32(inputs["L1_attn_bu"]).reshape(1, D),
        "bv1": f32(inputs["L1_attn_bv"]).reshape(1, D),
        "bg1": f32(inputs["L1_attn_bg1"]).reshape(1, DH),
        "bg2": f32(inputs["L1_attn_bg2"]).reshape(1, D),
        "convw": np.ascontiguousarray(f32(inputs["L1_attn_conv_w"])[:, 0, :].T).reshape(1, 3 * D),
        "convb": f32(inputs["L1_attn_conv_b"]).reshape(1, D),
        "wfcT": _lhsT_blocks(f32(inputs["L1_ffn_w_fc"]).T),
        "wpjT": _lhsT_blocks(f32(inputs["L1_ffn_w_proj"]).T),
    }
    wfc = f32(inputs["L1_ffn_w_fc"])
    wpj = f32(inputs["L1_ffn_w_proj"])
    in_maps = []
    for core in range(8):
        bidx, half = core // 2, core % 2
        sl = slice(2048 * half, 2048 * half + 2048)
        xTc = np.ascontiguousarray(x[bidx, sl, :].T).reshape(NT, P, TOK)
        fl = np.zeros((1, 64), dtype=np.float32)
        fl[0, 0] = float(half)
        fl[0, 1] = 1.0 - float(half)
        for k in range(8):
            for j in range(4):
                fl[0, 32 + 4 * k + j] = 1.0 if k == core else 0.0
        wf = wfc.reshape(8, 1024, 2048)[core].reshape(16, 128, 1024)
        wp = wpj.reshape(8, 256, 8192)[core].reshape(16, 128, 1024)
        wslc = np.ascontiguousarray(np.concatenate([wf, wp], axis=0))
        m = dict(shared)
        m["xT"] = xTc
        m["flags"] = fl
        m["wsl"] = wslc
        in_maps.append(m)
    return in_maps


_LAST_RES = None


def kernel(**inputs):
    global _LAST_RES
    upto = int(inputs.pop("_upto", 4))
    in_maps = make_inmaps(inputs)
    nc = _get_nc(upto)
    res = run_bass_kernel_spmd(nc, in_maps, core_ids=list(range(8)))
    _LAST_RES = res
    out = np.empty((4, 4096, D), dtype=np.float32)
    for core in range(8):
        bidx, half = core // 2, core % 2
        o = np.asarray(res.results[core]["outT"]).reshape(D, TOK)
        out[bidx, 2048 * half:2048 * half + 2048, :] = o.T
    return out


# version 06
# speedup vs baseline: 1.2474x; 1.0666x over previous
import numpy as np
import ml_dtypes
from contextlib import ExitStack

import concourse.bass as bass
import concourse.mybir as mybir
from concourse.bass_utils import run_bass_kernel_spmd

F32 = mybir.dt.float32
BF16 = mybir.dt.bfloat16
FP8 = mybir.dt.float8e4
U8 = mybir.dt.uint8
ALU = mybir.AluOpType
ACTF = mybir.ActivationFunctionType
AX = mybir.AxisListType

P = 128
D = 2048
NT = 16
TOK = 2048
CH = 512
NCH = TOK // CH
INTER = 8192
NJ = 64
DH = 512
NDH = 4
K1 = 1.0 / (np.sqrt(2048.0) * 64.0)
K2 = 1.0 / (np.sqrt(8192.0) * 64.0)
EPS = 1e-5
ARENA = 193536


def _butterfly_levels(n):
    out = []
    h = 1
    while h < n:
        pairs = []
        g = 0
        while g < n:
            for k in range(h):
                pairs.append((g + k, g + h + k))
            g += 2 * h
        out.append(pairs)
        h *= 2
    return out


class _B:
    pass


def _mk():
    b = _B()
    import os as _os
    nc = bass.Bass("TRN2", target_bir_lowering=False,
                   detect_race_conditions=_os.environ.get("KERNEL_RACE_DETECT", "0") == "1")
    b.nc = nc
    b.stack = ExitStack()
    dr = lambda name, shape, dt, kind="Internal": nc.dram_tensor(name, shape, dt, kind=kind)
    b.xT = dr("xT", [NT, P, TOK], F32, "ExternalInput").ap()
    b.flags = dr("flags", [1, 64], F32, "ExternalInput").ap()
    b.wsl = dr("wsl", [32, P, 1024], F32, "ExternalInput").ap()
    b.outT = dr("outT", [NT, P, TOK], F32, "ExternalOutput").ap()
    b.dbg = dr("dbg", [1, 128], F32, "ExternalOutput").ap()
    b.wvT = dr("wvT", [16, P, NT, P], F32, "ExternalInput").ap()
    b.woT = dr("woT", [16, P, NT, P], F32, "ExternalInput").ap()
    b.bv0 = dr("bv0", [1, D], F32, "ExternalInput").ap()
    b.bo0 = dr("bo0", [1, D], F32, "ExternalInput").ap()
    b.wupT = dr("wupT", [NJ, P, NT, P], F32, "ExternalInput").ap()
    b.wgT = dr("wgT", [NJ, P, NT, P], F32, "ExternalInput").ap()
    b.wdT = dr("wdT", [NT, P, NJ, P], F32, "ExternalInput").ap()
    b.su_ug = dr("su_ug", [1, D], F32, "ExternalInput").ap()
    b.su_dp = dr("su_dp", [1, INTER], F32, "ExternalInput").ap()
    b.ws_ug = dr("ws_ug", [1, 2 * INTER], F32, "ExternalInput").ap()
    b.ws_dp = dr("ws_dp", [1, D], F32, "ExternalInput").ap()
    b.wuT = dr("wuT", [NT, P, NT, P], F32, "ExternalInput").ap()
    b.wvT1 = dr("wvT1", [NT, P, NT, P], F32, "ExternalInput").ap()
    b.wg1T = dr("wg1T", [NDH, P, NT, P], F32, "ExternalInput").ap()
    b.wg2T = dr("wg2T", [NT, P, NDH, P], F32, "ExternalInput").ap()
    b.bu1 = dr("bu1", [1, D], F32, "ExternalInput").ap()
    b.bv1 = dr("bv1", [1, D], F32, "ExternalInput").ap()
    b.bg1 = dr("bg1", [1, DH], F32, "ExternalInput").ap()
    b.bg2 = dr("bg2", [1, D], F32, "ExternalInput").ap()
    b.convw = dr("convw", [1, 3 * D], F32, "ExternalInput").ap()
    b.convb = dr("convb", [1, D], F32, "ExternalInput").ap()
    b.wfcT = dr("wfcT", [NJ, P, NT, P], F32, "ExternalInput").ap()
    b.wpjT = dr("wpjT", [NT, P, NJ, P], F32, "ExternalInput").ap()
    b.resid = dr("resid", [NT, P, TOK], F32).ap()
    b.zdr = dr("zdr", [NJ, P, TOK], BF16).ap()
    b.hdr = dr("hdr", [NJ, P, TOK], BF16).ap()
    b.xlndr = dr("xlndr", [NT, P, TOK], BF16).ap()
    b.gatesdr = dr("gatesdr", [NT, P, TOK], BF16).ap()
    b.kcdr = dr("kcdr", [NT, P, TOK], BF16).ap()
    b.xln4dr = dr("xln4dr", [NT, P, TOK], F32).ap()
    b.ydr = dr("ydr", [NT, P, TOK], BF16).ap()
    b.cc1i = dr("cc1i", [P, 48], F32)
    b.cc1o = dr("cc1o", [P, 48], F32)
    b.cc2i = dr("cc2i", [1, 32], F32)
    b.cc2o = dr("cc2o", [1, 32], F32)
    b.cc3i = dr("cc3i", [1, 32], F32)
    b.cc3o = dr("cc3o", [1, 32], F32)
    b.ccmb = dr("ccmb", [1, 32], F32)
    b.ccmc = dr("ccmc", [1, 32], F32)
    Hm = np.array([[1.0]])
    while Hm.shape[0] < P:
        Hm = np.block([[Hm, Hm], [Hm, -Hm]])
    b.h128c = nc.inline_tensor(np.ascontiguousarray(Hm.astype(ml_dtypes.bfloat16)), "h128c").ap()
    b.eyec = nc.inline_tensor(np.eye(P, dtype=np.float32), "eyec").ap()

    st = b.stack
    sb = lambda name, shape, dt: st.enter_context(nc.sbuf_tensor(name, shape, dt))
    b.arena = sb("arena", [P, ARENA], U8)
    b.gbias = sb("gbias", [P, 416], F32)
    b.gb_off = {}
    b.gb_next = 0
    b.grows = sb("grows", [1, 2 * CH], F32)
    b.grows2 = sb("grows2", [1, CH], F32)
    b.gflags = sb("gflags", [1, 64], F32)
    b.gwvp = sb("gwvp", [P, NT * P], BF16)
    b.gmo = sb("gmo", [P, NT * P], BF16)
    b.gh128 = sb("gh128", [P, P], BF16)
    b.gh128su = sb("gh128su", [P, 2 * P], BF16)
    b.geye = sb("geye", [P, P], F32)
    b.gones_bf = sb("gones_bf", [P, 1], BF16)
    b.ginv_bf = sb("ginv_bf", [P, 1], BF16)
    b.gones_row = sb("gones_row", [1, P], F32)
    b.gstkc = sb("gstkc", [P, NT], F32)
    b.ghalo = sb("ghalo", [P, 48], F32)
    b.gedge = sb("gedge", [P, 4 * NT], F32)
    b.gscal = sb("gscal", [P, 8], F32)
    b.gmax = sb("gmax", [P, 8], F32)
    b.gcc2 = sb("gcc2", [1, 32], F32)
    b.grow1 = sb("grow1", [1, 16], F32)
    b.gjunk = sb("gjunk", [1, 8], F32)
    b.pbig0 = st.enter_context(nc.psum_tensor("pbig0", [P, 4 * CH], F32))
    b.pbig1 = st.enter_context(nc.psum_tensor("pbig1", [P, 4 * CH], F32))
    sem = lambda name: st.enter_context(nc.semaphore(name))
    b.sW = sem("sW")
    b.sD = sem("sD")
    b.sPE = sem("sPE")
    b.sACT = sem("sACT")
    b.sDVE = sem("sDVE")
    b.sCC = sem("sCC")
    b.sQ = [sem(f"sQ{i}") for i in range(8)]
    b.cQ = [0] * 8
    b.cW = b.cD = b.cPE = b.cACT = b.cDVE = b.cCC = 0
    return b


def gb_alloc(b, name, n):
    off = b.gb_next
    assert off + n <= 416, (name, off, n)
    b.gb_off[name] = (off, n)
    b.gb_next = off + n


def gb(b, name, i=0, n=1):
    off, tot = b.gb_off[name]
    assert i + n <= tot, (name, i, n)
    return b.gbias[:, off + i: off + i + n]


def ar(b, off, dt, cols):
    esz = {F32: 4, BF16: 2, FP8: 1}[dt]
    assert off % 512 == 0 and off + cols * esz <= ARENA, (off, cols)
    return b.arena[:, off: off + cols * esz].bitcast(dt)


def vpad(b, v):
    v.tensor_copy(b.gjunk[:, 4:6], b.gjunk[:, 0:2])
    v.tensor_copy(b.gjunk[:, 6:8], b.gjunk[:, 0:2])


def wait_all(b, e):
    if b.cD:
        e.wait_ge(b.sD, b.cD)
    if b.cW:
        e.wait_ge(b.sW, b.cW)


def _t3(ap, nt=NT):
    return ap.rearrange("l (T t) -> l T t", T=nt)


def emit_setup(b):
    nc = b.nc
    for nm, n in [("bo", NT), ("bu", NT), ("bv1", NT), ("bg2", NT), ("convb", NT),
                  ("bg1", NDH), ("bvp", 1), ("sug", NT), ("sdp", NJ),
                  ("wsu", NJ), ("wsg", NJ), ("wsd", NT), ("cw", 3 * NT),
                  ("bvtmp", NT), ("isodd", 1), ("iseven", 1)]:
        gb_alloc(b, nm, n)
    wtmp = ar(b, 0, F32, NT * P)
    wacc = ar(b, 16384, F32, NT * P)
    wred = ar(b, 32768, F32, P)

    with nc.Block() as blk, nc.allow_non_contiguous_dma(reason="tiny one-time bias loads"):
        @blk.sync
        def _(s):
            def ld(dst, src):
                s.dma_start(out=dst, in_=src).then_inc(b.sW, 16)
                b.cW += 16
            ld(b.gflags[:, :], b.flags)
            ld(b.gh128[:, :], b.h128c)
            ld(b.geye[:, :], b.eyec)
            def ldb(nm, src, ntiles):
                ld(gb(b, nm, 0, ntiles), src.rearrange("o (T l) -> l (o T)", l=P))
            ldb("bo", b.bo0, NT)
            ldb("bu", b.bu1, NT)
            ldb("bv1", b.bv1, NT)
            ldb("bg2", b.bg2, NT)
            ldb("convb", b.convb, NT)
            ldb("bg1", b.bg1, NDH)
            ldb("bvtmp", b.bv0, NT)
            ldb("sug", b.su_ug, NT)
            ldb("sdp", b.su_dp, NJ)
            ldb("wsd", b.ws_dp, NT)
            ld(gb(b, "wsu", 0, NJ), b.ws_ug[:, 0:INTER].rearrange("o (T l) -> l (o T)", l=P))
            ld(gb(b, "wsg", 0, NJ), b.ws_ug[:, INTER:].rearrange("o (T l) -> l (o T)", l=P))
            ld(gb(b, "cw", 0, 3 * NT), b.convw.rearrange("o (k T l) -> l (o k T)", l=P, T=NT))

        @blk.vector
        def _(v):
            v.memset(b.gones_bf[:, :], 1.0)
            v.memset(b.ginv_bf[:, :], 1.0 / 2048.0)
            v.memset(b.gones_row[:, :], 1.0)
            v.memset(b.gstkc[:, :], 0.0)
            v.memset(b.gmax[:, :], 0.0)
            v.memset(b.gedge[:, :], 0.0)
            v.memset(b.gscal[:, :], 0.0)
            v.memset(b.grow1[:, :], 0.0)
            v.memset(b.gjunk[:, :], 0.0)

    with nc.Block() as blk:
        @blk.vector
        def _(v):
            wait_all(b, v)
            for nm in ("wsu", "wsg", "wsd"):
                n = b.gb_off[nm][1]
                v.tensor_single_scalar(gb(b, nm, 0, n), gb(b, nm, 0, n), 64.0, ALU.mult)
            for nm, k in (("sug", K1), ("sdp", K2)):
                n = b.gb_off[nm][1]
                v.tensor_single_scalar(gb(b, nm, 0, n), gb(b, nm, 0, n), float(k), ALU.mult)
            v.reduce_sum(gb(b, "bvp"), gb(b, "bvtmp", 0, NT), AX.X)

    with nc.Block() as blk:
        @blk.tensor
        def _(t):
            t.matmul(b.pbig0[:, 0:2], b.gones_row[:, :], b.gflags[:, 0:2], start=True, stop=True)
    with nc.Block() as blk:
        @blk.vector
        def _(v):
            v.tensor_copy(gb(b, "isodd"), b.pbig0[:, 0:1])
            v.tensor_copy(gb(b, "iseven"), b.pbig0[:, 1:2])

    w0, d0 = b.cW, b.cDVE
    srcs = ([("wv", h) for h in range(16)] + [("wo", ob) for ob in range(16)]
            + [("wsl", p) for p in range(32)])
    WT2 = [0, 8192]

    def _src_ap(kind, i):
        if kind == "wv":
            return b.wvT[i].rearrange("l T c -> l (T c)"), NT * P
        if kind == "wo":
            return b.woT[i].rearrange("l T c -> l (T c)"), NT * P
        return b.wsl[i], 1024

    q0 = [b.cQ[0], b.cQ[1]]
    with nc.Block() as blk:
        @blk.sync
        def _(s):
            for k, (kind, i) in enumerate(srcs):
                if k >= 2:
                    s.wait_ge(b.sDVE, d0 + k - 1)
                apx, cols = _src_ap(kind, i)
                s.dma_start(out=ar(b, WT2[k % 2], F32, cols), in_=apx).then_inc(b.sQ[k % 2], 16)
                b.cQ[k % 2] += 16

        @blk.vector
        def _(v):
            for k, (kind, i) in enumerate(srcs):
                v.wait_ge(b.sQ[k % 2], q0[k % 2] + 16 * (k // 2 + 1))
                apx, cols = _src_ap(kind, i)
                buf = ar(b, WT2[k % 2], F32, cols)
                if kind == "wv":
                    if i == 0:
                        v.tensor_copy(wacc, buf)
                    else:
                        v.tensor_add(wacc, wacc, buf)
                    if i == 15:
                        vpad(b, v)
                        v.tensor_copy(b.gwvp[:, :], wacc)
                elif kind == "wo":
                    v.reduce_sum(wred, buf.rearrange("l (T c) -> l c T", T=NT), AX.X)
                    vpad(b, v)
                    v.tensor_copy(b.gmo[:, i * P:(i + 1) * P], wred)
                else:
                    col = 3 if i < 16 else 4
                    v.tensor_reduce(b.gmax[:, 2:3], buf, AX.X, ALU.max, apply_absolute_value=True)
                    vpad(b, v)
                    v.tensor_tensor(b.gmax[:, col:col + 1], b.gmax[:, col:col + 1], b.gmax[:, 2:3], ALU.max)
                v.drain().then_inc(b.sDVE, 1)
                b.cDVE += 1


def emit_stat_rows_and_rep(b):
    nc = b.nc
    d0, a0, p0 = b.cDVE, b.cACT, b.cPE
    with nc.Block() as blk:
        @blk.vector
        def _(v):
            g2 = b.grows2
            tmp = b.grows[:, 0:CH]
            v.tensor_copy(g2[:, 0:CH], b.pbig0[0:1, 0:CH])
            v.tensor_tensor(tmp, g2[:, 0:CH], g2[:, 0:CH], ALU.mult)
            v.tensor_tensor(tmp, b.pbig0[0:1, CH:2 * CH], tmp, ALU.subtract)
            v.tensor_single_scalar(tmp, tmp, EPS, ALU.add)
            v.reciprocal(b.grows[:, CH:2 * CH], tmp).then_inc(b.sDVE, 1)
            b.cDVE += 1
            v.wait_ge(b.sACT, a0 + 1)
            v.tensor_tensor(b.grows[:, CH:2 * CH], b.grows2[:, 0:CH], b.grows[:, 0:CH], ALU.mult)
            v.tensor_single_scalar(b.grows[:, CH:2 * CH], b.grows[:, CH:2 * CH],
                                   -1.0, ALU.mult).then_inc(b.sDVE, 1)
            b.cDVE += 1

        @blk.scalar
        def _(a):
            a.wait_ge(b.sDVE, d0 + 1)
            a.activation(b.grows[:, 0:CH], b.grows[:, CH:2 * CH], ACTF.Sqrt).then_inc(b.sACT, 1)
            b.cACT += 1

        @blk.tensor
        def _(t):
            t.wait_ge(b.sDVE, d0 + 2)
            t.matmul(b.pbig0[:, 2 * CH:3 * CH], b.gones_row[:, :], b.grows[:, 0:CH], start=True, stop=True)
            t.matmul(b.pbig0[:, 3 * CH:4 * CH], b.gones_row[:, :], b.grows[:, CH:2 * CH], start=True, stop=True)


def emit_stats_from(b, xb, x2b, nt=NT):
    nc = b.nc
    with nc.Block() as blk:
        @blk.tensor
        def _(t):
            for T in range(nt):
                t.matmul(b.pbig0[0:1, 0:CH], b.ginv_bf[:, 0:1], xb[:, T * CH:(T + 1) * CH],
                         start=(T == 0), stop=(T == nt - 1))
            for T in range(nt):
                t.matmul(b.pbig0[0:1, CH:2 * CH], b.ginv_bf[:, 0:1], x2b[:, T * CH:(T + 1) * CH],
                         start=(T == 0), stop=(T == nt - 1))
    emit_stat_rows_and_rep(b)


def emit_preln(b, c, src_dram, affine_dst=None, affine_nt=NT):
    nc = b.nc
    c0 = c * CH
    rc = ar(b, 0, F32, NT * CH)
    xb = ar(b, 33280, BF16, NT * CH)
    x2b = ar(b, 50176, BF16, NT * CH)
    dd0, d0, a0, p0 = b.cD, b.cDVE, b.cACT, b.cPE
    with nc.Block() as blk:
        @blk.gpsimd
        def _(g):
            wait_all(b, g)
            g.dma_start(out=_t3(rc), in_=src_dram[:, :, c0:c0 + CH].rearrange("T l t -> l T t")).then_inc(b.sD, 16)
            b.cD += 16

        @blk.vector
        def _(v):
            v.wait_ge(b.sD, b.cD)
            v.tensor_copy(xb, rc).then_inc(b.sDVE, 1)
            b.cDVE += 1
            v.wait_ge(b.sPE, p0 + 1)
            g2 = b.grows2
            tmp = b.grows[:, 0:CH]
            v.tensor_copy(g2[:, 0:CH], b.pbig0[0:1, 0:CH])
            v.tensor_tensor(tmp, g2[:, 0:CH], g2[:, 0:CH], ALU.mult)
            v.tensor_tensor(tmp, b.pbig0[0:1, CH:2 * CH], tmp, ALU.subtract)
            v.tensor_single_scalar(tmp, tmp, EPS, ALU.add)
            v.reciprocal(b.grows[:, CH:2 * CH], tmp).then_inc(b.sDVE, 1)
            b.cDVE += 1
            v.wait_ge(b.sACT, a0 + 2)
            v.tensor_tensor(b.grows[:, CH:2 * CH], b.grows2[:, 0:CH], b.grows[:, 0:CH], ALU.mult)
            v.tensor_single_scalar(b.grows[:, CH:2 * CH], b.grows[:, CH:2 * CH],
                                   -1.0, ALU.mult).then_inc(b.sDVE, 1)
            b.cDVE += 1
            if affine_dst is not None:
                v.wait_ge(b.sPE, p0 + 2)
                for T in range(affine_nt):
                    sl = slice(T * CH, (T + 1) * CH)
                    v.tensor_tensor(affine_dst[:, sl], rc[:, sl], b.pbig0[:, 2 * CH:3 * CH], ALU.mult)
                for T in range(affine_nt):
                    sl = slice(T * CH, (T + 1) * CH)
                    v.tensor_tensor(affine_dst[:, sl], affine_dst[:, sl], b.pbig0[:, 3 * CH:4 * CH], ALU.add)

        @blk.scalar
        def _(a):
            a.wait_ge(b.sD, b.cD)
            a.activation(x2b, rc, ACTF.Square).then_inc(b.sACT, 1)
            b.cACT += 1
            a.wait_ge(b.sDVE, d0 + 2)
            a.activation(b.grows[:, 0:CH], b.grows[:, CH:2 * CH], ACTF.Sqrt).then_inc(b.sACT, 1)
            b.cACT += 1

        @blk.tensor
        def _(t):
            t.wait_ge(b.sDVE, d0 + 1)
            t.wait_ge(b.sACT, a0 + 1)
            for T in range(NT):
                t.matmul(b.pbig0[0:1, 0:CH], b.ginv_bf[:, 0:1], xb[:, T * CH:(T + 1) * CH],
                         start=(T == 0), stop=(T == NT - 1))
            for T in range(NT):
                t.matmul(b.pbig0[0:1, CH:2 * CH], b.ginv_bf[:, 0:1], x2b[:, T * CH:(T + 1) * CH],
                         start=(T == 0), stop=(T == NT - 1))
            t.drain().then_inc(b.sPE, 1)
            b.cPE += 1
            t.wait_ge(b.sDVE, d0 + 3)
            t.matmul(b.pbig0[:, 2 * CH:3 * CH], b.gones_row[:, :], b.grows[:, 0:CH], start=True, stop=True)
            t.matmul(b.pbig0[:, 3 * CH:4 * CH], b.gones_row[:, :], b.grows[:, CH:2 * CH],
                     start=True, stop=True).then_inc(b.sPE, 1)
            b.cPE += 1


def emit_affine_to(b, srcf32, dst, nt=NT):
    nc = b.nc
    with nc.Block() as blk:
        @blk.vector
        def _(v):
            for T in range(nt):
                sl = slice(T * CH, (T + 1) * CH)
                v.tensor_tensor(dst[:, sl], srcf32[:, sl], b.pbig0[:, 2 * CH:3 * CH], ALU.mult)
            for T in range(nt):
                sl = slice(T * CH, (T + 1) * CH)
                v.tensor_tensor(dst[:, sl], dst[:, sl], b.pbig0[:, 3 * CH:4 * CH], ALU.add)


def emit_postln_resid(b, c, ybuf, rc, t1, dst_dram, nt=NT):
    nc = b.nc
    c0 = c * CH
    d0 = b.cDVE
    with nc.Block() as blk:
        @blk.vector
        def _(v):
            for T in range(nt):
                sl = slice(T * CH, (T + 1) * CH)
                v.tensor_tensor(t1[:, sl], ybuf[:, sl], b.pbig0[:, 2 * CH:3 * CH], ALU.mult)
            for T in range(nt):
                sl = slice(T * CH, (T + 1) * CH)
                v.tensor_tensor(t1[:, sl], t1[:, sl], b.pbig0[:, 3 * CH:4 * CH], ALU.add)
            for T in range(nt):
                sl = slice(T * CH, (T + 1) * CH)
                v.tensor_tensor(t1[:, sl], t1[:, sl], rc[:, sl], ALU.add)
            v.drain().then_inc(b.sDVE, 1)
            b.cDVE += 1

        @blk.gpsimd
        def _(g):
            g.wait_ge(b.sDVE, d0 + 1)
            g.dma_start(out=dst_dram[:, :, c0:c0 + CH].rearrange("T l t -> l T t"), in_=_t3(t1)).then_inc(b.sD, 16)
            b.cD += 16


def emit_l0_attn(b):
    nc = b.nc
    XLN, Y, SQ, WB, T1 = 67584, 84480, 101376, 103424, 103936
    for c in range(NCH):
        rc = ar(b, 0, F32, NT * CH)
        xln = ar(b, XLN, BF16, NT * CH)
        emit_preln(b, c, b.xT, affine_dst=xln)
        ybuf = ar(b, Y, BF16, NT * CH)
        sq = ar(b, SQ, BF16, 2 * CH)
        wb = ar(b, WB, BF16, CH)
        t1 = ar(b, T1, F32, NT * CH)
        with nc.Block() as blk:
            @blk.tensor
            def _(t):
                for T in range(NT):
                    t.matmul(b.pbig1[:, 0:CH], b.gwvp[:, T * P:(T + 1) * P], xln[:, T * CH:(T + 1) * CH],
                             start=(T == 0), stop=(T == NT - 1))
        with nc.Block() as blk:
            @blk.scalar
            def _(a):
                a.activation(wb, b.pbig1[:, 0:CH], ACTF.Identity, bias=gb(b, "bvp"))
        a0, p0 = b.cACT, b.cPE
        with nc.Block() as blk:
            @blk.tensor
            def _(t):
                for ob in range(NT):
                    if ob >= 2:
                        t.wait_ge(b.sACT, a0 + 2 * (ob - 1))
                    t.matmul(b.pbig1[:, (1 + ob % 2) * CH:(2 + ob % 2) * CH],
                             b.gmo[:, ob * P:(ob + 1) * P], wb, start=True, stop=True)
                    if ob >= 1:
                        t.wait_ge(b.sACT, a0 + 2 * ob)
                        pob = ob - 1
                        t.matmul(b.pbig0[0:1, 0:CH], b.ginv_bf[:, 0:1], ybuf[:, pob * CH:(pob + 1) * CH],
                                 start=(pob == 0), stop=False)
                        t.matmul(b.pbig0[0:1, CH:2 * CH], b.ginv_bf[:, 0:1],
                                 sq[:, (pob % 2) * CH:(pob % 2 + 1) * CH],
                                 start=(pob == 0), stop=False)
                    t.drain().then_inc(b.sPE, 1)
                    b.cPE += 1
                t.wait_ge(b.sACT, a0 + 2 * NT)
                pob = NT - 1
                t.matmul(b.pbig0[0:1, 0:CH], b.ginv_bf[:, 0:1], ybuf[:, pob * CH:(pob + 1) * CH],
                         start=False, stop=True)
                t.matmul(b.pbig0[0:1, CH:2 * CH], b.ginv_bf[:, 0:1],
                         sq[:, (pob % 2) * CH:(pob % 2 + 1) * CH],
                         start=False, stop=True)

            @blk.scalar
            def _(a):
                for ob in range(NT):
                    a.wait_ge(b.sPE, p0 + ob + 1)
                    a.activation(ybuf[:, ob * CH:(ob + 1) * CH], b.pbig1[:, (1 + ob % 2) * CH:(2 + ob % 2) * CH],
                                 ACTF.Identity, bias=gb(b, "bo", ob))
                    a.activation(sq[:, (ob % 2) * CH:(ob % 2 + 1) * CH],
                                 ybuf[:, ob * CH:(ob + 1) * CH], ACTF.Square).then_inc(b.sACT, 2)
                    b.cACT += 2
        emit_stat_rows_and_rep(b)
        emit_postln_resid(b, c, ybuf, rc, t1, b.resid)


def emit_l0_ffn(b):
    nc = b.nc
    XQ, BUFA, BUFB = 101376, 67584, 84480
    xq = ar(b, XQ, BF16, NT * TOK)
    for c in range(NCH):
        rc = ar(b, 0, F32, NT * CH)
        bufa = ar(b, BUFA, BF16, NT * CH)
        bufb = ar(b, BUFB, BF16, NT * CH)
        emit_preln(b, c, b.resid, affine_dst=bufb)
        with nc.Block() as blk:
            @blk.vector
            def _(v):
                for T in range(NT):
                    sl = slice(T * CH, (T + 1) * CH)
                    v.tensor_scalar(bufa[:, sl], bufb[:, sl], gb(b, "sug", T), None, ALU.mult)
        bufs, cur = [bufa, bufb], 0
        for pairs in _butterfly_levels(NT):
            src, dst = bufs[cur], bufs[1 - cur]
            with nc.Block() as blk:
                @blk.vector
                def _(v):
                    for (i, j) in pairs:
                        v.tensor_tensor(dst[:, i * CH:(i + 1) * CH], src[:, i * CH:(i + 1) * CH],
                                        src[:, j * CH:(j + 1) * CH], ALU.add)
                        v.tensor_tensor(dst[:, j * CH:(j + 1) * CH], src[:, i * CH:(i + 1) * CH],
                                        src[:, j * CH:(j + 1) * CH], ALU.subtract)
            cur = 1 - cur
        final = bufs[cur]
        for grp in range(NT // 4):
            with nc.Block() as blk:
                @blk.tensor
                def _(t):
                    for k in range(4):
                        T = grp * 4 + k
                        t.matmul(b.pbig1[:, k * CH:(k + 1) * CH], b.gh128[:, :],
                                 final[:, T * CH:(T + 1) * CH], start=True, stop=True)
            with nc.Block() as blk:
                @blk.scalar
                def _(a):
                    for k in range(4):
                        T = grp * 4 + k
                        a.activation(xq[:, T * TOK + c * CH: T * TOK + c * CH + CH],
                                     b.pbig1[:, k * CH:(k + 1) * CH], ACTF.Copy)
    if b.sub_l0ffn >= 2:
        emit_a2(b, xq)
    if b.sub_l0ffn >= 3:
        emit_a3(b)


def emit_a2_serial(b, xq):
    nc = b.nc
    WUP = [0, 8192]
    WG = [16384, 24576]
    WUPB = [32768, 36864]
    WGB = [40960, 45056]
    SILU = 49152
    XMID = 53248
    ZSB = [57344, 61440]
    for j in range(NJ):
        with nc.Block() as blk:
            @blk.sync
            def _(s):
                if j == 0:
                    wait_all(b, s)
                s.dma_start(out=ar(b, WUP[0], F32, NT * P),
                            in_=b.wupT[j].rearrange("l T c -> l (T c)")).then_inc(b.sW, 16)
                s.dma_start(out=ar(b, WG[0], F32, NT * P),
                            in_=b.wgT[j].rearrange("l T c -> l (T c)")).then_inc(b.sW, 16)
                b.cW += 32
                s.wait_ge(b.sW, b.cW)

            @blk.vector
            def _(v):
                v.wait_ge(b.sW, b.cW)
                v.tensor_copy(ar(b, WUPB[0], BF16, NT * P), ar(b, WUP[0], F32, NT * P))
                v.tensor_copy(ar(b, WGB[0], BF16, NT * P), ar(b, WG[0], F32, NT * P))
                v.tensor_scalar(b.gh128su[:, 0:P], b.gh128[:, :], gb(b, "sdp", j), None, ALU.mult)
        with nc.Block() as blk:
            @blk.tensor
            def _(t):
                wupb = ar(b, WUPB[0], BF16, NT * P)
                wgb = ar(b, WGB[0], BF16, NT * P)
                for T in range(NT):
                    for g in range(4):
                        t.matmul(b.pbig0[:, g * CH:(g + 1) * CH], wupb[:, T * P:(T + 1) * P],
                                 xq[:, T * TOK + g * CH: T * TOK + (g + 1) * CH],
                                 start=(T == 0), stop=(T == NT - 1))
                for T in range(NT):
                    for g in range(4):
                        t.matmul(b.pbig1[:, g * CH:(g + 1) * CH], wgb[:, T * P:(T + 1) * P],
                                 xq[:, T * TOK + g * CH: T * TOK + (g + 1) * CH],
                                 start=(T == 0), stop=(T == NT - 1))
        with nc.Block() as blk:
            @blk.scalar
            def _(a):
                a.activation(ar(b, SILU, BF16, TOK), b.pbig1[:, :], ACTF.Silu, scale=gb(b, "wsg", j))
        with nc.Block() as blk:
            @blk.vector
            def _(v):
                v.scalar_tensor_tensor(ar(b, XMID, BF16, TOK), b.pbig0[:, :], gb(b, "wsu", j),
                                       ar(b, SILU, BF16, TOK), ALU.mult, ALU.mult)
        with nc.Block() as blk:
            @blk.tensor
            def _(t):
                xmid = ar(b, XMID, BF16, TOK)
                for g in range(4):
                    t.matmul(b.pbig0[:, g * CH:(g + 1) * CH], b.gh128su[:, 0:P],
                             xmid[:, g * CH:(g + 1) * CH], start=True, stop=True)
        with nc.Block() as blk:
            @blk.scalar
            def _(a):
                a.activation(ar(b, ZSB[0], BF16, TOK), b.pbig0[:, :], ACTF.Copy)
        with nc.Block() as blk:
            @blk.gpsimd
            def _(g):
                g.dma_start(out=b.zdr[j], in_=ar(b, ZSB[0], BF16, TOK)).then_inc(b.sD, 16)
                b.cD += 16
                g.wait_ge(b.sD, b.cD)


def emit_a2_min(b, xq):
    nc = b.nc
    WUP = [0, 8192]
    WG = [16384, 24576]
    WUPB = [32768, 36864]
    WGB = [40960, 45056]
    w0, d0, p0 = b.cW, b.cDVE, b.cPE
    with nc.Block() as blk:
        @blk.sync
        def _(s):
            wait_all(b, s)
            for j in (0, 1):
                s.dma_start(out=ar(b, WUP[j % 2], F32, NT * P),
                            in_=b.wupT[j].rearrange("l T c -> l (T c)")).then_inc(b.sW, 16)
                s.dma_start(out=ar(b, WG[j % 2], F32, NT * P),
                            in_=b.wgT[j].rearrange("l T c -> l (T c)")).then_inc(b.sW, 16)
                b.cW += 32
            for j in range(NJ - 2):
                s.wait_ge(b.sDVE, d0 + j + 1)
                s.dma_start(out=ar(b, WUP[j % 2], F32, NT * P),
                            in_=b.wupT[j + 2].rearrange("l T c -> l (T c)")).then_inc(b.sW, 16)
                s.dma_start(out=ar(b, WG[j % 2], F32, NT * P),
                            in_=b.wgT[j + 2].rearrange("l T c -> l (T c)")).then_inc(b.sW, 16)
                b.cW += 32

        @blk.vector
        def _(v):
            for j in range(NJ):
                v.wait_ge(b.sW, w0 + 32 * (j + 1))
                if j >= 2:
                    v.wait_ge(b.sPE, p0 + j - 1)
                v.tensor_copy(ar(b, WUPB[j % 2], BF16, NT * P), ar(b, WUP[j % 2], F32, NT * P))
                v.tensor_copy(ar(b, WGB[j % 2], BF16, NT * P), ar(b, WG[j % 2], F32, NT * P))
                v.drain().then_inc(b.sDVE, 1)
                b.cDVE += 1

        @blk.tensor
        def _(t):
            for j in range(NJ):
                t.wait_ge(b.sDVE, d0 + j + 1)
                wupb = ar(b, WUPB[j % 2], BF16, NT * P)
                wgb = ar(b, WGB[j % 2], BF16, NT * P)
                for T in range(NT):
                    for g in range(4):
                        t.matmul(b.pbig0[:, g * CH:(g + 1) * CH], wupb[:, T * P:(T + 1) * P],
                                 xq[:, T * TOK + g * CH: T * TOK + (g + 1) * CH],
                                 start=(T == 0), stop=(T == NT - 1))
                for T in range(NT):
                    for g in range(4):
                        t.matmul(b.pbig1[:, g * CH:(g + 1) * CH], wgb[:, T * P:(T + 1) * P],
                                 xq[:, T * TOK + g * CH: T * TOK + (g + 1) * CH],
                                 start=(T == 0), stop=(T == NT - 1))
                t.drain().then_inc(b.sPE, 1)
                b.cPE += 1


def emit_a2_minsilu(b, xq):
    nc = b.nc
    WUP = [0, 8192]
    WG = [16384, 24576]
    WUPB = [32768, 36864]
    WGB = [40960, 45056]
    SILU = 49152
    w0, d0, p0, a0 = b.cW, b.cDVE, b.cPE, b.cACT
    with nc.Block() as blk:
        @blk.sync
        def _(s):
            wait_all(b, s)
            for j in (0, 1):
                s.dma_start(out=ar(b, WUP[j % 2], F32, NT * P),
                            in_=b.wupT[j].rearrange("l T c -> l (T c)")).then_inc(b.sW, 16)
                s.dma_start(out=ar(b, WG[j % 2], F32, NT * P),
                            in_=b.wgT[j].rearrange("l T c -> l (T c)")).then_inc(b.sW, 16)
                b.cW += 32
            for j in range(NJ - 2):
                s.wait_ge(b.sDVE, d0 + j + 1)
                s.dma_start(out=ar(b, WUP[j % 2], F32, NT * P),
                            in_=b.wupT[j + 2].rearrange("l T c -> l (T c)")).then_inc(b.sW, 16)
                s.dma_start(out=ar(b, WG[j % 2], F32, NT * P),
                            in_=b.wgT[j + 2].rearrange("l T c -> l (T c)")).then_inc(b.sW, 16)
                b.cW += 32

        @blk.vector
        def _(v):
            import os as _os4
            dvp = _os4.environ.get("A2_DVE_PSUM", "0") == "1"
            for j in range(NJ):
                v.wait_ge(b.sW, w0 + 32 * (j + 1))
                if j >= 2:
                    v.wait_ge(b.sPE, p0 + j - 1)
                v.tensor_copy(ar(b, WUPB[j % 2], BF16, NT * P), ar(b, WUP[j % 2], F32, NT * P))
                v.tensor_copy(ar(b, WGB[j % 2], BF16, NT * P), ar(b, WG[j % 2], F32, NT * P))
                if dvp and j > 0:
                    v.tensor_copy(ar(b, SILU, BF16, TOK), b.pbig0[:, :])
                v.drain().then_inc(b.sDVE, 1)
                b.cDVE += 1

        @blk.tensor
        def _(t):
            for j in range(NJ):
                t.wait_ge(b.sDVE, d0 + j + 1)
                if j > 0:
                    t.wait_ge(b.sACT, a0 + j)
                wupb = ar(b, WUPB[j % 2], BF16, NT * P)
                wgb = ar(b, WGB[j % 2], BF16, NT * P)
                for T in range(NT):
                    for g in range(4):
                        t.matmul(b.pbig0[:, g * CH:(g + 1) * CH], wupb[:, T * P:(T + 1) * P],
                                 xq[:, T * TOK + g * CH: T * TOK + (g + 1) * CH],
                                 start=(T == 0), stop=(T == NT - 1))
                for T in range(NT):
                    for g in range(4):
                        t.matmul(b.pbig1[:, g * CH:(g + 1) * CH], wgb[:, T * P:(T + 1) * P],
                                 xq[:, T * TOK + g * CH: T * TOK + (g + 1) * CH],
                                 start=(T == 0), stop=(T == NT - 1))
                t.drain().then_inc(b.sPE, 1)
                b.cPE += 1

        @blk.scalar
        def _(a):
            for j in range(NJ):
                a.wait_ge(b.sPE, p0 + j + 1)
                import os as _os3
                if _os3.environ.get("A2_SILU_SBUF", "0") == "1":
                    a.activation(ar(b, SILU, BF16, TOK), xq[:, 0:TOK], ACTF.Silu,
                                 scale=gb(b, "wsg", j)).then_inc(b.sACT, 1)
                else:
                    a.activation(ar(b, SILU, BF16, TOK), b.pbig1[:, :], ACTF.Silu,
                                 scale=gb(b, "wsg", j)).then_inc(b.sACT, 1)
                b.cACT += 1


def emit_a2(b, xq):
    import os as _os2
    if _os2.environ.get("A2_MODE", "pipe") == "serial":
        return emit_a2_serial(b, xq)
    if _os2.environ.get("A2_MODE", "pipe") == "min":
        return emit_a2_min(b, xq)
    if _os2.environ.get("A2_MODE", "pipe") == "minsilu":
        return emit_a2_minsilu(b, xq)
    noz = _os2.environ.get("A2_NOZ", "0") == "1"
    nc = b.nc
    WUP = [0, 8192]
    WG = [16384, 24576]
    WUPB = [32768, 36864]
    WGB = [40960, 45056]
    SILU = 49152
    XMID = 53248
    ZSB = [57344, 61440]
    w0, d0, a0, p0, dd0 = b.cW, b.cDVE, b.cACT, b.cPE, b.cD
    q0 = list(b.cQ)
    with nc.Block() as blk:
        @blk.sync
        def _(s):
            wait_all(b, s)
            for j in (0, 1):
                s.dma_start(out=ar(b, WUP[j % 2], F32, NT * P),
                            in_=b.wupT[j].rearrange("l T c -> l (T c)")).then_inc(b.sQ[j % 2], 16)
                s.dma_start(out=ar(b, WG[j % 2], F32, NT * P),
                            in_=b.wgT[j].rearrange("l T c -> l (T c)")).then_inc(b.sQ[2 + j % 2], 16)
                b.cQ[j % 2] += 16
                b.cQ[2 + j % 2] += 16
            for j in range(NJ - 2):
                s.wait_ge(b.sDVE, d0 + 2 * j + 1)
                s.dma_start(out=ar(b, WUP[j % 2], F32, NT * P),
                            in_=b.wupT[j + 2].rearrange("l T c -> l (T c)")).then_inc(b.sQ[j % 2], 16)
                s.dma_start(out=ar(b, WG[j % 2], F32, NT * P),
                            in_=b.wgT[j + 2].rearrange("l T c -> l (T c)")).then_inc(b.sQ[2 + j % 2], 16)
                b.cQ[j % 2] += 16
                b.cQ[2 + j % 2] += 16

        @blk.vector
        def _(v):
            for j in range(NJ):
                v.wait_ge(b.sQ[j % 2], q0[j % 2] + 16 * (j // 2 + 1))
                v.wait_ge(b.sQ[2 + j % 2], q0[2 + j % 2] + 16 * (j // 2 + 1))
                if j >= 2:
                    v.wait_ge(b.sPE, p0 + 2 * (j - 2) + 2)
                v.tensor_copy(ar(b, WUPB[j % 2], BF16, NT * P), ar(b, WUP[j % 2], F32, NT * P))
                v.tensor_copy(ar(b, WGB[j % 2], BF16, NT * P), ar(b, WG[j % 2], F32, NT * P))
                v.tensor_scalar(b.gh128su[:, (j % 2) * P:(j % 2 + 1) * P], b.gh128[:, :],
                                gb(b, "sdp", j), None, ALU.mult)
                v.drain().then_inc(b.sDVE, 1)
                b.cDVE += 1
                v.wait_ge(b.sACT, a0 + 2 * j + 1)
                v.scalar_tensor_tensor(ar(b, XMID, BF16, TOK), b.pbig0[:, :], gb(b, "wsu", j),
                                       ar(b, SILU, BF16, TOK), ALU.mult, ALU.mult)
                v.drain().then_inc(b.sDVE, 1)
                b.cDVE += 1

        @blk.tensor
        def _(t):
            for j in range(NJ):
                t.wait_ge(b.sDVE, d0 + 2 * j + 1)
                if j > 0:
                    t.wait_ge(b.sACT, a0 + 2 * j)
                wupb = ar(b, WUPB[j % 2], BF16, NT * P)
                wgb = ar(b, WGB[j % 2], BF16, NT * P)
                for T in range(NT):
                    for g in range(4):
                        t.matmul(b.pbig0[:, g * CH:(g + 1) * CH], wupb[:, T * P:(T + 1) * P],
                                 xq[:, T * TOK + g * CH: T * TOK + (g + 1) * CH],
                                 start=(T == 0), stop=(T == NT - 1))
                for T in range(NT):
                    for g in range(4):
                        t.matmul(b.pbig1[:, g * CH:(g + 1) * CH], wgb[:, T * P:(T + 1) * P],
                                 xq[:, T * TOK + g * CH: T * TOK + (g + 1) * CH],
                                 start=(T == 0), stop=(T == NT - 1))
                t.drain().then_inc(b.sPE, 1)
                b.cPE += 1
                t.wait_ge(b.sDVE, d0 + 2 * j + 2)
                if not noz:
                    xmid = ar(b, XMID, BF16, TOK)
                    for g in range(4):
                        t.matmul(b.pbig0[:, g * CH:(g + 1) * CH], b.gh128su[:, (j % 2) * P:(j % 2 + 1) * P],
                                 xmid[:, g * CH:(g + 1) * CH], start=True, stop=True)
                t.drain().then_inc(b.sPE, 1)
                b.cPE += 1

        @blk.scalar
        def _(a):
            for j in range(NJ):
                a.wait_ge(b.sPE, p0 + 2 * j + 1)
                a.activation(ar(b, SILU, BF16, TOK), b.pbig1[:, :], ACTF.Silu, scale=gb(b, "wsg", j))
                a.drain().then_inc(b.sACT, 1)
                b.cACT += 1
                a.wait_ge(b.sPE, p0 + 2 * j + 2)
                if (not noz) and j >= 2:
                    a.wait_ge(b.sQ[4 + j % 2], q0[4 + j % 2] + 16 * ((j - 2) // 2 + 1))
                if not noz:
                    a.activation(ar(b, ZSB[j % 2], BF16, TOK), b.pbig0[:, :], ACTF.Copy)
                a.drain().then_inc(b.sACT, 1)
                b.cACT += 1

        @blk.gpsimd
        def _(g):
            if not noz:
                for j in range(NJ):
                    g.wait_ge(b.sACT, a0 + 2 * j + 2)
                    g.dma_start(out=b.zdr[j], in_=ar(b, ZSB[j % 2], BF16, TOK)).then_inc(b.sQ[4 + j % 2], 16)
                    b.cQ[4 + j % 2] += 16


def emit_a3(b):
    nc = b.nc
    ZA, ZB = 0, 65536
    WD = [131072, 139264]
    WDB = [147456, 151552]
    YB, SQ = 155648, 172032
    T1 = 32768
    for c in range(NCH):
        c0 = c * CH
        za = ar(b, ZA, BF16, NJ * CH)
        zb = ar(b, ZB, BF16, NJ * CH)
        with nc.Block() as blk:
            @blk.gpsimd
            def _(g):
                wait_all(b, g)
                for j0 in range(0, NJ, 4):
                    g.dma_start(out=_t3(za[:, j0 * CH:(j0 + 4) * CH], nt=4),
                                in_=b.zdr[j0:j0 + 4, :, c0:c0 + CH].rearrange("J l t -> l J t")).then_inc(b.sD, 16)
                    b.cD += 16
        with nc.Block() as blk:
            @blk.vector
            def _(v):
                v.wait_ge(b.sD, b.cD)
        bufs, cur = [za, zb], 0
        for pairs in _butterfly_levels(NJ):
            src, dst = bufs[cur], bufs[1 - cur]
            with nc.Block() as blk:
                @blk.vector
                def _(v):
                    for (i, j) in pairs:
                        v.tensor_tensor(dst[:, i * CH:(i + 1) * CH], src[:, i * CH:(i + 1) * CH],
                                        src[:, j * CH:(j + 1) * CH], ALU.add)
                        v.tensor_tensor(dst[:, j * CH:(j + 1) * CH], src[:, i * CH:(i + 1) * CH],
                                        src[:, j * CH:(j + 1) * CH], ALU.subtract)
            cur = 1 - cur
        zfin = bufs[cur]
        ybuf = ar(b, YB, BF16, NT * CH)
        sq = ar(b, SQ, BF16, 2 * CH)
        w0, d0, a0, p0 = b.cW, b.cDVE, b.cACT, b.cPE
        q0 = list(b.cQ)
        with nc.Block() as blk:
            @blk.sync
            def _(s):
                wait_all(b, s)
                for k in range(2):
                    ob, q = divmod(k, 4)
                    s.dma_start(out=ar(b, WD[k % 2], F32, NT * P),
                                in_=b.wdT[ob, :, q * NT:(q + 1) * NT, :].rearrange("l T c -> l (T c)")).then_inc(b.sQ[k % 2], 16)
                    b.cQ[k % 2] += 16
                for k in range(4 * NT - 2):
                    ob, q = divmod(k + 2, 4)
                    s.wait_ge(b.sDVE, d0 + k + 1)
                    s.dma_start(out=ar(b, WD[k % 2], F32, NT * P),
                                in_=b.wdT[ob, :, q * NT:(q + 1) * NT, :].rearrange("l T c -> l (T c)")).then_inc(b.sQ[k % 2], 16)
                    b.cQ[k % 2] += 16

            @blk.vector
            def _(v):
                for k in range(4 * NT):
                    v.wait_ge(b.sQ[k % 2], q0[k % 2] + 16 * (k // 2 + 1))
                    if k >= 2:
                        v.wait_ge(b.sPE, p0 + k - 1)
                    v.tensor_copy(ar(b, WDB[k % 2], BF16, NT * P), ar(b, WD[k % 2], F32, NT * P))
                    v.drain().then_inc(b.sDVE, 1)
                    b.cDVE += 1

            @blk.tensor
            def _(t):
                for k in range(4 * NT):
                    ob, q = divmod(k, 4)
                    t.wait_ge(b.sDVE, d0 + k + 1)
                    if q == 0 and ob >= 2:
                        t.wait_ge(b.sACT, a0 + 2 * (ob - 1))
                    wdb = ar(b, WDB[k % 2], BF16, NT * P)
                    for Ti in range(NT):
                        i = q * NT + Ti
                        t.matmul(b.pbig1[:, (ob % 2) * CH:(ob % 2 + 1) * CH],
                                 wdb[:, Ti * P:(Ti + 1) * P], zfin[:, i * CH:(i + 1) * CH],
                                 start=(i == 0), stop=(i == NJ - 1))
                    if q == 3 and ob >= 1:
                        t.wait_ge(b.sACT, a0 + 2 * ob)
                        pob = ob - 1
                        t.matmul(b.pbig0[0:1, 0:CH], b.ginv_bf[:, 0:1], ybuf[:, pob * CH:(pob + 1) * CH],
                                 start=(pob == 0), stop=False)
                        t.matmul(b.pbig0[0:1, CH:2 * CH], b.ginv_bf[:, 0:1],
                                 sq[:, (pob % 2) * CH:(pob % 2 + 1) * CH],
                                 start=(pob == 0), stop=False)
                    t.drain().then_inc(b.sPE, 1)
                    b.cPE += 1
                t.wait_ge(b.sACT, a0 + 2 * NT)
                pob = NT - 1
                t.matmul(b.pbig0[0:1, 0:CH], b.ginv_bf[:, 0:1], ybuf[:, pob * CH:(pob + 1) * CH],
                         start=False, stop=True)
                t.matmul(b.pbig0[0:1, CH:2 * CH], b.ginv_bf[:, 0:1],
                         sq[:, (pob % 2) * CH:(pob % 2 + 1) * CH],
                         start=False, stop=True)

            @blk.scalar
            def _(a):
                for ob in range(NT):
                    a.wait_ge(b.sPE, p0 + 4 * (ob + 1))
                    a.activation(ybuf[:, ob * CH:(ob + 1) * CH], b.pbig1[:, (ob % 2) * CH:(ob % 2 + 1) * CH],
                                 ACTF.Identity, scale=gb(b, "wsd", ob))
                    a.activation(sq[:, (ob % 2) * CH:(ob % 2 + 1) * CH],
                                 ybuf[:, ob * CH:(ob + 1) * CH], ACTF.Square).then_inc(b.sACT, 2)
                    b.cACT += 2
        emit_stat_rows_and_rep(b)
        rc = ar(b, ZA, F32, NT * CH)
        t1 = ar(b, T1, F32, NT * CH)
        with nc.Block() as blk:
            @blk.gpsimd
            def _(g):
                wait_all(b, g)
                g.dma_start(out=_t3(rc), in_=b.resid[:, :, c0:c0 + CH].rearrange("T l t -> l T t")).then_inc(b.sD, 16)
                b.cD += 16
        with nc.Block() as blk:
            @blk.vector
            def _(v):
                v.wait_ge(b.sD, b.cD)
        emit_postln_resid(b, c, ybuf, rc, t1, b.resid)


def emit_l1_attn(b):
    nc = b.nc
    XLN = 67584
    for c in range(NCH):
        rc = ar(b, 0, F32, NT * CH)
        xln = ar(b, XLN, BF16, NT * CH)
        emit_preln(b, c, b.resid, affine_dst=xln)
        c0 = c * CH
        with nc.Block() as blk:
            @blk.vector
            def _(v):
                if c == 0:
                    for T in range(NT):
                        v.tensor_copy(b.gedge[:, T:T + 1], xln[:, T * CH:T * CH + 1])
                if c == NCH - 1:
                    for T in range(NT):
                        v.tensor_copy(b.gedge[:, NT + T:NT + T + 1], xln[:, T * CH + CH - 1:T * CH + CH])

            @blk.gpsimd
            def _(g):
                g.dma_start(out=b.xlndr[:, :, c0:c0 + CH].rearrange("T l t -> l T t"), in_=_t3(xln)).then_inc(b.sD, 16)
                b.cD += 16

    WG1B, WG2B, WTMP = 33280, 49664, 0
    XLNC, HT, GATE = 98816, 115200, 119296
    with nc.Block() as blk:
        @blk.sync
        def _(s):
            wait_all(b, s)
            s.dma_start(out=ar(b, WTMP, F32, NDH * NT * P).rearrange("l (r T c) -> l r (T c)", r=NDH, T=NT),
                        in_=b.wg1T.rearrange("r l T c -> l r (T c)")).then_inc(b.sW, 16)
            b.cW += 16
            s.wait_ge(b.sW, b.cW)

        @blk.vector
        def _(v):
            v.wait_ge(b.sW, b.cW)
            v.tensor_copy(ar(b, WG1B, BF16, NDH * NT * P), ar(b, WTMP, F32, NDH * NT * P))
    with nc.Block() as blk:
        @blk.sync
        def _(s):
            s.dma_start(out=ar(b, WTMP, F32, NT * NDH * P).rearrange("l (r T c) -> l r (T c)", r=NT, T=NDH),
                        in_=b.wg2T.rearrange("r l T c -> l r (T c)")).then_inc(b.sW, 16)
            b.cW += 16
            s.wait_ge(b.sW, b.cW)

        @blk.vector
        def _(v):
            v.wait_ge(b.sW, b.cW)
            v.tensor_copy(ar(b, WG2B, BF16, NT * NDH * P), ar(b, WTMP, F32, NT * NDH * P))
    XLNC2 = [XLNC, 135680]
    qb2 = [b.cQ[6], b.cQ[7]]
    with nc.Block() as blk:
        @blk.gpsimd
        def _(g):
            wait_all(b, g)
            g.dma_start(out=_t3(ar(b, XLNC2[0], BF16, NT * CH)),
                        in_=b.xlndr[:, :, 0:CH].rearrange("T l t -> l T t")).then_inc(b.sQ[6], 16)
            b.cQ[6] += 16
    for c in range(NCH):
        c0 = c * CH
        xlnc = ar(b, XLNC2[c % 2], BF16, NT * CH)
        ht = ar(b, HT, BF16, NDH * CH)
        gate = ar(b, GATE, BF16, NT * CH)
        wg1b = ar(b, WG1B, BF16, NDH * NT * P)
        wg2b = ar(b, WG2B, BF16, NT * NDH * P)
        with nc.Block() as blk:
            @blk.gpsimd
            def _(g):
                if c + 1 < NCH:
                    g.dma_start(out=_t3(ar(b, XLNC2[(c + 1) % 2], BF16, NT * CH)),
                                in_=b.xlndr[:, :, (c + 1) * CH:(c + 2) * CH].rearrange("T l t -> l T t")).then_inc(b.sQ[6 + (c + 1) % 2], 16)
                    b.cQ[6 + (c + 1) % 2] += 16

            @blk.tensor
            def _(t):
                t.wait_ge(b.sQ[6 + c % 2], qb2[c % 2] + 16 * (c // 2 + 1))
                for db in range(NDH):
                    for T in range(NT):
                        t.matmul(b.pbig1[:, db * CH:(db + 1) * CH],
                                 wg1b[:, (db * NT + T) * P:(db * NT + T + 1) * P],
                                 xlnc[:, T * CH:(T + 1) * CH], start=(T == 0), stop=(T == NT - 1))
        with nc.Block() as blk:
            @blk.scalar
            def _(a):
                for db in range(NDH):
                    a.activation(ht[:, db * CH:(db + 1) * CH], b.pbig1[:, db * CH:(db + 1) * CH],
                                 ACTF.Relu, bias=gb(b, "bg1", db))
        a0, p0 = b.cACT, b.cPE
        with nc.Block() as blk:
            @blk.tensor
            def _(t):
                for grpi in range(NT // 4):
                    if grpi > 0:
                        t.wait_ge(b.sACT, a0 + grpi)
                    for k in range(4):
                        rb = grpi * 4 + k
                        for db in range(NDH):
                            t.matmul(b.pbig0[:, k * CH:(k + 1) * CH],
                                     wg2b[:, (rb * NDH + db) * P:(rb * NDH + db + 1) * P],
                                     ht[:, db * CH:(db + 1) * CH], start=(db == 0), stop=(db == NDH - 1))
                    t.drain().then_inc(b.sPE, 1)
                    b.cPE += 1

            @blk.scalar
            def _(a):
                a.wait_ge(b.sD, b.cD)
                for grpi in range(NT // 4):
                    a.wait_ge(b.sPE, p0 + grpi + 1)
                    for k in range(4):
                        rb = grpi * 4 + k
                        a.activation(gate[:, rb * CH:(rb + 1) * CH], b.pbig0[:, k * CH:(k + 1) * CH],
                                     ACTF.Sigmoid, bias=gb(b, "bg2", rb))
                    a.drain().then_inc(b.sACT, 1)
                    b.cACT += 1
        with nc.Block() as blk:
            @blk.gpsimd
            def _(g):
                g.dma_start(out=b.gatesdr[:, :, c0:c0 + CH].rearrange("T l t -> l T t"), in_=_t3(gate)).then_inc(b.sD, 16)
                b.cD += 16

    WUBF, XLNC3, GATE3, UG, KC, WTMP3 = 0, 65536, 81920, 98304, 114688, 131072
    wubf = ar(b, WUBF, BF16, NT * NT * P)
    for rb in range(NT):
        with nc.Block() as blk:
            @blk.sync
            def _(s):
                if rb == 0:
                    wait_all(b, s)
                s.dma_start(out=ar(b, WTMP3, F32, NT * P),
                            in_=b.wuT[rb].rearrange("l T c -> l (T c)")).then_inc(b.sW, 16)
                b.cW += 16
                s.wait_ge(b.sW, b.cW)

            @blk.vector
            def _(v):
                v.wait_ge(b.sW, b.cW)
                v.tensor_copy(wubf[:, rb * NT * P:(rb + 1) * NT * P], ar(b, WTMP3, F32, NT * P))
    XLNC3P = [XLNC3, 139264]
    GATE3P = [GATE3, 155648]
    qb3 = [b.cQ[6], b.cQ[7]]
    with nc.Block() as blk:
        @blk.gpsimd
        def _(g):
            wait_all(b, g)
            g.dma_start(out=_t3(ar(b, XLNC3P[0], BF16, NT * CH)),
                        in_=b.xlndr[:, :, 0:CH].rearrange("T l t -> l T t")).then_inc(b.sQ[6], 16)
            g.dma_start(out=_t3(ar(b, GATE3P[0], BF16, NT * CH)),
                        in_=b.gatesdr[:, :, 0:CH].rearrange("T l t -> l T t")).then_inc(b.sQ[6], 16)
            b.cQ[6] += 32
    for c in range(NCH):
        c0 = c * CH
        xlnc = ar(b, XLNC3P[c % 2], BF16, NT * CH)
        gate = ar(b, GATE3P[c % 2], BF16, NT * CH)
        ug = ar(b, UG, BF16, NT * CH)
        kc = ar(b, KC, BF16, NT * CH)
        with nc.Block() as blk:
            @blk.gpsimd
            def _(g):
                if c + 1 < NCH:
                    g.dma_start(out=_t3(ar(b, XLNC3P[(c + 1) % 2], BF16, NT * CH)),
                                in_=b.xlndr[:, :, (c + 1) * CH:(c + 2) * CH].rearrange("T l t -> l T t")).then_inc(b.sQ[6 + (c + 1) % 2], 16)
                    g.dma_start(out=_t3(ar(b, GATE3P[(c + 1) % 2], BF16, NT * CH)),
                                in_=b.gatesdr[:, :, (c + 1) * CH:(c + 2) * CH].rearrange("T l t -> l T t")).then_inc(b.sQ[6 + (c + 1) % 2], 16)
                    b.cQ[6 + (c + 1) % 2] += 32
        a0, p0 = b.cACT, b.cPE
        with nc.Block() as blk:
            @blk.tensor
            def _(t):
                t.wait_ge(b.sQ[6 + c % 2], qb3[c % 2] + 32 * (c // 2 + 1))
                for grpi in range(NT // 4):
                    if grpi > 0:
                        t.wait_ge(b.sACT, a0 + grpi)
                    for k in range(4):
                        rb = grpi * 4 + k
                        for T in range(NT):
                            t.matmul(b.pbig0[:, k * CH:(k + 1) * CH],
                                     wubf[:, (rb * NT + T) * P:(rb * NT + T + 1) * P],
                                     xlnc[:, T * CH:(T + 1) * CH], start=(T == 0), stop=(T == NT - 1))
                    t.drain().then_inc(b.sPE, 1)
                    b.cPE += 1

            @blk.scalar
            def _(a):
                for grpi in range(NT // 4):
                    a.wait_ge(b.sPE, p0 + grpi + 1)
                    for k in range(4):
                        rb = grpi * 4 + k
                        a.activation(ug[:, rb * CH:(rb + 1) * CH], b.pbig0[:, k * CH:(k + 1) * CH],
                                     ACTF.Identity, bias=gb(b, "bu", rb))
                    a.drain().then_inc(b.sACT, 1)
                    b.cACT += 1
        with nc.Block() as blk:
            @blk.vector
            def _(v):
                for rb in range(NT):
                    sl = slice(rb * CH, (rb + 1) * CH)
                    v.tensor_tensor(ug[:, sl], ug[:, sl], gate[:, sl], ALU.mult)
        with nc.Block() as blk:
            @blk.vector
            def _(v):
                v.wait_ge(b.sD, b.cD)
                for rb in range(NT):
                    sl = slice(rb * CH, (rb + 1) * CH)
                    v.tensor_tensor_scan(kc[:, sl], ug[:, sl], ug[:, sl], b.gstkc[:, rb:rb + 1],
                                         ALU.add, ALU.bypass)
                for rb in range(NT):
                    v.tensor_copy(b.gstkc[:, rb:rb + 1], kc[:, rb * CH + CH - 1:rb * CH + CH])
        with nc.Block() as blk:
            @blk.gpsimd
            def _(g):
                g.dma_start(out=b.kcdr[:, :, c0:c0 + CH].rearrange("T l t -> l T t"), in_=_t3(kc)).then_inc(b.sD, 16)
                b.cD += 16

    CONTRIB = 131072
    contrib = ar(b, CONTRIB, F32, 48)
    with nc.Block() as blk:
        @blk.vector
        def _(v):
            v.tensor_scalar(contrib[:, 0:NT], b.gedge[:, NT:2 * NT], gb(b, "iseven"), None, ALU.mult)
            v.tensor_scalar(contrib[:, NT:2 * NT], b.gedge[:, 0:NT], gb(b, "isodd"), None, ALU.mult)
            v.tensor_scalar(contrib[:, 2 * NT:3 * NT], b.gstkc[:, :], gb(b, "iseven"), None, ALU.mult)
    with nc.Block() as blk:
        @blk.gpsimd
        def _(g):
            wait_all(b, g)
            g.dma_start(out=b.cc1i.ap(), in_=contrib).then_inc(b.sD, 16)
            b.cD += 16
            g.wait_ge(b.sD, b.cD)
            g.collective_compute("AllReduce", ALU.add, replica_groups=[[0, 1], [2, 3], [4, 5], [6, 7]],
                                 ins=[b.cc1i.ap()], outs=[b.cc1o.ap()]).then_inc(b.sCC, 1)
            b.cCC += 1
            g.wait_ge(b.sCC, b.cCC)
            g.dma_start(out=b.ghalo[:, :], in_=b.cc1o.ap()).then_inc(b.sD, 16)
            b.cD += 16
    with nc.Block() as blk:
        @blk.vector
        def _(v):
            v.wait_ge(b.sD, b.cD)
            v.tensor_scalar(b.gedge[:, 2 * NT:3 * NT], b.ghalo[:, 0:NT], gb(b, "isodd"), None, ALU.mult)
            v.tensor_scalar(b.gedge[:, 3 * NT:4 * NT], b.ghalo[:, NT:2 * NT], gb(b, "iseven"), None, ALU.mult)
            v.tensor_scalar(b.gstkc[:, :], b.ghalo[:, 2 * NT:3 * NT], gb(b, "isodd"), None, ALU.mult)

    WV1 = [65536, 73728]
    WV1B = [81920, 86016]
    XLNC4, GATE4, VG, KCC, GO = 90112, 106496, 122880, 139264, 155648
    XPAD, TC1, EDGL, EDGR, YB4, SQ4 = 172032, 175104, 177664, 178176, 106496, 65536
    RC4, T14 = 122880, 65536
    for c in range(NCH):
        c0 = c * CH
        xlnc = ar(b, XLNC4, BF16, NT * CH)
        gate = ar(b, GATE4, BF16, NT * CH)
        vg = ar(b, VG, BF16, NT * CH)
        kcc = ar(b, KCC, BF16, NT * CH)
        go = ar(b, GO, BF16, NT * CH)
        ybuf = ar(b, YB4, BF16, NT * CH)
        sq = ar(b, SQ4, BF16, 2 * CH)
        edgl = ar(b, EDGL, BF16, NT)
        edgr = ar(b, EDGR, BF16, NT)
        with nc.Block() as blk:
            @blk.gpsimd
            def _(g):
                wait_all(b, g)
                g.dma_start(out=_t3(xlnc), in_=b.xlndr[:, :, c0:c0 + CH].rearrange("T l t -> l T t")).then_inc(b.sD, 16)
                g.dma_start(out=_t3(gate), in_=b.gatesdr[:, :, c0:c0 + CH].rearrange("T l t -> l T t")).then_inc(b.sD, 16)
                g.dma_start(out=_t3(kcc), in_=b.kcdr[:, :, c0:c0 + CH].rearrange("T l t -> l T t")).then_inc(b.sD, 16)
                b.cD += 48
                if c > 0:
                    g.dma_start(out=_t3(edgl), in_=b.xlndr[:, :, c0 - 1:c0].rearrange("T l t -> l T t"), allow_slow_non_contiguous=True).then_inc(b.sD, 16)
                    b.cD += 16
                if c < NCH - 1:
                    g.dma_start(out=_t3(edgr), in_=b.xlndr[:, :, c0 + CH:c0 + CH + 1].rearrange("T l t -> l T t"), allow_slow_non_contiguous=True).then_inc(b.sD, 16)
                    b.cD += 16
        w0, d0, a0, p0 = b.cW, b.cDVE, b.cACT, b.cPE
        q0 = list(b.cQ)
        with nc.Block() as blk:
            @blk.sync
            def _(s):
                for rb in (0, 1):
                    s.dma_start(out=ar(b, WV1[rb % 2], F32, NT * P),
                                in_=b.wvT1[rb].rearrange("l T c -> l (T c)")).then_inc(b.sQ[rb % 2], 16)
                    b.cQ[rb % 2] += 16
                for rb in range(NT - 2):
                    s.wait_ge(b.sDVE, d0 + rb + 1)
                    s.dma_start(out=ar(b, WV1[rb % 2], F32, NT * P),
                                in_=b.wvT1[rb + 2].rearrange("l T c -> l (T c)")).then_inc(b.sQ[rb % 2], 16)
                    b.cQ[rb % 2] += 16

            @blk.vector
            def _(v):
                for rb in range(NT):
                    v.wait_ge(b.sQ[rb % 2], q0[rb % 2] + 16 * (rb // 2 + 1))
                    if rb >= 2:
                        v.wait_ge(b.sPE, p0 + rb - 1)
                    v.tensor_copy(ar(b, WV1B[rb % 2], BF16, NT * P), ar(b, WV1[rb % 2], F32, NT * P))
                    v.drain().then_inc(b.sDVE, 1)
                    b.cDVE += 1

            @blk.tensor
            def _(t):
                t.wait_ge(b.sD, b.cD)
                for rb in range(NT):
                    t.wait_ge(b.sDVE, d0 + rb + 1)
                    if rb >= 4:
                        t.wait_ge(b.sACT, a0 + rb - 3)
                    wv1b = ar(b, WV1B[rb % 2], BF16, NT * P)
                    for T in range(NT):
                        t.matmul(b.pbig1[:, (rb % 4) * CH:(rb % 4 + 1) * CH],
                                 wv1b[:, T * P:(T + 1) * P], xlnc[:, T * CH:(T + 1) * CH],
                                 start=(T == 0), stop=(T == NT - 1))
                    t.drain().then_inc(b.sPE, 1)
                    b.cPE += 1

            @blk.scalar
            def _(a):
                for rb in range(NT):
                    a.wait_ge(b.sPE, p0 + rb + 1)
                    a.activation(vg[:, rb * CH:(rb + 1) * CH], b.pbig1[:, (rb % 4) * CH:(rb % 4 + 1) * CH],
                                 ACTF.Identity, bias=gb(b, "bv1", rb)).then_inc(b.sACT, 1)
                    b.cACT += 1
        with nc.Block() as blk:
            @blk.vector
            def _(v):
                for rb in range(NT):
                    sl = slice(rb * CH, (rb + 1) * CH)
                    v.tensor_tensor(vg[:, sl], vg[:, sl], gate[:, sl], ALU.mult)
                for rb in range(NT):
                    sl = slice(rb * CH, (rb + 1) * CH)
                    v.scalar_tensor_tensor(go[:, sl], kcc[:, sl], b.gstkc[:, rb:rb + 1], vg[:, sl],
                                           ALU.add, ALU.mult)
        xpad = ar(b, XPAD, F32, CH + 2)
        tc1 = ar(b, TC1, F32, CH)
        w0, d0, a0, p0 = b.cW, b.cDVE, b.cACT, b.cPE
        with nc.Block() as blk:
            @blk.tensor
            def _(t):
                for ob in range(NT):
                    if ob >= 4:
                        t.wait_ge(b.sDVE, d0 + ob - 3)
                    for rT in range(NT):
                        t.matmul(b.pbig1[:, (ob % 4) * CH:(ob % 4 + 1) * CH],
                                 wubf[:, (ob * NT + rT) * P:(ob * NT + rT + 1) * P],
                                 go[:, rT * CH:(rT + 1) * CH], start=(rT == 0), stop=(rT == NT - 1))
                    if ob >= 1:
                        t.wait_ge(b.sACT, a0 + ob)
                        pob = ob - 1
                        t.matmul(b.pbig0[0:1, 0:CH], b.ginv_bf[:, 0:1], ybuf[:, pob * CH:(pob + 1) * CH],
                                 start=(pob == 0), stop=False)
                        t.matmul(b.pbig0[0:1, CH:2 * CH], b.ginv_bf[:, 0:1],
                                 sq[:, (pob % 2) * CH:(pob % 2 + 1) * CH],
                                 start=(pob == 0), stop=False)
                    t.drain().then_inc(b.sPE, 1)
                    b.cPE += 1
                t.wait_ge(b.sACT, a0 + NT)
                pob = NT - 1
                t.matmul(b.pbig0[0:1, 0:CH], b.ginv_bf[:, 0:1], ybuf[:, pob * CH:(pob + 1) * CH],
                         start=False, stop=True)
                t.matmul(b.pbig0[0:1, CH:2 * CH], b.ginv_bf[:, 0:1],
                         sq[:, (pob % 2) * CH:(pob % 2 + 1) * CH],
                         start=False, stop=True)

            @blk.vector
            def _(v):
                for ob in range(NT):
                    v.wait_ge(b.sPE, p0 + ob + 1)
                    v.tensor_copy(xpad[:, 1:CH + 1], xlnc[:, ob * CH:(ob + 1) * CH])
                    if c == 0:
                        v.tensor_copy(xpad[:, 0:1], b.gedge[:, 2 * NT + ob:2 * NT + ob + 1])
                    else:
                        v.tensor_copy(xpad[:, 0:1], edgl[:, ob:ob + 1])
                    if c == NCH - 1:
                        v.tensor_copy(xpad[:, CH + 1:CH + 2], b.gedge[:, 3 * NT + ob:3 * NT + ob + 1])
                    else:
                        v.tensor_copy(xpad[:, CH + 1:CH + 2], edgr[:, ob:ob + 1])
                    v.tensor_scalar(tc1, xpad[:, 1:CH + 1], gb(b, "cw", NT + ob), gb(b, "convb", ob),
                                    ALU.mult, ALU.add)
                    v.scalar_tensor_tensor(tc1, xpad[:, 0:CH], gb(b, "cw", ob), tc1, ALU.mult, ALU.add)
                    v.scalar_tensor_tensor(tc1, xpad[:, 2:CH + 2], gb(b, "cw", 2 * NT + ob), tc1, ALU.mult, ALU.add)
                    v.tensor_tensor(ybuf[:, ob * CH:(ob + 1) * CH], tc1,
                                    b.pbig1[:, (ob % 4) * CH:(ob % 4 + 1) * CH], ALU.add)
                    v.drain().then_inc(b.sDVE, 1)
                    b.cDVE += 1

            @blk.scalar
            def _(a):
                for ob in range(NT):
                    a.wait_ge(b.sDVE, d0 + ob + 1)
                    a.activation(sq[:, (ob % 2) * CH:(ob % 2 + 1) * CH],
                                 ybuf[:, ob * CH:(ob + 1) * CH], ACTF.Square).then_inc(b.sACT, 1)
                    b.cACT += 1
        emit_stat_rows_and_rep(b)
        rc = ar(b, RC4, F32, NT * CH)
        t1 = ar(b, T14, F32, NT * CH)
        with nc.Block() as blk:
            @blk.gpsimd
            def _(g):
                wait_all(b, g)
                g.dma_start(out=_t3(rc), in_=b.resid[:, :, c0:c0 + CH].rearrange("T l t -> l T t")).then_inc(b.sD, 16)
                b.cD += 16
        with nc.Block() as blk:
            @blk.vector
            def _(v):
                v.wait_ge(b.sD, b.cD)
        emit_postln_resid(b, c, ybuf, rc, t1, b.resid)


def emit_l1_ffn(b, final_out):
    nc = b.nc
    XLN4 = 67584
    for c in range(NCH):
        rc = ar(b, 0, F32, NT * CH)
        xln4 = ar(b, XLN4, F32, NT * CH)
        emit_preln(b, c, b.resid, affine_dst=xln4)
        c0 = c * CH
        with nc.Block() as blk:
            @blk.vector
            def _(v):
                v.tensor_reduce(b.gmax[:, 2:3], xln4, AX.X, ALU.max, apply_absolute_value=True)
                vpad(b, v)
                v.tensor_tensor(b.gmax[:, 0:1], b.gmax[:, 0:1], b.gmax[:, 2:3], ALU.max)

            @blk.gpsimd
            def _(g):
                g.dma_start(out=b.xln4dr[:, :, c0:c0 + CH].rearrange("T l t -> l T t"), in_=_t3(xln4)).then_inc(b.sD, 16)
                b.cD += 16
    emit_scalar_allreduce(b, [(0, 0), (3, 1), (4, 2)], b.cc2i, b.cc2o)
    with nc.Block() as blk:
        @blk.gpsimd
        def _(g):
            g.dma_start(out=b.dbg[:, 0:16], in_=b.grow1[:, 0:16]).then_inc(b.sD, 16)
            b.cD += 16
    with nc.Block() as blk:
        @blk.vector
        def _(v):
            v.tensor_single_scalar(b.grow1[:, 3:6], b.grow1[:, 0:3], 1.0 / 448.0, ALU.mult)
            vpad(b, v)
            v.tensor_single_scalar(b.grow1[:, 3:6], b.grow1[:, 3:6], 1e-6, ALU.max)
            vpad(b, v)
            v.reciprocal(b.grow1[:, 6:9], b.grow1[:, 3:6])
            vpad(b, v)
            v.tensor_single_scalar(b.grow1[:, 6:9], b.grow1[:, 6:9], 0.5, ALU.mult)
            vpad(b, v)
    with nc.Block() as blk:
        @blk.gpsimd
        def _(g):
            g.dma_start(out=b.dbg[:, 16:32], in_=b.grow1[:, 0:16]).then_inc(b.sD, 16)
            b.cD += 16
    with nc.Block() as blk:
        @blk.tensor
        def _(t):
            t.matmul(b.pbig0[:, 0:3], b.gones_row[:, :], b.grow1[:, 6:9], start=True, stop=True)
    with nc.Block() as blk:
        @blk.vector
        def _(v):
            v.tensor_copy(b.gscal[:, 0:3], b.pbig0[:, 0:3])

    X8, XL4C, QT = 0, 33280, 66560
    x8 = ar(b, X8, FP8, NT * TOK)
    for c in range(NCH):
        c0 = c * CH
        xl4c = ar(b, XL4C, F32, NT * CH)
        qt = ar(b, QT, F32, NT * CH)
        with nc.Block() as blk:
            @blk.gpsimd
            def _(g):
                wait_all(b, g)
                g.dma_start(out=_t3(xl4c), in_=b.xln4dr[:, :, c0:c0 + CH].rearrange("T l t -> l T t")).then_inc(b.sD, 16)
                b.cD += 16
        with nc.Block() as blk:
            @blk.vector
            def _(v):
                v.wait_ge(b.sD, b.cD)
                v.tensor_scalar(qt, xl4c, b.gscal[:, 0:1], None, ALU.mult)
                v.tensor_copy(x8.rearrange("l (T t) -> l T t", T=NT)[:, :, c0:c0 + CH],
                              qt.rearrange("l (T t) -> l T t", T=NT))
    emit_c2_fc(b, x8)
    emit_scalar_allreduce(b, [(1, 0)], b.cc3i, b.cc3o)
    with nc.Block() as blk:
        @blk.vector
        def _(v):
            v.tensor_tensor(b.grow1[:, 9:10], b.grow1[:, 3:4], b.grow1[:, 4:5], ALU.mult)
            vpad(b, v)
            v.tensor_single_scalar(b.grow1[:, 9:10], b.grow1[:, 9:10], 4.0, ALU.mult)
            vpad(b, v)
            v.tensor_tensor(b.grow1[:, 10:11], b.grow1[:, 9:10], b.grow1[:, 9:10], ALU.mult)
            vpad(b, v)
            v.tensor_tensor(b.grow1[:, 11:12], b.grow1[:, 0:1], b.grow1[:, 10:11], ALU.mult)
            vpad(b, v)
            v.tensor_single_scalar(b.grow1[:, 11:12], b.grow1[:, 11:12], 1.0 / 448.0, ALU.mult)
            vpad(b, v)
            v.tensor_single_scalar(b.grow1[:, 11:12], b.grow1[:, 11:12], 1e-6, ALU.max)
            vpad(b, v)
            v.tensor_single_scalar(b.grow1[:, 12:13], b.grow1[:, 11:12], 2.0, ALU.mult)
            vpad(b, v)
            v.reciprocal(b.grow1[:, 13:14], b.grow1[:, 12:13])
            vpad(b, v)
            v.tensor_tensor(b.grow1[:, 13:14], b.grow1[:, 10:11], b.grow1[:, 13:14], ALU.mult)
            vpad(b, v)
            v.tensor_tensor(b.grow1[:, 14:15], b.grow1[:, 11:12], b.grow1[:, 5:6], ALU.mult)
            vpad(b, v)
            v.tensor_single_scalar(b.grow1[:, 14:15], b.grow1[:, 14:15], 4.0, ALU.mult)
            vpad(b, v)
    with nc.Block() as blk:
        @blk.tensor
        def _(t):
            t.matmul(b.pbig0[:, 0:2], b.gones_row[:, :], b.grow1[:, 13:15], start=True, stop=True)
    with nc.Block() as blk:
        @blk.vector
        def _(v):
            v.tensor_copy(b.gscal[:, 3:5], b.pbig0[:, 0:2])
    with nc.Block() as blk:
        @blk.vector
        def _(v):
            v.tensor_copy(b.gcc2[:, 0:16], b.grow1[:, 0:16])

        @blk.gpsimd
        def _(g):
            g.dma_start(out=b.dbg[:, 100:116], in_=b.grow1[:, 0:16]).then_inc(b.sD, 16)
            g.dma_start(out=b.dbg[:, 116:124], in_=b.gscal[0:1, 0:8]).then_inc(b.sD, 16)
            g.dma_start(out=b.dbg[:, 48:56], in_=b.gjunk[:, 0:8]).then_inc(b.sD, 16)
            b.cD += 48
    emit_c4_proj(b)
    emit_c5_final(b, final_out)


def emit_scalar_allreduce(b, colmap, cci, cco):
    nc = b.nc
    with nc.Block() as blk:
        @blk.tensor
        def _(t):
            for i, (src, _) in enumerate(colmap):
                t.matmul(b.pbig0[0:1, i * P:(i + 1) * P], b.gmax[:, src:src + 1], b.geye[:, :],
                         start=True, stop=True)
    with nc.Block() as blk:
        @blk.vector
        def _(v):
            for i, (_, dst) in enumerate(colmap):
                v.reduce_max(b.grows2[:, i:i + 1], b.pbig0[0:1, i * P:(i + 1) * P], AX.X)
            vpad(b, v)
            v.memset(b.gcc2[:, :], 0.0)
            for i in range(len(colmap)):
                for s in range(8):
                    v.tensor_copy(b.gcc2[:, 4 * s + i:4 * s + i + 1], b.grows2[:, i:i + 1])
            vpad(b, v)
            v.tensor_tensor(b.gcc2[:, :], b.gcc2[:, :], b.gflags[:, 32:64], ALU.mult)
    with nc.Block() as blk:
        @blk.gpsimd
        def _(g):
            wait_all(b, g)
            if cci is b.cc2i:
                g.dma_start(out=b.dbg[:, 64:96], in_=b.gcc2[:, :]).then_inc(b.sD, 16)
                g.dma_start(out=b.dbg[:, 96:100], in_=b.grows2[:, 0:4]).then_inc(b.sD, 16)
                b.cD += 32
            g.dma_start(out=cci.ap(), in_=b.gcc2[:, :]).then_inc(b.sD, 16)
            b.cD += 16
            g.wait_ge(b.sD, b.cD)
            rounds = [
                ([[0, 1], [2, 3], [4, 5], [6, 7]], cci.ap(), cco.ap()),
                ([[0, 2], [1, 3], [4, 6], [5, 7]], cco.ap(), b.ccmb.ap()),
                ([[0, 4], [1, 5], [2, 6], [3, 7]], b.ccmb.ap(), b.ccmc.ap()),
            ]
            for rg, rin, rout in rounds:
                g.collective_compute("AllReduce", ALU.add, replica_groups=rg,
                                     ins=[rin], outs=[rout]).then_inc(b.sCC, 1)
                b.cCC += 1
                g.wait_ge(b.sCC, b.cCC)
            g.dma_start(out=b.gcc2[:, :], in_=b.ccmc.ap()).then_inc(b.sD, 16)
            b.cD += 16
    with nc.Block() as blk:
        @blk.vector
        def _(v):
            v.wait_ge(b.sD, b.cD)
            v.reduce_max(b.grows2[:, 0:4], b.gcc2.rearrange("o (s k) -> o k s", s=8), AX.X)
            vpad(b, v)
            for i, (_, dst) in enumerate(colmap):
                v.tensor_copy(b.grow1[:, dst:dst + 1], b.grows2[:, i:i + 1])
            if cci is b.cc2i:
                v.tensor_copy(b.grows[:, 900:908], b.gcc2[:, 0:8])
                v.tensor_copy(b.grows[:, 908:912], b.grows2[:, 0:4])
                v.tensor_copy(b.grows[:, 912:916], b.grow1[:, 0:4])
    if cci is b.cc2i:
        with nc.Block() as blk:
            @blk.gpsimd
            def _(g):
                g.dma_start(out=b.dbg[:, 32:48], in_=b.grows[:, 900:916]).then_inc(b.sD, 16)
                b.cD += 16


def emit_c2_fc(b, x8):
    nc = b.nc
    WFC = [99840, 108032]
    W8FC = [116224, 118272]
    QW = 120320
    RELU = 129024
    HS2 = [137728, 141824, 145920, 150016]
    w0, d0, a0, p0, dd0 = b.cW, b.cDVE, b.cACT, b.cPE, b.cD
    q0 = list(b.cQ)
    with nc.Block() as blk:
        @blk.sync
        def _(s):
            wait_all(b, s)
            for j in (0, 1):
                s.dma_start(out=ar(b, WFC[j % 2], F32, NT * P),
                            in_=b.wfcT[j].rearrange("l T c -> l (T c)")).then_inc(b.sQ[j % 2], 16)
                b.cQ[j % 2] += 16
            for j in range(NJ - 2):
                s.wait_ge(b.sDVE, d0 + 2 * j + 1)
                s.dma_start(out=ar(b, WFC[j % 2], F32, NT * P),
                            in_=b.wfcT[j + 2].rearrange("l T c -> l (T c)")).then_inc(b.sQ[j % 2], 16)
                b.cQ[j % 2] += 16

        @blk.vector
        def _(v):
            for j in range(NJ):
                v.wait_ge(b.sQ[j % 2], q0[j % 2] + 16 * (j // 2 + 1))
                if j >= 2:
                    v.wait_ge(b.sPE, p0 + j - 1)
                v.tensor_scalar(ar(b, QW, F32, NT * P), ar(b, WFC[j % 2], F32, NT * P),
                                b.gscal[:, 1:2], None, ALU.mult)
                v.tensor_copy(ar(b, W8FC[j % 2], FP8, NT * P), ar(b, QW, F32, NT * P))
                v.drain().then_inc(b.sDVE, 2)
                b.cDVE += 2
                if j >= 3:
                    jm = j - 3
                    v.wait_ge(b.sACT, a0 + 2 * jm + 2)
                    v.tensor_reduce(b.gmax[:, 2:3], ar(b, HS2[jm % 4], BF16, TOK), AX.X, ALU.max)
                    vpad(b, v)
                    v.tensor_tensor(b.gmax[:, 1:2], b.gmax[:, 1:2], b.gmax[:, 2:3], ALU.max)
            for jm in range(NJ - 3, NJ):
                v.wait_ge(b.sACT, a0 + 2 * jm + 2)
                v.tensor_reduce(b.gmax[:, 2:3], ar(b, HS2[jm % 4], BF16, TOK), AX.X, ALU.max)
                vpad(b, v)
                v.tensor_tensor(b.gmax[:, 1:2], b.gmax[:, 1:2], b.gmax[:, 2:3], ALU.max)

        @blk.tensor
        def _(t):
            for j in range(NJ):
                t.wait_ge(b.sDVE, d0 + 2 * j + 1)
                if j >= 2:
                    t.wait_ge(b.sACT, a0 + 2 * (j - 1))
                w8 = ar(b, W8FC[j % 2], FP8, NT * P)
                pb = b.pbig0 if j % 2 == 0 else b.pbig1
                for T in range(NT):
                    for g in range(4):
                        t.matmul(pb[:, g * CH:(g + 1) * CH], w8[:, T * P:(T + 1) * P],
                                 x8.rearrange("l (T t) -> l T t", T=NT)[:, T, g * CH:(g + 1) * CH],
                                 start=(T == 0), stop=(T == NT - 1))
                t.drain().then_inc(b.sPE, 1)
                b.cPE += 1

        @blk.scalar
        def _(a):
            for j in range(NJ):
                a.wait_ge(b.sPE, p0 + j + 1)
                if j >= 4:
                    a.wait_ge(b.sQ[4 + j % 4], q0[4 + j % 4] + 16 * ((j - 4) // 4 + 1))
                pb = b.pbig0 if j % 2 == 0 else b.pbig1
                a.activation(ar(b, RELU, F32, TOK), pb[:, :], ACTF.Relu)
                a.drain().then_inc(b.sACT, 1)
                b.cACT += 1
                a.activation(ar(b, HS2[j % 4], BF16, TOK), ar(b, RELU, F32, TOK), ACTF.Square)
                a.drain().then_inc(b.sACT, 1)
                b.cACT += 1

        @blk.gpsimd
        def _(g):
            for j in range(NJ):
                g.wait_ge(b.sACT, a0 + 2 * j + 2)
                g.dma_start(out=b.hdr[j], in_=ar(b, HS2[j % 4], BF16, TOK)).then_inc(b.sQ[4 + j % 4], 16)
                b.cQ[4 + j % 4] += 16


def emit_c4_proj(b):
    nc = b.nc
    H8 = 0
    WPJ = [131072, 139264]
    W8PJ = [147456, 155648]
    QW4 = 164352
    HS2L = [173056, 177152]
    YT = [181760, 185856]
    h8 = ar(b, H8, FP8, NJ * TOK)
    d0, dd0 = b.cDVE, b.cD
    q0 = list(b.cQ)
    with nc.Block() as blk:
        @blk.gpsimd
        def _(g):
            wait_all(b, g)
            for j in range(NJ):
                if j >= 2:
                    g.wait_ge(b.sDVE, d0 + j - 1)
                g.dma_start(out=ar(b, HS2L[j % 2], BF16, TOK), in_=b.hdr[j]).then_inc(b.sQ[6 + j % 2], 16)
                b.cQ[6 + j % 2] += 16

        @blk.vector
        def _(v):
            for j in range(NJ):
                v.wait_ge(b.sQ[6 + j % 2], q0[6 + j % 2] + 16 * (j // 2 + 1))
                v.tensor_scalar(ar(b, QW4, F32, TOK), ar(b, HS2L[j % 2], BF16, TOK),
                                b.gscal[:, 3:4], None, ALU.mult)
                v.tensor_copy(h8[:, j * TOK:(j + 1) * TOK], ar(b, QW4, F32, TOK))
                v.drain().then_inc(b.sDVE, 1)
                b.cDVE += 1
    w0, d0, a0, p0, dd4 = b.cW, b.cDVE, b.cACT, b.cPE, b.cD
    q0 = list(b.cQ)
    with nc.Block() as blk:
        @blk.sync
        def _(s):
            wait_all(b, s)
            for k in range(2):
                ob, q = divmod(k, 4)
                s.dma_start(out=ar(b, WPJ[k % 2], F32, NT * P),
                            in_=b.wpjT[ob, :, q * NT:(q + 1) * NT, :].rearrange("l T c -> l (T c)")).then_inc(b.sQ[k % 2], 16)
                b.cQ[k % 2] += 16
            for k in range(4 * NT - 2):
                ob, q = divmod(k + 2, 4)
                s.wait_ge(b.sDVE, d0 + k + 1)
                s.dma_start(out=ar(b, WPJ[k % 2], F32, NT * P),
                            in_=b.wpjT[ob, :, q * NT:(q + 1) * NT, :].rearrange("l T c -> l (T c)")).then_inc(b.sQ[k % 2], 16)
                b.cQ[k % 2] += 16

        @blk.vector
        def _(v):
            for k in range(4 * NT):
                ob, q = divmod(k, 4)
                v.wait_ge(b.sQ[k % 2], q0[k % 2] + 16 * (k // 2 + 1))
                if q == 0 and ob >= 2:
                    v.wait_ge(b.sPE, p0 + ob - 1)
                v.tensor_scalar(ar(b, QW4, F32, NT * P), ar(b, WPJ[k % 2], F32, NT * P),
                                b.gscal[:, 2:3], None, ALU.mult)
                v.tensor_copy(ar(b, W8PJ[ob % 2] + q * 2048, FP8, NT * P), ar(b, QW4, F32, NT * P))
                v.drain().then_inc(b.sDVE, 1)
                b.cDVE += 1

        @blk.tensor
        def _(t):
            for ob in range(NT):
                t.wait_ge(b.sDVE, d0 + 4 * (ob + 1))
                if ob >= 2:
                    t.wait_ge(b.sACT, a0 + ob - 1)
                pb = b.pbig0 if ob % 2 == 0 else b.pbig1
                for q in range(4):
                    w8 = ar(b, W8PJ[ob % 2] + q * 2048, FP8, NT * P)
                    for Ti in range(NT):
                        i = q * NT + Ti
                        for g in range(4):
                            t.matmul(pb[:, g * CH:(g + 1) * CH], w8[:, Ti * P:(Ti + 1) * P],
                                     h8[:, i * TOK + g * CH: i * TOK + (g + 1) * CH],
                                     start=(i == 0), stop=(i == NJ - 1))
                t.drain().then_inc(b.sPE, 1)
                b.cPE += 1

        @blk.scalar
        def _(a):
            for ob in range(NT):
                a.wait_ge(b.sPE, p0 + ob + 1)
                if ob >= 2:
                    a.wait_ge(b.sQ[2 + ob % 2], q0[2 + ob % 2] + 16 * ((ob - 2) // 2 + 1))
                pb = b.pbig0 if ob % 2 == 0 else b.pbig1
                a.activation(ar(b, YT[ob % 2], BF16, TOK), pb[:, :], ACTF.Identity,
                             scale=b.gscal[:, 4:5])
                a.drain().then_inc(b.sACT, 1)
                b.cACT += 1

        @blk.gpsimd
        def _(g):
            for ob in range(NT):
                g.wait_ge(b.sACT, a0 + ob + 1)
                g.dma_start(out=b.ydr[ob], in_=ar(b, YT[ob % 2], BF16, TOK)).then_inc(b.sQ[2 + ob % 2], 16)
                b.cQ[2 + ob % 2] += 16


def emit_c5_final(b, final_out):
    nc = b.nc
    YC, Y2C, RC, T1 = 0, 16384, 33280, 66560
    for c in range(NCH):
        c0 = c * CH
        yc = ar(b, YC, BF16, NT * CH)
        y2c = ar(b, Y2C, BF16, NT * CH)
        rc = ar(b, RC, F32, NT * CH)
        t1 = ar(b, T1, F32, NT * CH)
        with nc.Block() as blk:
            @blk.gpsimd
            def _(g):
                wait_all(b, g)
                g.dma_start(out=_t3(yc), in_=b.ydr[:, :, c0:c0 + CH].rearrange("T l t -> l T t")).then_inc(b.sD, 16)
                g.dma_start(out=_t3(rc), in_=b.resid[:, :, c0:c0 + CH].rearrange("T l t -> l T t")).then_inc(b.sD, 16)
                b.cD += 32
        with nc.Block() as blk:
            @blk.scalar
            def _(a):
                a.wait_ge(b.sD, b.cD)
                a.activation(y2c, yc, ACTF.Square)
        emit_stats_from(b, yc, y2c)
        emit_postln_resid(b, c, yc, rc, t1, final_out)


def emit_out_debug(b):
    nc = b.nc
    for c in range(NCH):
        c0 = c * CH
        rc = ar(b, 0, F32, NT * CH)
        with nc.Block() as blk:
            @blk.gpsimd
            def _(g):
                wait_all(b, g)
                g.dma_start(out=_t3(rc), in_=b.resid[:, :, c0:c0 + CH].rearrange("T l t -> l T t")).then_inc(b.sD, 16)
                b.cD += 16
                g.wait_ge(b.sD, b.cD)
                g.dma_start(out=b.outT[:, :, c0:c0 + CH].rearrange("T l t -> l T t"), in_=_t3(rc)).then_inc(b.sD, 16)
                b.cD += 16


def emit_final_wait(b):
    nc = b.nc
    with nc.Block() as blk:
        @blk.gpsimd
        def _(g):
            g.wait_ge(b.sD, b.cD)
            if b.cW:
                g.wait_ge(b.sW, b.cW)


def build(upto=4):
    b = _mk()
    b.sub_l0ffn = {21: 1, 22: 2}.get(upto, 3)
    if upto in (21, 22):
        upto = 2
    emit_setup(b)
    emit_l0_attn(b)
    if upto >= 2:
        emit_l0_ffn(b)
    if upto >= 3:
        emit_l1_attn(b)
    if upto >= 4:
        emit_l1_ffn(b, b.outT)
    else:
        emit_out_debug(b)
    emit_final_wait(b)
    b.stack.close()
    return b.nc


def _lhsT_blocks(WT):
    K, M = WT.shape
    return np.ascontiguousarray(
        WT.reshape(K // 128, 128, M // 128, 128).transpose(2, 1, 0, 3))


_NC_CACHE = {}


def _get_nc(upto=4):
    if upto not in _NC_CACHE:
        _NC_CACHE[upto] = build(upto)
    return _NC_CACHE[upto]


def make_inmaps(inputs):
    f32 = lambda a: np.ascontiguousarray(np.asarray(a), dtype=np.float32)
    x = f32(inputs["L0_attn_x"])
    shared = {
        "wvT": _lhsT_blocks(f32(inputs["L0_attn_Wv"]).T),
        "woT": _lhsT_blocks(f32(inputs["L0_attn_Wo"]).T),
        "bv0": f32(inputs["L0_attn_bv"]).reshape(1, D),
        "bo0": f32(inputs["L0_attn_bo"]).reshape(1, D),
        "wupT": _lhsT_blocks(f32(inputs["L0_ffn_W_up"]).T),
        "wgT": _lhsT_blocks(f32(inputs["L0_ffn_W_gate"]).T),
        "wdT": _lhsT_blocks(f32(inputs["L0_ffn_W_down"]).T),
        "su_ug": f32(inputs["L0_ffn_SU_ug"]).reshape(1, D),
        "su_dp": f32(inputs["L0_ffn_SU_dp"]).reshape(1, INTER),
        "ws_ug": f32(inputs["L0_ffn_Wscale_ug"]).reshape(1, 2 * INTER),
        "ws_dp": f32(inputs["L0_ffn_Wscale_dp"]).reshape(1, D),
        "wuT": _lhsT_blocks(f32(inputs["L1_attn_Wu"]).T),
        "wvT1": _lhsT_blocks(f32(inputs["L1_attn_Wv"]).T),
        "wg1T": _lhsT_blocks(f32(inputs["L1_attn_Wg1"]).T),
        "wg2T": _lhsT_blocks(f32(inputs["L1_attn_Wg2"]).T),
        "bu1": f32(inputs["L1_attn_bu"]).reshape(1, D),
        "bv1": f32(inputs["L1_attn_bv"]).reshape(1, D),
        "bg1": f32(inputs["L1_attn_bg1"]).reshape(1, DH),
        "bg2": f32(inputs["L1_attn_bg2"]).reshape(1, D),
        "convw": np.ascontiguousarray(f32(inputs["L1_attn_conv_w"])[:, 0, :].T).reshape(1, 3 * D),
        "convb": f32(inputs["L1_attn_conv_b"]).reshape(1, D),
        "wfcT": _lhsT_blocks(f32(inputs["L1_ffn_w_fc"]).T),
        "wpjT": _lhsT_blocks(f32(inputs["L1_ffn_w_proj"]).T),
    }
    wfc = f32(inputs["L1_ffn_w_fc"])
    wpj = f32(inputs["L1_ffn_w_proj"])
    in_maps = []
    for core in range(8):
        bidx, half = core // 2, core % 2
        sl = slice(2048 * half, 2048 * half + 2048)
        xTc = np.ascontiguousarray(x[bidx, sl, :].T).reshape(NT, P, TOK)
        fl = np.zeros((1, 64), dtype=np.float32)
        fl[0, 0] = float(half)
        fl[0, 1] = 1.0 - float(half)
        for k in range(8):
            for j in range(4):
                fl[0, 32 + 4 * k + j] = 1.0 if k == core else 0.0
        wf = wfc.reshape(8, 1024, 2048)[core].reshape(16, 128, 1024)
        wp = wpj.reshape(8, 256, 8192)[core].reshape(16, 128, 1024)
        wslc = np.ascontiguousarray(np.concatenate([wf, wp], axis=0))
        m = dict(shared)
        m["xT"] = xTc
        m["flags"] = fl
        m["wsl"] = wslc
        in_maps.append(m)
    return in_maps


_LAST_RES = None


def kernel(**inputs):
    global _LAST_RES
    upto = int(inputs.pop("_upto", 4))
    in_maps = make_inmaps(inputs)
    nc = _get_nc(upto)
    res = run_bass_kernel_spmd(nc, in_maps, core_ids=list(range(8)))
    _LAST_RES = res
    out = np.empty((4, 4096, D), dtype=np.float32)
    for core in range(8):
        bidx, half = core // 2, core % 2
        o = np.asarray(res.results[core]["outT"]).reshape(D, TOK)
        out[bidx, 2048 * half:2048 * half + 2048, :] = o.T
    return out


# version 08
# speedup vs baseline: 1.2733x; 1.0208x over previous
import numpy as np
import ml_dtypes
from contextlib import ExitStack

import concourse.bass as bass
import concourse.mybir as mybir
from concourse.bass_utils import run_bass_kernel_spmd

F32 = mybir.dt.float32
BF16 = mybir.dt.bfloat16
FP8 = mybir.dt.float8e4
U8 = mybir.dt.uint8
ALU = mybir.AluOpType
ACTF = mybir.ActivationFunctionType
AX = mybir.AxisListType

P = 128
D = 2048
NT = 16
TOK = 2048
CH = 512
NCH = TOK // CH
INTER = 8192
NJ = 64
DH = 512
NDH = 4
K1 = 1.0 / (np.sqrt(2048.0) * 64.0)
K2 = 1.0 / (np.sqrt(8192.0) * 64.0)
EPS = 1e-5
ARENA = 193536


def _butterfly_levels(n):
    out = []
    h = 1
    while h < n:
        pairs = []
        g = 0
        while g < n:
            for k in range(h):
                pairs.append((g + k, g + h + k))
            g += 2 * h
        out.append(pairs)
        h *= 2
    return out


class _B:
    pass


def _mk():
    b = _B()
    import os as _os
    nc = bass.Bass("TRN2", target_bir_lowering=False,
                   detect_race_conditions=_os.environ.get("KERNEL_RACE_DETECT", "0") == "1")
    b.nc = nc
    b.stack = ExitStack()
    dr = lambda name, shape, dt, kind="Internal": nc.dram_tensor(name, shape, dt, kind=kind)
    b.xT = dr("xT", [NT, P, TOK], F32, "ExternalInput").ap()
    b.flags = dr("flags", [1, 64], F32, "ExternalInput").ap()
    b.wsl = dr("wsl", [32, P, 1024], F32, "ExternalInput").ap()
    b.outT = dr("outT", [NT, P, TOK], F32, "ExternalOutput").ap()
    b.wvT = dr("wvT", [16, P, NT, P], F32, "ExternalInput").ap()
    b.woT = dr("woT", [16, P, NT, P], F32, "ExternalInput").ap()
    b.bv0 = dr("bv0", [1, D], F32, "ExternalInput").ap()
    b.bo0 = dr("bo0", [1, D], F32, "ExternalInput").ap()
    b.wupT = dr("wupT", [NJ, P, NT, P], F32, "ExternalInput").ap()
    b.wgT = dr("wgT", [NJ, P, NT, P], F32, "ExternalInput").ap()
    b.wdT = dr("wdT", [NT, P, NJ, P], F32, "ExternalInput").ap()
    b.su_ug = dr("su_ug", [1, D], F32, "ExternalInput").ap()
    b.su_dp = dr("su_dp", [1, INTER], F32, "ExternalInput").ap()
    b.ws_ug = dr("ws_ug", [1, 2 * INTER], F32, "ExternalInput").ap()
    b.ws_dp = dr("ws_dp", [1, D], F32, "ExternalInput").ap()
    b.wuT = dr("wuT", [NT, P, NT, P], F32, "ExternalInput").ap()
    b.wvT1 = dr("wvT1", [NT, P, NT, P], F32, "ExternalInput").ap()
    b.wg1T = dr("wg1T", [NDH, P, NT, P], F32, "ExternalInput").ap()
    b.wg2T = dr("wg2T", [NT, P, NDH, P], F32, "ExternalInput").ap()
    b.bu1 = dr("bu1", [1, D], F32, "ExternalInput").ap()
    b.bv1 = dr("bv1", [1, D], F32, "ExternalInput").ap()
    b.bg1 = dr("bg1", [1, DH], F32, "ExternalInput").ap()
    b.bg2 = dr("bg2", [1, D], F32, "ExternalInput").ap()
    b.convw = dr("convw", [1, 3 * D], F32, "ExternalInput").ap()
    b.convb = dr("convb", [1, D], F32, "ExternalInput").ap()
    b.wfcT = dr("wfcT", [NJ, P, NT, P], F32, "ExternalInput").ap()
    b.wpjT = dr("wpjT", [NT, P, NJ, P], F32, "ExternalInput").ap()
    b.resid = dr("resid", [NT, P, TOK], F32).ap()
    b.zdr = dr("zdr", [NJ, P, TOK], BF16).ap()
    b.hdr = dr("hdr", [NJ, P, TOK], BF16).ap()
    b.xlndr = dr("xlndr", [NT, P, TOK], BF16).ap()
    b.gatesdr = dr("gatesdr", [NT, P, TOK], BF16).ap()
    b.kcdr = dr("kcdr", [NT, P, TOK], BF16).ap()
    b.xln4dr = dr("xln4dr", [NT, P, TOK], F32).ap()
    b.ydr = dr("ydr", [NT, P, TOK], BF16).ap()
    b.cc1i = dr("cc1i", [P, 48], F32)
    b.cc1o = dr("cc1o", [P, 48], F32)
    b.cc2i = dr("cc2i", [1, 32], F32)
    b.cc2o = dr("cc2o", [1, 32], F32)
    b.cc3i = dr("cc3i", [1, 32], F32)
    b.cc3o = dr("cc3o", [1, 32], F32)
    b.ccmb = dr("ccmb", [1, 32], F32)
    b.ccmc = dr("ccmc", [1, 32], F32)
    Hm = np.array([[1.0]])
    while Hm.shape[0] < P:
        Hm = np.block([[Hm, Hm], [Hm, -Hm]])
    b.h128c = nc.inline_tensor(np.ascontiguousarray(Hm.astype(ml_dtypes.bfloat16)), "h128c").ap()
    b.eyec = nc.inline_tensor(np.eye(P, dtype=np.float32), "eyec").ap()

    st = b.stack
    sb = lambda name, shape, dt: st.enter_context(nc.sbuf_tensor(name, shape, dt))
    b.arena = sb("arena", [P, ARENA], U8)
    b.gbias = sb("gbias", [P, 416], F32)
    b.gb_off = {}
    b.gb_next = 0
    b.grows = sb("grows", [1, 2 * CH], F32)
    b.grows2 = sb("grows2", [1, CH], F32)
    b.gflags = sb("gflags", [1, 64], F32)
    b.gwvp = sb("gwvp", [P, NT * P], BF16)
    b.gmo = sb("gmo", [P, NT * P], BF16)
    b.gh128 = sb("gh128", [P, P], BF16)
    b.gh128su = sb("gh128su", [P, 2 * P], BF16)
    b.geye = sb("geye", [P, P], F32)
    b.gones_bf = sb("gones_bf", [P, 1], BF16)
    b.ginv_bf = sb("ginv_bf", [P, 1], BF16)
    b.gones_row = sb("gones_row", [1, P], F32)
    b.gstkc = sb("gstkc", [P, NT], F32)
    b.ghalo = sb("ghalo", [P, 48], F32)
    b.gedge = sb("gedge", [P, 4 * NT], F32)
    b.gscal = sb("gscal", [P, 8], F32)
    b.gmax = sb("gmax", [P, 8], F32)
    b.gcc2 = sb("gcc2", [1, 32], F32)
    b.grow1 = sb("grow1", [1, 16], F32)
    b.gjunk = sb("gjunk", [1, 8], F32)
    b.pbig0 = st.enter_context(nc.psum_tensor("pbig0", [P, 4 * CH], F32))
    b.pbig1 = st.enter_context(nc.psum_tensor("pbig1", [P, 4 * CH], F32))
    sem = lambda name: st.enter_context(nc.semaphore(name))
    b.sW = sem("sW")
    b.sD = sem("sD")
    b.sPE = sem("sPE")
    b.sACT = sem("sACT")
    b.sDVE = sem("sDVE")
    b.sCC = sem("sCC")
    b.sQ = [sem(f"sQ{i}") for i in range(8)]
    b.cQ = [0] * 8
    b.cW = b.cD = b.cPE = b.cACT = b.cDVE = b.cCC = 0
    return b


def gb_alloc(b, name, n):
    off = b.gb_next
    assert off + n <= 416, (name, off, n)
    b.gb_off[name] = (off, n)
    b.gb_next = off + n


def gb(b, name, i=0, n=1):
    off, tot = b.gb_off[name]
    assert i + n <= tot, (name, i, n)
    return b.gbias[:, off + i: off + i + n]


def ar(b, off, dt, cols):
    esz = {F32: 4, BF16: 2, FP8: 1}[dt]
    assert off % 512 == 0 and off + cols * esz <= ARENA, (off, cols)
    return b.arena[:, off: off + cols * esz].bitcast(dt)


def vpad(b, v):
    v.tensor_copy(b.gjunk[:, 4:6], b.gjunk[:, 0:2])
    v.tensor_copy(b.gjunk[:, 6:8], b.gjunk[:, 0:2])


def wait_all(b, e):
    if b.cD:
        e.wait_ge(b.sD, b.cD)
    if b.cW:
        e.wait_ge(b.sW, b.cW)


def _t3(ap, nt=NT):
    return ap.rearrange("l (T t) -> l T t", T=nt)


def emit_setup(b):
    nc = b.nc
    for nm, n in [("bo", NT), ("bu", NT), ("bv1", NT), ("bg2", NT), ("convb", NT),
                  ("bg1", NDH), ("bvp", 1), ("sug", NT), ("sdp", NJ),
                  ("wsu", NJ), ("wsg", NJ), ("wsd", NT), ("cw", 3 * NT),
                  ("bvtmp", NT), ("isodd", 1), ("iseven", 1)]:
        gb_alloc(b, nm, n)
    wtmp = ar(b, 0, F32, NT * P)
    wacc = ar(b, 16384, F32, NT * P)
    wred = ar(b, 32768, F32, P)

    with nc.Block() as blk, nc.allow_non_contiguous_dma(reason="tiny one-time bias loads"):
        @blk.sync
        def _(s):
            def ld(dst, src):
                s.dma_start(out=dst, in_=src).then_inc(b.sW, 16)
                b.cW += 16
            ld(b.gflags[:, :], b.flags)
            ld(b.gh128[:, :], b.h128c)
            ld(b.geye[:, :], b.eyec)
            def ldb(nm, src, ntiles):
                ld(gb(b, nm, 0, ntiles), src.rearrange("o (T l) -> l (o T)", l=P))
            ldb("bo", b.bo0, NT)
            ldb("bu", b.bu1, NT)
            ldb("bv1", b.bv1, NT)
            ldb("bg2", b.bg2, NT)
            ldb("convb", b.convb, NT)
            ldb("bg1", b.bg1, NDH)
            ldb("bvtmp", b.bv0, NT)
            ldb("sug", b.su_ug, NT)
            ldb("sdp", b.su_dp, NJ)
            ldb("wsd", b.ws_dp, NT)
            ld(gb(b, "wsu", 0, NJ), b.ws_ug[:, 0:INTER].rearrange("o (T l) -> l (o T)", l=P))
            ld(gb(b, "wsg", 0, NJ), b.ws_ug[:, INTER:].rearrange("o (T l) -> l (o T)", l=P))
            ld(gb(b, "cw", 0, 3 * NT), b.convw.rearrange("o (k T l) -> l (o k T)", l=P, T=NT))

        @blk.vector
        def _(v):
            v.memset(b.gones_bf[:, :], 1.0)
            v.memset(b.ginv_bf[:, :], 1.0 / 2048.0)
            v.memset(b.gones_row[:, :], 1.0)
            v.memset(b.gstkc[:, :], 0.0)
            v.memset(b.gmax[:, :], 0.0)
            v.memset(b.gedge[:, :], 0.0)
            v.memset(b.gscal[:, :], 0.0)
            v.memset(b.grow1[:, :], 0.0)
            v.memset(b.gjunk[:, :], 0.0)

    with nc.Block() as blk:
        @blk.vector
        def _(v):
            wait_all(b, v)
            for nm in ("wsu", "wsg", "wsd"):
                n = b.gb_off[nm][1]
                v.tensor_single_scalar(gb(b, nm, 0, n), gb(b, nm, 0, n), 64.0, ALU.mult)
            for nm, k in (("sug", K1), ("sdp", K2)):
                n = b.gb_off[nm][1]
                v.tensor_single_scalar(gb(b, nm, 0, n), gb(b, nm, 0, n), float(k), ALU.mult)
            v.reduce_sum(gb(b, "bvp"), gb(b, "bvtmp", 0, NT), AX.X)

    with nc.Block() as blk:
        @blk.tensor
        def _(t):
            t.matmul(b.pbig0[:, 0:2], b.gones_row[:, :], b.gflags[:, 0:2], start=True, stop=True)
    with nc.Block() as blk:
        @blk.vector
        def _(v):
            v.tensor_copy(gb(b, "isodd"), b.pbig0[:, 0:1])
            v.tensor_copy(gb(b, "iseven"), b.pbig0[:, 1:2])

    w0, d0 = b.cW, b.cDVE
    srcs = ([("wv", h) for h in range(16)] + [("wo", ob) for ob in range(16)]
            + [("wsl", p) for p in range(32)])
    WT2 = [0, 8192]

    def _src_ap(kind, i):
        if kind == "wv":
            return b.wvT[i].rearrange("l T c -> l (T c)"), NT * P
        if kind == "wo":
            return b.woT[i].rearrange("l T c -> l (T c)"), NT * P
        return b.wsl[i], 1024

    q0 = [b.cQ[0], b.cQ[1]]
    with nc.Block() as blk:
        @blk.sync
        def _(s):
            for k, (kind, i) in enumerate(srcs):
                if k >= 2:
                    s.wait_ge(b.sDVE, d0 + k - 1)
                apx, cols = _src_ap(kind, i)
                s.dma_start(out=ar(b, WT2[k % 2], F32, cols), in_=apx).then_inc(b.sQ[k % 2], 16)
                b.cQ[k % 2] += 16

        @blk.vector
        def _(v):
            for k, (kind, i) in enumerate(srcs):
                v.wait_ge(b.sQ[k % 2], q0[k % 2] + 16 * (k // 2 + 1))
                apx, cols = _src_ap(kind, i)
                buf = ar(b, WT2[k % 2], F32, cols)
                if kind == "wv":
                    if i == 0:
                        v.tensor_copy(wacc, buf)
                    else:
                        v.tensor_add(wacc, wacc, buf)
                    if i == 15:
                        vpad(b, v)
                        v.tensor_copy(b.gwvp[:, :], wacc)
                elif kind == "wo":
                    v.reduce_sum(wred, buf.rearrange("l (T c) -> l c T", T=NT), AX.X)
                    vpad(b, v)
                    v.tensor_copy(b.gmo[:, i * P:(i + 1) * P], wred)
                else:
                    col = 3 if i < 16 else 4
                    v.tensor_reduce(b.gmax[:, 2:3], buf, AX.X, ALU.max, apply_absolute_value=True)
                    vpad(b, v)
                    v.tensor_tensor(b.gmax[:, col:col + 1], b.gmax[:, col:col + 1], b.gmax[:, 2:3], ALU.max)
                v.drain().then_inc(b.sDVE, 1)
                b.cDVE += 1


def emit_stat_rows_and_rep(b):
    nc = b.nc
    d0, a0, p0 = b.cDVE, b.cACT, b.cPE
    with nc.Block() as blk:
        @blk.vector
        def _(v):
            g2 = b.grows2
            tmp = b.grows[:, 0:CH]
            v.tensor_copy(g2[:, 0:CH], b.pbig0[0:1, 0:CH])
            v.tensor_tensor(tmp, g2[:, 0:CH], g2[:, 0:CH], ALU.mult)
            v.tensor_tensor(tmp, b.pbig0[0:1, CH:2 * CH], tmp, ALU.subtract)
            v.tensor_single_scalar(tmp, tmp, EPS, ALU.add)
            v.reciprocal(b.grows[:, CH:2 * CH], tmp).then_inc(b.sDVE, 1)
            b.cDVE += 1
            v.wait_ge(b.sACT, a0 + 1)
            v.tensor_tensor(b.grows[:, CH:2 * CH], b.grows2[:, 0:CH], b.grows[:, 0:CH], ALU.mult)
            v.tensor_single_scalar(b.grows[:, CH:2 * CH], b.grows[:, CH:2 * CH],
                                   -1.0, ALU.mult).then_inc(b.sDVE, 1)
            b.cDVE += 1

        @blk.scalar
        def _(a):
            a.wait_ge(b.sDVE, d0 + 1)
            a.activation(b.grows[:, 0:CH], b.grows[:, CH:2 * CH], ACTF.Sqrt).then_inc(b.sACT, 1)
            b.cACT += 1

        @blk.tensor
        def _(t):
            t.wait_ge(b.sDVE, d0 + 2)
            t.matmul(b.pbig0[:, 2 * CH:3 * CH], b.gones_row[:, :], b.grows[:, 0:CH], start=True, stop=True)
            t.matmul(b.pbig0[:, 3 * CH:4 * CH], b.gones_row[:, :], b.grows[:, CH:2 * CH], start=True, stop=True)


def emit_stats_from(b, xb, x2b, nt=NT):
    nc = b.nc
    with nc.Block() as blk:
        @blk.tensor
        def _(t):
            for T in range(nt):
                t.matmul(b.pbig0[0:1, 0:CH], b.ginv_bf[:, 0:1], xb[:, T * CH:(T + 1) * CH],
                         start=(T == 0), stop=(T == nt - 1))
            for T in range(nt):
                t.matmul(b.pbig0[0:1, CH:2 * CH], b.ginv_bf[:, 0:1], x2b[:, T * CH:(T + 1) * CH],
                         start=(T == 0), stop=(T == nt - 1))
    emit_stat_rows_and_rep(b)


def emit_preln(b, c, src_dram, affine_dst=None, affine_nt=NT):
    nc = b.nc
    c0 = c * CH
    rc = ar(b, 0, F32, NT * CH)
    xb = ar(b, 33280, BF16, NT * CH)
    x2b = ar(b, 50176, BF16, NT * CH)
    dd0, d0, a0, p0 = b.cD, b.cDVE, b.cACT, b.cPE
    with nc.Block() as blk:
        @blk.gpsimd
        def _(g):
            wait_all(b, g)
            g.dma_start(out=_t3(rc), in_=src_dram[:, :, c0:c0 + CH].rearrange("T l t -> l T t")).then_inc(b.sD, 16)
            b.cD += 16

        @blk.vector
        def _(v):
            v.wait_ge(b.sD, b.cD)
            v.tensor_copy(xb, rc).then_inc(b.sDVE, 1)
            b.cDVE += 1
            v.wait_ge(b.sPE, p0 + 1)
            g2 = b.grows2
            tmp = b.grows[:, 0:CH]
            v.tensor_copy(g2[:, 0:CH], b.pbig0[0:1, 0:CH])
            v.tensor_tensor(tmp, g2[:, 0:CH], g2[:, 0:CH], ALU.mult)
            v.tensor_tensor(tmp, b.pbig0[0:1, CH:2 * CH], tmp, ALU.subtract)
            v.tensor_single_scalar(tmp, tmp, EPS, ALU.add)
            v.reciprocal(b.grows[:, CH:2 * CH], tmp).then_inc(b.sDVE, 1)
            b.cDVE += 1
            v.wait_ge(b.sACT, a0 + 2)
            v.tensor_tensor(b.grows[:, CH:2 * CH], b.grows2[:, 0:CH], b.grows[:, 0:CH], ALU.mult)
            v.tensor_single_scalar(b.grows[:, CH:2 * CH], b.grows[:, CH:2 * CH],
                                   -1.0, ALU.mult).then_inc(b.sDVE, 1)
            b.cDVE += 1
            if affine_dst is not None:
                v.wait_ge(b.sPE, p0 + 2)
                for T in range(affine_nt):
                    sl = slice(T * CH, (T + 1) * CH)
                    v.tensor_tensor(affine_dst[:, sl], rc[:, sl], b.pbig0[:, 2 * CH:3 * CH], ALU.mult)
                for T in range(affine_nt):
                    sl = slice(T * CH, (T + 1) * CH)
                    v.tensor_tensor(affine_dst[:, sl], affine_dst[:, sl], b.pbig0[:, 3 * CH:4 * CH], ALU.add)

        @blk.scalar
        def _(a):
            a.wait_ge(b.sD, b.cD)
            a.activation(x2b, rc, ACTF.Square).then_inc(b.sACT, 1)
            b.cACT += 1
            a.wait_ge(b.sDVE, d0 + 2)
            a.activation(b.grows[:, 0:CH], b.grows[:, CH:2 * CH], ACTF.Sqrt).then_inc(b.sACT, 1)
            b.cACT += 1

        @blk.tensor
        def _(t):
            t.wait_ge(b.sDVE, d0 + 1)
            t.wait_ge(b.sACT, a0 + 1)
            for T in range(NT):
                t.matmul(b.pbig0[0:1, 0:CH], b.ginv_bf[:, 0:1], xb[:, T * CH:(T + 1) * CH],
                         start=(T == 0), stop=(T == NT - 1))
            for T in range(NT):
                t.matmul(b.pbig0[0:1, CH:2 * CH], b.ginv_bf[:, 0:1], x2b[:, T * CH:(T + 1) * CH],
                         start=(T == 0), stop=(T == NT - 1))
            t.drain().then_inc(b.sPE, 1)
            b.cPE += 1
            t.wait_ge(b.sDVE, d0 + 3)
            t.matmul(b.pbig0[:, 2 * CH:3 * CH], b.gones_row[:, :], b.grows[:, 0:CH], start=True, stop=True)
            t.matmul(b.pbig0[:, 3 * CH:4 * CH], b.gones_row[:, :], b.grows[:, CH:2 * CH],
                     start=True, stop=True).then_inc(b.sPE, 1)
            b.cPE += 1


def emit_affine_to(b, srcf32, dst, nt=NT):
    nc = b.nc
    with nc.Block() as blk:
        @blk.vector
        def _(v):
            for T in range(nt):
                sl = slice(T * CH, (T + 1) * CH)
                v.tensor_tensor(dst[:, sl], srcf32[:, sl], b.pbig0[:, 2 * CH:3 * CH], ALU.mult)
            for T in range(nt):
                sl = slice(T * CH, (T + 1) * CH)
                v.tensor_tensor(dst[:, sl], dst[:, sl], b.pbig0[:, 3 * CH:4 * CH], ALU.add)


def emit_postln_resid(b, c, ybuf, rc, t1, dst_dram, nt=NT):
    nc = b.nc
    c0 = c * CH
    d0 = b.cDVE
    with nc.Block() as blk:
        @blk.vector
        def _(v):
            v.wait_ge(b.sD, b.cD)
            for T in range(nt):
                sl = slice(T * CH, (T + 1) * CH)
                v.tensor_tensor(t1[:, sl], ybuf[:, sl], b.pbig0[:, 2 * CH:3 * CH], ALU.mult)
            for T in range(nt):
                sl = slice(T * CH, (T + 1) * CH)
                v.tensor_tensor(t1[:, sl], t1[:, sl], b.pbig0[:, 3 * CH:4 * CH], ALU.add)
            for T in range(nt):
                sl = slice(T * CH, (T + 1) * CH)
                v.tensor_tensor(t1[:, sl], t1[:, sl], rc[:, sl], ALU.add)
            v.drain().then_inc(b.sDVE, 1)
            b.cDVE += 1

        @blk.gpsimd
        def _(g):
            g.wait_ge(b.sDVE, d0 + 1)
            g.dma_start(out=dst_dram[:, :, c0:c0 + CH].rearrange("T l t -> l T t"), in_=_t3(t1)).then_inc(b.sD, 16)
            b.cD += 16


def emit_l0_attn(b):
    nc = b.nc
    XLN, Y, SQ, WB, T1 = 67584, 84480, 101376, 103424, 103936
    for c in range(NCH):
        rc = ar(b, 0, F32, NT * CH)
        xln = ar(b, XLN, BF16, NT * CH)
        emit_preln(b, c, b.xT, affine_dst=xln)
        ybuf = ar(b, Y, BF16, NT * CH)
        sq = ar(b, SQ, BF16, 2 * CH)
        wb = ar(b, WB, BF16, CH)
        t1 = ar(b, T1, F32, NT * CH)
        with nc.Block() as blk:
            @blk.tensor
            def _(t):
                for T in range(NT):
                    t.matmul(b.pbig1[:, 0:CH], b.gwvp[:, T * P:(T + 1) * P], xln[:, T * CH:(T + 1) * CH],
                             start=(T == 0), stop=(T == NT - 1))
        with nc.Block() as blk:
            @blk.scalar
            def _(a):
                a.activation(wb, b.pbig1[:, 0:CH], ACTF.Identity, bias=gb(b, "bvp"))
        a0, p0 = b.cACT, b.cPE
        with nc.Block() as blk:
            @blk.tensor
            def _(t):
                for ob in range(NT):
                    if ob >= 2:
                        t.wait_ge(b.sACT, a0 + 2 * (ob - 1))
                    t.matmul(b.pbig1[:, (1 + ob % 2) * CH:(2 + ob % 2) * CH],
                             b.gmo[:, ob * P:(ob + 1) * P], wb, start=True, stop=True)
                    if ob >= 1:
                        t.wait_ge(b.sACT, a0 + 2 * ob)
                        pob = ob - 1
                        t.matmul(b.pbig0[0:1, 0:CH], b.ginv_bf[:, 0:1], ybuf[:, pob * CH:(pob + 1) * CH],
                                 start=(pob == 0), stop=False)
                        t.matmul(b.pbig0[0:1, CH:2 * CH], b.ginv_bf[:, 0:1],
                                 sq[:, (pob % 2) * CH:(pob % 2 + 1) * CH],
                                 start=(pob == 0), stop=False)
                    t.drain().then_inc(b.sPE, 1)
                    b.cPE += 1
                t.wait_ge(b.sACT, a0 + 2 * NT)
                pob = NT - 1
                t.matmul(b.pbig0[0:1, 0:CH], b.ginv_bf[:, 0:1], ybuf[:, pob * CH:(pob + 1) * CH],
                         start=False, stop=True)
                t.matmul(b.pbig0[0:1, CH:2 * CH], b.ginv_bf[:, 0:1],
                         sq[:, (pob % 2) * CH:(pob % 2 + 1) * CH],
                         start=False, stop=True)

            @blk.scalar
            def _(a):
                for ob in range(NT):
                    a.wait_ge(b.sPE, p0 + ob + 1)
                    a.activation(ybuf[:, ob * CH:(ob + 1) * CH], b.pbig1[:, (1 + ob % 2) * CH:(2 + ob % 2) * CH],
                                 ACTF.Identity, bias=gb(b, "bo", ob))
                    a.activation(sq[:, (ob % 2) * CH:(ob % 2 + 1) * CH],
                                 ybuf[:, ob * CH:(ob + 1) * CH], ACTF.Square).then_inc(b.sACT, 2)
                    b.cACT += 2
        emit_stat_rows_and_rep(b)
        emit_postln_resid(b, c, ybuf, rc, t1, b.resid)


def emit_l0_ffn(b):
    nc = b.nc
    XQ, BUFA, BUFB = 101376, 67584, 84480
    xq = ar(b, XQ, BF16, NT * TOK)
    for c in range(NCH):
        rc = ar(b, 0, F32, NT * CH)
        bufa = ar(b, BUFA, BF16, NT * CH)
        bufb = ar(b, BUFB, BF16, NT * CH)
        emit_preln(b, c, b.resid, affine_dst=bufb)
        with nc.Block() as blk:
            @blk.vector
            def _(v):
                for T in range(NT):
                    sl = slice(T * CH, (T + 1) * CH)
                    v.tensor_scalar(bufa[:, sl], bufb[:, sl], gb(b, "sug", T), None, ALU.mult)
        bufs, cur = [bufa, bufb], 0
        for pairs in _butterfly_levels(NT):
            src, dst = bufs[cur], bufs[1 - cur]
            with nc.Block() as blk:
                @blk.vector
                def _(v):
                    for (i, j) in pairs:
                        v.tensor_tensor(dst[:, i * CH:(i + 1) * CH], src[:, i * CH:(i + 1) * CH],
                                        src[:, j * CH:(j + 1) * CH], ALU.add)
                        v.tensor_tensor(dst[:, j * CH:(j + 1) * CH], src[:, i * CH:(i + 1) * CH],
                                        src[:, j * CH:(j + 1) * CH], ALU.subtract)
            cur = 1 - cur
        final = bufs[cur]
        for grp in range(NT // 4):
            with nc.Block() as blk:
                @blk.tensor
                def _(t):
                    for k in range(4):
                        T = grp * 4 + k
                        t.matmul(b.pbig1[:, k * CH:(k + 1) * CH], b.gh128[:, :],
                                 final[:, T * CH:(T + 1) * CH], start=True, stop=True)
            with nc.Block() as blk:
                @blk.scalar
                def _(a):
                    for k in range(4):
                        T = grp * 4 + k
                        a.activation(xq[:, T * TOK + c * CH: T * TOK + c * CH + CH],
                                     b.pbig1[:, k * CH:(k + 1) * CH], ACTF.Copy)
    if b.sub_l0ffn >= 2:
        emit_a2(b, xq)
    if b.sub_l0ffn >= 3:
        emit_a3(b)


def emit_a2_serial(b, xq):
    nc = b.nc
    WUP = [0, 8192]
    WG = [16384, 24576]
    WUPB = [32768, 36864]
    WGB = [40960, 45056]
    SILU = 49152
    XMID = 53248
    ZSB = [57344, 61440]
    for j in range(NJ):
        with nc.Block() as blk:
            @blk.sync
            def _(s):
                if j == 0:
                    wait_all(b, s)
                s.dma_start(out=ar(b, WUP[0], F32, NT * P),
                            in_=b.wupT[j].rearrange("l T c -> l (T c)")).then_inc(b.sW, 16)
                s.dma_start(out=ar(b, WG[0], F32, NT * P),
                            in_=b.wgT[j].rearrange("l T c -> l (T c)")).then_inc(b.sW, 16)
                b.cW += 32
                s.wait_ge(b.sW, b.cW)

            @blk.vector
            def _(v):
                v.wait_ge(b.sW, b.cW)
                v.tensor_copy(ar(b, WUPB[0], BF16, NT * P), ar(b, WUP[0], F32, NT * P))
                v.tensor_copy(ar(b, WGB[0], BF16, NT * P), ar(b, WG[0], F32, NT * P))
                v.tensor_scalar(b.gh128su[:, 0:P], b.gh128[:, :], gb(b, "sdp", j), None, ALU.mult)
        with nc.Block() as blk:
            @blk.tensor
            def _(t):
                wupb = ar(b, WUPB[0], BF16, NT * P)
                wgb = ar(b, WGB[0], BF16, NT * P)
                for T in range(NT):
                    for g in range(4):
                        t.matmul(b.pbig0[:, g * CH:(g + 1) * CH], wupb[:, T * P:(T + 1) * P],
                                 xq[:, T * TOK + g * CH: T * TOK + (g + 1) * CH],
                                 start=(T == 0), stop=(T == NT - 1))
                for T in range(NT):
                    for g in range(4):
                        t.matmul(b.pbig1[:, g * CH:(g + 1) * CH], wgb[:, T * P:(T + 1) * P],
                                 xq[:, T * TOK + g * CH: T * TOK + (g + 1) * CH],
                                 start=(T == 0), stop=(T == NT - 1))
        with nc.Block() as blk:
            @blk.scalar
            def _(a):
                a.activation(ar(b, SILU, BF16, TOK), b.pbig1[:, :], ACTF.Silu, scale=gb(b, "wsg", j))
        with nc.Block() as blk:
            @blk.vector
            def _(v):
                v.scalar_tensor_tensor(ar(b, XMID, BF16, TOK), b.pbig0[:, :], gb(b, "wsu", j),
                                       ar(b, SILU, BF16, TOK), ALU.mult, ALU.mult)
        with nc.Block() as blk:
            @blk.tensor
            def _(t):
                xmid = ar(b, XMID, BF16, TOK)
                for g in range(4):
                    t.matmul(b.pbig0[:, g * CH:(g + 1) * CH], b.gh128su[:, 0:P],
                             xmid[:, g * CH:(g + 1) * CH], start=True, stop=True)
        with nc.Block() as blk:
            @blk.scalar
            def _(a):
                a.activation(ar(b, ZSB[0], BF16, TOK), b.pbig0[:, :], ACTF.Copy)
        with nc.Block() as blk:
            @blk.gpsimd
            def _(g):
                g.dma_start(out=b.zdr[j], in_=ar(b, ZSB[0], BF16, TOK)).then_inc(b.sD, 16)
                b.cD += 16
                g.wait_ge(b.sD, b.cD)


def emit_a2_min(b, xq):
    nc = b.nc
    WUP = [0, 8192]
    WG = [16384, 24576]
    WUPB = [32768, 36864]
    WGB = [40960, 45056]
    w0, d0, p0 = b.cW, b.cDVE, b.cPE
    with nc.Block() as blk:
        @blk.sync
        def _(s):
            wait_all(b, s)
            for j in (0, 1):
                s.dma_start(out=ar(b, WUP[j % 2], F32, NT * P),
                            in_=b.wupT[j].rearrange("l T c -> l (T c)")).then_inc(b.sW, 16)
                s.dma_start(out=ar(b, WG[j % 2], F32, NT * P),
                            in_=b.wgT[j].rearrange("l T c -> l (T c)")).then_inc(b.sW, 16)
                b.cW += 32
            for j in range(NJ - 2):
                s.wait_ge(b.sDVE, d0 + j + 1)
                s.dma_start(out=ar(b, WUP[j % 2], F32, NT * P),
                            in_=b.wupT[j + 2].rearrange("l T c -> l (T c)")).then_inc(b.sW, 16)
                s.dma_start(out=ar(b, WG[j % 2], F32, NT * P),
                            in_=b.wgT[j + 2].rearrange("l T c -> l (T c)")).then_inc(b.sW, 16)
                b.cW += 32

        @blk.vector
        def _(v):
            for j in range(NJ):
                v.wait_ge(b.sW, w0 + 32 * (j + 1))
                if j >= 2:
                    v.wait_ge(b.sPE, p0 + j - 1)
                v.tensor_copy(ar(b, WUPB[j % 2], BF16, NT * P), ar(b, WUP[j % 2], F32, NT * P))
                v.tensor_copy(ar(b, WGB[j % 2], BF16, NT * P), ar(b, WG[j % 2], F32, NT * P))
                v.drain().then_inc(b.sDVE, 1)
                b.cDVE += 1

        @blk.tensor
        def _(t):
            for j in range(NJ):
                t.wait_ge(b.sDVE, d0 + j + 1)
                wupb = ar(b, WUPB[j % 2], BF16, NT * P)
                wgb = ar(b, WGB[j % 2], BF16, NT * P)
                for T in range(NT):
                    for g in range(4):
                        t.matmul(b.pbig0[:, g * CH:(g + 1) * CH], wupb[:, T * P:(T + 1) * P],
                                 xq[:, T * TOK + g * CH: T * TOK + (g + 1) * CH],
                                 start=(T == 0), stop=(T == NT - 1))
                for T in range(NT):
                    for g in range(4):
                        t.matmul(b.pbig1[:, g * CH:(g + 1) * CH], wgb[:, T * P:(T + 1) * P],
                                 xq[:, T * TOK + g * CH: T * TOK + (g + 1) * CH],
                                 start=(T == 0), stop=(T == NT - 1))
                t.drain().then_inc(b.sPE, 1)
                b.cPE += 1


def emit_a2_minsilu(b, xq):
    nc = b.nc
    WUP = [0, 8192]
    WG = [16384, 24576]
    WUPB = [32768, 36864]
    WGB = [40960, 45056]
    SILU = 49152
    w0, d0, p0, a0 = b.cW, b.cDVE, b.cPE, b.cACT
    with nc.Block() as blk:
        @blk.sync
        def _(s):
            wait_all(b, s)
            for j in (0, 1):
                s.dma_start(out=ar(b, WUP[j % 2], F32, NT * P),
                            in_=b.wupT[j].rearrange("l T c -> l (T c)")).then_inc(b.sW, 16)
                s.dma_start(out=ar(b, WG[j % 2], F32, NT * P),
                            in_=b.wgT[j].rearrange("l T c -> l (T c)")).then_inc(b.sW, 16)
                b.cW += 32
            for j in range(NJ - 2):
                s.wait_ge(b.sDVE, d0 + j + 1)
                s.dma_start(out=ar(b, WUP[j % 2], F32, NT * P),
                            in_=b.wupT[j + 2].rearrange("l T c -> l (T c)")).then_inc(b.sW, 16)
                s.dma_start(out=ar(b, WG[j % 2], F32, NT * P),
                            in_=b.wgT[j + 2].rearrange("l T c -> l (T c)")).then_inc(b.sW, 16)
                b.cW += 32

        @blk.vector
        def _(v):
            import os as _os4
            dvp = _os4.environ.get("A2_DVE_PSUM", "0") == "1"
            for j in range(NJ):
                v.wait_ge(b.sW, w0 + 32 * (j + 1))
                if j >= 2:
                    v.wait_ge(b.sPE, p0 + j - 1)
                v.tensor_copy(ar(b, WUPB[j % 2], BF16, NT * P), ar(b, WUP[j % 2], F32, NT * P))
                v.tensor_copy(ar(b, WGB[j % 2], BF16, NT * P), ar(b, WG[j % 2], F32, NT * P))
                if dvp and j > 0:
                    v.tensor_copy(ar(b, SILU, BF16, TOK), b.pbig0[:, :])
                v.drain().then_inc(b.sDVE, 1)
                b.cDVE += 1

        @blk.tensor
        def _(t):
            for j in range(NJ):
                t.wait_ge(b.sDVE, d0 + j + 1)
                if j > 0:
                    t.wait_ge(b.sACT, a0 + j)
                wupb = ar(b, WUPB[j % 2], BF16, NT * P)
                wgb = ar(b, WGB[j % 2], BF16, NT * P)
                for T in range(NT):
                    for g in range(4):
                        t.matmul(b.pbig0[:, g * CH:(g + 1) * CH], wupb[:, T * P:(T + 1) * P],
                                 xq[:, T * TOK + g * CH: T * TOK + (g + 1) * CH],
                                 start=(T == 0), stop=(T == NT - 1))
                for T in range(NT):
                    for g in range(4):
                        t.matmul(b.pbig1[:, g * CH:(g + 1) * CH], wgb[:, T * P:(T + 1) * P],
                                 xq[:, T * TOK + g * CH: T * TOK + (g + 1) * CH],
                                 start=(T == 0), stop=(T == NT - 1))
                t.drain().then_inc(b.sPE, 1)
                b.cPE += 1

        @blk.scalar
        def _(a):
            for j in range(NJ):
                a.wait_ge(b.sPE, p0 + j + 1)
                import os as _os3
                if _os3.environ.get("A2_SILU_SBUF", "0") == "1":
                    a.activation(ar(b, SILU, BF16, TOK), xq[:, 0:TOK], ACTF.Silu,
                                 scale=gb(b, "wsg", j)).then_inc(b.sACT, 1)
                else:
                    a.activation(ar(b, SILU, BF16, TOK), b.pbig1[:, :], ACTF.Silu,
                                 scale=gb(b, "wsg", j)).then_inc(b.sACT, 1)
                b.cACT += 1


def emit_a2(b, xq):
    import os as _os2
    if _os2.environ.get("A2_MODE", "pipe") == "serial":
        return emit_a2_serial(b, xq)
    if _os2.environ.get("A2_MODE", "pipe") == "min":
        return emit_a2_min(b, xq)
    if _os2.environ.get("A2_MODE", "pipe") == "minsilu":
        return emit_a2_minsilu(b, xq)
    noz = _os2.environ.get("A2_NOZ", "0") == "1"
    nc = b.nc
    WUP = [0, 8192]
    WG = [16384, 24576]
    WUPB = [32768, 36864]
    WGB = [40960, 45056]
    SILU = 49152
    XMID = 53248
    ZSB = [57344, 61440]
    w0, d0, a0, p0, dd0 = b.cW, b.cDVE, b.cACT, b.cPE, b.cD
    q0 = list(b.cQ)
    with nc.Block() as blk:
        @blk.sync
        def _(s):
            wait_all(b, s)
            for j in (0, 1):
                s.dma_start(out=ar(b, WUP[j % 2], F32, NT * P),
                            in_=b.wupT[j].rearrange("l T c -> l (T c)")).then_inc(b.sQ[j % 2], 16)
                s.dma_start(out=ar(b, WG[j % 2], F32, NT * P),
                            in_=b.wgT[j].rearrange("l T c -> l (T c)")).then_inc(b.sQ[2 + j % 2], 16)
                b.cQ[j % 2] += 16
                b.cQ[2 + j % 2] += 16
            for j in range(NJ - 2):
                s.wait_ge(b.sDVE, d0 + 2 * j + 1)
                s.dma_start(out=ar(b, WUP[j % 2], F32, NT * P),
                            in_=b.wupT[j + 2].rearrange("l T c -> l (T c)")).then_inc(b.sQ[j % 2], 16)
                s.dma_start(out=ar(b, WG[j % 2], F32, NT * P),
                            in_=b.wgT[j + 2].rearrange("l T c -> l (T c)")).then_inc(b.sQ[2 + j % 2], 16)
                b.cQ[j % 2] += 16
                b.cQ[2 + j % 2] += 16

        @blk.vector
        def _(v):
            for j in range(NJ):
                v.wait_ge(b.sQ[j % 2], q0[j % 2] + 16 * (j // 2 + 1))
                v.wait_ge(b.sQ[2 + j % 2], q0[2 + j % 2] + 16 * (j // 2 + 1))
                if j >= 2:
                    v.wait_ge(b.sPE, p0 + 3 * (j - 2) + 3)
                v.tensor_copy(ar(b, WUPB[j % 2], BF16, NT * P), ar(b, WUP[j % 2], F32, NT * P))
                v.tensor_copy(ar(b, WGB[j % 2], BF16, NT * P), ar(b, WG[j % 2], F32, NT * P))
                v.tensor_scalar(b.gh128su[:, (j % 2) * P:(j % 2 + 1) * P], b.gh128[:, :],
                                gb(b, "sdp", j), None, ALU.mult)
                v.drain().then_inc(b.sDVE, 1)
                b.cDVE += 1
                v.wait_ge(b.sACT, a0 + 2 * j + 1)
                v.wait_ge(b.sPE, p0 + 3 * j + 2)
                v.scalar_tensor_tensor(ar(b, XMID, BF16, TOK), b.pbig0[:, :], gb(b, "wsu", j),
                                       ar(b, SILU, BF16, TOK), ALU.mult, ALU.mult)
                v.drain().then_inc(b.sDVE, 1)
                b.cDVE += 1

        @blk.tensor
        def _(t):
            for j in range(NJ):
                t.wait_ge(b.sDVE, d0 + 2 * j + 1)
                if j > 0:
                    t.wait_ge(b.sACT, a0 + 2 * j - 1)
                wupb = ar(b, WUPB[j % 2], BF16, NT * P)
                wgb = ar(b, WGB[j % 2], BF16, NT * P)
                for T in range(NT):
                    for g in range(4):
                        t.matmul(b.pbig1[:, g * CH:(g + 1) * CH], wgb[:, T * P:(T + 1) * P],
                                 xq[:, T * TOK + g * CH: T * TOK + (g + 1) * CH],
                                 start=(T == 0), stop=(T == NT - 1))
                t.drain().then_inc(b.sPE, 1)
                b.cPE += 1
                if j > 0:
                    t.wait_ge(b.sACT, a0 + 2 * j)
                for T in range(NT):
                    for g in range(4):
                        t.matmul(b.pbig0[:, g * CH:(g + 1) * CH], wupb[:, T * P:(T + 1) * P],
                                 xq[:, T * TOK + g * CH: T * TOK + (g + 1) * CH],
                                 start=(T == 0), stop=(T == NT - 1))
                t.drain().then_inc(b.sPE, 1)
                b.cPE += 1
                t.wait_ge(b.sDVE, d0 + 2 * j + 2)
                if not noz:
                    xmid = ar(b, XMID, BF16, TOK)
                    for g in range(4):
                        t.matmul(b.pbig0[:, g * CH:(g + 1) * CH], b.gh128su[:, (j % 2) * P:(j % 2 + 1) * P],
                                 xmid[:, g * CH:(g + 1) * CH], start=True, stop=True)
                t.drain().then_inc(b.sPE, 1)
                b.cPE += 1

        @blk.scalar
        def _(a):
            for j in range(NJ):
                a.wait_ge(b.sPE, p0 + 3 * j + 1)
                a.activation(ar(b, SILU, BF16, TOK), b.pbig1[:, :], ACTF.Silu, scale=gb(b, "wsg", j))
                a.drain().then_inc(b.sACT, 1)
                b.cACT += 1
                a.wait_ge(b.sPE, p0 + 3 * j + 3)
                if (not noz) and j >= 2:
                    a.wait_ge(b.sQ[4 + j % 2], q0[4 + j % 2] + 16 * ((j - 2) // 2 + 1))
                if not noz:
                    a.activation(ar(b, ZSB[j % 2], BF16, TOK), b.pbig0[:, :], ACTF.Copy)
                a.drain().then_inc(b.sACT, 1)
                b.cACT += 1

        @blk.gpsimd
        def _(g):
            if not noz:
                for j in range(NJ):
                    g.wait_ge(b.sACT, a0 + 2 * j + 2)
                    g.dma_start(out=b.zdr[j], in_=ar(b, ZSB[j % 2], BF16, TOK)).then_inc(b.sQ[4 + j % 2], 16)
                    b.cQ[4 + j % 2] += 16


def emit_a3(b):
    nc = b.nc
    ZA, ZB = 0, 65536
    WD = [131072, 139264]
    WDB = [147456, 151552]
    YB, SQ = 155648, 172032
    T1 = 32768
    for c in range(NCH):
        c0 = c * CH
        za = ar(b, ZA, BF16, NJ * CH)
        zb = ar(b, ZB, BF16, NJ * CH)
        with nc.Block() as blk:
            @blk.gpsimd
            def _(g):
                wait_all(b, g)
                for j0 in range(0, NJ, 4):
                    g.dma_start(out=_t3(za[:, j0 * CH:(j0 + 4) * CH], nt=4),
                                in_=b.zdr[j0:j0 + 4, :, c0:c0 + CH].rearrange("J l t -> l J t")).then_inc(b.sD, 16)
                    b.cD += 16
        with nc.Block() as blk:
            @blk.vector
            def _(v):
                v.wait_ge(b.sD, b.cD)
        bufs, cur = [za, zb], 0
        for pairs in _butterfly_levels(NJ):
            src, dst = bufs[cur], bufs[1 - cur]
            with nc.Block() as blk:
                @blk.vector
                def _(v):
                    for (i, j) in pairs:
                        v.tensor_tensor(dst[:, i * CH:(i + 1) * CH], src[:, i * CH:(i + 1) * CH],
                                        src[:, j * CH:(j + 1) * CH], ALU.add)
                        v.tensor_tensor(dst[:, j * CH:(j + 1) * CH], src[:, i * CH:(i + 1) * CH],
                                        src[:, j * CH:(j + 1) * CH], ALU.subtract)
            cur = 1 - cur
        zfin = bufs[cur]
        ybuf = ar(b, YB, BF16, NT * CH)
        sq = ar(b, SQ, BF16, 2 * CH)
        w0, d0, a0, p0 = b.cW, b.cDVE, b.cACT, b.cPE
        q0 = list(b.cQ)
        with nc.Block() as blk:
            @blk.sync
            def _(s):
                wait_all(b, s)
                for k in range(2):
                    ob, q = divmod(k, 4)
                    s.dma_start(out=ar(b, WD[k % 2], F32, NT * P),
                                in_=b.wdT[ob, :, q * NT:(q + 1) * NT, :].rearrange("l T c -> l (T c)")).then_inc(b.sQ[k % 2], 16)
                    b.cQ[k % 2] += 16
                for k in range(4 * NT - 2):
                    ob, q = divmod(k + 2, 4)
                    s.wait_ge(b.sDVE, d0 + k + 1)
                    s.dma_start(out=ar(b, WD[k % 2], F32, NT * P),
                                in_=b.wdT[ob, :, q * NT:(q + 1) * NT, :].rearrange("l T c -> l (T c)")).then_inc(b.sQ[k % 2], 16)
                    b.cQ[k % 2] += 16

            @blk.vector
            def _(v):
                for k in range(4 * NT):
                    v.wait_ge(b.sQ[k % 2], q0[k % 2] + 16 * (k // 2 + 1))
                    if k >= 2:
                        v.wait_ge(b.sPE, p0 + k - 1)
                    v.tensor_copy(ar(b, WDB[k % 2], BF16, NT * P), ar(b, WD[k % 2], F32, NT * P))
                    v.drain().then_inc(b.sDVE, 1)
                    b.cDVE += 1

            @blk.tensor
            def _(t):
                for k in range(4 * NT):
                    ob, q = divmod(k, 4)
                    t.wait_ge(b.sDVE, d0 + k + 1)
                    if q == 0 and ob >= 2:
                        t.wait_ge(b.sACT, a0 + 2 * (ob - 1))
                    wdb = ar(b, WDB[k % 2], BF16, NT * P)
                    for Ti in range(NT):
                        i = q * NT + Ti
                        t.matmul(b.pbig1[:, (ob % 2) * CH:(ob % 2 + 1) * CH],
                                 wdb[:, Ti * P:(Ti + 1) * P], zfin[:, i * CH:(i + 1) * CH],
                                 start=(i == 0), stop=(i == NJ - 1))
                    if q == 3 and ob >= 1:
                        t.wait_ge(b.sACT, a0 + 2 * ob)
                        pob = ob - 1
                        t.matmul(b.pbig0[0:1, 0:CH], b.ginv_bf[:, 0:1], ybuf[:, pob * CH:(pob + 1) * CH],
                                 start=(pob == 0), stop=False)
                        t.matmul(b.pbig0[0:1, CH:2 * CH], b.ginv_bf[:, 0:1],
                                 sq[:, (pob % 2) * CH:(pob % 2 + 1) * CH],
                                 start=(pob == 0), stop=False)
                    t.drain().then_inc(b.sPE, 1)
                    b.cPE += 1
                t.wait_ge(b.sACT, a0 + 2 * NT)
                pob = NT - 1
                t.matmul(b.pbig0[0:1, 0:CH], b.ginv_bf[:, 0:1], ybuf[:, pob * CH:(pob + 1) * CH],
                         start=False, stop=True)
                t.matmul(b.pbig0[0:1, CH:2 * CH], b.ginv_bf[:, 0:1],
                         sq[:, (pob % 2) * CH:(pob % 2 + 1) * CH],
                         start=False, stop=True)

            @blk.scalar
            def _(a):
                for ob in range(NT):
                    a.wait_ge(b.sPE, p0 + 4 * (ob + 1))
                    a.activation(ybuf[:, ob * CH:(ob + 1) * CH], b.pbig1[:, (ob % 2) * CH:(ob % 2 + 1) * CH],
                                 ACTF.Identity, scale=gb(b, "wsd", ob))
                    a.activation(sq[:, (ob % 2) * CH:(ob % 2 + 1) * CH],
                                 ybuf[:, ob * CH:(ob + 1) * CH], ACTF.Square).then_inc(b.sACT, 2)
                    b.cACT += 2
        emit_stat_rows_and_rep(b)
        rc = ar(b, ZA, F32, NT * CH)
        t1 = ar(b, T1, F32, NT * CH)
        with nc.Block() as blk:
            @blk.gpsimd
            def _(g):
                wait_all(b, g)
                g.dma_start(out=_t3(rc), in_=b.resid[:, :, c0:c0 + CH].rearrange("T l t -> l T t")).then_inc(b.sD, 16)
                b.cD += 16
        with nc.Block() as blk:
            @blk.vector
            def _(v):
                v.wait_ge(b.sD, b.cD)
        emit_postln_resid(b, c, ybuf, rc, t1, b.resid)


def emit_l1_attn(b):
    nc = b.nc
    XLN = 67584
    for c in range(NCH):
        rc = ar(b, 0, F32, NT * CH)
        xln = ar(b, XLN, BF16, NT * CH)
        emit_preln(b, c, b.resid, affine_dst=xln)
        c0 = c * CH
        with nc.Block() as blk:
            @blk.vector
            def _(v):
                if c == 0:
                    for T in range(NT):
                        v.tensor_copy(b.gedge[:, T:T + 1], xln[:, T * CH:T * CH + 1])
                if c == NCH - 1:
                    for T in range(NT):
                        v.tensor_copy(b.gedge[:, NT + T:NT + T + 1], xln[:, T * CH + CH - 1:T * CH + CH])

            @blk.gpsimd
            def _(g):
                g.dma_start(out=b.xlndr[:, :, c0:c0 + CH].rearrange("T l t -> l T t"), in_=_t3(xln)).then_inc(b.sD, 16)
                b.cD += 16

    WG1B, WG2B, WTMP = 33280, 49664, 0
    XLNC, HT, GATE = 98816, 115200, 119296
    with nc.Block() as blk:
        @blk.sync
        def _(s):
            wait_all(b, s)
            s.dma_start(out=ar(b, WTMP, F32, NDH * NT * P).rearrange("l (r T c) -> l r (T c)", r=NDH, T=NT),
                        in_=b.wg1T.rearrange("r l T c -> l r (T c)")).then_inc(b.sW, 16)
            b.cW += 16
            s.wait_ge(b.sW, b.cW)

        @blk.vector
        def _(v):
            v.wait_ge(b.sW, b.cW)
            v.tensor_copy(ar(b, WG1B, BF16, NDH * NT * P), ar(b, WTMP, F32, NDH * NT * P))
    with nc.Block() as blk:
        @blk.sync
        def _(s):
            s.dma_start(out=ar(b, WTMP, F32, NT * NDH * P).rearrange("l (r T c) -> l r (T c)", r=NT, T=NDH),
                        in_=b.wg2T.rearrange("r l T c -> l r (T c)")).then_inc(b.sW, 16)
            b.cW += 16
            s.wait_ge(b.sW, b.cW)

        @blk.vector
        def _(v):
            v.wait_ge(b.sW, b.cW)
            v.tensor_copy(ar(b, WG2B, BF16, NT * NDH * P), ar(b, WTMP, F32, NT * NDH * P))
    XLNC2 = [XLNC, 135680]
    qb2 = [b.cQ[6], b.cQ[7]]
    with nc.Block() as blk:
        @blk.gpsimd
        def _(g):
            wait_all(b, g)
            g.dma_start(out=_t3(ar(b, XLNC2[0], BF16, NT * CH)),
                        in_=b.xlndr[:, :, 0:CH].rearrange("T l t -> l T t")).then_inc(b.sQ[6], 16)
            b.cQ[6] += 16
    for c in range(NCH):
        c0 = c * CH
        xlnc = ar(b, XLNC2[c % 2], BF16, NT * CH)
        ht = ar(b, HT, BF16, NDH * CH)
        gate = ar(b, GATE, BF16, NT * CH)
        wg1b = ar(b, WG1B, BF16, NDH * NT * P)
        wg2b = ar(b, WG2B, BF16, NT * NDH * P)
        with nc.Block() as blk:
            @blk.gpsimd
            def _(g):
                if c + 1 < NCH:
                    g.dma_start(out=_t3(ar(b, XLNC2[(c + 1) % 2], BF16, NT * CH)),
                                in_=b.xlndr[:, :, (c + 1) * CH:(c + 2) * CH].rearrange("T l t -> l T t")).then_inc(b.sQ[6 + (c + 1) % 2], 16)
                    b.cQ[6 + (c + 1) % 2] += 16

            @blk.tensor
            def _(t):
                t.wait_ge(b.sQ[6 + c % 2], qb2[c % 2] + 16 * (c // 2 + 1))
                for db in range(NDH):
                    for T in range(NT):
                        t.matmul(b.pbig1[:, db * CH:(db + 1) * CH],
                                 wg1b[:, (db * NT + T) * P:(db * NT + T + 1) * P],
                                 xlnc[:, T * CH:(T + 1) * CH], start=(T == 0), stop=(T == NT - 1))
        with nc.Block() as blk:
            @blk.scalar
            def _(a):
                for db in range(NDH):
                    a.activation(ht[:, db * CH:(db + 1) * CH], b.pbig1[:, db * CH:(db + 1) * CH],
                                 ACTF.Relu, bias=gb(b, "bg1", db))
        a0, p0 = b.cACT, b.cPE
        with nc.Block() as blk:
            @blk.tensor
            def _(t):
                for grpi in range(NT // 4):
                    if grpi > 0:
                        t.wait_ge(b.sACT, a0 + grpi)
                    for k in range(4):
                        rb = grpi * 4 + k
                        for db in range(NDH):
                            t.matmul(b.pbig0[:, k * CH:(k + 1) * CH],
                                     wg2b[:, (rb * NDH + db) * P:(rb * NDH + db + 1) * P],
                                     ht[:, db * CH:(db + 1) * CH], start=(db == 0), stop=(db == NDH - 1))
                    t.drain().then_inc(b.sPE, 1)
                    b.cPE += 1

            @blk.scalar
            def _(a):
                a.wait_ge(b.sD, b.cD)
                for grpi in range(NT // 4):
                    a.wait_ge(b.sPE, p0 + grpi + 1)
                    for k in range(4):
                        rb = grpi * 4 + k
                        a.activation(gate[:, rb * CH:(rb + 1) * CH], b.pbig0[:, k * CH:(k + 1) * CH],
                                     ACTF.Sigmoid, bias=gb(b, "bg2", rb))
                    a.drain().then_inc(b.sACT, 1)
                    b.cACT += 1
        with nc.Block() as blk:
            @blk.gpsimd
            def _(g):
                g.dma_start(out=b.gatesdr[:, :, c0:c0 + CH].rearrange("T l t -> l T t"), in_=_t3(gate)).then_inc(b.sD, 16)
                b.cD += 16

    WUBF, XLNC3, GATE3, UG, KC, WTMP3 = 0, 65536, 81920, 98304, 114688, 131072
    wubf = ar(b, WUBF, BF16, NT * NT * P)
    for rb in range(NT):
        with nc.Block() as blk:
            @blk.sync
            def _(s):
                if rb == 0:
                    wait_all(b, s)
                s.dma_start(out=ar(b, WTMP3, F32, NT * P),
                            in_=b.wuT[rb].rearrange("l T c -> l (T c)")).then_inc(b.sW, 16)
                b.cW += 16
                s.wait_ge(b.sW, b.cW)

            @blk.vector
            def _(v):
                v.wait_ge(b.sW, b.cW)
                v.tensor_copy(wubf[:, rb * NT * P:(rb + 1) * NT * P], ar(b, WTMP3, F32, NT * P))
    XLNC3P = [XLNC3, 139264]
    GATE3P = [GATE3, 155648]
    qb3 = [b.cQ[6], b.cQ[7]]
    with nc.Block() as blk:
        @blk.gpsimd
        def _(g):
            wait_all(b, g)
            g.dma_start(out=_t3(ar(b, XLNC3P[0], BF16, NT * CH)),
                        in_=b.xlndr[:, :, 0:CH].rearrange("T l t -> l T t")).then_inc(b.sQ[6], 16)
            g.dma_start(out=_t3(ar(b, GATE3P[0], BF16, NT * CH)),
                        in_=b.gatesdr[:, :, 0:CH].rearrange("T l t -> l T t")).then_inc(b.sQ[6], 16)
            b.cQ[6] += 32
    for c in range(NCH):
        c0 = c * CH
        xlnc = ar(b, XLNC3P[c % 2], BF16, NT * CH)
        gate = ar(b, GATE3P[c % 2], BF16, NT * CH)
        ug = ar(b, UG, BF16, NT * CH)
        kc = ar(b, KC, BF16, NT * CH)
        with nc.Block() as blk:
            @blk.gpsimd
            def _(g):
                if c + 1 < NCH:
                    g.dma_start(out=_t3(ar(b, XLNC3P[(c + 1) % 2], BF16, NT * CH)),
                                in_=b.xlndr[:, :, (c + 1) * CH:(c + 2) * CH].rearrange("T l t -> l T t")).then_inc(b.sQ[6 + (c + 1) % 2], 16)
                    g.dma_start(out=_t3(ar(b, GATE3P[(c + 1) % 2], BF16, NT * CH)),
                                in_=b.gatesdr[:, :, (c + 1) * CH:(c + 2) * CH].rearrange("T l t -> l T t")).then_inc(b.sQ[6 + (c + 1) % 2], 16)
                    b.cQ[6 + (c + 1) % 2] += 32
        a0, p0 = b.cACT, b.cPE
        with nc.Block() as blk:
            @blk.tensor
            def _(t):
                t.wait_ge(b.sQ[6 + c % 2], qb3[c % 2] + 32 * (c // 2 + 1))
                for grpi in range(NT // 4):
                    if grpi > 0:
                        t.wait_ge(b.sACT, a0 + grpi)
                    for k in range(4):
                        rb = grpi * 4 + k
                        for T in range(NT):
                            t.matmul(b.pbig0[:, k * CH:(k + 1) * CH],
                                     wubf[:, (rb * NT + T) * P:(rb * NT + T + 1) * P],
                                     xlnc[:, T * CH:(T + 1) * CH], start=(T == 0), stop=(T == NT - 1))
                    t.drain().then_inc(b.sPE, 1)
                    b.cPE += 1

            @blk.scalar
            def _(a):
                for grpi in range(NT // 4):
                    a.wait_ge(b.sPE, p0 + grpi + 1)
                    for k in range(4):
                        rb = grpi * 4 + k
                        a.activation(ug[:, rb * CH:(rb + 1) * CH], b.pbig0[:, k * CH:(k + 1) * CH],
                                     ACTF.Identity, bias=gb(b, "bu", rb))
                    a.drain().then_inc(b.sACT, 1)
                    b.cACT += 1
        with nc.Block() as blk:
            @blk.vector
            def _(v):
                for rb in range(NT):
                    sl = slice(rb * CH, (rb + 1) * CH)
                    v.tensor_tensor(ug[:, sl], ug[:, sl], gate[:, sl], ALU.mult)
        with nc.Block() as blk:
            @blk.vector
            def _(v):
                v.wait_ge(b.sD, b.cD)
                for rb in range(NT):
                    sl = slice(rb * CH, (rb + 1) * CH)
                    v.tensor_tensor_scan(kc[:, sl], ug[:, sl], ug[:, sl], b.gstkc[:, rb:rb + 1],
                                         ALU.add, ALU.bypass)
                for rb in range(NT):
                    v.tensor_copy(b.gstkc[:, rb:rb + 1], kc[:, rb * CH + CH - 1:rb * CH + CH])
        with nc.Block() as blk:
            @blk.gpsimd
            def _(g):
                g.dma_start(out=b.kcdr[:, :, c0:c0 + CH].rearrange("T l t -> l T t"), in_=_t3(kc)).then_inc(b.sD, 16)
                b.cD += 16

    CONTRIB = 131072
    contrib = ar(b, CONTRIB, F32, 48)
    with nc.Block() as blk:
        @blk.vector
        def _(v):
            v.tensor_scalar(contrib[:, 0:NT], b.gedge[:, NT:2 * NT], gb(b, "iseven"), None, ALU.mult)
            v.tensor_scalar(contrib[:, NT:2 * NT], b.gedge[:, 0:NT], gb(b, "isodd"), None, ALU.mult)
            v.tensor_scalar(contrib[:, 2 * NT:3 * NT], b.gstkc[:, :], gb(b, "iseven"), None, ALU.mult)
    with nc.Block() as blk:
        @blk.gpsimd
        def _(g):
            wait_all(b, g)
            g.dma_start(out=b.cc1i.ap(), in_=contrib).then_inc(b.sD, 16)
            b.cD += 16
            g.wait_ge(b.sD, b.cD)
            g.collective_compute("AllReduce", ALU.add, replica_groups=[[0, 1], [2, 3], [4, 5], [6, 7]],
                                 ins=[b.cc1i.ap()], outs=[b.cc1o.ap()]).then_inc(b.sCC, 1)
            b.cCC += 1
            g.wait_ge(b.sCC, b.cCC)
            g.dma_start(out=b.ghalo[:, :], in_=b.cc1o.ap()).then_inc(b.sD, 16)
            b.cD += 16
    with nc.Block() as blk:
        @blk.vector
        def _(v):
            v.wait_ge(b.sD, b.cD)
            v.tensor_scalar(b.gedge[:, 2 * NT:3 * NT], b.ghalo[:, 0:NT], gb(b, "isodd"), None, ALU.mult)
            v.tensor_scalar(b.gedge[:, 3 * NT:4 * NT], b.ghalo[:, NT:2 * NT], gb(b, "iseven"), None, ALU.mult)
            v.tensor_scalar(b.gstkc[:, :], b.ghalo[:, 2 * NT:3 * NT], gb(b, "isodd"), None, ALU.mult)

    WV1 = [65536, 73728]
    WV1B = [81920, 86016]
    XLNC4, GATE4, VG, KCC, GO = 90112, 106496, 122880, 139264, 155648
    XPAD, TC1, EDGL, EDGR, YB4, SQ4 = 172032, 175104, 177664, 178176, 106496, 65536
    RC4, T14 = 122880, 65536
    for c in range(NCH):
        c0 = c * CH
        xlnc = ar(b, XLNC4, BF16, NT * CH)
        gate = ar(b, GATE4, BF16, NT * CH)
        vg = ar(b, VG, BF16, NT * CH)
        kcc = ar(b, KCC, BF16, NT * CH)
        go = ar(b, GO, BF16, NT * CH)
        ybuf = ar(b, YB4, BF16, NT * CH)
        sq = ar(b, SQ4, BF16, 2 * CH)
        edgl = ar(b, EDGL, BF16, NT)
        edgr = ar(b, EDGR, BF16, NT)
        with nc.Block() as blk:
            @blk.gpsimd
            def _(g):
                wait_all(b, g)
                g.dma_start(out=_t3(xlnc), in_=b.xlndr[:, :, c0:c0 + CH].rearrange("T l t -> l T t")).then_inc(b.sD, 16)
                g.dma_start(out=_t3(gate), in_=b.gatesdr[:, :, c0:c0 + CH].rearrange("T l t -> l T t")).then_inc(b.sD, 16)
                g.dma_start(out=_t3(kcc), in_=b.kcdr[:, :, c0:c0 + CH].rearrange("T l t -> l T t")).then_inc(b.sD, 16)
                b.cD += 48
                if c > 0:
                    g.dma_start(out=_t3(edgl), in_=b.xlndr[:, :, c0 - 1:c0].rearrange("T l t -> l T t"), allow_slow_non_contiguous=True).then_inc(b.sD, 16)
                    b.cD += 16
                if c < NCH - 1:
                    g.dma_start(out=_t3(edgr), in_=b.xlndr[:, :, c0 + CH:c0 + CH + 1].rearrange("T l t -> l T t"), allow_slow_non_contiguous=True).then_inc(b.sD, 16)
                    b.cD += 16
        w0, d0, a0, p0 = b.cW, b.cDVE, b.cACT, b.cPE
        q0 = list(b.cQ)
        with nc.Block() as blk:
            @blk.sync
            def _(s):
                for rb in (0, 1):
                    s.dma_start(out=ar(b, WV1[rb % 2], F32, NT * P),
                                in_=b.wvT1[rb].rearrange("l T c -> l (T c)")).then_inc(b.sQ[rb % 2], 16)
                    b.cQ[rb % 2] += 16
                for rb in range(NT - 2):
                    s.wait_ge(b.sDVE, d0 + rb + 1)
                    s.dma_start(out=ar(b, WV1[rb % 2], F32, NT * P),
                                in_=b.wvT1[rb + 2].rearrange("l T c -> l (T c)")).then_inc(b.sQ[rb % 2], 16)
                    b.cQ[rb % 2] += 16

            @blk.vector
            def _(v):
                for rb in range(NT):
                    v.wait_ge(b.sQ[rb % 2], q0[rb % 2] + 16 * (rb // 2 + 1))
                    if rb >= 2:
                        v.wait_ge(b.sPE, p0 + rb - 1)
                    v.tensor_copy(ar(b, WV1B[rb % 2], BF16, NT * P), ar(b, WV1[rb % 2], F32, NT * P))
                    v.drain().then_inc(b.sDVE, 1)
                    b.cDVE += 1

            @blk.tensor
            def _(t):
                t.wait_ge(b.sD, b.cD)
                for rb in range(NT):
                    t.wait_ge(b.sDVE, d0 + rb + 1)
                    if rb >= 4:
                        t.wait_ge(b.sACT, a0 + rb - 3)
                    wv1b = ar(b, WV1B[rb % 2], BF16, NT * P)
                    for T in range(NT):
                        t.matmul(b.pbig1[:, (rb % 4) * CH:(rb % 4 + 1) * CH],
                                 wv1b[:, T * P:(T + 1) * P], xlnc[:, T * CH:(T + 1) * CH],
                                 start=(T == 0), stop=(T == NT - 1))
                    t.drain().then_inc(b.sPE, 1)
                    b.cPE += 1

            @blk.scalar
            def _(a):
                for rb in range(NT):
                    a.wait_ge(b.sPE, p0 + rb + 1)
                    a.activation(vg[:, rb * CH:(rb + 1) * CH], b.pbig1[:, (rb % 4) * CH:(rb % 4 + 1) * CH],
                                 ACTF.Identity, bias=gb(b, "bv1", rb)).then_inc(b.sACT, 1)
                    b.cACT += 1
        with nc.Block() as blk:
            @blk.vector
            def _(v):
                for rb in range(NT):
                    sl = slice(rb * CH, (rb + 1) * CH)
                    v.tensor_tensor(vg[:, sl], vg[:, sl], gate[:, sl], ALU.mult)
                for rb in range(NT):
                    sl = slice(rb * CH, (rb + 1) * CH)
                    v.scalar_tensor_tensor(go[:, sl], kcc[:, sl], b.gstkc[:, rb:rb + 1], vg[:, sl],
                                           ALU.add, ALU.mult)
        xpad = ar(b, XPAD, F32, CH + 2)
        tc1 = ar(b, TC1, F32, CH)
        w0, d0, a0, p0 = b.cW, b.cDVE, b.cACT, b.cPE
        with nc.Block() as blk:
            @blk.tensor
            def _(t):
                for ob in range(NT):
                    if ob >= 4:
                        t.wait_ge(b.sDVE, d0 + ob - 3)
                    for rT in range(NT):
                        t.matmul(b.pbig1[:, (ob % 4) * CH:(ob % 4 + 1) * CH],
                                 wubf[:, (ob * NT + rT) * P:(ob * NT + rT + 1) * P],
                                 go[:, rT * CH:(rT + 1) * CH], start=(rT == 0), stop=(rT == NT - 1))
                    if ob >= 1:
                        t.wait_ge(b.sACT, a0 + ob)
                        pob = ob - 1
                        t.matmul(b.pbig0[0:1, 0:CH], b.ginv_bf[:, 0:1], ybuf[:, pob * CH:(pob + 1) * CH],
                                 start=(pob == 0), stop=False)
                        t.matmul(b.pbig0[0:1, CH:2 * CH], b.ginv_bf[:, 0:1],
                                 sq[:, (pob % 2) * CH:(pob % 2 + 1) * CH],
                                 start=(pob == 0), stop=False)
                    t.drain().then_inc(b.sPE, 1)
                    b.cPE += 1
                t.wait_ge(b.sACT, a0 + NT)
                pob = NT - 1
                t.matmul(b.pbig0[0:1, 0:CH], b.ginv_bf[:, 0:1], ybuf[:, pob * CH:(pob + 1) * CH],
                         start=False, stop=True)
                t.matmul(b.pbig0[0:1, CH:2 * CH], b.ginv_bf[:, 0:1],
                         sq[:, (pob % 2) * CH:(pob % 2 + 1) * CH],
                         start=False, stop=True)

            @blk.vector
            def _(v):
                for ob in range(NT):
                    v.wait_ge(b.sPE, p0 + ob + 1)
                    v.tensor_copy(xpad[:, 1:CH + 1], xlnc[:, ob * CH:(ob + 1) * CH])
                    if c == 0:
                        v.tensor_copy(xpad[:, 0:1], b.gedge[:, 2 * NT + ob:2 * NT + ob + 1])
                    else:
                        v.tensor_copy(xpad[:, 0:1], edgl[:, ob:ob + 1])
                    if c == NCH - 1:
                        v.tensor_copy(xpad[:, CH + 1:CH + 2], b.gedge[:, 3 * NT + ob:3 * NT + ob + 1])
                    else:
                        v.tensor_copy(xpad[:, CH + 1:CH + 2], edgr[:, ob:ob + 1])
                    v.tensor_scalar(tc1, xpad[:, 1:CH + 1], gb(b, "cw", NT + ob), gb(b, "convb", ob),
                                    ALU.mult, ALU.add)
                    v.scalar_tensor_tensor(tc1, xpad[:, 0:CH], gb(b, "cw", ob), tc1, ALU.mult, ALU.add)
                    v.scalar_tensor_tensor(tc1, xpad[:, 2:CH + 2], gb(b, "cw", 2 * NT + ob), tc1, ALU.mult, ALU.add)
                    v.tensor_tensor(ybuf[:, ob * CH:(ob + 1) * CH], tc1,
                                    b.pbig1[:, (ob % 4) * CH:(ob % 4 + 1) * CH], ALU.add)
                    v.drain().then_inc(b.sDVE, 1)
                    b.cDVE += 1

            @blk.scalar
            def _(a):
                for ob in range(NT):
                    a.wait_ge(b.sDVE, d0 + ob + 1)
                    a.activation(sq[:, (ob % 2) * CH:(ob % 2 + 1) * CH],
                                 ybuf[:, ob * CH:(ob + 1) * CH], ACTF.Square).then_inc(b.sACT, 1)
                    b.cACT += 1
        emit_stat_rows_and_rep(b)
        rc = ar(b, RC4, F32, NT * CH)
        t1 = ar(b, T14, F32, NT * CH)
        with nc.Block() as blk:
            @blk.gpsimd
            def _(g):
                wait_all(b, g)
                g.dma_start(out=_t3(rc), in_=b.resid[:, :, c0:c0 + CH].rearrange("T l t -> l T t")).then_inc(b.sD, 16)
                b.cD += 16
        with nc.Block() as blk:
            @blk.vector
            def _(v):
                v.wait_ge(b.sD, b.cD)
        emit_postln_resid(b, c, ybuf, rc, t1, b.resid)


def emit_l1_ffn(b, final_out):
    nc = b.nc
    XLN4 = 67584
    for c in range(NCH):
        rc = ar(b, 0, F32, NT * CH)
        xln4 = ar(b, XLN4, F32, NT * CH)
        emit_preln(b, c, b.resid, affine_dst=xln4)
        c0 = c * CH
        with nc.Block() as blk:
            @blk.vector
            def _(v):
                v.tensor_reduce(b.gmax[:, 2:3], xln4, AX.X, ALU.max, apply_absolute_value=True)
                vpad(b, v)
                v.tensor_tensor(b.gmax[:, 0:1], b.gmax[:, 0:1], b.gmax[:, 2:3], ALU.max)

            @blk.gpsimd
            def _(g):
                g.dma_start(out=b.xln4dr[:, :, c0:c0 + CH].rearrange("T l t -> l T t"), in_=_t3(xln4)).then_inc(b.sD, 16)
                b.cD += 16
    emit_scalar_allreduce(b, [(0, 0), (3, 1), (4, 2)], b.cc2i, b.cc2o)
    with nc.Block() as blk:
        @blk.vector
        def _(v):
            v.tensor_single_scalar(b.grow1[:, 3:6], b.grow1[:, 0:3], 1.0 / 448.0, ALU.mult)
            vpad(b, v)
            v.tensor_single_scalar(b.grow1[:, 3:6], b.grow1[:, 3:6], 1e-6, ALU.max)
            vpad(b, v)
            v.reciprocal(b.grow1[:, 6:9], b.grow1[:, 3:6])
            vpad(b, v)
            v.tensor_single_scalar(b.grow1[:, 6:9], b.grow1[:, 6:9], 0.5, ALU.mult)
            vpad(b, v)
    with nc.Block() as blk:
        @blk.tensor
        def _(t):
            t.matmul(b.pbig0[:, 0:3], b.gones_row[:, :], b.grow1[:, 6:9], start=True, stop=True)
    with nc.Block() as blk:
        @blk.vector
        def _(v):
            v.tensor_copy(b.gscal[:, 0:3], b.pbig0[:, 0:3])

    X8, XL4C, QT = 0, 33280, 66560
    x8 = ar(b, X8, FP8, NT * TOK)
    for c in range(NCH):
        c0 = c * CH
        xl4c = ar(b, XL4C, F32, NT * CH)
        qt = ar(b, QT, F32, NT * CH)
        with nc.Block() as blk:
            @blk.gpsimd
            def _(g):
                wait_all(b, g)
                g.dma_start(out=_t3(xl4c), in_=b.xln4dr[:, :, c0:c0 + CH].rearrange("T l t -> l T t")).then_inc(b.sD, 16)
                b.cD += 16
        with nc.Block() as blk:
            @blk.vector
            def _(v):
                v.wait_ge(b.sD, b.cD)
                v.tensor_scalar(qt, xl4c, b.gscal[:, 0:1], None, ALU.mult)
                v.tensor_copy(x8.rearrange("l (T t) -> l T t", T=NT)[:, :, c0:c0 + CH],
                              qt.rearrange("l (T t) -> l T t", T=NT))
    emit_c2_fc(b, x8)
    emit_scalar_allreduce(b, [(1, 0)], b.cc3i, b.cc3o)
    with nc.Block() as blk:
        @blk.vector
        def _(v):
            v.tensor_tensor(b.grow1[:, 9:10], b.grow1[:, 3:4], b.grow1[:, 4:5], ALU.mult)
            vpad(b, v)
            v.tensor_single_scalar(b.grow1[:, 9:10], b.grow1[:, 9:10], 4.0, ALU.mult)
            vpad(b, v)
            v.tensor_tensor(b.grow1[:, 10:11], b.grow1[:, 9:10], b.grow1[:, 9:10], ALU.mult)
            vpad(b, v)
            v.tensor_tensor(b.grow1[:, 11:12], b.grow1[:, 0:1], b.grow1[:, 10:11], ALU.mult)
            vpad(b, v)
            v.tensor_single_scalar(b.grow1[:, 11:12], b.grow1[:, 11:12], 1.0 / 448.0, ALU.mult)
            vpad(b, v)
            v.tensor_single_scalar(b.grow1[:, 11:12], b.grow1[:, 11:12], 1e-6, ALU.max)
            vpad(b, v)
            v.tensor_single_scalar(b.grow1[:, 12:13], b.grow1[:, 11:12], 2.0, ALU.mult)
            vpad(b, v)
            v.reciprocal(b.grow1[:, 13:14], b.grow1[:, 12:13])
            vpad(b, v)
            v.tensor_tensor(b.grow1[:, 13:14], b.grow1[:, 10:11], b.grow1[:, 13:14], ALU.mult)
            vpad(b, v)
            v.tensor_tensor(b.grow1[:, 14:15], b.grow1[:, 11:12], b.grow1[:, 5:6], ALU.mult)
            vpad(b, v)
            v.tensor_single_scalar(b.grow1[:, 14:15], b.grow1[:, 14:15], 4.0, ALU.mult)
            vpad(b, v)
    with nc.Block() as blk:
        @blk.tensor
        def _(t):
            t.matmul(b.pbig0[:, 0:2], b.gones_row[:, :], b.grow1[:, 13:15], start=True, stop=True)
    with nc.Block() as blk:
        @blk.vector
        def _(v):
            v.tensor_copy(b.gscal[:, 3:5], b.pbig0[:, 0:2])
    emit_c4_proj(b)
    emit_c5_final(b, final_out)


def emit_scalar_allreduce(b, colmap, cci, cco):
    nc = b.nc
    with nc.Block() as blk:
        @blk.tensor
        def _(t):
            for i, (src, _) in enumerate(colmap):
                t.matmul(b.pbig0[0:1, i * P:(i + 1) * P], b.gmax[:, src:src + 1], b.geye[:, :],
                         start=True, stop=True)
    with nc.Block() as blk:
        @blk.vector
        def _(v):
            for i, (_, dst) in enumerate(colmap):
                v.reduce_max(b.grows2[:, i:i + 1], b.pbig0[0:1, i * P:(i + 1) * P], AX.X)
            vpad(b, v)
            v.memset(b.gcc2[:, :], 0.0)
            for i in range(len(colmap)):
                for s in range(8):
                    v.tensor_copy(b.gcc2[:, 4 * s + i:4 * s + i + 1], b.grows2[:, i:i + 1])
            vpad(b, v)
            v.tensor_tensor(b.gcc2[:, :], b.gcc2[:, :], b.gflags[:, 32:64], ALU.mult)
    with nc.Block() as blk:
        @blk.gpsimd
        def _(g):
            wait_all(b, g)
            g.dma_start(out=cci.ap(), in_=b.gcc2[:, :]).then_inc(b.sD, 16)
            b.cD += 16
            g.wait_ge(b.sD, b.cD)
            rounds = [
                ([[0, 1], [2, 3], [4, 5], [6, 7]], cci.ap(), cco.ap()),
                ([[0, 2], [1, 3], [4, 6], [5, 7]], cco.ap(), b.ccmb.ap()),
                ([[0, 4], [1, 5], [2, 6], [3, 7]], b.ccmb.ap(), b.ccmc.ap()),
            ]
            for rg, rin, rout in rounds:
                g.collective_compute("AllReduce", ALU.add, replica_groups=rg,
                                     ins=[rin], outs=[rout]).then_inc(b.sCC, 1)
                b.cCC += 1
                g.wait_ge(b.sCC, b.cCC)
            g.dma_start(out=b.gcc2[:, :], in_=b.ccmc.ap()).then_inc(b.sD, 16)
            b.cD += 16
    with nc.Block() as blk:
        @blk.vector
        def _(v):
            v.wait_ge(b.sD, b.cD)
            v.reduce_max(b.grows2[:, 0:4], b.gcc2.rearrange("o (s k) -> o k s", s=8), AX.X)
            vpad(b, v)
            for i, (_, dst) in enumerate(colmap):
                v.tensor_copy(b.grow1[:, dst:dst + 1], b.grows2[:, i:i + 1])


def emit_c2_fc(b, x8):
    nc = b.nc
    WFC = [99840, 108032]
    W8FC = [116224, 118272]
    QW = 120320
    RELU = 129024
    HS2 = [137728, 141824, 145920, 150016]
    w0, d0, a0, p0, dd0 = b.cW, b.cDVE, b.cACT, b.cPE, b.cD
    q0 = list(b.cQ)
    with nc.Block() as blk:
        @blk.sync
        def _(s):
            wait_all(b, s)
            for j in (0, 1):
                s.dma_start(out=ar(b, WFC[j % 2], F32, NT * P),
                            in_=b.wfcT[j].rearrange("l T c -> l (T c)")).then_inc(b.sQ[j % 2], 16)
                b.cQ[j % 2] += 16
            for j in range(NJ - 2):
                s.wait_ge(b.sDVE, d0 + 2 * j + 1)
                s.dma_start(out=ar(b, WFC[j % 2], F32, NT * P),
                            in_=b.wfcT[j + 2].rearrange("l T c -> l (T c)")).then_inc(b.sQ[j % 2], 16)
                b.cQ[j % 2] += 16

        @blk.vector
        def _(v):
            for j in range(NJ):
                v.wait_ge(b.sQ[j % 2], q0[j % 2] + 16 * (j // 2 + 1))
                if j >= 2:
                    v.wait_ge(b.sPE, p0 + j - 1)
                v.tensor_scalar(ar(b, QW, F32, NT * P), ar(b, WFC[j % 2], F32, NT * P),
                                b.gscal[:, 1:2], None, ALU.mult)
                v.tensor_copy(ar(b, W8FC[j % 2], FP8, NT * P), ar(b, QW, F32, NT * P))
                v.drain().then_inc(b.sDVE, 2)
                b.cDVE += 2
                if j >= 3:
                    jm = j - 3
                    v.wait_ge(b.sACT, a0 + 2 * jm + 2)
                    v.tensor_reduce(b.gmax[:, 2:3], ar(b, HS2[jm % 4], BF16, TOK), AX.X, ALU.max)
                    vpad(b, v)
                    v.tensor_tensor(b.gmax[:, 1:2], b.gmax[:, 1:2], b.gmax[:, 2:3], ALU.max)
            for jm in range(NJ - 3, NJ):
                v.wait_ge(b.sACT, a0 + 2 * jm + 2)
                v.tensor_reduce(b.gmax[:, 2:3], ar(b, HS2[jm % 4], BF16, TOK), AX.X, ALU.max)
                vpad(b, v)
                v.tensor_tensor(b.gmax[:, 1:2], b.gmax[:, 1:2], b.gmax[:, 2:3], ALU.max)

        @blk.tensor
        def _(t):
            for j in range(NJ):
                t.wait_ge(b.sDVE, d0 + 2 * j + 1)
                if j >= 2:
                    t.wait_ge(b.sACT, a0 + 2 * (j - 1))
                w8 = ar(b, W8FC[j % 2], FP8, NT * P)
                pb = b.pbig0 if j % 2 == 0 else b.pbig1
                for T in range(NT):
                    for g in range(4):
                        t.matmul(pb[:, g * CH:(g + 1) * CH], w8[:, T * P:(T + 1) * P],
                                 x8.rearrange("l (T t) -> l T t", T=NT)[:, T, g * CH:(g + 1) * CH],
                                 start=(T == 0), stop=(T == NT - 1))
                t.drain().then_inc(b.sPE, 1)
                b.cPE += 1

        @blk.scalar
        def _(a):
            for j in range(NJ):
                a.wait_ge(b.sPE, p0 + j + 1)
                if j >= 4:
                    a.wait_ge(b.sQ[4 + j % 4], q0[4 + j % 4] + 16 * ((j - 4) // 4 + 1))
                pb = b.pbig0 if j % 2 == 0 else b.pbig1
                a.activation(ar(b, RELU, F32, TOK), pb[:, :], ACTF.Relu)
                a.drain().then_inc(b.sACT, 1)
                b.cACT += 1
                a.activation(ar(b, HS2[j % 4], BF16, TOK), ar(b, RELU, F32, TOK), ACTF.Square)
                a.drain().then_inc(b.sACT, 1)
                b.cACT += 1

        @blk.gpsimd
        def _(g):
            for j in range(NJ):
                g.wait_ge(b.sACT, a0 + 2 * j + 2)
                g.dma_start(out=b.hdr[j], in_=ar(b, HS2[j % 4], BF16, TOK)).then_inc(b.sQ[4 + j % 4], 16)
                b.cQ[4 + j % 4] += 16


def emit_c4_proj(b):
    nc = b.nc
    H8 = 0
    WPJ = [131072, 139264]
    W8PJ = [147456, 155648]
    QW4 = 164352
    HS2L = [173056, 177152]
    YT = [181760, 185856]
    h8 = ar(b, H8, FP8, NJ * TOK)
    d0, dd0 = b.cDVE, b.cD
    q0 = list(b.cQ)
    with nc.Block() as blk:
        @blk.gpsimd
        def _(g):
            wait_all(b, g)
            for j in range(NJ):
                if j >= 2:
                    g.wait_ge(b.sDVE, d0 + j - 1)
                g.dma_start(out=ar(b, HS2L[j % 2], BF16, TOK), in_=b.hdr[j]).then_inc(b.sQ[6 + j % 2], 16)
                b.cQ[6 + j % 2] += 16

        @blk.vector
        def _(v):
            for j in range(NJ):
                v.wait_ge(b.sQ[6 + j % 2], q0[6 + j % 2] + 16 * (j // 2 + 1))
                v.tensor_scalar(ar(b, QW4, F32, TOK), ar(b, HS2L[j % 2], BF16, TOK),
                                b.gscal[:, 3:4], None, ALU.mult)
                v.tensor_copy(h8[:, j * TOK:(j + 1) * TOK], ar(b, QW4, F32, TOK))
                v.drain().then_inc(b.sDVE, 1)
                b.cDVE += 1
    w0, d0, a0, p0, dd4 = b.cW, b.cDVE, b.cACT, b.cPE, b.cD
    q0 = list(b.cQ)
    with nc.Block() as blk:
        @blk.sync
        def _(s):
            wait_all(b, s)
            for k in range(2):
                ob, q = divmod(k, 4)
                s.dma_start(out=ar(b, WPJ[k % 2], F32, NT * P),
                            in_=b.wpjT[ob, :, q * NT:(q + 1) * NT, :].rearrange("l T c -> l (T c)")).then_inc(b.sQ[k % 2], 16)
                b.cQ[k % 2] += 16
            for k in range(4 * NT - 2):
                ob, q = divmod(k + 2, 4)
                s.wait_ge(b.sDVE, d0 + k + 1)
                s.dma_start(out=ar(b, WPJ[k % 2], F32, NT * P),
                            in_=b.wpjT[ob, :, q * NT:(q + 1) * NT, :].rearrange("l T c -> l (T c)")).then_inc(b.sQ[k % 2], 16)
                b.cQ[k % 2] += 16

        @blk.vector
        def _(v):
            for k in range(4 * NT):
                ob, q = divmod(k, 4)
                v.wait_ge(b.sQ[k % 2], q0[k % 2] + 16 * (k // 2 + 1))
                if q == 0 and ob >= 2:
                    v.wait_ge(b.sPE, p0 + ob - 1)
                v.tensor_scalar(ar(b, QW4, F32, NT * P), ar(b, WPJ[k % 2], F32, NT * P),
                                b.gscal[:, 2:3], None, ALU.mult)
                v.tensor_copy(ar(b, W8PJ[ob % 2] + q * 2048, FP8, NT * P), ar(b, QW4, F32, NT * P))
                v.drain().then_inc(b.sDVE, 1)
                b.cDVE += 1

        @blk.tensor
        def _(t):
            for ob in range(NT):
                t.wait_ge(b.sDVE, d0 + 4 * (ob + 1))
                if ob >= 2:
                    t.wait_ge(b.sACT, a0 + ob - 1)
                pb = b.pbig0 if ob % 2 == 0 else b.pbig1
                for q in range(4):
                    w8 = ar(b, W8PJ[ob % 2] + q * 2048, FP8, NT * P)
                    for Ti in range(NT):
                        i = q * NT + Ti
                        for g in range(4):
                            t.matmul(pb[:, g * CH:(g + 1) * CH], w8[:, Ti * P:(Ti + 1) * P],
                                     h8[:, i * TOK + g * CH: i * TOK + (g + 1) * CH],
                                     start=(i == 0), stop=(i == NJ - 1))
                t.drain().then_inc(b.sPE, 1)
                b.cPE += 1

        @blk.scalar
        def _(a):
            for ob in range(NT):
                a.wait_ge(b.sPE, p0 + ob + 1)
                if ob >= 2:
                    a.wait_ge(b.sQ[2 + ob % 2], q0[2 + ob % 2] + 16 * ((ob - 2) // 2 + 1))
                pb = b.pbig0 if ob % 2 == 0 else b.pbig1
                a.activation(ar(b, YT[ob % 2], BF16, TOK), pb[:, :], ACTF.Identity,
                             scale=b.gscal[:, 4:5])
                a.drain().then_inc(b.sACT, 1)
                b.cACT += 1

        @blk.gpsimd
        def _(g):
            for ob in range(NT):
                g.wait_ge(b.sACT, a0 + ob + 1)
                g.dma_start(out=b.ydr[ob], in_=ar(b, YT[ob % 2], BF16, TOK)).then_inc(b.sQ[2 + ob % 2], 16)
                b.cQ[2 + ob % 2] += 16


def emit_c5_final(b, final_out):
    nc = b.nc
    YCP = [0, 99840]
    Y2C = 16384
    RCP = [33280, 116736]
    T1 = 66560
    qc5 = [b.cQ[6], b.cQ[7]]
    with nc.Block() as blk:
        @blk.gpsimd
        def _(g):
            wait_all(b, g)
            g.dma_start(out=_t3(ar(b, YCP[0], BF16, NT * CH)),
                        in_=b.ydr[:, :, 0:CH].rearrange("T l t -> l T t")).then_inc(b.sQ[6], 16)
            g.dma_start(out=_t3(ar(b, RCP[0], F32, NT * CH)),
                        in_=b.resid[:, :, 0:CH].rearrange("T l t -> l T t")).then_inc(b.sQ[6], 16)
            b.cQ[6] += 32
    for c in range(NCH):
        c0 = c * CH
        yc = ar(b, YCP[c % 2], BF16, NT * CH)
        y2c = ar(b, Y2C, BF16, NT * CH)
        rc = ar(b, RCP[c % 2], F32, NT * CH)
        t1 = ar(b, T1, F32, NT * CH)
        with nc.Block() as blk:
            @blk.gpsimd
            def _(g):
                if c + 1 < NCH:
                    g.dma_start(out=_t3(ar(b, YCP[(c + 1) % 2], BF16, NT * CH)),
                                in_=b.ydr[:, :, (c + 1) * CH:(c + 2) * CH].rearrange("T l t -> l T t")).then_inc(b.sQ[6 + (c + 1) % 2], 16)
                    g.dma_start(out=_t3(ar(b, RCP[(c + 1) % 2], F32, NT * CH)),
                                in_=b.resid[:, :, (c + 1) * CH:(c + 2) * CH].rearrange("T l t -> l T t")).then_inc(b.sQ[6 + (c + 1) % 2], 16)
                    b.cQ[6 + (c + 1) % 2] += 32

            @blk.scalar
            def _(a):
                a.wait_ge(b.sQ[6 + c % 2], qc5[c % 2] + 32 * (c // 2 + 1))
                a.activation(y2c, yc, ACTF.Square)
        emit_stats_from(b, yc, y2c)
        emit_postln_resid(b, c, yc, rc, t1, final_out)


def emit_out_debug(b):
    nc = b.nc
    for c in range(NCH):
        c0 = c * CH
        rc = ar(b, 0, F32, NT * CH)
        with nc.Block() as blk:
            @blk.gpsimd
            def _(g):
                wait_all(b, g)
                g.dma_start(out=_t3(rc), in_=b.resid[:, :, c0:c0 + CH].rearrange("T l t -> l T t")).then_inc(b.sD, 16)
                b.cD += 16
                g.wait_ge(b.sD, b.cD)
                g.dma_start(out=b.outT[:, :, c0:c0 + CH].rearrange("T l t -> l T t"), in_=_t3(rc)).then_inc(b.sD, 16)
                b.cD += 16


def emit_final_wait(b):
    nc = b.nc
    with nc.Block() as blk:
        @blk.gpsimd
        def _(g):
            g.wait_ge(b.sD, b.cD)
            if b.cW:
                g.wait_ge(b.sW, b.cW)


def build(upto=4):
    b = _mk()
    b.sub_l0ffn = {21: 1, 22: 2}.get(upto, 3)
    if upto in (21, 22):
        upto = 2
    emit_setup(b)
    emit_l0_attn(b)
    if upto >= 2:
        emit_l0_ffn(b)
    if upto >= 3:
        emit_l1_attn(b)
    if upto >= 4:
        emit_l1_ffn(b, b.outT)
    else:
        emit_out_debug(b)
    emit_final_wait(b)
    b.stack.close()
    return b.nc


def _lhsT_blocks(WT):
    K, M = WT.shape
    return np.ascontiguousarray(
        WT.reshape(K // 128, 128, M // 128, 128).transpose(2, 1, 0, 3))


_NC_CACHE = {}


def _get_nc(upto=4):
    if upto not in _NC_CACHE:
        _NC_CACHE[upto] = build(upto)
    return _NC_CACHE[upto]


def make_inmaps(inputs):
    f32 = lambda a: np.ascontiguousarray(np.asarray(a), dtype=np.float32)
    x = f32(inputs["L0_attn_x"])
    shared = {
        "wvT": _lhsT_blocks(f32(inputs["L0_attn_Wv"]).T),
        "woT": _lhsT_blocks(f32(inputs["L0_attn_Wo"]).T),
        "bv0": f32(inputs["L0_attn_bv"]).reshape(1, D),
        "bo0": f32(inputs["L0_attn_bo"]).reshape(1, D),
        "wupT": _lhsT_blocks(f32(inputs["L0_ffn_W_up"]).T),
        "wgT": _lhsT_blocks(f32(inputs["L0_ffn_W_gate"]).T),
        "wdT": _lhsT_blocks(f32(inputs["L0_ffn_W_down"]).T),
        "su_ug": f32(inputs["L0_ffn_SU_ug"]).reshape(1, D),
        "su_dp": f32(inputs["L0_ffn_SU_dp"]).reshape(1, INTER),
        "ws_ug": f32(inputs["L0_ffn_Wscale_ug"]).reshape(1, 2 * INTER),
        "ws_dp": f32(inputs["L0_ffn_Wscale_dp"]).reshape(1, D),
        "wuT": _lhsT_blocks(f32(inputs["L1_attn_Wu"]).T),
        "wvT1": _lhsT_blocks(f32(inputs["L1_attn_Wv"]).T),
        "wg1T": _lhsT_blocks(f32(inputs["L1_attn_Wg1"]).T),
        "wg2T": _lhsT_blocks(f32(inputs["L1_attn_Wg2"]).T),
        "bu1": f32(inputs["L1_attn_bu"]).reshape(1, D),
        "bv1": f32(inputs["L1_attn_bv"]).reshape(1, D),
        "bg1": f32(inputs["L1_attn_bg1"]).reshape(1, DH),
        "bg2": f32(inputs["L1_attn_bg2"]).reshape(1, D),
        "convw": np.ascontiguousarray(f32(inputs["L1_attn_conv_w"])[:, 0, :].T).reshape(1, 3 * D),
        "convb": f32(inputs["L1_attn_conv_b"]).reshape(1, D),
        "wfcT": _lhsT_blocks(f32(inputs["L1_ffn_w_fc"]).T),
        "wpjT": _lhsT_blocks(f32(inputs["L1_ffn_w_proj"]).T),
    }
    wfc = f32(inputs["L1_ffn_w_fc"])
    wpj = f32(inputs["L1_ffn_w_proj"])
    in_maps = []
    for core in range(8):
        bidx, half = core // 2, core % 2
        sl = slice(2048 * half, 2048 * half + 2048)
        xTc = np.ascontiguousarray(x[bidx, sl, :].T).reshape(NT, P, TOK)
        fl = np.zeros((1, 64), dtype=np.float32)
        fl[0, 0] = float(half)
        fl[0, 1] = 1.0 - float(half)
        for k in range(8):
            for j in range(4):
                fl[0, 32 + 4 * k + j] = 1.0 if k == core else 0.0
        wf = wfc.reshape(8, 1024, 2048)[core].reshape(16, 128, 1024)
        wp = wpj.reshape(8, 256, 8192)[core].reshape(16, 128, 1024)
        wslc = np.ascontiguousarray(np.concatenate([wf, wp], axis=0))
        m = dict(shared)
        m["xT"] = xTc
        m["flags"] = fl
        m["wsl"] = wslc
        in_maps.append(m)
    return in_maps


_LAST_RES = None


def kernel(**inputs):
    global _LAST_RES
    upto = int(inputs.pop("_upto", 4))
    in_maps = make_inmaps(inputs)
    nc = _get_nc(upto)
    res = run_bass_kernel_spmd(nc, in_maps, core_ids=list(range(8)))
    _LAST_RES = res
    out = np.empty((4, 4096, D), dtype=np.float32)
    for core in range(8):
        bidx, half = core // 2, core % 2
        o = np.asarray(res.results[core]["outT"]).reshape(D, TOK)
        out[bidx, 2048 * half:2048 * half + 2048, :] = o.T
    return out
